# version 06
# speedup vs baseline: 2.3673x; 2.0772x over previous
import os
os.environ.setdefault("TRN_ALLOW_ICI_ALL_TO_ALL", "1")

import numpy as np
import ml_dtypes
from contextlib import ExitStack

import concourse.bass as bass
import concourse.mybir as mybir
from concourse.bass_utils import run_bass_kernel_spmd

F32 = mybir.dt.float32
BF16 = mybir.dt.bfloat16
I32 = mybir.dt.int32
U8 = mybir.dt.uint8
AF = mybir.ActivationFunctionType
ALU = mybir.AluOpType
AX = mybir.AxisListType

NCORES = 8
B = 2
SEQ = 8192
D = 1024
SLOC = SEQ // NCORES
T = B * SLOC
NF = D // 128
LN_EPS = 1e-5
TWO_PI = 6.283185307179586
HALF_PI = 1.5707963267948966
BF = ml_dtypes.bfloat16


def _ap_like(x):
    return x is not None and hasattr(x, "offset") and hasattr(x, "tensor") \
        and hasattr(x, "ap")


def _ap_key_env(x):
    try:
        name = x.tensor.name
    except Exception:
        name = str(x.tensor)
    off = int(x.offset)
    span = 1
    try:
        for step, count in x.ap:
            span += abs(int(step)) * (int(count) - 1)
    except Exception:
        span = 1 << 30
    return name, off, off + span


_RW = {
    "dma_start": ((), (), ("out",), ("in_",)),
    "matmul": ((0,), (0, 1, 2), (), ()),
    "transpose": ((0,), (1, 2), (), ()),
    "activation": ((0,), (1,), ("accum_out",), ("bias", "scale")),
    "copy_predicated": ((0,), (0, 1, 2), (), ()),
    "memset": ((0,), (), (), ()),
    "memzero": ((0,), (), (), ()),
    "iota": ((0,), (), (), ()),
    "store": ((0,), (), (), ()),
    "collective_compute": ((), (), (), ()),
}
_DMA_RING = 12


class P:

    def __init__(self, name, strict=True):
        self.nc = bass.Bass("TRN2", target_bir_lowering=False)
        self.name = name
        self.strict = strict
        self.ops = []
        self.cnts = {"T": 0, "V": 0, "S": 0, "G": 0, "CC": 0}
        for k in ("Ds", "Dg"):
            for s in range(_DMA_RING):
                self.cnts[f"{k}{s}"] = 0
        self._ndma = {"Ds": 0, "Dg": 0}
        self._acc = {}
        self._floor = {}
        self._barrier = {}
        self._klg = {}
        self._stack = None
        self._scopes = []

    ENG_CLS = {"tensor": "T", "vector": "V", "scalar": "S", "gpsimd": "G",
               "sync": None}

    def inp(self, name, shape, dtype=F32):
        return self.nc.dram_tensor(name, list(shape), dtype, kind="ExternalInput").ap()

    def out(self, name, shape, dtype=F32):
        return self.nc.dram_tensor(name, list(shape), dtype, kind="ExternalOutput").ap()

    def sb(self, name, shape, dtype=F32):
        return self._stack.enter_context(self.nc.sbuf_tensor(name, list(shape), dtype))

    def ps(self, name, shape, dtype=F32):
        return self._stack.enter_context(self.nc.psum_tensor(name, list(shape), dtype))

    def push_scope(self):
        s = ExitStack()
        self._scopes.append(self._stack)
        self._stack = s

    def pop_scope(self):
        self._stack.close()
        self._stack = self._scopes.pop()
        self._barrier = {k: v for k, v in self.cnts.items() if v > 0}

    def _note(self, ap, is_write, cls, cnt_after, needed):
        name, lo, hi = _ap_key_env(ap)
        lst = self._acc.setdefault(name, [])
        flo = self._floor.get(name)
        if flo:
            for k, v in flo.items():
                if needed.get(k, 0) < v:
                    needed[k] = v
        for (l2, h2, rw2, cls2, cnt2) in lst:
            if cls2 == cls and cnt2 == cnt_after:
                continue
            if l2 < hi and lo < h2:
                if is_write or rw2 == "w":
                    if needed.get(cls2, 0) < cnt2:
                        needed[cls2] = cnt2
        lst.append((lo, hi, "w" if is_write else "r", cls, cnt_after))
        if len(lst) > 48:
            dropped = lst[:24]
            del lst[:24]
            f = self._floor.setdefault(name, {})
            for (_, _, _, c2, n2) in dropped:
                if f.get(c2, 0) < n2:
                    f[c2] = n2

    def op(self, engine, method, *args, _inc=1, _cls=None, **kwargs):
        def emit(eng, m=method, a=args, k=kwargs):
            return getattr(eng, m)(*a, **k)

        if _cls is None:
            _cls = self.ENG_CLS[engine] or "G"
        if self.strict:
            self.ops.append((engine, emit, dict(self.cnts), _cls, _inc))
            self.cnts[_cls] += _inc
            return

        needed = dict(self._barrier)
        rule = _RW.get(method)
        w_idx, r_idx, w_kw, r_kw = rule if rule else ((0,), (), (), ())
        cnt_after = self.cnts[_cls] + _inc
        aps_w, aps_r = [], []
        if method == "collective_compute":
            for a in kwargs.get("ins", []):
                aps_r.append(a)
            for a in kwargs.get("outs", []):
                aps_w.append(a)
        else:
            for i, a in enumerate(args):
                if not _ap_like(a):
                    continue
                if i in w_idx:
                    aps_w.append(a)
                if i in r_idx or (rule is None and i not in w_idx) or \
                   (rule is not None and i not in w_idx and i not in r_idx):
                    aps_r.append(a)
            for kwn, a in kwargs.items():
                if not _ap_like(a):
                    continue
                if kwn in w_kw:
                    aps_w.append(a)
                else:
                    aps_r.append(a)
        for a in aps_r:
            self._note(a, False, _cls, cnt_after, needed)
        for a in aps_w:
            self._note(a, True, _cls, cnt_after, needed)
        needed.pop(None, None)
        if engine == "tensor":
            needed.pop("T", None)
        clos = dict(needed)
        for k, v in list(needed.items()):
            c2 = self._klg.get(k, {}).get(v)
            if c2:
                for k3, v3 in c2.items():
                    if clos.get(k3, 0) < v3:
                        clos[k3] = v3
        pruned = {}
        for k, v in needed.items():
            dominated = False
            for k2, v2 in needed.items():
                if k2 == k:
                    continue
                c2 = self._klg.get(k2, {}).get(v2)
                if c2 and c2.get(k, 0) >= v:
                    dominated = True
                    break
            if not dominated:
                pruned[k] = v
        clos[_cls] = cnt_after
        self._klg.setdefault(_cls, {})[cnt_after] = clos
        self.ops.append((engine, emit, pruned, _cls, _inc))
        self.cnts[_cls] += _inc

    def vop(self, method, *a, **k):
        self.op("vector", method, *a, **k)

    def sop(self, method, *a, **k):
        self.op("scalar", method, *a, **k)

    def gop(self, method, *a, **k):
        self.op("gpsimd", method, *a, **k)

    def dma(self, out, in_, engine="sync"):
        base = "Dg" if engine == "gpsimd" else "Ds"
        if self.strict:
            self.op(engine, "dma_start", _inc=16, _cls=base + "0", out=out,
                    in_=in_)
            return
        j = self._ndma[base]
        self._ndma[base] = j + 1
        slot = j % _DMA_RING
        cls = f"{base}{slot}"
        extra = {}
        if j >= _DMA_RING:
            extra[cls] = 16 * (j // _DMA_RING)
        self.op(engine, "dma_start", _inc=16, _cls=cls, out=out, in_=in_)
        if extra:
            eng, emit, needed, cls2, inc = self.ops[-1]
            for k2, v2 in extra.items():
                if needed.get(k2, 0) < v2:
                    needed[k2] = v2

    def mm(self, out, lhsT, rhs, start=True, stop=True, tile_position=None):
        self.op("tensor", "matmul", out, lhsT, rhs, start=start, stop=stop,
                tile_position=tile_position)

    def transpose(self, out_ps, in_sb, ident):
        self.op("tensor", "transpose", out_ps, in_sb, ident)

    def collective(self, kind, op_, rg, ins, outs):
        self.op("gpsimd", "collective_compute", kind, op_, rg,
                ins=ins, outs=outs, _inc=1, _cls="CC")

    def build(self, body):
        with ExitStack() as stack:
            self._stack = stack
            sems = {k: stack.enter_context(self.nc.semaphore("sem" + k))
                    for k in self.cnts.keys()}
            block = stack.enter_context(self.nc.Block())
            body(self)
            by = {}
            for eng, emit, waits, cls, inc in self.ops:
                by.setdefault(eng, []).append((emit, waits, cls, inc))
            totals = dict(self.cnts)

            def make(eng_name):
                def f(eng):
                    done = {k: 0 for k in self.cnts.keys()}
                    own = self.ENG_CLS.get(eng_name)
                    for emit, waits, cls, inc in by.get(eng_name, []):
                        for k, v in waits.items():
                            if v > done[k]:
                                eng.wait_ge(sems[k], v)
                                done[k] = v
                        r = emit(eng)
                        if r is not None:
                            r.then_inc(sems[cls], inc)
                        else:
                            eng.nop().then_inc(sems[cls], inc)
                    if eng_name == "sync":
                        for k, v in totals.items():
                            if v > done[k]:
                                eng.wait_ge(sems[k], v)
                return f

            block.sync(make("sync"))
            block.tensor(make("tensor"))
            block.vector(make("vector"))
            block.scalar(make("scalar"))
            block.gpsimd(make("gpsimd"))
        return self.nc


def make_ctx(p, TW, n_rows=4):
    ctx = {}
    ctx["TW"] = TW
    ctx["onesP"] = p.sb("cx_onesP", [128, 1], BF16)
    ctx["ones1f"] = p.sb("cx_ones1f", [1, 128], F32)
    ctx["scr16a"] = p.sb("cx_scr16a", [128, 512], BF16)
    ctx["scr16b"] = p.sb("cx_scr16b", [128, 512], BF16)
    ctx["scrfa"] = p.sb("cx_scrfa", [128, 512], F32)
    ctx["scrfb"] = p.sb("cx_scrfb", [128, 512], F32)
    ctx["rows"] = [p.sb(f"cx_row{i}", [1, TW], F32) for i in range(n_rows)]
    ctx["rps"] = p.ps("cx_rps", [1, 512], F32)
    ctx["rps2"] = p.ps("cx_rps2", [1, 512], F32)
    ctx["bc1"] = p.ps("cx_bc1", [128, 512], F32)
    ctx["bc2"] = p.ps("cx_bc2", [128, 512], F32)
    ctx["mmps"] = p.ps("cx_mmps", [128, 512], F32)
    ctx["wt"] = p.sb("cx_wt", [128, 32 * 128], BF16)
    ctx["epsr"] = p.sb("cx_epsr", [1, 1], F32)
    p.vop("memset", ctx["onesP"][:], 1.0)
    p.vop("memset", ctx["ones1f"][:], 1.0)
    p.vop("memset", ctx["epsr"][:], LN_EPS)
    return ctx


def ntiles(tw, n=512):
    return [(i * n, min((i + 1) * n, tw)) for i in range((tw + n - 1) // n)]


def h_norm(p, ctx, read_chunk, write_chunk, nchunk, rms=False, nfeat=None, TW=None):
    TW = TW or ctx["TW"]
    onesP = ctx["onesP"]
    ones1f = ctx["ones1f"]
    scr16 = ctx["scr16a"]
    rps, rps2 = ctx["rps"], ctx["rps2"]
    mean, var, rstd, tmp = ctx["rows"][:4]
    bc1, bc2 = ctx["bc1"], ctx["bc2"]
    nfeat = nfeat or (nchunk * 128)

    for c0, c1 in ntiles(TW):
        w = c1 - c0
        for fc in range(nchunk):
            src = read_chunk(fc)
            if not rms:
                p.vop("tensor_copy", scr16[:, 0:w], src[:, c0:c1])
                p.mm(rps[0:1, 0:w], onesP[:, 0:1], scr16[:, 0:w],
                     start=(fc == 0), stop=(fc == nchunk - 1))
            p.sop("activation", scr16[:, 0:w], src[:, c0:c1], AF.Square)
            p.mm(rps2[0:1, 0:w], onesP[:, 0:1], scr16[:, 0:w],
                 start=(fc == 0), stop=(fc == nchunk - 1))
        if not rms:
            p.vop("tensor_scalar", mean[0:1, c0:c1], rps[0:1, 0:w],
                  1.0 / nfeat, None, ALU.mult)
        p.vop("tensor_scalar", var[0:1, c0:c1], rps2[0:1, 0:w],
              1.0 / nfeat, None, ALU.mult)
    if not rms:
        p.sop("activation", tmp[0:1, 0:TW], mean[0:1, 0:TW], AF.Square)
        p.vop("tensor_sub", var[0:1, 0:TW], var[0:1, 0:TW], tmp[0:1, 0:TW])
    p.sop("activation", tmp[0:1, 0:TW], var[0:1, 0:TW], AF.Sqrt,
          bias=ctx["epsr"][0:1, 0:1])
    p.vop("reciprocal", rstd[0:1, 0:TW], tmp[0:1, 0:TW])
    if not rms:
        p.vop("tensor_mul", tmp[0:1, 0:TW], mean[0:1, 0:TW], rstd[0:1, 0:TW])
    for c0, c1 in ntiles(TW):
        w = c1 - c0
        p.mm(bc1[:, 0:w], ones1f[0:1, :], rstd[0:1, c0:c1])
        if not rms:
            p.mm(bc2[:, 0:w], ones1f[0:1, :], tmp[0:1, c0:c1])
        for fc in range(nchunk):
            src = read_chunk(fc)
            p.vop("tensor_mul", ctx["scrfa"][:, 0:w], src[:, c0:c1], bc1[:, 0:w])
            if not rms:
                p.vop("tensor_sub", ctx["scrfa"][:, 0:w],
                      ctx["scrfa"][:, 0:w], bc2[:, 0:w])
            write_chunk(fc, c0, c1)


def h_ln_to(p, ctx, src_sb, dst16, nchunk, gamma=None, rms=False, TW=None,
            src_stride=None, dst_stride=None, src_off=0, dst_off=0):
    TW = TW or ctx["TW"]
    sst = src_stride or TW
    dst_st = dst_stride or TW

    def rd(fc):
        return src_sb[:, src_off + fc * sst: src_off + fc * sst + TW]

    def wr(fc, c0, c1):
        dst = dst16[:, dst_off + fc * dst_st + c0: dst_off + fc * dst_st + c1]
        if gamma is None:
            p.vop("tensor_copy", dst, ctx["scrfa"][:, 0:c1 - c0])
        else:
            p.vop("tensor_scalar", dst, ctx["scrfa"][:, 0:c1 - c0],
                  gamma[:, fc:fc + 1], None, ALU.mult)
    h_norm(p, ctx, rd, wr, nchunk, rms=rms, TW=TW)


def h_proj(p, ctx, wT_dram, xin16, kchunks, mlist, epilogue, TW=None,
           xin_off=0, w_row0=0):
    TW = TW or ctx["TW"]
    wt = ctx["wt"]
    ps = ctx["mmps"]
    dbl = (2 * kchunks * 128 <= 4096)
    for mi, m in enumerate(mlist):
        so = (mi % 2) * kchunks * 128 if dbl else 0
        p.dma(wt[:, so: so + kchunks * 128].rearrange("f (k m) -> f k m", k=kchunks),
              wT_dram[w_row0: w_row0 + kchunks * 128,
                      m * 128:(m + 1) * 128].rearrange("(k f) m -> f k m", f=128),
              engine="gpsimd")
        for c0, c1 in ntiles(TW):
            w = c1 - c0
            for k in range(kchunks):
                p.mm(ps[:, 0:w], wt[:, so + k * 128: so + (k + 1) * 128],
                     xin16[:, xin_off + k * TW + c0: xin_off + k * TW + c1],
                     start=(k == 0), stop=(k == kchunks - 1))
            epilogue(p, mi, c0, c1, ps)


def h_epi_copy(dst16, TW):
    def epi(p, mi, c0, c1, ps):
        p.vop("tensor_copy", dst16[:, mi * TW + c0: mi * TW + c1], ps[:, 0:c1 - c0])
    return epi


def h_epi_add_f32(dstf, TW):
    def epi(p, mi, c0, c1, ps):
        d = dstf[:, mi * TW + c0: mi * TW + c1]
        p.vop("tensor_add", d, d, ps[:, 0:c1 - c0])
    return epi


def build_launch1():
    p = P("launch1")
    C = 128
    NCH = SEQ // C

    xb = p.inp("xb", [NF, 128, SEQ])
    w_inT = p.inp("w_inT", [1024, 768])
    mask_in = p.inp("mask_intra", [64, 128], U8)
    o_out = p.out("o", [2, 128, SEQ])

    def body(p):
        ctx = make_ctx(p, C)
        xc = p.sb("xc", [128, NF * C], F32)
        xln = p.sb("xln", [128, NF * C], BF16)
        wsb = p.sb("wsb", [128, NF * 768], BF16)
        S0 = p.sb("S0", [128, 128], F32)
        S1 = p.sb("S1", [128, 128], F32)
        S016 = p.sb("S016", [128, 128], BF16)
        S116 = p.sb("S116", [128, 128], BF16)
        qch = p.sb("qch", [128, C], BF16)
        kch = p.sb("kch", [128, C], F32)
        vch = p.sb("vch", [128, C], BF16)
        gch = p.sb("gch", [128, C], F32)
        Ach = p.sb("Ach", [128, C], F32)
        q1b = p.sb("q1b", [128, C], BF16)
        k1b = p.sb("k1b", [128, C], BF16)
        qib = p.sb("qib", [128, C], BF16)
        k2b = p.sb("k2b", [128, C], BF16)
        vTa = p.sb("vTa", [64, 128], BF16)
        vTb = p.sb("vTb", [64, 128], BF16)
        kTa = p.sb("kTa", [64, 128], BF16)
        kTb = p.sb("kTb", [64, 128], BF16)
        P16 = p.sb("P16", [64, 128], BF16)
        Ecol = p.sb("Ecol", [128, 4], F32)
        zcol = p.sb("zcol", [128, 1], F32)
        osb = p.sb("osb", [128, C], F32)
        msk = p.sb("msk", [64, 128], U8)
        ident = p.sb("ident", [128, 128], BF16)
        tps = p.ps("tps", [64, 128], BF16)

        p.dma(msk[:, :], mask_in[:, :])
        p.vop("memset", S0[:], 0.0)
        p.vop("memset", S1[:], 0.0)
        p.vop("memset", zcol[:], 0.0)
        idi = p.sb("idi", [128, 128], I32)
        idp = p.sb("idp", [128, 1], I32)
        p.gop("iota", idi[:, :], [[1, 128]], base=0, channel_multiplier=0)
        p.gop("iota", idp[:, :], [[0, 1]], base=0, channel_multiplier=1)
        idf = p.sb("idf", [128, 128], F32)
        idpf = p.sb("idpf", [128, 1], F32)
        p.vop("tensor_copy", idf[:, :], idi[:, :])
        p.vop("tensor_copy", idpf[:, :], idp[:, :])
        p.vop("tensor_scalar", idf[:, :], idf[:, :], idpf[:, 0:1], None, ALU.is_equal)
        p.vop("tensor_copy", ident[:, :], idf[:, :])

        for k in range(NF):
            p.dma(wsb[:, k * 768:(k + 1) * 768],
                  w_inT[k * 128:(k + 1) * 128, :], engine="gpsimd")

        for ch in range(NCH):
            s0 = ch * C
            for fc in range(NF):
                p.dma(xc[:, fc * C:(fc + 1) * C], xb[fc, :, s0:s0 + C])
            h_ln_to(p, ctx, xc, xln, NF)

            for unit in range(2):
                S = S0 if unit == 0 else S1
                S16 = S016 if unit == 0 else S116
                for mg in range(3):
                    for k in range(NF):
                        p.mm(ctx["mmps"][:, 0:C],
                             wsb[:, k * 768 + unit * 384 + mg * 128:
                                 k * 768 + unit * 384 + (mg + 1) * 128],
                             xln[:, k * C:(k + 1) * C],
                             start=(k == 0), stop=(k == NF - 1))
                    if mg == 0:
                        p.vop("tensor_copy", vch[:, :], ctx["mmps"][:, 0:C])
                    elif mg == 1:
                        p.sop("activation", qch[:, :], ctx["mmps"][:, 0:C], AF.Silu)
                    else:
                        p.sop("activation", kch[:, :], ctx["mmps"][:, 0:C],
                              AF.Sigmoid, scale=-1.0)
                        p.vop("tensor_scalar", gch[:, :], kch[:, :], -1.0, 1.0,
                              ALU.mult, ALU.add)
                        p.sop("activation", gch[:, :], gch[:, :], AF.Ln)
                p.vop("tensor_tensor_scan", Ach[:, :], gch[:, :], gch[:, :],
                      zcol[:, 0:1], ALU.add, ALU.bypass)

                for j in range(4):
                    lo, hi = 32 * j, 32 * (j + 1)
                    amid = Ach[:, lo + 15:lo + 16]
                    aend = Ach[:, hi - 1:hi]
                    abase = Ach[:, lo - 1:lo] if j > 0 else zcol[:, 0:1]
                    namid = Ecol[:, 0:1]
                    nabase = Ecol[:, 1:2]
                    p.vop("tensor_scalar", namid, amid, -1.0, None, ALU.mult)
                    p.vop("tensor_scalar", nabase, abase, -1.0, None, ALU.mult)
                    p.sop("activation", ctx["scrfa"][:, 0:32], Ach[:, lo:hi],
                          AF.Exp, bias=namid)
                    p.vop("tensor_mul", q1b[:, lo:hi], qch[:, lo:hi],
                          ctx["scrfa"][:, 0:32])
                    p.sop("activation", ctx["scrfa"][:, 0:32], Ach[:, lo:hi],
                          AF.Exp, bias=amid, scale=-1.0)
                    p.vop("tensor_mul", k1b[:, lo:hi], kch[:, lo:hi],
                          ctx["scrfa"][:, 0:32])
                    p.sop("activation", ctx["scrfa"][:, 0:32], Ach[:, lo:hi],
                          AF.Exp, bias=nabase)
                    p.vop("tensor_mul", qib[:, lo:hi], qch[:, lo:hi],
                          ctx["scrfa"][:, 0:32])
                    p.sop("activation", ctx["scrfa"][:, 0:32], Ach[:, lo:hi],
                          AF.Exp, bias=aend, scale=-1.0)
                    p.vop("tensor_mul", k2b[:, lo:hi], kch[:, lo:hi],
                          ctx["scrfa"][:, 0:32])

                for src, dsts in ((k2b, (kTa, kTb)), (vch, (vTa, vTb))):
                    for hfi, dst in enumerate(dsts):
                        p.transpose(tps[:, :], src[:, hfi * 64:(hfi + 1) * 64], ident[:, :])
                        p.vop("tensor_copy", dst[:, :], tps[0:64, :])

                obank = ctx["bc1"]
                pbank = ctx["bc2"]
                sbank = ctx["mmps"]
                for j in range(4):
                    lo, hi = 32 * j, 32 * (j + 1)
                    aend = Ach[:, hi - 1:hi]
                    abase = Ach[:, lo - 1:lo] if j > 0 else zcol[:, 0:1]
                    pb = (j % 2) * 32
                    p.mm(pbank[pb:pb + 32, 0:32], k1b[:, lo:hi], q1b[:, lo:hi],
                         tile_position=(0, pb))
                    p.vop("memset", P16[pb:pb + 32, lo:hi], 0.0)
                    p.vop("copy_predicated", P16[pb:pb + 32, lo:hi],
                          msk[pb:pb + 32, lo:hi], pbank[pb:pb + 32, 0:32])
                    p.vop("tensor_copy", S16[:, :], S[:, :])
                    p.mm(obank[:, lo:hi], S16[:, :], qib[:, lo:hi],
                         start=True, stop=False)
                    vT = vTa if j < 2 else vTb
                    vbase = (j % 2) * 32
                    p.mm(obank[:, lo:hi], vT[vbase:vbase + 32, :],
                         P16[pb:pb + 32, lo:hi], start=False, stop=True)
                    kT = kTa if j < 2 else kTb
                    p.mm(sbank[:, 0:128], kT[vbase:vbase + 32, :],
                         vT[vbase:vbase + 32, :])
                    p.vop("tensor_scalar", Ecol[:, 1:2], abase, -1.0, None, ALU.mult)
                    p.sop("activation", Ecol[:, 2:3], aend, AF.Exp, bias=Ecol[:, 1:2])
                    p.vop("scalar_tensor_tensor", S[:, :], S[:, :], Ecol[:, 2:3],
                          sbank[:, 0:128], ALU.mult, ALU.add)

                p.vop("tensor_copy", osb[:, :], obank[:, 0:C])
                p.dma(o_out[unit, :, s0:s0 + C], osb[:, :])

    nc = p.build(body)
    return nc


def stage_launch1(inputs, x0):
    w_inT = np.ascontiguousarray(np.asarray(inputs["L0_attn_W_in"]).T)
    mask = (np.arange(32)[:, None] <= np.arange(32)[None, :])
    mask4 = np.tile(mask, (2, 4)).astype(np.uint8)
    xb_feat = []
    for b in range(B):
        xb_feat.append(np.ascontiguousarray(
            np.asarray(x0[b]).reshape(SEQ, NF, 128).transpose(1, 2, 0)))
    maps = []
    for c in range(NCORES):
        b = c // 4
        h0 = (2 * c) % 8
        groups = []
        for unit in range(2):
            h = h0 + unit
            for base in (0, 1024, 2048):
                groups.append(w_inT[:, base + h * 128: base + (h + 1) * 128])
        wcat = np.ascontiguousarray(np.concatenate(groups, axis=1))
        maps.append({"xb": xb_feat[b], "w_inT": wcat, "mask_intra": mask4})
    return maps


def build_launch2():
    p = P("launch2")
    TW = 1024

    x0 = p.inp("x0", [NF, 128, T])
    og_in = p.inp("og", [NF, 128, T])
    wg1T = p.inp("wg1T", [1024, 128])
    wg2T = p.inp("wg2T", [128, 1024])
    lnw = p.inp("lnw", [128, NF])
    woutT = p.inp("woutT", [1024, 1024])
    g1in = p.inp("g1", [128, NF])
    wdown = p.inp("wdown", [1024, 512])
    wq = p.inp("wq", [512, 512])
    wq_sw = p.inp("wq_sw", [512, 512])
    wk = p.inp("wk", [512, 512])
    wk_sw = p.inp("wk_sw", [512, 512])
    wv = p.inp("wv", [512, 512])
    pos_i = p.inp("pos", [1, T], I32)
    invfreq = p.inp("invfreq", [128, 1])
    signc = p.inp("signc", [128, 1])

    x1_out = p.out("x1", [NF, 128, T])
    q_out = p.out("q", [4, 128, T], BF16)
    k_out = p.out("k", [4, 128, T], BF16)
    v_out = p.out("vv", [4, 128, T], BF16)

    def body(p):
        ctx = make_ctx(p, TW)
        xsb = p.sb("xsb", [128, NF * TW], F32)
        a16 = p.sb("a16", [128, NF * TW], BF16)
        b16 = p.sb("b16", [128, NF * TW], BF16)
        c16 = p.sb("c16", [128, 4 * TW], BF16)
        d16 = p.sb("d16", [128, 4 * TW], BF16)
        t1 = p.sb("t1", [128, TW], BF16)
        gam = p.sb("gam", [128, NF], F32)
        cosT = p.sb("cosT", [128, TW], F32)
        sinTs = p.sb("sinTs", [128, TW], F32)
        ivf = p.sb("ivf", [128, 1], F32)
        sgn = p.sb("sgn", [128, 1], F32)
        posi_sb = p.sb("posi_sb", [1, TW], I32)
        angi = p.sb("angi", [128, 512], I32)

        p.dma(ivf[:, :], invfreq[:, :])
        p.dma(sgn[:, :], signc[:, :])

        for half in range(2):
            t0 = half * TW
            for fc in range(NF):
                p.dma(xsb[:, fc * TW:(fc + 1) * TW], x0[fc, :, t0:t0 + TW])
                p.dma(b16[:, fc * TW:(fc + 1) * TW], og_in[fc, :, t0:t0 + TW],
                      engine="gpsimd")

            h_ln_to(p, ctx, xsb, a16, NF)

            h_proj(p, ctx, wg1T, a16, NF, [0], h_epi_copy(t1, TW))

            def epi_gate(p_, mi, c0, c1, ps):
                p_.sop("activation", ctx["scr16b"][:, 0:c1 - c0], ps[:, 0:c1 - c0],
                       AF.Sigmoid)
                dst = b16[:, mi * TW + c0: mi * TW + c1]
                p_.vop("tensor_mul", dst, dst, ctx["scr16b"][:, 0:c1 - c0])
            h_proj(p, ctx, wg2T, t1, 1, list(range(NF)), epi_gate)

            p.dma(gam[:, :], lnw[:, :])
            h_ln_to(p, ctx, b16, b16, NF, gamma=gam)

            h_proj(p, ctx, woutT, b16, NF, list(range(NF)), h_epi_add_f32(xsb, TW))
            for fc in range(NF):
                p.dma(x1_out[fc, :, t0:t0 + TW], xsb[:, fc * TW:(fc + 1) * TW])

            h_ln_to(p, ctx, xsb, xsb, NF)
            p.dma(gam[:, :], g1in[:, :])
            h_ln_to(p, ctx, xsb, a16, NF, gamma=gam, rms=True)

            h_proj(p, ctx, wdown, a16, NF, [0, 1, 2, 3], h_epi_copy(c16, TW))

            p.dma(posi_sb[0:1, :], pos_i[0:1, t0:t0 + TW])
            p.vop("tensor_copy", ctx["rows"][0][0:1, :], posi_sb[0:1, :])
            for c0, c1 in ntiles(TW):
                w = c1 - c0
                p.mm(ctx["bc1"][:, 0:w], ctx["ones1f"][0:1, :],
                     ctx["rows"][0][0:1, c0:c1])
                p.vop("tensor_scalar", ctx["scrfa"][:, 0:w], ctx["bc1"][:, 0:w],
                      ivf[:, 0:1], None, ALU.mult)
                for dst, phase in ((sinTs, 0.0), (cosT, HALF_PI)):
                    if phase != 0.0:
                        p.vop("tensor_scalar", ctx["scrfb"][:, 0:w],
                              ctx["scrfa"][:, 0:w], phase, None, ALU.add)
                    else:
                        p.vop("tensor_copy", ctx["scrfb"][:, 0:w],
                              ctx["scrfa"][:, 0:w])
                    p.vop("tensor_scalar", ctx["scrfb"][:, 0:w],
                          ctx["scrfb"][:, 0:w], 1.0 / TWO_PI, None, ALU.mult)
                    p.vop("tensor_copy", angi[:, 0:w], ctx["scrfb"][:, 0:w])
                    p.vop("tensor_copy", dst[:, c0:c1], angi[:, 0:w])
                    p.vop("scalar_tensor_tensor", ctx["scrfb"][:, 0:w],
                          dst[:, c0:c1], -TWO_PI, ctx["scrfa"][:, 0:w],
                          ALU.mult, ALU.add)
                    if phase != 0.0:
                        p.vop("tensor_scalar", ctx["scrfb"][:, 0:w],
                              ctx["scrfb"][:, 0:w], phase, None, ALU.add)
                    p.vop("tensor_scalar", dst[:, c0:c1], ctx["scrfb"][:, 0:w],
                          np.pi, None, ALU.is_gt)
                    p.vop("scalar_tensor_tensor", ctx["scrfb"][:, 0:w],
                          dst[:, c0:c1], -TWO_PI, ctx["scrfb"][:, 0:w],
                          ALU.mult, ALU.add)
                    p.vop("tensor_scalar", dst[:, c0:c1], ctx["scrfb"][:, 0:w],
                          -np.pi, None, ALU.is_lt)
                    p.vop("scalar_tensor_tensor", ctx["scrfb"][:, 0:w],
                          dst[:, c0:c1], TWO_PI, ctx["scrfb"][:, 0:w],
                          ALU.mult, ALU.add)
                    p.sop("activation", dst[:, c0:c1], ctx["scrfb"][:, 0:w], AF.Sin)
            p.vop("tensor_scalar", sinTs[:, :], sinTs[:, :], sgn[:, 0:1], None, ALU.mult)

            h_proj(p, ctx, wq, c16, 4, [0, 1, 2, 3], h_epi_copy(b16, TW))

            def epi_k(p_, mi, c0, c1, ps):
                d = b16[:, (4 + mi) * TW + c0: (4 + mi) * TW + c1]
                p_.vop("tensor_copy", d, ps[:, 0:c1 - c0])
            h_proj(p, ctx, wk, c16, 4, [0, 1, 2, 3], epi_k)
            h_proj(p, ctx, wv, c16, 4, [0, 1, 2, 3], h_epi_copy(d16, TW))

            def mk_rope_epi(base):
                def epi(p_, mi, c0, c1, ps):
                    dst = b16[:, (base + mi) * TW + c0: (base + mi) * TW + c1]
                    p_.vop("tensor_mul", ctx["scrfb"][:, 0:c1 - c0], ps[:, 0:c1 - c0],
                           sinTs[:, c0:c1])
                    p_.vop("tensor_mul", ctx["scrfa"][:, 0:c1 - c0], dst, cosT[:, c0:c1])
                    p_.vop("tensor_add", dst, ctx["scrfa"][:, 0:c1 - c0],
                           ctx["scrfb"][:, 0:c1 - c0])
                return epi
            h_proj(p, ctx, wq_sw, c16, 4, [0, 1, 2, 3], mk_rope_epi(0))
            h_proj(p, ctx, wk_sw, c16, 4, [0, 1, 2, 3], mk_rope_epi(4))

            for m in range(4):
                p.dma(q_out[m, :, t0:t0 + TW], b16[:, m * TW:(m + 1) * TW])
                p.dma(k_out[m, :, t0:t0 + TW], b16[:, (4 + m) * TW:(5 + m) * TW])
                p.dma(v_out[m, :, t0:t0 + TW], d16[:, m * TW:(m + 1) * TW])

    nc = p.build(body)
    return nc


def build_launch3():
    p = P("launch3")
    HD = 32
    QT = 512
    KC = 128
    KHI = [(SEQ - SLOC + (qt + 1) * QT) // KC for qt in range(2)]

    x1 = p.inp("x1", [NF, 128, T])
    q_in = p.inp("q", [4, 128, T], BF16)
    kf = p.inp("kf", [B, 4, 128, SEQ], BF16)
    vfT = p.inp("vfT", [B, SEQ, 512], BF16)
    wupT = p.inp("wupT", [512, 1024])
    masks = p.inp("masks", [2, KHI[1], 128, QT], U8)
    res_out = p.out("res", [NF, 128, T])

    def body(p):
        ctx = make_ctx(p, T)
        xsb = p.sb("xsb", [128, NF * T], F32)
        qown = p.sb("qown", [128, 4 * T], BF16)
        Kg = p.sb("Kg", [128, SEQ], BF16)
        VTg = p.sb("VTg", [128, 64 * 128], BF16)
        aos = p.sb("aos", [32, 16 * SLOC], BF16)
        P16 = p.sb("P16", [128, QT], BF16)
        mskb = p.sb("mskb", [128, QT], BF16)
        negc = p.sb("negc", [128, 1], F32)
        cmaxq = p.sb("cmaxq", [128, 1], F32)
        cmaxk = p.sb("cmaxk", [128, 1], F32)
        crow = p.sb("crow", [1, 4], F32)
        identf = p.sb("identf", [128, 128], F32)
        wup_sb = p.sb("wup_sb", [32, 16 * 128], BF16)
        ops_ = p.ps("ops_", [32, QT], F32)
        osb = p.sb("osb", [32, QT], F32)

        idi = p.sb("idi", [128, 128], I32)
        idp = p.sb("idp", [128, 1], I32)
        idpf = p.sb("idpf", [128, 1], F32)
        p.gop("iota", idi[:, :], [[1, 128]], base=0, channel_multiplier=0)
        p.gop("iota", idp[:, :], [[0, 1]], base=0, channel_multiplier=1)
        p.vop("tensor_copy", identf[:, :], idi[:, :])
        p.vop("tensor_copy", idpf[:, :], idp[:, :])
        p.vop("tensor_scalar", identf[:, :], identf[:, :], idpf[:, 0:1], None,
              ALU.is_equal)

        for fc in range(NF):
            p.dma(xsb[:, fc * T:(fc + 1) * T], x1[fc, :, :])
        h_ln_to(p, ctx, xsb, xsb, NF)
        for m in range(4):
            p.dma(qown[:, m * T:(m + 1) * T], q_in[m, :, :])


        def colmax_abs(dst11, col):
            p.transpose(ctx["rps"][0:1, 0:128], col, identf[:, :])
            p.vop("reduce_max", dst11, ctx["rps"][0:1, 0:128], AX.X)

        for b in range(B):
            tb0 = b * SLOC
            for g in range(4):
                p.dma(Kg[:, :], kf[b, g, :, :])
                vslice = vfT[b, :, 128 * g:128 * (g + 1)]
                for i in range(8):
                    p.dma(VTg[:, i * 8 * 128:(i + 1) * 8 * 128].rearrange(
                              "p (kc h) -> p kc h", kc=8),
                          vslice[i * 1024:(i + 1) * 1024, :].rearrange(
                              "(kc q) h -> q kc h", q=128))
                p.vop("reduce_max", cmaxq[:, 0:1],
                      qown[:, g * T + tb0: g * T + tb0 + SLOC], AX.X,
                      apply_absolute_value=True)
                p.vop("reduce_max", cmaxk[:, 0:1], Kg[:, :], AX.X,
                      apply_absolute_value=True)
                colmax_abs(crow[0:1, 0:1], cmaxq[:, 0:1])
                colmax_abs(crow[0:1, 1:2], cmaxk[:, 0:1])
                p.vop("tensor_mul", crow[0:1, 2:3], crow[0:1, 0:1], crow[0:1, 1:2])
                p.vop("tensor_scalar", crow[0:1, 2:3], crow[0:1, 2:3],
                      -float(np.sqrt(HD)), None, ALU.mult)
                p.mm(ctx["bc1"][:, 0:1], ctx["ones1f"][0:1, :], crow[0:1, 2:3])
                p.vop("tensor_copy", negc[:, 0:1], ctx["bc1"][:, 0:1])

                for h in range(4):
                    hb = 32 * h
                    for qt in range(2):
                        ql = tb0 + qt * QT
                        khi = KHI[qt]
                        for kc in range(khi):
                            last = (kc == khi - 1)
                            p.dma(mskb[:, 0:QT], masks[qt, kc, :, :],
                                  engine="gpsimd")
                            p.mm(ctx["bc2"][:, 0:QT],
                                 Kg[hb:hb + 32, kc * KC:(kc + 1) * KC],
                                 qown[hb:hb + 32, g * T + ql: g * T + ql + QT],
                                 tile_position=(hb, 0))
                            p.sop("activation", P16[:, 0:QT], ctx["bc2"][:, 0:QT],
                                  AF.Exp, bias=negc[:, 0:1],
                                  scale=1.0 / float(np.sqrt(HD)))
                            p.vop("tensor_mul", P16[:, 0:QT], P16[:, 0:QT],
                                  mskb[:, 0:QT])
                            p.mm(ctx["rps"][0:1, 0:QT], ctx["onesP"][:, 0:1],
                                 P16[:, 0:QT], start=(kc == 0), stop=last)
                            p.mm(ops_[0:32, 0:QT],
                                 VTg[:, kc * 128 + hb: kc * 128 + hb + 32],
                                 P16[:, 0:QT], start=(kc == 0), stop=last)
                        p.vop("reciprocal", ctx["rows"][0][0:1, 0:QT],
                              ctx["rps"][0:1, 0:QT])
                        p.mm(ctx["mmps"][0:32, 0:QT], ctx["ones1f"][0:1, 0:32],
                             ctx["rows"][0][0:1, 0:QT])
                        p.vop("tensor_copy", osb[:, 0:QT], ops_[0:32, 0:QT])
                        p.vop("tensor_mul",
                              aos[:, (g * 4 + h) * SLOC + qt * QT:
                                  (g * 4 + h) * SLOC + (qt + 1) * QT],
                              osb[:, 0:QT], ctx["mmps"][0:32, 0:QT])

            for mi in range(NF):
                for s in range(16):
                    p.dma(wup_sb[0:32, s * 128:(s + 1) * 128],
                          wupT[32 * s:32 * (s + 1), mi * 128:(mi + 1) * 128],
                          engine="gpsimd")
                for c0, c1 in ntiles(SLOC):
                    w = c1 - c0
                    for s in range(16):
                        p.mm(ctx["mmps"][:, 0:w],
                             wup_sb[0:32, s * 128:(s + 1) * 128],
                             aos[:, s * SLOC + c0: s * SLOC + c1],
                             start=(s == 0), stop=(s == 15))
                    d = xsb[:, mi * T + tb0 + c0: mi * T + tb0 + c1]
                    p.vop("tensor_add", d, d, ctx["mmps"][:, 0:w])

        for fc in range(NF):
            p.dma(res_out[fc, :, :], xsb[:, fc * T:(fc + 1) * T])

    nc = p.build(body)
    return nc


def stage_launch3(inputs, x1_slices, q_slices, k_slices, v_slices):
    KHI1 = (SEQ - SLOC + 2 * 512) // 128
    kfull = np.empty((B, 4, 128, SEQ), dtype=BF)
    vfull_T = np.empty((B, SEQ, 512), dtype=BF)
    for c in range(NCORES):
        ks = np.asarray(k_slices[c]).reshape(4, 128, B, SLOC)
        vs = np.asarray(v_slices[c]).reshape(4, 128, B, SLOC)
        for b in range(B):
            kfull[b, :, :, SLOC * c:SLOC * (c + 1)] = ks[:, :, b, :]
            vfull_T[b, SLOC * c:SLOC * (c + 1), :] = \
                vs[:, :, b, :].reshape(512, SLOC).T
    wupT = np.ascontiguousarray(np.asarray(inputs["L0_ffn_w_up"]))
    maps = []
    for c in range(NCORES):
        m = np.empty((2, KHI1, 128, 512), dtype=np.uint8)
        for qt in range(2):
            q0 = 1024 * c + qt * 512
            for kc in range(KHI1):
                kpos = kc * 128 + np.arange(128)[:, None]
                qpos = q0 + np.arange(512)[None, :]
                m[qt, kc] = (kpos <= qpos).astype(np.uint8)
        maps.append({"x1": x1_slices[c], "q": q_slices[c], "kf": kfull,
                     "vfT": vfull_T, "wupT": wupT, "masks": m})
    return maps


def build_launch4():
    p = P("launch4")
    NEXP = 8
    TWH = 1024

    x1 = p.inp("x1", [NF, 128, T])
    res_in = p.inp("res", [NF, 128, T])
    g2in = p.inp("g2", [128, NF])
    router_w = p.inp("router_w", [1024, 8])
    w1_in = p.inp("w1", [NEXP, 1024, 2048])
    w3_in = p.inp("w3", [NEXP, 1024, 2048])
    w2_in = p.inp("w2", [NEXP, 2048, 1024])
    wbT = p.inp("wbT", [1024, 64])
    wvT = p.inp("wvT", [1024, 1024])
    bv_in = p.inp("bv", [128, NF])

    x2_out = p.out("x2", [NF, 128, T])
    lat_out = p.out("lat", [64, T], BF16)
    v1_out = p.out("v1", [NF, 128, T], BF16)

    def body(p):
        ctx = make_ctx(p, T)
        rsb = p.sb("rsb", [128, NF * T], F32)
        t16 = p.sb("t16", [128, NF * T], BF16)
        xg = p.sb("xg", [128, NF * TWH], BF16)
        hid = p.sb("hid", [128, 16 * TWH], BF16)
        gtok = p.sb("gtok", [128, 16 * NEXP], F32)
        growE = p.sb("growE", [1, TWH], F32)
        rwsb = p.sb("rwsb", [128, NF * 8], BF16)
        gam = p.sb("gam", [128, NF], F32)
        bv = p.sb("bv_sb", [128, NF], F32)
        lat16 = p.sb("lat16", [64, T], BF16)
        identf2 = p.sb("identf2", [128, 128], BF16)
        ninf = p.sb("ninf", [128, 8], F32)
        ninfC = p.sb("ninfC", [128, 8], F32)
        l2nd = p.sb("l2nd", [128, 8], F32)
        lg8 = p.sb("lg8", [128, 8], F32)
        m1c = p.sb("m1c", [128, 1], F32)
        m2c = p.sb("m2c", [128, 1], F32)
        m3c = p.sb("m3c", [128, 1], F32)
        msk1 = p.sb("msk1", [128, 8], U8)
        msk2 = p.sb("msk2", [128, 8], U8)
        g8 = p.sb("g8", [128, 8], F32)

        idi = p.sb("idi", [128, 128], I32)
        idp = p.sb("idp", [128, 1], I32)
        idpf = p.sb("idpf", [128, 1], F32)
        idff = p.sb("idff", [128, 128], F32)
        p.gop("iota", idi[:, :], [[1, 128]], base=0, channel_multiplier=0)
        p.gop("iota", idp[:, :], [[0, 1]], base=0, channel_multiplier=1)
        p.vop("tensor_copy", idff[:, :], idi[:, :])
        p.vop("tensor_copy", idpf[:, :], idp[:, :])
        p.vop("tensor_scalar", idff[:, :], idff[:, :], idpf[:, 0:1], None,
              ALU.is_equal)
        p.vop("tensor_copy", identf2[:, :], idff[:, :])

        p.dma(gam[:, :], g2in[:, :])
        p.dma(bv[:, :], bv_in[:, :])
        for fc in range(NF):
            p.dma(rsb[:, fc * T:(fc + 1) * T], res_in[fc, :, :])
            p.dma(rwsb[:, fc * 8:(fc + 1) * 8], router_w[fc * 128:(fc + 1) * 128, :],
                  engine="gpsimd")
        h_ln_to(p, ctx, rsb, t16, NF, gamma=gam, rms=True)

        for tt in range(16):
            for fc in range(NF):
                p.mm(ctx["mmps"][0:128, 0:8],
                     t16[:, fc * T + tt * 128: fc * T + (tt + 1) * 128],
                     rwsb[:, fc * 8:(fc + 1) * 8],
                     start=(fc == 0), stop=(fc == NF - 1))
            p.vop("tensor_copy", lg8[:, :], ctx["mmps"][0:128, 0:8])
            p.vop("reduce_max", m1c[:, 0:1], lg8[:, :], AX.X)
            p.vop("tensor_scalar", msk1[:, :], lg8[:, :], m1c[:, 0:1], None,
                  ALU.is_ge)
            p.vop("tensor_copy", l2nd[:, :], msk1[:, :])
            p.vop("scalar_tensor_tensor", l2nd[:, :], l2nd[:, :], -1e30,
                  lg8[:, :], ALU.mult, ALU.add)
            p.vop("reduce_max", m2c[:, 0:1], l2nd[:, :], AX.X)
            p.vop("tensor_scalar", msk2[:, :], lg8[:, :], m2c[:, 0:1], None,
                  ALU.is_ge)
            p.vop("tensor_scalar", m3c[:, 0:1], m2c[:, 0:1], -1.0, None, ALU.mult)
            p.sop("activation", m3c[:, 0:1], m1c[:, 0:1], AF.Sigmoid,
                  bias=m3c[:, 0:1])
            p.vop("tensor_copy", g8[:, :], msk1[:, :])
            p.vop("tensor_scalar", g8[:, :], g8[:, :], m3c[:, 0:1], None, ALU.mult)
            p.vop("tensor_copy", ninf[:, :], msk2[:, :])
            p.vop("tensor_copy", lg8[:, :], msk1[:, :])
            p.vop("tensor_mul", lg8[:, :], lg8[:, :], ninf[:, :])
            p.vop("tensor_sub", ninf[:, :], ninf[:, :], lg8[:, :])
            p.vop("tensor_scalar", m1c[:, 0:1], m3c[:, 0:1], -1.0, 1.0,
                  ALU.mult, ALU.add)
            p.vop("tensor_scalar", ninf[:, :], ninf[:, :], m1c[:, 0:1], None,
                  ALU.mult)
            p.vop("tensor_add", gtok[:, tt * 8:(tt + 1) * 8], g8[:, :], ninf[:, :])

        for half in range(2):
            t0 = half * TWH
            for e in range(NEXP):
                for tt in range(8):
                    gtt = half * 8 + tt
                    p.mm(ctx["rps"][0:1, 0:128],
                         gtok[:, gtt * 8 + e: gtt * 8 + e + 1], idff[:, :])
                    p.vop("tensor_copy", growE[0:1, tt * 128:(tt + 1) * 128],
                          ctx["rps"][0:1, 0:128])
                for c0, c1 in ntiles(TWH):
                    w = c1 - c0
                    p.mm(ctx["bc1"][:, 0:w], ctx["ones1f"][0:1, :],
                         growE[0:1, c0:c1])
                    for fc in range(NF):
                        p.vop("tensor_mul", xg[:, fc * TWH + c0: fc * TWH + c1],
                              t16[:, fc * T + t0 + c0: fc * T + t0 + c1],
                              ctx["bc1"][:, 0:w])
                def epi_h1(p_, mi, c0, c1, ps):
                    p_.sop("activation", hid[:, mi * TWH + c0: mi * TWH + c1],
                           ps[:, 0:c1 - c0], AF.Silu)
                def rd_t(k):
                    return t16[:, k * T + t0: k * T + t0 + TWH]
                h_proj_view(p, ctx, w1_in, e, rd_t, NF, list(range(16)), epi_h1,
                            TW=TWH)
                def epi_h3(p_, mi, c0, c1, ps):
                    d = hid[:, mi * TWH + c0: mi * TWH + c1]
                    p_.vop("tensor_mul", d, d, ps[:, 0:c1 - c0])
                h_proj_view(p, ctx, w3_in, e, lambda k: xg[:, k * TWH:(k + 1) * TWH],
                            NF, list(range(16)), epi_h3, TW=TWH)
                def epi_y(p_, mi, c0, c1, ps):
                    d = rsb[:, mi * T + t0 + c0: mi * T + t0 + c1]
                    p_.vop("tensor_add", d, d, ps[:, 0:c1 - c0])
                h_proj_view(p, ctx, w2_in, e, lambda k: hid[:, k * TWH:(k + 1) * TWH],
                            16, list(range(NF)), epi_y, TW=TWH)

        for fc in range(NF):
            for c0, c1 in ntiles(T):
                p.dma(ctx["scrfa"][:, 0:c1 - c0], x1[fc, :, c0:c1])
                p.vop("tensor_add", rsb[:, fc * T + c0: fc * T + c1],
                      rsb[:, fc * T + c0: fc * T + c1], ctx["scrfa"][:, 0:c1 - c0])
            p.dma(x2_out[fc, :, :], rsb[:, fc * T:(fc + 1) * T])

        h_ln_to(p, ctx, rsb, t16, NF)
        for c0, c1 in ntiles(T):
            w = c1 - c0
            for k in range(NF):
                p.dma(ctx["wt"][:, k * 128:(k + 1) * 128][:, 0:64],
                      wbT[k * 128:(k + 1) * 128, :], engine="gpsimd")
            for k in range(NF):
                p.mm(ctx["mmps"][0:64, 0:w], ctx["wt"][:, k * 128: k * 128 + 64],
                     t16[:, k * T + c0: k * T + c1],
                     start=(k == 0), stop=(k == NF - 1))
            p.vop("tensor_copy", lat16[:, c0:c1], ctx["mmps"][0:64, 0:w])
        for i in range(2):
            p.dma(lat_out[:, i * TWH:(i + 1) * TWH], lat16[:, i * TWH:(i + 1) * TWH])

        def epi_v1(p_, mi, c0, c1, ps):
            p_.vop("tensor_scalar", ctx["scr16a"][:, 0:c1 - c0], ps[:, 0:c1 - c0],
                   bv[:, mi:mi + 1], None, ALU.add)
            p_.dma(v1_out[mi, :, c0:c1], ctx["scr16a"][:, 0:c1 - c0])
        h_proj(p, ctx, wvT, t16, NF, list(range(NF)), epi_v1)

    nc = p.build(body)
    return nc


def h_proj_view(p, ctx, w3d, e, read_k, kchunks, mlist, epilogue, TW):
    wt = ctx["wt"]
    ps = ctx["mmps"]
    dbl = (2 * kchunks * 128 <= 4096)
    for mi, m in enumerate(mlist):
        so = (mi % 2) * kchunks * 128 if dbl else 0
        p.dma(wt[:, so: so + kchunks * 128].rearrange("f (k m) -> f k m", k=kchunks),
              w3d[e, 0:kchunks * 128, m * 128:(m + 1) * 128].rearrange(
                  "(k f) m -> f k m", f=128), engine="gpsimd")
        for c0, c1 in ntiles(TW):
            w = c1 - c0
            for k in range(kchunks):
                p.mm(ps[:, 0:w], wt[:, so + k * 128: so + (k + 1) * 128],
                     read_k(k)[:, c0:c1],
                     start=(k == 0), stop=(k == kchunks - 1))
            epilogue(p, mi, c0, c1, ps)


def stage_launch4(inputs, x1_slices, res_slices):
    g2 = np.ascontiguousarray(np.asarray(inputs["L0_ffn_g2"]).reshape(NF, 128).T)
    router_w = np.ascontiguousarray(np.asarray(inputs["L0_ffn_router_w"]))
    w1 = np.ascontiguousarray(np.asarray(inputs["L0_ffn_w1"]))
    w3 = np.ascontiguousarray(np.asarray(inputs["L0_ffn_w3"]))
    w2 = np.ascontiguousarray(np.asarray(inputs["L0_ffn_w2"]))
    wbT = np.ascontiguousarray(np.asarray(inputs["L1_attn_W_basis"]).T)
    wvT = np.ascontiguousarray(np.asarray(inputs["L1_attn_W_v"]).T)
    bv = np.ascontiguousarray(np.asarray(inputs["L1_attn_b_v"]).reshape(NF, 128).T)
    maps = []
    for c in range(NCORES):
        maps.append({"x1": x1_slices[c], "res": res_slices[c], "g2": g2,
                     "router_w": router_w, "w1": w1, "w3": w3, "w2": w2,
                     "wbT": wbT, "wvT": wvT, "bv": bv})
    return maps


def build_launch5():
    p = P("launch5")
    QT = 512
    KC = 128
    KHI = [(SEQ - SLOC + (qt + 1) * QT) // KC for qt in range(2)]

    x2 = p.inp("x2", [NF, 128, T])
    lato = p.inp("lato", [64, T], BF16)
    latf = p.inp("latf", [B, 64, SEQ], BF16)
    v1fT = p.inp("v1fT", [B, SEQ, 1024], BF16)
    woT = p.inp("woT", [1024, 1024])
    bo_in = p.inp("bo", [128, NF])
    masks = p.inp("masks", [2, KHI[1], 128, QT], U8)
    x3_out = p.out("x3", [NF, 128, T])

    def body(p):
        ctx = make_ctx(p, T)
        xsb = p.sb("xsb", [128, NF * T], F32)
        lo = p.sb("lo", [64, T], BF16)
        lf = p.sb("lf", [64, SEQ], BF16)
        vT = p.sb("vT", [128, 1024], BF16)
        aoL = p.sb("aoL", [128, NF * QT], BF16)
        P16 = p.sb("P16", [128, QT], BF16)
        mskb = p.sb("mskb", [128, QT], BF16)
        negc = p.sb("negc", [128, 1], F32)
        cm1 = p.sb("cm1", [128, 1], F32)
        cm2 = p.sb("cm2", [128, 1], F32)
        crow = p.sb("crow", [1, 4], F32)
        identf = p.sb("identf", [128, 128], F32)
        ysb = p.sb("ysb", [128, QT], F32)
        bo = p.sb("bo_sb", [128, NF], F32)
        y2 = p.ps("y2", [128, 512], F32)
        y3 = p.ps("y3", [128, 512], F32)

        idi = p.sb("idi", [128, 128], I32)
        idp = p.sb("idp", [128, 1], I32)
        idpf = p.sb("idpf", [128, 1], F32)
        p.gop("iota", idi[:, :], [[1, 128]], base=0, channel_multiplier=0)
        p.gop("iota", idp[:, :], [[0, 1]], base=0, channel_multiplier=1)
        p.vop("tensor_copy", identf[:, :], idi[:, :])
        p.vop("tensor_copy", idpf[:, :], idp[:, :])
        p.vop("tensor_scalar", identf[:, :], identf[:, :], idpf[:, 0:1], None,
              ALU.is_equal)

        p.dma(bo[:, :], bo_in[:, :])
        for fc in range(NF):
            p.dma(xsb[:, fc * T:(fc + 1) * T], x2[fc, :, :])
        p.dma(lo[:, :], lato[:, :])

        for b in range(B):
            tb0 = b * SLOC
            p.dma(lf[:, :], latf[b, :, :])
            p.vop("memset", cm1[:, :], 0.0)
            p.vop("memset", cm2[:, :], 0.0)
            p.vop("reduce_max", cm1[0:64, 0:1], lo[0:64, tb0:tb0 + SLOC], AX.X,
                  apply_absolute_value=True)
            p.vop("reduce_max", cm2[0:64, 0:1], lf[0:64, :], AX.X,
                  apply_absolute_value=True)
            p.transpose(ctx["rps"][0:1, 0:128], cm1[:, 0:1], identf[:, :])
            p.vop("reduce_max", crow[0:1, 0:1], ctx["rps"][0:1, 0:64], AX.X)
            p.transpose(ctx["rps"][0:1, 0:128], cm2[:, 0:1], identf[:, :])
            p.vop("reduce_max", crow[0:1, 1:2], ctx["rps"][0:1, 0:64], AX.X)
            p.vop("tensor_mul", crow[0:1, 2:3], crow[0:1, 0:1], crow[0:1, 1:2])
            p.vop("tensor_scalar", crow[0:1, 2:3], crow[0:1, 2:3],
                  -64.0 / 8.0, None, ALU.mult)
            p.mm(ctx["bc1"][:, 0:1], ctx["ones1f"][0:1, :], crow[0:1, 2:3])
            p.vop("tensor_copy", negc[:, 0:1], ctx["bc1"][:, 0:1])

            for qt in range(2):
                ql = tb0 + qt * QT
                khi = KHI[qt]
                for dh in range(2):
                    ybanks = [ctx["bc1"], ctx["mmps"], y2, y3]
                    for kc in range(khi):
                        last = (kc == khi - 1)
                        p.mm(ctx["bc2"][:, 0:QT], lf[0:64, kc * KC:(kc + 1) * KC],
                             lo[0:64, ql:ql + QT])
                        p.sop("activation", P16[:, 0:QT], ctx["bc2"][:, 0:QT],
                              AF.Exp, bias=negc[:, 0:1], scale=1.0 / 8.0)
                        p.dma(mskb[:, 0:QT], masks[qt, kc, :, :], engine="gpsimd")
                        p.vop("tensor_mul", P16[:, 0:QT], P16[:, 0:QT],
                              mskb[:, 0:QT])
                        if dh == 0:
                            p.mm(ctx["rps"][0:1, 0:QT], ctx["onesP"][:, 0:1],
                                 P16[:, 0:QT], start=(kc == 0), stop=last)
                        p.dma(vT[:, :], v1fT[b, kc * KC:(kc + 1) * KC, :],
                              engine="gpsimd")
                        for mc in range(4):
                            p.mm(ybanks[mc][:, 0:QT],
                                 vT[:, (dh * 4 + mc) * 128:(dh * 4 + mc + 1) * 128],
                                 P16[:, 0:QT], start=(kc == 0), stop=last)
                    if dh == 0:
                        p.vop("reciprocal", ctx["rows"][0][0:1, 0:QT],
                              ctx["rps"][0:1, 0:QT])
                    for mc in range(4):
                        p.vop("tensor_copy", ysb[:, 0:QT], ybanks[mc][:, 0:QT])
                        p.mm(ctx["bc2"][:, 0:QT], ctx["ones1f"][0:1, :],
                             ctx["rows"][0][0:1, 0:QT])
                        p.vop("tensor_mul",
                              aoL[:, (dh * 4 + mc) * QT:(dh * 4 + mc + 1) * QT],
                              ysb[:, 0:QT], ctx["bc2"][:, 0:QT])
                def epi_x3(p_, mi, c0, c1, ps):
                    d = xsb[:, mi * T + ql + c0: mi * T + ql + c1]
                    p_.vop("tensor_scalar", ctx["scrfa"][:, 0:c1 - c0],
                           ps[:, 0:c1 - c0], bo[:, mi:mi + 1], None, ALU.add)
                    p_.vop("tensor_add", d, d, ctx["scrfa"][:, 0:c1 - c0])
                h_proj(p, ctx, woT, aoL, NF, list(range(NF)), epi_x3, TW=QT)

        for fc in range(NF):
            p.dma(x3_out[fc, :, :], xsb[:, fc * T:(fc + 1) * T])

    nc = p.build(body)
    return nc


def stage_launch5(inputs, x2_slices, lat_slices, v1_slices, masks_per_core):
    latf = np.empty((B, 64, SEQ), dtype=BF)
    v1fT = np.empty((B, SEQ, 1024), dtype=BF)
    for c in range(NCORES):
        ls = np.asarray(lat_slices[c]).reshape(64, B, SLOC)
        vs = np.asarray(v1_slices[c]).reshape(NF, 128, B, SLOC)
        for b in range(B):
            latf[b, :, SLOC * c:SLOC * (c + 1)] = ls[:, b, :]
            v1fT[b, SLOC * c:SLOC * (c + 1), :] = \
                vs[:, :, b, :].reshape(1024, SLOC).T
    woT = np.ascontiguousarray(np.asarray(inputs["L1_attn_W_o"]).T)
    bo = np.ascontiguousarray(np.asarray(inputs["L1_attn_b_o"]).reshape(NF, 128).T)
    maps = []
    for c in range(NCORES):
        maps.append({"x2": x2_slices[c], "lato": lat_slices[c], "latf": latf,
                     "v1fT": v1fT, "woT": woT, "bo": bo,
                     "masks": masks_per_core[c]})
    return maps


def build_launch6():
    p = P("launch6")
    NEXP = 4
    TWQ = 512

    x3 = p.inp("x3", [NF, 128, T])
    router_w = p.inp("router_w", [1024, 4])
    router_b = p.inp("router_b", [1, 4])
    w1_in = p.inp("w1", [NEXP, 1024, 4096])
    w3_in = p.inp("w3", [NEXP, 1024, 4096])
    w2_in = p.inp("w2", [NEXP, 4096, 1024])
    wqT = p.inp("wqT", [1024, 1024])
    wkT = p.inp("wkT", [1024, 1024])
    wvT = p.inp("wvT", [1024, 1024])
    bq_in = p.inp("bq", [128, NF])
    bk_in = p.inp("bk", [128, NF])
    bv_in = p.inp("bv", [128, NF])

    x4_out = p.out("x4", [NF, 128, T])
    q2_out = p.out("q2", [NF, 128, T], BF16)
    k2_out = p.out("k2", [NF, 128, T], BF16)
    v2_out = p.out("v2", [NF, 128, T], BF16)

    def body(p):
        ctx = make_ctx(p, T)
        xsb = p.sb("xsb", [128, NF * T], F32)
        t16 = p.sb("t16", [128, NF * T], BF16)
        xg = p.sb("xg", [128, NF * TWQ], BF16)
        hid = p.sb("hid", [128, 32 * TWQ], BF16)
        gtok = p.sb("gtok", [128, 16 * NEXP], F32)
        growE = p.sb("growE", [1, TWQ], F32)
        rwsb = p.sb("rwsb", [128, NF * 4], BF16)
        rbsb = p.sb("rbsb", [1, 4], F32)
        ninf = p.sb("ninf", [128, 4], F32)
        l2nd = p.sb("l2nd", [128, 4], F32)
        lg8 = p.sb("lg8", [128, 4], F32)
        m1c = p.sb("m1c", [128, 1], F32)
        m2c = p.sb("m2c", [128, 1], F32)
        m3c = p.sb("m3c", [128, 1], F32)
        msk1 = p.sb("msk1", [128, 4], U8)
        msk2 = p.sb("msk2", [128, 4], U8)
        g8 = p.sb("g8", [128, 4], F32)
        idff = p.sb("idff", [128, 128], F32)
        bcol = p.sb("bcol", [128, NF], F32)

        idi = p.sb("idi", [128, 128], I32)
        idp = p.sb("idp", [128, 1], I32)
        idpf = p.sb("idpf", [128, 1], F32)
        p.gop("iota", idi[:, :], [[1, 128]], base=0, channel_multiplier=0)
        p.gop("iota", idp[:, :], [[0, 1]], base=0, channel_multiplier=1)
        p.vop("tensor_copy", idff[:, :], idi[:, :])
        p.vop("tensor_copy", idpf[:, :], idp[:, :])
        p.vop("tensor_scalar", idff[:, :], idff[:, :], idpf[:, 0:1], None,
              ALU.is_equal)

        p.dma(rbsb[0:1, :], router_b[0:1, :])
        for fc in range(NF):
            p.dma(xsb[:, fc * T:(fc + 1) * T], x3[fc, :, :])
            p.dma(rwsb[:, fc * 4:(fc + 1) * 4], router_w[fc * 128:(fc + 1) * 128, :],
                  engine="gpsimd")
        h_ln_to(p, ctx, xsb, t16, NF)

        for tt in range(16):
            for fc in range(NF):
                p.mm(ctx["mmps"][0:128, 0:4],
                     t16[:, fc * T + tt * 128: fc * T + (tt + 1) * 128],
                     rwsb[:, fc * 4:(fc + 1) * 4],
                     start=(fc == 0), stop=False)
            p.mm(ctx["mmps"][0:128, 0:4], ctx["ones1f"][0:1, :], rbsb[0:1, 0:4],
                 start=False, stop=True)
            p.vop("tensor_copy", lg8[:, :], ctx["mmps"][0:128, 0:4])
            p.vop("reduce_max", m1c[:, 0:1], lg8[:, :], AX.X)
            p.vop("tensor_scalar", msk1[:, :], lg8[:, :], m1c[:, 0:1], None,
                  ALU.is_ge)
            p.vop("tensor_copy", l2nd[:, :], msk1[:, :])
            p.vop("scalar_tensor_tensor", l2nd[:, :], l2nd[:, :], -1e30,
                  lg8[:, :], ALU.mult, ALU.add)
            p.vop("reduce_max", m2c[:, 0:1], l2nd[:, :], AX.X)
            p.vop("tensor_scalar", msk2[:, :], lg8[:, :], m2c[:, 0:1], None,
                  ALU.is_ge)
            p.vop("tensor_scalar", m3c[:, 0:1], m2c[:, 0:1], -1.0, None, ALU.mult)
            p.sop("activation", m3c[:, 0:1], m1c[:, 0:1], AF.Sigmoid,
                  bias=m3c[:, 0:1])
            p.vop("tensor_copy", g8[:, :], msk1[:, :])
            p.vop("tensor_scalar", g8[:, :], g8[:, :], m3c[:, 0:1], None, ALU.mult)
            p.vop("tensor_copy", ninf[:, :], msk2[:, :])
            p.vop("tensor_copy", lg8[:, :], msk1[:, :])
            p.vop("tensor_mul", lg8[:, :], lg8[:, :], ninf[:, :])
            p.vop("tensor_sub", ninf[:, :], ninf[:, :], lg8[:, :])
            p.vop("tensor_scalar", m1c[:, 0:1], m3c[:, 0:1], -1.0, 1.0,
                  ALU.mult, ALU.add)
            p.vop("tensor_scalar", ninf[:, :], ninf[:, :], m1c[:, 0:1], None,
                  ALU.mult)
            p.vop("tensor_add", gtok[:, tt * NEXP:(tt + 1) * NEXP], g8[:, :],
                  ninf[:, :])

        for quarter in range(4):
            t0 = quarter * TWQ
            for e in range(NEXP):
                for tt in range(4):
                    gtt = quarter * 4 + tt
                    p.mm(ctx["rps"][0:1, 0:128],
                         gtok[:, gtt * NEXP + e: gtt * NEXP + e + 1], idff[:, :])
                    p.vop("tensor_copy", growE[0:1, tt * 128:(tt + 1) * 128],
                          ctx["rps"][0:1, 0:128])
                for c0, c1 in ntiles(TWQ):
                    w = c1 - c0
                    p.mm(ctx["bc1"][:, 0:w], ctx["ones1f"][0:1, :],
                         growE[0:1, c0:c1])
                    for fc in range(NF):
                        p.vop("tensor_mul", xg[:, fc * TWQ + c0: fc * TWQ + c1],
                              t16[:, fc * T + t0 + c0: fc * T + t0 + c1],
                              ctx["bc1"][:, 0:w])
                def epi_h1(p_, mi, c0, c1, ps):
                    p_.sop("activation", hid[:, mi * TWQ + c0: mi * TWQ + c1],
                           ps[:, 0:c1 - c0], AF.Silu)
                h_proj_view(p, ctx, w1_in, e,
                            lambda k: t16[:, k * T + t0: k * T + t0 + TWQ],
                            NF, list(range(32)), epi_h1, TW=TWQ)
                def epi_h3(p_, mi, c0, c1, ps):
                    d = hid[:, mi * TWQ + c0: mi * TWQ + c1]
                    p_.vop("tensor_mul", d, d, ps[:, 0:c1 - c0])
                h_proj_view(p, ctx, w3_in, e,
                            lambda k: xg[:, k * TWQ:(k + 1) * TWQ],
                            NF, list(range(32)), epi_h3, TW=TWQ)
                def epi_y(p_, mi, c0, c1, ps):
                    d = xsb[:, mi * T + t0 + c0: mi * T + t0 + c1]
                    p_.vop("tensor_add", d, d, ps[:, 0:c1 - c0])
                h_proj_view(p, ctx, w2_in, e,
                            lambda k: hid[:, k * TWQ:(k + 1) * TWQ],
                            32, list(range(NF)), epi_y, TW=TWQ)

        for fc in range(NF):
            p.dma(x4_out[fc, :, :], xsb[:, fc * T:(fc + 1) * T])

        h_ln_to(p, ctx, xsb, t16, NF)
        for wT, b_in, outT in ((wqT, bq_in, q2_out), (wkT, bk_in, k2_out),
                               (wvT, bv_in, v2_out)):
            p.dma(bcol[:, :], b_in[:, :])
            def epi_qkv(p_, mi, c0, c1, ps, o=outT):
                p_.vop("tensor_scalar", ctx["scr16a"][:, 0:c1 - c0],
                       ps[:, 0:c1 - c0], bcol[:, mi:mi + 1], None, ALU.add)
                p_.dma(o[mi, :, c0:c1], ctx["scr16a"][:, 0:c1 - c0])
            h_proj(p, ctx, wT, t16, NF, list(range(NF)), epi_qkv)

    nc = p.build(body)
    return nc


def stage_launch6(inputs, x3_slices):
    rw = np.ascontiguousarray(np.asarray(inputs["L1_ffn_router_w"]))
    rb = np.ascontiguousarray(np.asarray(inputs["L1_ffn_router_b"]).reshape(1, 4))
    w1 = np.ascontiguousarray(np.asarray(inputs["L1_ffn_w1"]))
    w3 = np.ascontiguousarray(np.asarray(inputs["L1_ffn_w3"]))
    w2 = np.ascontiguousarray(np.asarray(inputs["L1_ffn_w2"]))
    wqT = np.ascontiguousarray(np.asarray(inputs["L2_attn_Wq"]).T)
    wkT = np.ascontiguousarray(np.asarray(inputs["L2_attn_Wk"]).T)
    wvT = np.ascontiguousarray(np.asarray(inputs["L2_attn_Wv"]).T)
    bq = np.ascontiguousarray(np.asarray(inputs["L2_attn_bq"]).reshape(NF, 128).T)
    bk = np.ascontiguousarray(np.asarray(inputs["L2_attn_bk"]).reshape(NF, 128).T)
    bv = np.ascontiguousarray(np.asarray(inputs["L2_attn_bv"]).reshape(NF, 128).T)
    maps = []
    for c in range(NCORES):
        maps.append({"x3": x3_slices[c], "router_w": rw, "router_b": rb,
                     "w1": w1, "w3": w3, "w2": w2, "wqT": wqT, "wkT": wkT,
                     "wvT": wvT, "bq": bq, "bk": bk, "bv": bv})
    return maps


def build_launch7():
    p = P("launch7")
    HL = 1280

    x4 = p.inp("x4", [NF, 128, T])
    q2o = p.inp("q2o", [NF, 128, T], BF16)
    khalo = p.inp("khalo", [B, NF, 128, HL], BF16)
    vhaloT = p.inp("vhaloT", [B, HL, 1024], BF16)
    woT = p.inp("woT", [1024, 1024])
    bo_in = p.inp("bo", [128, NF])
    masks2 = p.inp("masks2", [B, 8, 128, 384], U8)
    wr_in = p.inp("wr", [128, NF])
    x5_out = p.out("x5", [NF, 128, T])
    sc_out = p.out("sc", [1, T])

    def body(p):
        ctx = make_ctx(p, T)
        xsb = p.sb("xsb", [128, NF * T], F32)
        qsb = p.sb("qsb", [128, NF * T], BF16)
        ksb = p.sb("ksb", [128, NF * HL], BF16)
        vsb = p.sb("vsb", [128, 10 * 1024], BF16)
        ao = p.sb("ao", [128, NF * SLOC], BF16)
        P16 = p.sb("P16", [128, 384], BF16)
        mskb = p.sb("mskb", [128, 384], BF16)
        negc = p.sb("negc", [128, 1], F32)
        cm1 = p.sb("cm1", [128, 1], F32)
        cm2 = p.sb("cm2", [128, 1], F32)
        crow = p.sb("crow", [1, 4], F32)
        identf = p.sb("identf", [128, 128], F32)
        osb = p.sb("osb", [64, 128], F32)
        bo = p.sb("bo_sb", [128, NF], F32)
        wrc = p.sb("wrc", [128, NF], F32)
        t16 = qsb

        idi = p.sb("idi", [128, 128], I32)
        idp = p.sb("idp", [128, 1], I32)
        idpf = p.sb("idpf", [128, 1], F32)
        p.gop("iota", idi[:, :], [[1, 128]], base=0, channel_multiplier=0)
        p.gop("iota", idp[:, :], [[0, 1]], base=0, channel_multiplier=1)
        p.vop("tensor_copy", identf[:, :], idi[:, :])
        p.vop("tensor_copy", idpf[:, :], idp[:, :])
        p.vop("tensor_scalar", identf[:, :], identf[:, :], idpf[:, 0:1], None,
              ALU.is_equal)

        p.dma(bo[:, :], bo_in[:, :])
        p.dma(wrc[:, :], wr_in[:, :])
        for fc in range(NF):
            p.dma(xsb[:, fc * T:(fc + 1) * T], x4[fc, :, :])
            p.dma(qsb[:, fc * T:(fc + 1) * T], q2o[fc, :, :])

        for b in range(B):
            tb0 = b * SLOC
            for fc in range(NF):
                p.dma(ksb[:, fc * HL:(fc + 1) * HL], khalo[b, fc, :, :])
            for kb in range(10):
                p.dma(vsb[:, kb * 1024:(kb + 1) * 1024],
                      vhaloT[b, kb * 128:(kb + 1) * 128, :])
            p.vop("memset", crow[0:1, 0:1], 0.0)
            p.vop("memset", crow[0:1, 1:2], 0.0)
            for fc in range(NF):
                for c0, c1 in ntiles(T):
                    p.sop("activation", ctx["scr16a"][:, 0:c1 - c0],
                          qsb[:, fc * T + c0: fc * T + c1], AF.Square)
                    p.mm(ctx["rps"][0:1, 0:c1 - c0], ctx["onesP"][:, 0:1],
                         ctx["scr16a"][:, 0:c1 - c0])
                    p.vop("reduce_max", crow[0:1, 2:3],
                          ctx["rps"][0:1, 0:c1 - c0], AX.X)
                    p.vop("tensor_tensor", crow[0:1, 0:1], crow[0:1, 0:1],
                          crow[0:1, 2:3], ALU.max)
                for c0, c1 in ntiles(HL):
                    p.sop("activation", ctx["scr16a"][:, 0:c1 - c0],
                          ksb[:, fc * HL + c0: fc * HL + c1], AF.Square)
                    p.mm(ctx["rps"][0:1, 0:c1 - c0], ctx["onesP"][:, 0:1],
                         ctx["scr16a"][:, 0:c1 - c0])
                    p.vop("reduce_max", crow[0:1, 2:3],
                          ctx["rps"][0:1, 0:c1 - c0], AX.X)
                    p.vop("tensor_tensor", crow[0:1, 1:2], crow[0:1, 1:2],
                          crow[0:1, 2:3], ALU.max)
            p.sop("activation", crow[0:1, 0:1], crow[0:1, 0:1], AF.Sqrt)
            p.sop("activation", crow[0:1, 1:2], crow[0:1, 1:2], AF.Sqrt)
            p.vop("tensor_mul", crow[0:1, 2:3], crow[0:1, 0:1], crow[0:1, 1:2])
            p.vop("tensor_scalar", crow[0:1, 2:3], crow[0:1, 2:3],
                  -1.0 / 8.0, None, ALU.mult)
            p.mm(ctx["bc1"][:, 0:1], ctx["ones1f"][0:1, :], crow[0:1, 2:3])
            p.vop("tensor_copy", negc[:, 0:1], ctx["bc1"][:, 0:1])

            for qb in range(8):
                p.dma(mskb[:, :], masks2[b, qb, :, :], engine="gpsimd")
                for h in range(16):
                    hfc = h // 2
                    hr = 64 * (h % 2)
                    for blk in range(3):
                        p.mm(ctx["bc2"][:, blk * 128:(blk + 1) * 128],
                             ksb[hr:hr + 64,
                                 hfc * HL + (qb + blk) * 128: hfc * HL + (qb + blk + 1) * 128],
                             qsb[hr:hr + 64,
                                 hfc * T + tb0 + qb * 128: hfc * T + tb0 + (qb + 1) * 128],
                             tile_position=(hr, 0))
                    p.sop("activation", P16[:, :], ctx["bc2"][:, 0:384], AF.Exp,
                          bias=negc[:, 0:1], scale=1.0 / 8.0)
                    p.vop("tensor_mul", P16[:, :], P16[:, :], mskb[:, :])
                    p.mm(ctx["rps"][0:1, 0:384], ctx["onesP"][:, 0:1], P16[:, :])
                    p.vop("tensor_copy", ctx["rows"][0][0:1, 0:384],
                          ctx["rps"][0:1, 0:384])
                    p.vop("tensor_add", ctx["rows"][0][0:1, 0:128],
                          ctx["rows"][0][0:1, 0:128], ctx["rows"][0][0:1, 128:256])
                    p.vop("tensor_add", ctx["rows"][0][0:1, 0:128],
                          ctx["rows"][0][0:1, 0:128], ctx["rows"][0][0:1, 256:384])
                    p.vop("reciprocal", ctx["rows"][1][0:1, 0:128],
                          ctx["rows"][0][0:1, 0:128])
                    for blk in range(3):
                        p.mm(ctx["mmps"][0:64, 0:128],
                             vsb[:, (qb + blk) * 1024 + h * 64:
                                 (qb + blk) * 1024 + (h + 1) * 64],
                             P16[:, blk * 128:(blk + 1) * 128],
                             start=(blk == 0), stop=(blk == 2))
                    p.mm(ctx["bc1"][:, 0:128], ctx["ones1f"][0:1, :],
                         ctx["rows"][1][0:1, 0:128])
                    p.vop("tensor_copy", osb[:, :], ctx["mmps"][0:64, 0:128])
                    p.vop("tensor_mul",
                          ao[hr:hr + 64, hfc * SLOC + qb * 128:
                             hfc * SLOC + (qb + 1) * 128],
                          osb[:, :], ctx["bc1"][0:64, 0:128])
            def epi_x5(p_, mi, c0, c1, ps):
                d = xsb[:, mi * T + tb0 + c0: mi * T + tb0 + c1]
                p_.vop("tensor_scalar", ctx["scrfa"][:, 0:c1 - c0],
                       ps[:, 0:c1 - c0], bo[:, mi:mi + 1], None, ALU.add)
                p_.vop("tensor_add", d, d, ctx["scrfa"][:, 0:c1 - c0])
            h_proj(p, ctx, woT, ao, NF, list(range(NF)), epi_x5, TW=SLOC)

        for fc in range(NF):
            p.dma(x5_out[fc, :, :], xsb[:, fc * T:(fc + 1) * T])

        h_ln_to(p, ctx, xsb, t16, NF)
        for c0, c1 in ntiles(T):
            w = c1 - c0
            for fc in range(NF):
                p.vop("tensor_scalar", ctx["scr16a"][:, 0:w],
                      t16[:, fc * T + c0: fc * T + c1], wrc[:, fc:fc + 1],
                      None, ALU.mult)
                p.mm(ctx["rps"][0:1, 0:w], ctx["onesP"][:, 0:1],
                     ctx["scr16a"][:, 0:w], start=(fc == 0), stop=(fc == NF - 1))
            p.vop("tensor_copy", ctx["rows"][3][0:1, c0:c1], ctx["rps"][0:1, 0:w])
        p.dma(sc_out[0:1, :], ctx["rows"][3][0:1, :])

    nc = p.build(body)
    return nc


def stage_launch7(inputs, x4_slices, q2_slices, k2_slices, v2_slices):
    HL = 1280
    k2f = np.zeros((B, 1024, SEQ), dtype=BF)
    v2f = np.zeros((B, 1024, SEQ), dtype=BF)
    for c in range(NCORES):
        ks = np.asarray(k2_slices[c]).reshape(NF, 128, B, SLOC)
        vs = np.asarray(v2_slices[c]).reshape(NF, 128, B, SLOC)
        for b in range(B):
            k2f[b, :, SLOC * c:SLOC * (c + 1)] = ks[:, :, b, :].reshape(1024, SLOC)
            v2f[b, :, SLOC * c:SLOC * (c + 1)] = vs[:, :, b, :].reshape(1024, SLOC)
    woT = np.ascontiguousarray(np.asarray(inputs["L2_attn_Wo"]).T)
    bo = np.ascontiguousarray(np.asarray(inputs["L2_attn_bo"]).reshape(NF, 128).T)
    wr = np.ascontiguousarray(np.asarray(inputs["L2_ffn_w_router"]).reshape(NF, 128).T)
    maps = []
    for c in range(NCORES):
        lo_g = SLOC * c - 128
        khalo = np.zeros((B, NF, 128, HL), dtype=BF)
        vhaloT = np.zeros((B, HL, 1024), dtype=BF)
        for b in range(B):
            for j in range(HL):
                g = lo_g + j
                if 0 <= g < SEQ:
                    khalo[b, :, :, j] = k2f[b, :, g].reshape(NF, 128)
                    vhaloT[b, j, :] = v2f[b, :, g]
        m2 = np.zeros((B, 8, 128, 384), dtype=np.uint8)
        for b in range(B):
            for qb in range(8):
                qg = SLOC * c + qb * 128 + np.arange(128)[None, :]
                for blk in range(3):
                    kg = lo_g + (qb + blk) * 128 + np.arange(128)[:, None]
                    ok = (np.abs(kg - qg) <= 128) & (kg >= 0) & (kg < SEQ)
                    m2[b, qb, :, blk * 128:(blk + 1) * 128] = ok.astype(np.uint8)
        maps.append({"x4": x4_slices[c], "q2o": q2_slices[c], "khalo": khalo,
                     "vhaloT": vhaloT, "woT": woT, "bo": bo, "masks2": m2,
                     "wr": wr})
    return maps


def build_launch8():
    p = P("launch8")
    TWQ = 512

    x5 = p.inp("x5", [NF, 128, T])
    scf = p.inp("scf", [B, SEQ])
    sco_cols = p.inp("sco_cols", [128, 16])
    pos_cols = p.inp("pos_cols", [128, 16])
    W1m = p.inp("W1m", [1024, 4096])
    b1m = p.inp("b1m", [128, 32])
    W2mT = p.inp("W2mT", [4096, 1024])
    b2m = p.inp("b2m", [128, NF])
    wqT3 = p.inp("wqT3", [1024, 1024])
    wkT3 = p.inp("wkT3", [1024, 1024])
    wvT3 = p.inp("wvT3", [1024, 1024])
    woT3 = p.inp("woT3", [1024, 1024])
    bq3 = p.inp("bq3", [128, NF])
    bk3 = p.inp("bk3", [128, NF])
    bv3 = p.inp("bv3", [128, NF])
    bo3 = p.inp("bo3", [128, NF])
    W1aT = p.inp("W1aT", [1024, 1024])
    b1a = p.inp("b1a", [128, NF])
    W2aT = p.inp("W2aT", [1024, 64])
    b2a_col = p.inp("b2a_col", [64, 1])
    sigT = p.inp("sigT", [64, 16])
    rw3 = p.inp("rw3", [1024, 8])
    rb3 = p.inp("rb3", [1, 8])
    rw1 = p.inp("rw1", [8, 1024, 512])
    rw3e = p.inp("rw3e", [8, 1024, 512])
    rw2 = p.inp("rw2", [8, 512, 1024])
    sw1 = p.inp("sw1", [2, 1024, 512])
    sw3 = p.inp("sw3", [2, 1024, 512])
    sw2 = p.inp("sw2", [2, 512, 1024])

    x8_out = p.out("x8", [NF, 128, T])

    def body(p):
        ctx = make_ctx(p, T)
        xsb = p.sb("xsb", [128, NF * T], F32)
        t16 = p.sb("t16", [128, NF * T], BF16)
        big = p.sb("big", [128, 16384], BF16)
        rankc = p.sb("rankc", [128, 16], F32)
        scch = p.sb("scch", [1, 512], F32)
        posr = p.sb("posr", [1, 512], F32)
        posi = p.sb("posi", [1, 512], I32)
        scols = p.sb("scols", [128, 16], F32)
        pcols = p.sb("pcols", [128, 16], F32)
        cnt = p.sb("cnt", [128, 1], F32)
        colA = p.sb("colA", [128, 1], F32)
        bcolA = p.sb("bcolA", [128, 32], F32)
        bcolB = p.sb("bcolB", [128, NF], F32)
        bcolC = p.sb("bcolC", [128, NF], F32)
        bcolD = p.sb("bcolD", [128, NF], F32)
        b2acol = p.sb("b2acol", [64, 1], F32)
        idff = p.sb("idff", [128, 128], F32)
        qc = p.sb("qc", [128, T], BF16)
        kc_ = p.sb("kc_", [128, T], BF16)
        vc = p.sb("vc", [128, T], BF16)
        grow = ctx["rows"][2]
        wmrow = ctx["rows"][3]
        r0row = ctx["rows"][0]
        r1row = ctx["rows"][1]
        gt16 = p.sb("gt16", [128, 16 * 16], F32)
        t64 = p.sb("t64", [64, T], BF16)
        gtokE = p.sb("gtokE", [128, 16 * 8], F32)
        ninf = p.sb("ninf", [128, 8], F32)
        l2nd = p.sb("l2nd", [128, 8], F32)
        lg8 = p.sb("lg8", [128, 8], F32)
        m1c = p.sb("m1c", [128, 1], F32)
        m2c = p.sb("m2c", [128, 1], F32)
        m3c = p.sb("m3c", [128, 1], F32)
        msk1 = p.sb("msk1", [128, 8], U8)
        msk2 = p.sb("msk2", [128, 8], U8)
        g8 = p.sb("g8", [128, 8], F32)
        rb8 = p.sb("rb8", [1, 8], F32)
        sg = p.sb("sg", [64, 16], BF16)
        rwsb = p.sb("rwsb8", [128, NF * 8], BF16)

        idi = p.sb("idi", [128, 128], I32)
        idp = p.sb("idp", [128, 1], I32)
        idpf = p.sb("idpf", [128, 1], F32)
        p.gop("iota", idi[:, :], [[1, 128]], base=0, channel_multiplier=0)
        p.gop("iota", idp[:, :], [[0, 1]], base=0, channel_multiplier=1)
        p.vop("tensor_copy", idff[:, :], idi[:, :])
        p.vop("tensor_copy", idpf[:, :], idp[:, :])
        p.vop("tensor_scalar", idff[:, :], idff[:, :], idpf[:, 0:1], None,
              ALU.is_equal)

        for fc in range(NF):
            p.dma(xsb[:, fc * T:(fc + 1) * T], x5[fc, :, :])
        p.dma(scols[:, :], sco_cols[:, :])
        p.dma(pcols[:, :], pos_cols[:, :])

        h_ln_to(p, ctx, xsb, t16, NF)
        for fc in range(NF):
            for c0, c1 in ntiles(T):
                p.vop("tensor_add", xsb[:, fc * T + c0: fc * T + c1],
                      xsb[:, fc * T + c0: fc * T + c1],
                      t16[:, fc * T + c0: fc * T + c1])
        for b in range(B):
            for tt in range(8):
                p.vop("memset", rankc[:, b * 8 + tt: b * 8 + tt + 1], 0.0)
            for j0 in range(0, SEQ, 512):
                p.dma(scch[0:1, :], scf[b:b + 1, j0:j0 + 512])
                p.gop("iota", posi[0:1, :], [[1, 512]], base=j0,
                      channel_multiplier=0)
                p.vop("tensor_copy", posr[0:1, :], posi[0:1, :])
                p.mm(ctx["bc1"][:, 0:512], ctx["ones1f"][0:1, :], scch[0:1, :])
                p.mm(ctx["bc2"][:, 0:512], ctx["ones1f"][0:1, :], posr[0:1, :])
                for tt in range(8):
                    ci = b * 8 + tt
                    p.vop("tensor_scalar", ctx["scrfa"][:, 0:512],
                          ctx["bc1"][:, 0:512], scols[:, ci:ci + 1], None,
                          ALU.is_gt)
                    p.vop("tensor_scalar", ctx["scrfb"][:, 0:512],
                          ctx["bc1"][:, 0:512], scols[:, ci:ci + 1], None,
                          ALU.is_equal)
                    p.vop("tensor_scalar", ctx["scr16a"][:, 0:512],
                          ctx["bc2"][:, 0:512], pcols[:, ci:ci + 1], None,
                          ALU.is_lt)
                    p.vop("tensor_mul", ctx["scrfb"][:, 0:512],
                          ctx["scrfb"][:, 0:512], ctx["scr16a"][:, 0:512])
                    p.vop("tensor_add", ctx["scrfa"][:, 0:512],
                          ctx["scrfa"][:, 0:512], ctx["scrfb"][:, 0:512])
                    p.vop("reduce_sum", cnt[:, 0:1], ctx["scrfa"][:, 0:512], AX.X)
                    p.vop("tensor_add", rankc[:, ci:ci + 1], rankc[:, ci:ci + 1],
                          cnt[:, 0:1])
        for b in range(B):
            tb0 = b * SLOC
            for tt in range(8):
                ci = b * 8 + tt
                p.vop("tensor_scalar", colA[:, 0:1], rankc[:, ci:ci + 1],
                      4096.0, None, ALU.is_lt)
                p.vop("tensor_mul", colA[:, 0:1], colA[:, 0:1], scols[:, ci:ci + 1]) \
                    if False else None
                p.mm(ctx["rps"][0:1, 0:128], colA[:, 0:1], idff[:, :])
                p.vop("tensor_copy", grow[0:1, tb0 + tt * 128: tb0 + (tt + 1) * 128],
                      ctx["rps"][0:1, 0:128])
                p.mm(ctx["rps"][0:1, 0:128], scols[:, ci:ci + 1], idff[:, :])
                p.vop("tensor_copy", wmrow[0:1, tb0 + tt * 128: tb0 + (tt + 1) * 128],
                      ctx["rps"][0:1, 0:128])
        p.sop("activation", wmrow[0:1, :], wmrow[0:1, :], AF.Sigmoid)
        p.vop("tensor_mul", wmrow[0:1, :], wmrow[0:1, :], grow[0:1, :])

        p.dma(bcolA[:, :], b1m[:, :])
        p.dma(bcolB[:, :], b2m[:, :])
        for quarter in range(4):
            t0 = quarter * TWQ

            def epi_g(p_, mi, c0, c1, ps):
                w = c1 - c0
                u = ctx["scrfa"]
                p_.vop("tensor_scalar", u[:, 0:w], ps[:, 0:w],
                       bcolA[:, mi:mi + 1], None, ALU.add)
                p_.sop("activation", ctx["scrfb"][:, 0:w], u[:, 0:w], AF.Square)
                p_.vop("tensor_mul", ctx["scrfb"][:, 0:w], ctx["scrfb"][:, 0:w],
                       u[:, 0:w])
                p_.vop("scalar_tensor_tensor", ctx["scrfb"][:, 0:w],
                       ctx["scrfb"][:, 0:w], 0.044715, u[:, 0:w],
                       ALU.mult, ALU.add)
                p_.sop("activation", ctx["scrfb"][:, 0:w], ctx["scrfb"][:, 0:w],
                       AF.Tanh, scale=0.7978845608028654)
                p_.vop("tensor_scalar", ctx["scrfb"][:, 0:w], ctx["scrfb"][:, 0:w],
                       1.0, None, ALU.add)
                p_.vop("tensor_scalar", u[:, 0:w], u[:, 0:w], 0.5, None, ALU.mult)
                p_.vop("tensor_mul", big[:, mi * TWQ + c0: mi * TWQ + c1],
                       u[:, 0:w], ctx["scrfb"][:, 0:w])

            def rd_t(k, t0=t0):
                return t16[:, k * T + t0: k * T + t0 + TWQ]
            h_proj2(p, ctx, W1m, rd_t, NF, list(range(32)), epi_g, TW=TWQ)

            def epi_u(p_, mi, c0, c1, ps, t0=t0):
                w = c1 - c0
                p_.mm(ctx["bc1"][:, 0:w], ctx["ones1f"][0:1, :],
                      wmrow[0:1, t0 + c0: t0 + c1])
                p_.vop("tensor_scalar", ctx["scrfa"][:, 0:w], ps[:, 0:w],
                       bcolB[:, mi:mi + 1], None, ALU.add)
                p_.vop("tensor_mul", ctx["scrfa"][:, 0:w], ctx["scrfa"][:, 0:w],
                       ctx["bc1"][:, 0:w])
                d = xsb[:, mi * T + t0 + c0: mi * T + t0 + c1]
                p_.vop("tensor_add", d, d, ctx["scrfa"][:, 0:w])
            h_proj2(p, ctx, W2mT, lambda k: big[:, k * TWQ:(k + 1) * TWQ],
                    32, list(range(NF)), epi_u, TW=TWQ)

        h_ln_to(p, ctx, xsb, t16, NF)
        p.dma(bcolB[:, :], b1a[:, :])

        def epi_r(p_, mi, c0, c1, ps):
            p_.vop("tensor_scalar", ctx["scrfa"][:, 0:c1 - c0], ps[:, 0:c1 - c0],
                   bcolB[:, mi:mi + 1], None, ALU.add)
            p_.sop("activation", big[:, mi * T + c0: mi * T + c1],
                   ctx["scrfa"][:, 0:c1 - c0], AF.Relu)
        h_proj(p, ctx, W1aT, t16, NF, list(range(NF)), epi_r)
        p.dma(b2acol[:, :], b2a_col[:, :])
        for c0, c1 in ntiles(T):
            w = c1 - c0
            for k in range(NF):
                p.dma(ctx["wt"][:, k * 128: k * 128 + 64],
                      W2aT[k * 128:(k + 1) * 128, :], engine="gpsimd")
            for k in range(NF):
                p.mm(ctx["mmps"][0:64, 0:w], ctx["wt"][:, k * 128: k * 128 + 64],
                     big[:, k * T + c0: k * T + c1],
                     start=(k == 0), stop=(k == NF - 1))
            p.vop("tensor_scalar", ctx["scrfa"][0:64, 0:w], ctx["mmps"][0:64, 0:w],
                  b2acol[0:64, 0:1], None, ALU.add)
            p.vop("tensor_copy", t64[:, c0:c1], ctx["scrfa"][0:64, 0:w])
        p.dma(sg[:, :], sigT[:, :], engine="gpsimd")
        for tt in range(16):
            p.mm(ctx["mmps"][0:128, 0:16], t64[:, tt * 128:(tt + 1) * 128],
                 sg[:, :])
            p.vop("tensor_copy", gt16[:, tt * 16:(tt + 1) * 16],
                  ctx["mmps"][0:128, 0:16])
            sl = gt16[:, tt * 16:(tt + 1) * 16]
            p.vop("reduce_max", m1c[:, 0:1], sl, AX.X)
            p.vop("tensor_scalar", m2c[:, 0:1], m1c[:, 0:1], -1.0, None, ALU.mult)
            p.sop("activation", sl, sl, AF.Exp, bias=m2c[:, 0:1])
            p.vop("reduce_sum", m1c[:, 0:1], sl, AX.X)
            p.vop("reciprocal", m2c[:, 0:1], m1c[:, 0:1])
            p.vop("tensor_scalar", sl, sl, m2c[:, 0:1], None, ALU.mult)

        p.dma(bcolA[:, 0:NF], bq3[:, :])
        p.dma(bcolB[:, :], bk3[:, :])
        p.dma(bcolC[:, :], bv3[:, :])
        p.dma(bcolD[:, :], bo3[:, :])
        for fc in range(NF):
            for wT, bcol, dst, scal in ((wqT3, bcolA, qc, 0.125),
                                        (wkT3, bcolB, kc_, None),
                                        (wvT3, bcolC, vc, None)):
                for k in range(NF):
                    p.dma(ctx["wt"][:, k * 128:(k + 1) * 128],
                          wT[k * 128:(k + 1) * 128, fc * 128:(fc + 1) * 128],
                          engine="gpsimd")
                for c0, c1 in ntiles(T):
                    w = c1 - c0
                    for k in range(NF):
                        p.mm(ctx["mmps"][:, 0:w], ctx["wt"][:, k * 128:(k + 1) * 128],
                             t16[:, k * T + c0: k * T + c1],
                             start=(k == 0), stop=(k == NF - 1))
                    if scal is None:
                        p.vop("tensor_scalar", dst[:, c0:c1], ctx["mmps"][:, 0:w],
                              bcol[:, fc:fc + 1], None, ALU.add)
                    else:
                        p.vop("tensor_scalar", dst[:, c0:c1], ctx["mmps"][:, 0:w],
                              bcol[:, fc:fc + 1], scal, ALU.add, ALU.mult)
            for hh in range(2):
                hr = hh * 64
                h_ = fc * 2 + hh
                for tt in range(16):
                    p.mm(ctx["rps"][0:1, 0:128],
                         gt16[:, tt * 16 + h_: tt * 16 + h_ + 1], idff[:, :])
                    p.vop("tensor_copy", grow[0:1, tt * 128:(tt + 1) * 128],
                          ctx["rps"][0:1, 0:128])
                for t_ in range(2):
                    for s_ in range(2):
                        rrow = r0row if s_ == 0 else r1row
                        for c0, c1 in ntiles(SLOC):
                            w = c1 - c0
                            p.vop("tensor_mul", ctx["scrfa"][:, 0:w],
                                  qc[:, t_ * SLOC + c0: t_ * SLOC + c1],
                                  kc_[:, s_ * SLOC + c0: s_ * SLOC + c1])
                            p.vop("tensor_copy", ctx["scr16a"][:, 0:w],
                                  ctx["scrfa"][:, 0:w])
                            p.mm(ctx["rps"][0:1, 0:w],
                                 ctx["onesP"][hr:hr + 64, 0:1],
                                 ctx["scr16a"][hr:hr + 64, 0:w],
                                 tile_position=(hr, 0))
                            p.vop("tensor_copy", rrow[0:1, c0:c1],
                                  ctx["rps"][0:1, 0:w])
                    p.vop("tensor_sub", r0row[0:1, :], r0row[0:1, :], r1row[0:1, :])
                    p.sop("activation", r0row[0:1, :], r0row[0:1, :], AF.Sigmoid)
                    for c0, c1 in ntiles(SLOC):
                        w = c1 - c0
                        p.mm(ctx["bc1"][:, 0:w], ctx["ones1f"][0:1, :],
                             r0row[0:1, c0:c1])
                        p.vop("tensor_sub", ctx["scrfa"][:, 0:w],
                              vc[:, 0 * SLOC + c0: 0 * SLOC + c1],
                              vc[:, 1 * SLOC + c0: 1 * SLOC + c1])
                        p.vop("tensor_mul", ctx["scrfa"][:, 0:w],
                              ctx["scrfa"][:, 0:w], ctx["bc1"][:, 0:w])
                        p.vop("tensor_add", ctx["scrfa"][:, 0:w],
                              ctx["scrfa"][:, 0:w],
                              vc[:, 1 * SLOC + c0: 1 * SLOC + c1])
                        p.mm(ctx["bc2"][:, 0:w], ctx["ones1f"][0:1, :],
                             grow[0:1, t_ * SLOC + c0: t_ * SLOC + c1])
                        p.vop("tensor_mul", ctx["scrfa"][hr:hr + 64, 0:w],
                              ctx["scrfa"][hr:hr + 64, 0:w],
                              ctx["bc2"][hr:hr + 64, 0:w])
                        p.vop("tensor_copy",
                              big[hr:hr + 64, fc * T + t_ * SLOC + c0:
                                  fc * T + t_ * SLOC + c1],
                              ctx["scrfa"][hr:hr + 64, 0:w])

        def epi_x7(p_, mi, c0, c1, ps):
            d = xsb[:, mi * T + c0: mi * T + c1]
            p_.vop("tensor_scalar", ctx["scrfa"][:, 0:c1 - c0], ps[:, 0:c1 - c0],
                   bcolD[:, mi:mi + 1], None, ALU.add)
            p_.vop("tensor_add", d, d, ctx["scrfa"][:, 0:c1 - c0])
        h_proj(p, ctx, woT3, big, NF, list(range(NF)), epi_x7)

        h_ln_to(p, ctx, xsb, t16, NF)
        p.dma(rb8[0:1, :], rb3[0:1, :])
        for fc in range(NF):
            p.dma(rwsb[:, fc * 8:(fc + 1) * 8], rw3[fc * 128:(fc + 1) * 128, :],
                  engine="gpsimd")
        for tt in range(16):
            for fc in range(NF):
                p.mm(ctx["mmps"][0:128, 0:8],
                     t16[:, fc * T + tt * 128: fc * T + (tt + 1) * 128],
                     rwsb[:, fc * 8:(fc + 1) * 8],
                     start=(fc == 0), stop=False)
            p.mm(ctx["mmps"][0:128, 0:8], ctx["ones1f"][0:1, :], rb8[0:1, 0:8],
                 start=False, stop=True)
            p.vop("tensor_copy", lg8[:, :], ctx["mmps"][0:128, 0:8])
            p.vop("reduce_max", m1c[:, 0:1], lg8[:, :], AX.X)
            p.vop("tensor_scalar", msk1[:, :], lg8[:, :], m1c[:, 0:1], None,
                  ALU.is_ge)
            p.vop("tensor_copy", l2nd[:, :], msk1[:, :])
            p.vop("scalar_tensor_tensor", l2nd[:, :], l2nd[:, :], -1e30,
                  lg8[:, :], ALU.mult, ALU.add)
            p.vop("reduce_max", m2c[:, 0:1], l2nd[:, :], AX.X)
            p.vop("tensor_scalar", msk2[:, :], lg8[:, :], m2c[:, 0:1], None,
                  ALU.is_ge)
            p.vop("tensor_scalar", m3c[:, 0:1], m2c[:, 0:1], -1.0, None, ALU.mult)
            p.sop("activation", m3c[:, 0:1], m1c[:, 0:1], AF.Sigmoid,
                  bias=m3c[:, 0:1])
            p.vop("tensor_copy", g8[:, :], msk1[:, :])
            p.vop("tensor_scalar", g8[:, :], g8[:, :], m3c[:, 0:1], None, ALU.mult)
            p.vop("tensor_copy", ninf[:, :], msk2[:, :])
            p.vop("tensor_copy", lg8[:, :], msk1[:, :])
            p.vop("tensor_mul", lg8[:, :], lg8[:, :], ninf[:, :])
            p.vop("tensor_sub", ninf[:, :], ninf[:, :], lg8[:, :])
            p.vop("tensor_scalar", m1c[:, 0:1], m3c[:, 0:1], -1.0, 1.0,
                  ALU.mult, ALU.add)
            p.vop("tensor_scalar", ninf[:, :], ninf[:, :], m1c[:, 0:1], None,
                  ALU.mult)
            p.vop("tensor_add", gtokE[:, tt * 8:(tt + 1) * 8], g8[:, :], ninf[:, :])

        for e in range(10):
            routed = e < 8
            if routed:
                for tt in range(16):
                    p.mm(ctx["rps"][0:1, 0:128],
                         gtokE[:, tt * 8 + e: tt * 8 + e + 1], idff[:, :])
                    p.vop("tensor_copy", grow[0:1, tt * 128:(tt + 1) * 128],
                          ctx["rps"][0:1, 0:128])
            wa = rw1 if routed else sw1
            wb = rw3e if routed else sw3
            wc = rw2 if routed else sw2
            ei = e if routed else e - 8

            def epi_h1(p_, mi, c0, c1, ps):
                p_.sop("activation", big[:, mi * T + c0: mi * T + c1],
                       ps[:, 0:c1 - c0], AF.Silu)
            h_proj_view(p, ctx, wa, ei, lambda k: t16[:, k * T:(k + 1) * T],
                        NF, [0, 1, 2, 3], epi_h1, TW=T)

            def epi_h3(p_, mi, c0, c1, ps):
                d = big[:, mi * T + c0: mi * T + c1]
                p_.vop("tensor_mul", d, d, ps[:, 0:c1 - c0])
            h_proj_view(p, ctx, wb, ei, lambda k: t16[:, k * T:(k + 1) * T],
                        NF, [0, 1, 2, 3], epi_h3, TW=T)

            def epi_y(p_, mi, c0, c1, ps, routed=routed):
                w = c1 - c0
                if routed:
                    p_.mm(ctx["bc1"][:, 0:w], ctx["ones1f"][0:1, :],
                          grow[0:1, c0:c1])
                    p_.vop("tensor_copy", ctx["scrfa"][:, 0:w], ps[:, 0:w])
                    p_.vop("tensor_mul", ctx["scrfa"][:, 0:w],
                           ctx["scrfa"][:, 0:w], ctx["bc1"][:, 0:w])
                    d = xsb[:, mi * T + c0: mi * T + c1]
                    p_.vop("tensor_add", d, d, ctx["scrfa"][:, 0:w])
                else:
                    d = xsb[:, mi * T + c0: mi * T + c1]
                    p_.vop("tensor_add", d, d, ps[:, 0:w])
            h_proj_view(p, ctx, wc, ei, lambda k: big[:, k * T:(k + 1) * T],
                        4, list(range(NF)), epi_y, TW=T)

        for fc in range(NF):
            p.dma(x8_out[fc, :, :], xsb[:, fc * T:(fc + 1) * T])

    nc = p.build(body)
    return nc


def h_proj2(p, ctx, wT_dram, read_k, kchunks, mlist, epilogue, TW):
    wt = ctx["wt"]
    ps = ctx["mmps"]
    rk = read_k if callable(read_k) else (lambda k: read_k)
    dbl = (2 * kchunks * 128 <= 4096)
    for mi, m in enumerate(mlist):
        so = (mi % 2) * kchunks * 128 if dbl else 0
        p.dma(wt[:, so: so + kchunks * 128].rearrange("f (k m) -> f k m", k=kchunks),
              wT_dram[0:kchunks * 128, m * 128:(m + 1) * 128].rearrange(
                  "(k f) m -> f k m", f=128), engine="gpsimd")
        for c0, c1 in ntiles(TW):
            w = c1 - c0
            for k in range(kchunks):
                p.mm(ps[:, 0:w], wt[:, so + k * 128: so + (k + 1) * 128],
                     rk(k)[:, c0:c1], start=(k == 0), stop=(k == kchunks - 1))
            epilogue(p, mi, c0, c1, ps)


def stage_launch8(inputs, x5_slices, sc_slices):
    scf = np.empty((B, SEQ), dtype=np.float32)
    for c in range(NCORES):
        s = np.asarray(sc_slices[c]).reshape(1, B, SLOC)
        for b in range(B):
            scf[b, SLOC * c:SLOC * (c + 1)] = s[0, b]

    def rs(v):
        return np.ascontiguousarray(np.asarray(v).reshape(NF, 128).T)
    maps = []
    for c in range(NCORES):
        sco = np.asarray(sc_slices[c]).reshape(T)
        sco_cols = np.ascontiguousarray(sco.reshape(16, 128).T).astype(np.float32)
        poso = np.concatenate([np.arange(SLOC * c, SLOC * (c + 1),
                                         dtype=np.float32)] * B)
        pos_cols = np.ascontiguousarray(poso.reshape(16, 128).T)
        maps.append({
            "x5": x5_slices[c], "scf": scf, "sco_cols": sco_cols,
            "pos_cols": pos_cols,
            "W1m": np.ascontiguousarray(np.asarray(inputs["L2_ffn_W1"])),
            "b1m": np.ascontiguousarray(np.asarray(inputs["L2_ffn_b1"]).reshape(32, 128).T),
            "W2mT": np.ascontiguousarray(np.asarray(inputs["L2_ffn_W2"])),
            "b2m": rs(inputs["L2_ffn_b2"]),
            "wqT3": np.ascontiguousarray(np.asarray(inputs["L3_attn_Wq"]).T),
            "wkT3": np.ascontiguousarray(np.asarray(inputs["L3_attn_Wk"]).T),
            "wvT3": np.ascontiguousarray(np.asarray(inputs["L3_attn_Wv"]).T),
            "woT3": np.ascontiguousarray(np.asarray(inputs["L3_attn_Wo"]).T),
            "bq3": rs(inputs["L3_attn_bq"]), "bk3": rs(inputs["L3_attn_bk"]),
            "bv3": rs(inputs["L3_attn_bv"]), "bo3": rs(inputs["L3_attn_bo"]),
            "W1aT": np.ascontiguousarray(np.asarray(inputs["L3_attn_W1"]).T),
            "b1a": rs(inputs["L3_attn_b1"]),
            "W2aT": np.ascontiguousarray(np.asarray(inputs["L3_attn_W2"]).T),
            "b2a_col": np.ascontiguousarray(np.asarray(inputs["L3_attn_b2"]).reshape(64, 1)),
            "sigT": np.ascontiguousarray(np.asarray(inputs["L3_attn_head_signatures"]).T),
            "rw3": np.ascontiguousarray(np.asarray(inputs["L3_ffn_router_w"])),
            "rb3": np.ascontiguousarray(np.asarray(inputs["L3_ffn_router_b"]).reshape(1, 8)),
            "rw1": np.ascontiguousarray(np.asarray(inputs["L3_ffn_routed_w1"])),
            "rw3e": np.ascontiguousarray(np.asarray(inputs["L3_ffn_routed_w3"])),
            "rw2": np.ascontiguousarray(np.asarray(inputs["L3_ffn_routed_w2"])),
            "sw1": np.ascontiguousarray(np.asarray(inputs["L3_ffn_shared_w1"])),
            "sw3": np.ascontiguousarray(np.asarray(inputs["L3_ffn_shared_w3"])),
            "sw2": np.ascontiguousarray(np.asarray(inputs["L3_ffn_shared_w2"])),
        })
    return maps


def fm(x):
    xr = np.asarray(x).reshape(B, NCORES, SLOC, NF, 128).transpose(3, 4, 0, 1, 2)
    return [np.ascontiguousarray(xr[:, :, :, c, :].reshape(NF, 128, T))
            for c in range(NCORES)]


def fm_inv(slices):
    out = np.empty((B, NCORES, SLOC, NF, 128), dtype=np.asarray(slices[0]).dtype)
    for c in range(NCORES):
        xr = np.asarray(slices[c]).reshape(NF, 128, B, SLOC)
        out[:, c] = xr.transpose(2, 3, 0, 1)
    return out.reshape(B, SEQ, NF * 128)


def swap_perm(n):
    q = np.arange(n)
    q[0::2] += 1
    q[1::2] -= 1
    return q


def rope_invfreq_col():
    hd = 32
    inv = 1.0 / (10000.0 ** (np.arange(0, hd, 2, dtype=np.float64) / hd))
    col = np.zeros(128, dtype=np.float32)
    for r in range(128):
        col[r] = inv[(r % 32) // 2]
    return col.reshape(128, 1)


def rope_sign_col():
    col = np.where(np.arange(128) % 2 == 0, -1.0, 1.0).astype(np.float32)
    return col.reshape(128, 1)


def stage_launch2(inputs, x0_slices, og_slices):
    wg1T = np.ascontiguousarray(inputs["L0_attn_W_g1"].T)
    wg2T = np.ascontiguousarray(inputs["L0_attn_W_g2"].T)
    lnw = np.ascontiguousarray(np.asarray(inputs["L0_attn_ln_w"]).reshape(NF, 128).T)
    woutT = np.ascontiguousarray(inputs["L0_attn_W_out"].T)
    g1 = np.ascontiguousarray(np.asarray(inputs["L0_ffn_g1"]).reshape(NF, 128).T)
    wdown = np.ascontiguousarray(inputs["L0_ffn_w_down"])
    sp = swap_perm(512)
    wq = np.ascontiguousarray(inputs["L0_ffn_wq"])
    wk = np.ascontiguousarray(inputs["L0_ffn_wk"])
    wv = np.ascontiguousarray(inputs["L0_ffn_wv"])
    wq_sw = np.ascontiguousarray(wq[:, sp])
    wk_sw = np.ascontiguousarray(wk[:, sp])
    ivf = rope_invfreq_col()
    sgn = rope_sign_col()
    maps = []
    for c in range(NCORES):
        pos = np.concatenate([np.arange(SLOC * c, SLOC * (c + 1), dtype=np.int32)] * B)
        maps.append({
            "x0": x0_slices[c], "og": og_slices[c],
            "wg1T": wg1T, "wg2T": wg2T, "lnw": lnw, "woutT": woutT,
            "g1": g1, "wdown": wdown, "wq": wq, "wq_sw": wq_sw,
            "wk": wk, "wk_sw": wk_sw, "wv": wv,
            "pos": pos.reshape(1, T), "invfreq": ivf, "signc": sgn,
        })
    return maps



RG8 = [[0, 1, 2, 3, 4, 5, 6, 7]]


def build_fused(strict=False):
    p = P("fused", strict=strict)
    C = 128
    NCH = SEQ // C
    QT = 512
    KC = 128
    KHI = [(SEQ - SLOC + (qt + 1) * QT) // KC for qt in range(2)]
    HL = 1280
    TWQ = 512
    TWH = 1024

    xb = p.inp("xb", [NF, 128, SEQ])
    x0 = p.inp("x0", [NF, 128, T])
    w_inT = p.inp("w_inT", [1024, 768])
    mask_in = p.inp("mask_intra", [64, 128], U8)
    wg1T = p.inp("wg1T", [1024, 128])
    wg2T = p.inp("wg2T", [128, 1024])
    lnw = p.inp("lnw", [128, NF])
    woutT = p.inp("woutT", [1024, 1024])
    g1in = p.inp("g1", [128, NF])
    wdown = p.inp("wdown", [1024, 512])
    wq = p.inp("wq", [512, 512])
    wq_sw = p.inp("wq_sw", [512, 512])
    wk = p.inp("wk", [512, 512])
    wk_sw = p.inp("wk_sw", [512, 512])
    wv = p.inp("wv", [512, 512])
    pos_i = p.inp("pos", [1, T], I32)
    invfreq = p.inp("invfreq", [128, 1])
    signc = p.inp("signc", [128, 1])
    wupT = p.inp("wupT", [512, 1024])
    masks = p.inp("masks", [2, KHI[1], 128, QT], U8)
    g2in = p.inp("g2", [128, NF])
    router_w = p.inp("router_w", [1024, 8])
    w1_in = p.inp("w1", [8, 1024, 2048])
    w3_in = p.inp("w3", [8, 1024, 2048])
    w2_in = p.inp("w2", [8, 2048, 1024])
    wbT = p.inp("wbT", [1024, 64])
    wvT = p.inp("wvT", [1024, 1024])
    bv_in = p.inp("bv", [128, NF])
    woT = p.inp("woT", [1024, 1024])
    bo_in = p.inp("bo", [128, NF])
    router_w6 = p.inp("router_w6", [1024, 4])
    router_b6 = p.inp("router_b6", [1, 4])
    w1_6 = p.inp("w1_6", [4, 1024, 4096])
    w3_6 = p.inp("w3_6", [4, 1024, 4096])
    w2_6 = p.inp("w2_6", [4, 4096, 1024])
    wqT6 = p.inp("wqT6", [1024, 1024])
    wkT6 = p.inp("wkT6", [1024, 1024])
    wvT6 = p.inp("wvT6", [1024, 1024])
    bq6 = p.inp("bq6", [128, NF])
    bk6 = p.inp("bk6", [128, NF])
    bv6 = p.inp("bv6", [128, NF])
    woT7 = p.inp("woT7", [1024, 1024])
    bo7 = p.inp("bo7", [128, NF])
    masks2 = p.inp("masks2", [B, 8, 128, 384], U8)
    wr_in = p.inp("wr", [128, NF])
    pos_cols = p.inp("pos_cols", [128, 16])
    W1m = p.inp("W1m", [1024, 4096])
    b1m = p.inp("b1m", [128, 32])
    W2mT = p.inp("W2mT", [4096, 1024])
    b2m = p.inp("b2m", [128, NF])
    wqT3 = p.inp("wqT3", [1024, 1024])
    wkT3 = p.inp("wkT3", [1024, 1024])
    wvT3 = p.inp("wvT3", [1024, 1024])
    woT3 = p.inp("woT3", [1024, 1024])
    bq3 = p.inp("bq3", [128, NF])
    bk3 = p.inp("bk3", [128, NF])
    bv3 = p.inp("bv3", [128, NF])
    bo3 = p.inp("bo3", [128, NF])
    W1aT = p.inp("W1aT", [1024, 1024])
    b1a = p.inp("b1a", [128, NF])
    W2aT = p.inp("W2aT", [1024, 64])
    b2a_col = p.inp("b2a_col", [64, 1])
    sigT = p.inp("sigT", [64, 16])
    rw3 = p.inp("rw3", [1024, 8])
    rb3 = p.inp("rb3", [1, 8])
    rw1 = p.inp("rw1", [8, 1024, 512])
    rw3e = p.inp("rw3e", [8, 1024, 512])
    rw2 = p.inp("rw2", [8, 512, 1024])
    sw1 = p.inp("sw1", [2, 1024, 512])
    sw3 = p.inp("sw3", [2, 1024, 512])
    sw2 = p.inp("sw2", [2, 512, 1024])

    x8_out = p.out("x8", [NF, 128, T])

    cc1_in = p.nc.dram_tensor("cc1_in", [8 * 256, 1024], F32).ap()
    cc1_out = p.nc.dram_tensor("cc1_out", [8 * 256, 1024], F32).ap()
    cc2k_in = p.nc.dram_tensor("cc2k_in", [1024, 1024], BF16).ap()
    cc2k_out = p.nc.dram_tensor("cc2k_out", [8 * 1024, 1024], BF16).ap()
    cc2v_in = p.nc.dram_tensor("cc2v_in", [2048, 512], BF16).ap()
    cc2v_out = p.nc.dram_tensor("cc2v_out", [8 * 2048, 512], BF16).ap()
    cc3_in = p.nc.dram_tensor("cc3_in", [2176, 1024], BF16).ap()
    cc3_out = p.nc.dram_tensor("cc3_out", [8 * 2176, 1024], BF16).ap()
    cc4k_in = p.nc.dram_tensor("cc4k_in", [2048, 256], BF16).ap()
    cc4k_out = p.nc.dram_tensor("cc4k_out", [8 * 2048, 256], BF16).ap()
    cc4v_in = p.nc.dram_tensor("cc4v_in", [512, 1024], BF16).ap()
    cc4v_out = p.nc.dram_tensor("cc4v_out", [8 * 512, 1024], BF16).ap()
    cc5_in = p.nc.dram_tensor("cc5_in", [2, 1024], F32).ap()
    cc5_out = p.nc.dram_tensor("cc5_out", [16, 1024], F32).ap()
    qs0 = p.nc.dram_tensor("qs0", [4, 128, T], BF16).ap()
    xs1 = p.nc.dram_tensor("xs1", [NF, 128, T], F32).ap()
    qs2 = p.nc.dram_tensor("qs2", [NF, 128, T], BF16).ap()
    ks2 = p.nc.dram_tensor("ks2", [NF, 128, T], BF16).ap()
    vs2 = p.nc.dram_tensor("vs2", [NF, 128, T], BF16).ap()

    def body(p):
        ctx = make_ctx(p, T)
        xsb = p.sb("g_xsb", [128, NF * T], F32)
        lat16 = p.sb("g_lat16", [64, T], BF16)
        sco_cols = p.sb("g_sco_cols", [128, 16], F32)
        identf = p.sb("g_identf", [128, 128], F32)
        identb = p.sb("g_identb", [128, 128], BF16)
        idff = identf
        pidc = p.sb("g_pidc", [128, 1], F32)
        vt_sb = p.sb("g_vt", [128, 1024], BF16)

        idi = p.sb("g_idi", [128, 128], I32)
        idp = p.sb("g_idp", [128, 1], I32)
        idpf = p.sb("g_idpf", [128, 1], F32)
        pid_i = p.sb("g_pidi", [1, 1], I32)
        pid_f = p.sb("g_pidf", [1, 1], F32)
        p.gop("iota", idi[:, :], [[1, 128]], base=0, channel_multiplier=0)
        p.gop("iota", idp[:, :], [[0, 1]], base=0, channel_multiplier=1)
        p.vop("tensor_copy", identf[:, :], idi[:, :])
        p.vop("tensor_copy", idpf[:, :], idp[:, :])
        p.vop("tensor_scalar", identf[:, :], identf[:, :], idpf[:, 0:1], None,
              ALU.is_equal)
        p.vop("tensor_copy", identb[:, :], identf[:, :])

        def emit_pid(eng):
            pid = p.nc.partition_id(engines=[mybir.EngineType.Pool])
            return eng.store(pid_i[0:1, 0:1], pid)
        p.ops.append(("gpsimd", emit_pid, dict(p.cnts), "G", 1))
        p.cnts["G"] += 1
        if not p.strict:
            p._note(pid_i[0:1, 0:1], True, "G", p.cnts["G"], {})
        p.vop("tensor_copy", pid_f[:, :], pid_i[:, :])
        p.mm(ctx["bc1"][:, 0:1], ctx["ones1f"][0:1, :], pid_f[0:1, 0:1])
        p.vop("tensor_copy", pidc[:, 0:1], ctx["bc1"][:, 0:1])

        p.push_scope()
        xc = p.sb("s1_xc", [128, NF * C], F32)
        xln = p.sb("s1_xln", [128, NF * C], BF16)
        wsb = p.sb("s1_wsb", [128, NF * 768], BF16)
        S0 = p.sb("s1_S0", [128, 128], F32)
        S1 = p.sb("s1_S1", [128, 128], F32)
        S016 = p.sb("s1_S016", [128, 128], BF16)
        S116 = p.sb("s1_S116", [128, 128], BF16)
        qch = p.sb("s1_qch", [128, C], BF16)
        kch = p.sb("s1_kch", [128, C], F32)
        vch = p.sb("s1_vch", [128, C], BF16)
        gch = p.sb("s1_gch", [128, C], F32)
        Ach = p.sb("s1_Ach", [128, C], F32)
        q1b = p.sb("s1_q1b", [128, C], BF16)
        k1b = p.sb("s1_k1b", [128, C], BF16)
        qib = p.sb("s1_qib", [128, C], BF16)
        k2b = p.sb("s1_k2b", [128, C], BF16)
        vTa = p.sb("s1_vTa", [64, 128], BF16)
        vTb = p.sb("s1_vTb", [64, 128], BF16)
        kTa = p.sb("s1_kTa", [64, 128], BF16)
        kTb = p.sb("s1_kTb", [64, 128], BF16)
        P16 = p.sb("s1_P16", [64, 128], BF16)
        Ecol = p.sb("s1_Ecol", [128, 4], F32)
        zcol = p.sb("s1_zcol", [128, 1], F32)
        osb = p.sb("s1_osb", [128, C], F32)
        msk = p.sb("s1_msk", [64, 128], U8)
        tps = p.ps("s1_tps", [64, 128], BF16)

        p.dma(msk[:, :], mask_in[:, :])
        p.vop("memset", S0[:], 0.0)
        p.vop("memset", S1[:], 0.0)
        p.vop("memset", zcol[:], 0.0)
        for k in range(NF):
            p.dma(wsb[:, k * 768:(k + 1) * 768],
                  w_inT[k * 128:(k + 1) * 128, :], engine="gpsimd")

        for ch in range(NCH):
            s0_ = ch * C
            for fc in range(NF):
                p.dma(xc[:, fc * C:(fc + 1) * C], xb[fc, :, s0_:s0_ + C])
            h_ln_to(p, ctx, xc, xln, NF, TW=C)

            for unit in range(2):
                S = S0 if unit == 0 else S1
                S16 = S016 if unit == 0 else S116
                for mg in range(3):
                    for k in range(NF):
                        p.mm(ctx["mmps"][:, 0:C],
                             wsb[:, k * 768 + unit * 384 + mg * 128:
                                 k * 768 + unit * 384 + (mg + 1) * 128],
                             xln[:, k * C:(k + 1) * C],
                             start=(k == 0), stop=(k == NF - 1))
                    if mg == 0:
                        p.vop("tensor_copy", vch[:, :], ctx["mmps"][:, 0:C])
                    elif mg == 1:
                        p.sop("activation", qch[:, :], ctx["mmps"][:, 0:C], AF.Silu)
                    else:
                        p.sop("activation", kch[:, :], ctx["mmps"][:, 0:C],
                              AF.Sigmoid, scale=-1.0)
                        p.vop("tensor_scalar", gch[:, :], kch[:, :], -1.0, 1.0,
                              ALU.mult, ALU.add)
                        p.sop("activation", gch[:, :], gch[:, :], AF.Ln)
                p.vop("tensor_tensor_scan", Ach[:, :], gch[:, :], gch[:, :],
                      zcol[:, 0:1], ALU.add, ALU.bypass)
                for j in range(4):
                    lo, hi = 32 * j, 32 * (j + 1)
                    amid = Ach[:, lo + 15:lo + 16]
                    aend = Ach[:, hi - 1:hi]
                    abase = Ach[:, lo - 1:lo] if j > 0 else zcol[:, 0:1]
                    namid = Ecol[:, 0:1]
                    nabase = Ecol[:, 1:2]
                    p.vop("tensor_scalar", namid, amid, -1.0, None, ALU.mult)
                    p.vop("tensor_scalar", nabase, abase, -1.0, None, ALU.mult)
                    p.sop("activation", ctx["scrfa"][:, 0:32], Ach[:, lo:hi],
                          AF.Exp, bias=namid)
                    p.vop("tensor_mul", q1b[:, lo:hi], qch[:, lo:hi],
                          ctx["scrfa"][:, 0:32])
                    p.sop("activation", ctx["scrfa"][:, 0:32], Ach[:, lo:hi],
                          AF.Exp, bias=amid, scale=-1.0)
                    p.vop("tensor_mul", k1b[:, lo:hi], kch[:, lo:hi],
                          ctx["scrfa"][:, 0:32])
                    p.sop("activation", ctx["scrfa"][:, 0:32], Ach[:, lo:hi],
                          AF.Exp, bias=nabase)
                    p.vop("tensor_mul", qib[:, lo:hi], qch[:, lo:hi],
                          ctx["scrfa"][:, 0:32])
                    p.sop("activation", ctx["scrfa"][:, 0:32], Ach[:, lo:hi],
                          AF.Exp, bias=aend, scale=-1.0)
                    p.vop("tensor_mul", k2b[:, lo:hi], kch[:, lo:hi],
                          ctx["scrfa"][:, 0:32])
                for src_, dsts in ((k2b, (kTa, kTb)), (vch, (vTa, vTb))):
                    for hfi, dst in enumerate(dsts):
                        p.transpose(tps[:, :], src_[:, hfi * 64:(hfi + 1) * 64],
                                    identb[:, :])
                        p.vop("tensor_copy", dst[:, :], tps[0:64, :])
                obank = ctx["bc1"]
                pbank = ctx["bc2"]
                sbank = ctx["mmps"]
                for j in range(4):
                    lo, hi = 32 * j, 32 * (j + 1)
                    aend = Ach[:, hi - 1:hi]
                    abase = Ach[:, lo - 1:lo] if j > 0 else zcol[:, 0:1]
                    pb = (j % 2) * 32
                    p.mm(pbank[pb:pb + 32, 0:32], k1b[:, lo:hi], q1b[:, lo:hi],
                         tile_position=(0, pb))
                    p.vop("memset", P16[pb:pb + 32, lo:hi], 0.0)
                    p.vop("copy_predicated", P16[pb:pb + 32, lo:hi],
                          msk[pb:pb + 32, lo:hi], pbank[pb:pb + 32, 0:32])
                    p.vop("tensor_copy", S16[:, :], S[:, :])
                    p.mm(obank[:, lo:hi], S16[:, :], qib[:, lo:hi],
                         start=True, stop=False)
                    vT = vTa if j < 2 else vTb
                    vbase = (j % 2) * 32
                    p.mm(obank[:, lo:hi], vT[vbase:vbase + 32, :],
                         P16[pb:pb + 32, lo:hi], start=False, stop=True)
                    kT = kTa if j < 2 else kTb
                    p.mm(sbank[:, 0:128], kT[vbase:vbase + 32, :],
                         vT[vbase:vbase + 32, :])
                    p.vop("tensor_scalar", Ecol[:, 1:2], abase, -1.0, None, ALU.mult)
                    p.sop("activation", Ecol[:, 2:3], aend, AF.Exp, bias=Ecol[:, 1:2])
                    p.vop("scalar_tensor_tensor", S[:, :], S[:, :], Ecol[:, 2:3],
                          sbank[:, 0:128], ALU.mult, ALU.add)
                p.vop("tensor_copy", osb[:, :], obank[:, 0:C])
                p.dma(cc1_in[(ch // 8) * 256 + unit * 128:
                             (ch // 8) * 256 + (unit + 1) * 128,
                             (ch % 8) * 128:(ch % 8 + 1) * 128], osb[:, :])
        p.pop_scope()
        p.collective("AllToAll", ALU.bypass, RG8, [cc1_in.opt()], [cc1_out.opt()])

        p.push_scope()
        TW = 1024
        a16 = p.sb("s2_a16", [128, NF * TW], BF16)
        b16 = p.sb("s2_b16", [128, NF * TW], BF16)
        c16 = p.sb("s2_c16", [128, 4 * TW], BF16)
        d16 = p.sb("s2_d16", [128, 4 * TW], BF16)
        t1 = p.sb("s2_t1", [128, TW], BF16)
        gam = p.sb("s2_gam", [128, NF], F32)
        cosT = p.sb("s2_cosT", [128, TW], F32)
        sinTs = p.sb("s2_sinTs", [128, TW], F32)
        ivf = p.sb("s2_ivf", [128, 1], F32)
        sgn = p.sb("s2_sgn", [128, 1], F32)
        posi_sb = p.sb("s2_posi", [1, TW], I32)
        angi = p.sb("s2_angi", [128, 512], I32)
        tpsb = p.ps("s2_tps", [128, 128], BF16)

        p.dma(ivf[:, :], invfreq[:, :])
        p.dma(sgn[:, :], signc[:, :])

        for half in range(2):
            t0 = half * TW
            bsel = half
            for fc in range(NF):
                p.dma(xsb[:, fc * T + t0: fc * T + t0 + TW], x0[fc, :, t0:t0 + TW])
                srcb = (bsel * 4 + fc // 2) * 256 + (fc % 2) * 128
                p.dma(b16[:, fc * TW:(fc + 1) * TW],
                      cc1_out[srcb:srcb + 128, :], engine="gpsimd")

            h_ln_to(p, ctx, xsb, a16, NF, TW=TW, src_stride=T, src_off=t0)

            h_proj(p, ctx, wg1T, a16, NF, [0], h_epi_copy(t1, TW), TW=TW)

            def epi_gate(p_, mi, c0, c1, ps):
                p_.sop("activation", ctx["scr16b"][:, 0:c1 - c0], ps[:, 0:c1 - c0],
                       AF.Sigmoid)
                dst = b16[:, mi * TW + c0: mi * TW + c1]
                p_.vop("tensor_mul", dst, dst, ctx["scr16b"][:, 0:c1 - c0])
            h_proj(p, ctx, wg2T, t1, 1, list(range(NF)), epi_gate, TW=TW)

            p.dma(gam[:, :], lnw[:, :])
            h_ln_to(p, ctx, b16, b16, NF, gamma=gam, TW=TW)

            def epi_wout(p_, mi, c0, c1, ps):
                d = xsb[:, mi * T + t0 + c0: mi * T + t0 + c1]
                p_.vop("tensor_add", d, d, ps[:, 0:c1 - c0])
            h_proj(p, ctx, woutT, b16, NF, list(range(NF)), epi_wout, TW=TW)
            for fc in range(NF):
                p.dma(xs1[fc, :, t0:t0 + TW], xsb[:, fc * T + t0: fc * T + t0 + TW])

            h_ln_to(p, ctx, xsb, xsb, NF, TW=TW, src_stride=T, src_off=t0,
                    dst_stride=T, dst_off=t0)
            p.dma(gam[:, :], g1in[:, :])
            h_ln_to(p, ctx, xsb, a16, NF, gamma=gam, rms=True, TW=TW,
                    src_stride=T, src_off=t0)

            h_proj(p, ctx, wdown, a16, NF, [0, 1, 2, 3], h_epi_copy(c16, TW), TW=TW)

            p.dma(posi_sb[0:1, :], pos_i[0:1, t0:t0 + TW])
            p.vop("tensor_copy", ctx["rows"][0][0:1, 0:TW], posi_sb[0:1, :])
            for c0, c1 in ntiles(TW):
                w = c1 - c0
                p.mm(ctx["bc1"][:, 0:w], ctx["ones1f"][0:1, :],
                     ctx["rows"][0][0:1, c0:c1])
                p.vop("tensor_scalar", ctx["scrfa"][:, 0:w], ctx["bc1"][:, 0:w],
                      ivf[:, 0:1], None, ALU.mult)
                for dst, phase in ((sinTs, 0.0), (cosT, HALF_PI)):
                    if phase != 0.0:
                        p.vop("tensor_scalar", ctx["scrfb"][:, 0:w],
                              ctx["scrfa"][:, 0:w], phase, None, ALU.add)
                    else:
                        p.vop("tensor_copy", ctx["scrfb"][:, 0:w],
                              ctx["scrfa"][:, 0:w])
                    p.vop("tensor_scalar", ctx["scrfb"][:, 0:w],
                          ctx["scrfb"][:, 0:w], 1.0 / TWO_PI, None, ALU.mult)
                    p.vop("tensor_copy", angi[:, 0:w], ctx["scrfb"][:, 0:w])
                    p.vop("tensor_copy", dst[:, c0:c1], angi[:, 0:w])
                    p.vop("scalar_tensor_tensor", ctx["scrfb"][:, 0:w],
                          dst[:, c0:c1], -TWO_PI, ctx["scrfa"][:, 0:w],
                          ALU.mult, ALU.add)
                    if phase != 0.0:
                        p.vop("tensor_scalar", ctx["scrfb"][:, 0:w],
                              ctx["scrfb"][:, 0:w], phase, None, ALU.add)
                    p.vop("tensor_scalar", dst[:, c0:c1], ctx["scrfb"][:, 0:w],
                          np.pi, None, ALU.is_gt)
                    p.vop("scalar_tensor_tensor", ctx["scrfb"][:, 0:w],
                          dst[:, c0:c1], -TWO_PI, ctx["scrfb"][:, 0:w],
                          ALU.mult, ALU.add)
                    p.vop("tensor_scalar", dst[:, c0:c1], ctx["scrfb"][:, 0:w],
                          -np.pi, None, ALU.is_lt)
                    p.vop("scalar_tensor_tensor", ctx["scrfb"][:, 0:w],
                          dst[:, c0:c1], TWO_PI, ctx["scrfb"][:, 0:w],
                          ALU.mult, ALU.add)
                    p.sop("activation", dst[:, c0:c1], ctx["scrfb"][:, 0:w], AF.Sin)
            p.vop("tensor_scalar", sinTs[:, :], sinTs[:, :], sgn[:, 0:1], None,
                  ALU.mult)

            h_proj(p, ctx, wq, c16, 4, [0, 1, 2, 3], h_epi_copy(b16, TW), TW=TW)

            def epi_k(p_, mi, c0, c1, ps):
                d = b16[:, (4 + mi) * TW + c0: (4 + mi) * TW + c1]
                p_.vop("tensor_copy", d, ps[:, 0:c1 - c0])
            h_proj(p, ctx, wk, c16, 4, [0, 1, 2, 3], epi_k, TW=TW)
            h_proj(p, ctx, wv, c16, 4, [0, 1, 2, 3], h_epi_copy(d16, TW), TW=TW)

            def mk_rope_epi(base):
                def epi(p_, mi, c0, c1, ps):
                    dst = b16[:, (base + mi) * TW + c0: (base + mi) * TW + c1]
                    p_.vop("tensor_mul", ctx["scrfb"][:, 0:c1 - c0],
                           ps[:, 0:c1 - c0], sinTs[:, c0:c1])
                    p_.vop("tensor_mul", ctx["scrfa"][:, 0:c1 - c0], dst,
                           cosT[:, c0:c1])
                    p_.vop("tensor_add", dst, ctx["scrfa"][:, 0:c1 - c0],
                           ctx["scrfb"][:, 0:c1 - c0])
                return epi
            h_proj(p, ctx, wq_sw, c16, 4, [0, 1, 2, 3], mk_rope_epi(0), TW=TW)
            h_proj(p, ctx, wk_sw, c16, 4, [0, 1, 2, 3], mk_rope_epi(4), TW=TW)

            for m in range(4):
                p.dma(qs0[m, :, t0:t0 + TW], b16[:, m * TW:(m + 1) * TW])
                p.dma(cc2k_in[bsel * 512 + m * 128: bsel * 512 + (m + 1) * 128, :],
                      b16[:, (4 + m) * TW:(5 + m) * TW])
            for tt in range(8):
                for mc in range(4):
                    p.transpose(tpsb[:, :], d16[:, mc * TW + tt * 128:
                                                mc * TW + (tt + 1) * 128],
                                identb[:, :])
                    p.vop("tensor_copy", vt_sb[:, mc * 128:(mc + 1) * 128],
                          tpsb[:, :])
                p.dma(cc2v_in[bsel * 1024 + tt * 128: bsel * 1024 + (tt + 1) * 128, :],
                      vt_sb[:, 0:512])
        p.pop_scope()
        p.collective("AllGather", ALU.bypass, RG8, [cc2k_in.opt()], [cc2k_out.opt()])
        p.collective("AllGather", ALU.bypass, RG8, [cc2v_in.opt()], [cc2v_out.opt()])

        p.push_scope()
        qown = p.sb("s3_qown", [128, 4 * T], BF16)
        Kg = p.sb("s3_Kg", [128, SEQ], BF16)
        VTg = p.sb("s3_VTg", [128, 64 * 128], BF16)
        aos = p.sb("s3_aos", [32, 16 * SLOC], BF16)
        P16a = p.sb("s3_P16", [128, QT], BF16)
        mskb = p.sb("s3_mskb", [128, QT], BF16)
        negc = p.sb("s3_negc", [128, 1], F32)
        cmaxq = p.sb("s3_cmaxq", [128, 1], F32)
        cmaxk = p.sb("s3_cmaxk", [128, 1], F32)
        crow = p.sb("s3_crow", [1, 4], F32)
        wup_sb = p.sb("s3_wup", [32, 16 * 128], BF16)
        ops_ = p.ps("s3_ops", [32, QT], F32)
        ops2_ = p.ps("s3_ops2", [32, QT], F32)
        osb3 = p.sb("s3_osb", [32, QT], F32)

        h_ln_to(p, ctx, xsb, xsb, NF)
        for m in range(4):
            p.dma(qown[:, m * T:(m + 1) * T], qs0[m, :, :])

        def colmax_abs(dst11, col):
            p.transpose(ctx["rps"][0:1, 0:128], col, identf[:, :])
            p.vop("reduce_max", dst11, ctx["rps"][0:1, 0:128], AX.X)

        for b in range(B):
            tb0 = b * SLOC
            for g in range(4):
                for r in range(8):
                    p.dma(Kg[:, r * 1024:(r + 1) * 1024],
                          cc2k_out[r * 1024 + b * 512 + g * 128:
                                   r * 1024 + b * 512 + (g + 1) * 128, :])
                    p.dma(VTg[:, r * 1024:(r + 1) * 1024].rearrange(
                              "p (kc h) -> p kc h", kc=8),
                          cc2v_out[r * 2048 + b * 1024: r * 2048 + (b + 1) * 1024,
                                   128 * g:128 * (g + 1)].rearrange(
                              "(kc q) h -> q kc h", q=128))
                p.vop("reduce_max", cmaxq[:, 0:1],
                      qown[:, g * T + tb0: g * T + tb0 + SLOC], AX.X,
                      apply_absolute_value=True)
                p.vop("reduce_max", cmaxk[:, 0:1], Kg[:, :], AX.X,
                      apply_absolute_value=True)
                colmax_abs(crow[0:1, 0:1], cmaxq[:, 0:1])
                colmax_abs(crow[0:1, 1:2], cmaxk[:, 0:1])
                p.vop("tensor_mul", crow[0:1, 2:3], crow[0:1, 0:1], crow[0:1, 1:2])
                p.vop("tensor_scalar", crow[0:1, 2:3], crow[0:1, 2:3],
                      -float(np.sqrt(32.0)), None, ALU.mult)
                p.mm(ctx["bc1"][:, 0:1], ctx["ones1f"][0:1, :], crow[0:1, 2:3])
                p.vop("tensor_copy", negc[:, 0:1], ctx["bc1"][:, 0:1])

                for qt in range(2):
                    ql = tb0 + qt * QT
                    khi = KHI[qt]
                    for hp in (0, 2):
                        dens = [ctx["rps"], ctx["rps2"]]
                        obk = [ops_, ops2_]
                        for kc in range(khi):
                            last = (kc == khi - 1)
                            p.dma(mskb[:, 0:QT], masks[qt, kc, :, :],
                                  engine="gpsimd")
                            for hi_, h in enumerate((hp, hp + 1)):
                                hb = 32 * h
                                p.mm(ctx["bc2"][:, 0:QT],
                                     Kg[hb:hb + 32, kc * KC:(kc + 1) * KC],
                                     qown[hb:hb + 32, g * T + ql: g * T + ql + QT],
                                     tile_position=(hb, 0))
                                p.sop("activation", P16a[:, 0:QT],
                                      ctx["bc2"][:, 0:QT], AF.Exp,
                                      bias=negc[:, 0:1],
                                      scale=1.0 / float(np.sqrt(32.0)))
                                p.vop("tensor_mul", P16a[:, 0:QT], P16a[:, 0:QT],
                                      mskb[:, 0:QT])
                                p.mm(dens[hi_][0:1, 0:QT], ctx["onesP"][:, 0:1],
                                     P16a[:, 0:QT], start=(kc == 0), stop=last)
                                p.mm(obk[hi_][0:32, 0:QT],
                                     VTg[:, kc * 128 + hb: kc * 128 + hb + 32],
                                     P16a[:, 0:QT], start=(kc == 0), stop=last)
                        for hi_, h in enumerate((hp, hp + 1)):
                            p.vop("reciprocal", ctx["rows"][0][0:1, 0:QT],
                                  dens[hi_][0:1, 0:QT])
                            p.mm(ctx["mmps"][0:32, 0:QT], ctx["ones1f"][0:1, 0:32],
                                 ctx["rows"][0][0:1, 0:QT])
                            p.vop("tensor_copy", osb3[:, 0:QT], obk[hi_][0:32, 0:QT])
                            p.vop("tensor_mul",
                                  aos[:, (g * 4 + h) * SLOC + qt * QT:
                                      (g * 4 + h) * SLOC + (qt + 1) * QT],
                                  osb3[:, 0:QT], ctx["mmps"][0:32, 0:QT])

            for mi in range(NF):
                for s in range(16):
                    p.dma(wup_sb[0:32, s * 128:(s + 1) * 128],
                          wupT[32 * s:32 * (s + 1), mi * 128:(mi + 1) * 128],
                          engine="gpsimd")
                for c0, c1 in ntiles(SLOC):
                    w = c1 - c0
                    for s in range(16):
                        p.mm(ctx["mmps"][:, 0:w],
                             wup_sb[0:32, s * 128:(s + 1) * 128],
                             aos[:, s * SLOC + c0: s * SLOC + c1],
                             start=(s == 0), stop=(s == 15))
                    d = xsb[:, mi * T + tb0 + c0: mi * T + tb0 + c1]
                    p.vop("tensor_add", d, d, ctx["mmps"][:, 0:w])
        p.pop_scope()

        p.push_scope()
        t16 = p.sb("s4_t16", [128, NF * T], BF16)
        gam4 = p.sb("s4_gam", [128, NF], F32)
        gtok = p.sb("s4_gtok", [128, 16 * 8], F32)
        growE = p.sb("s4_growE", [1, TWH], F32)
        rwsb = p.sb("s4_rwsb", [128, NF * 8], BF16)
        ninf = p.sb("s4_ninf", [128, 8], F32)
        l2nd = p.sb("s4_l2nd", [128, 8], F32)
        lg8 = p.sb("s4_lg8", [128, 8], F32)
        m1c = p.sb("s4_m1c", [128, 1], F32)
        m2c = p.sb("s4_m2c", [128, 1], F32)
        m3c = p.sb("s4_m3c", [128, 1], F32)
        msk1 = p.sb("s4_msk1", [128, 8], U8)
        msk2 = p.sb("s4_msk2", [128, 8], U8)
        g8 = p.sb("s4_g8", [128, 8], F32)
        bvc = p.sb("s4_bvc", [128, NF], F32)

        p.dma(gam4[:, :], g2in[:, :])
        p.dma(bvc[:, :], bv_in[:, :])
        for fc in range(NF):
            p.dma(rwsb[:, fc * 8:(fc + 1) * 8],
                  router_w[fc * 128:(fc + 1) * 128, :], engine="gpsimd")
        h_ln_to(p, ctx, xsb, t16, NF, gamma=gam4, rms=True)

        for tt in range(16):
            for fc in range(NF):
                p.mm(ctx["mmps"][0:128, 0:8],
                     t16[:, fc * T + tt * 128: fc * T + (tt + 1) * 128],
                     rwsb[:, fc * 8:(fc + 1) * 8],
                     start=(fc == 0), stop=(fc == NF - 1))
            p.vop("tensor_copy", lg8[:, :], ctx["mmps"][0:128, 0:8])
            p.vop("reduce_max", m1c[:, 0:1], lg8[:, :], AX.X)
            p.vop("tensor_scalar", msk1[:, :], lg8[:, :], m1c[:, 0:1], None,
                  ALU.is_ge)
            p.vop("tensor_copy", l2nd[:, :], msk1[:, :])
            p.vop("scalar_tensor_tensor", l2nd[:, :], l2nd[:, :], -1e30,
                  lg8[:, :], ALU.mult, ALU.add)
            p.vop("reduce_max", m2c[:, 0:1], l2nd[:, :], AX.X)
            p.vop("tensor_scalar", msk2[:, :], lg8[:, :], m2c[:, 0:1], None,
                  ALU.is_ge)
            p.vop("tensor_scalar", m3c[:, 0:1], m2c[:, 0:1], -1.0, None, ALU.mult)
            p.sop("activation", m3c[:, 0:1], m1c[:, 0:1], AF.Sigmoid,
                  bias=m3c[:, 0:1])
            p.vop("tensor_copy", g8[:, :], msk1[:, :])
            p.vop("tensor_scalar", g8[:, :], g8[:, :], m3c[:, 0:1], None, ALU.mult)
            p.vop("tensor_copy", ninf[:, :], msk2[:, :])
            p.vop("tensor_copy", lg8[:, :], msk1[:, :])
            p.vop("tensor_mul", lg8[:, :], lg8[:, :], ninf[:, :])
            p.vop("tensor_sub", ninf[:, :], ninf[:, :], lg8[:, :])
            p.vop("tensor_scalar", m1c[:, 0:1], m3c[:, 0:1], -1.0, 1.0,
                  ALU.mult, ALU.add)
            p.vop("tensor_scalar", ninf[:, :], ninf[:, :], m1c[:, 0:1], None,
                  ALU.mult)
            p.vop("tensor_add", gtok[:, tt * 8:(tt + 1) * 8], g8[:, :], ninf[:, :])

        p.push_scope()
        xg4 = p.sb("s4_xg", [128, NF * TWH], BF16)
        hid4 = p.sb("s4_hid", [128, 16 * TWH], BF16)
        for half in range(2):
            t0 = half * TWH
            for e in range(8):
                for tt in range(8):
                    gtt = half * 8 + tt
                    p.mm(ctx["rps"][0:1, 0:128],
                         gtok[:, gtt * 8 + e: gtt * 8 + e + 1], idff[:, :])
                    p.vop("tensor_copy", growE[0:1, tt * 128:(tt + 1) * 128],
                          ctx["rps"][0:1, 0:128])
                for c0, c1 in ntiles(TWH):
                    w = c1 - c0
                    p.mm(ctx["bc1"][:, 0:w], ctx["ones1f"][0:1, :],
                         growE[0:1, c0:c1])
                    for fc in range(NF):
                        p.vop("tensor_mul", xg4[:, fc * TWH + c0: fc * TWH + c1],
                              t16[:, fc * T + t0 + c0: fc * T + t0 + c1],
                              ctx["bc1"][:, 0:w])
                def epi_h1(p_, mi, c0, c1, ps):
                    p_.sop("activation", hid4[:, mi * TWH + c0: mi * TWH + c1],
                           ps[:, 0:c1 - c0], AF.Silu)
                h_proj_view(p, ctx, w1_in, e,
                            lambda k: t16[:, k * T + t0: k * T + t0 + TWH],
                            NF, list(range(16)), epi_h1, TW=TWH)
                def epi_h3(p_, mi, c0, c1, ps):
                    d = hid4[:, mi * TWH + c0: mi * TWH + c1]
                    p_.vop("tensor_mul", d, d, ps[:, 0:c1 - c0])
                h_proj_view(p, ctx, w3_in, e,
                            lambda k: xg4[:, k * TWH:(k + 1) * TWH],
                            NF, list(range(16)), epi_h3, TW=TWH)
                def epi_y(p_, mi, c0, c1, ps):
                    d = xsb[:, mi * T + t0 + c0: mi * T + t0 + c1]
                    p_.vop("tensor_add", d, d, ps[:, 0:c1 - c0])
                h_proj_view(p, ctx, w2_in, e,
                            lambda k: hid4[:, k * TWH:(k + 1) * TWH],
                            16, list(range(NF)), epi_y, TW=TWH)
        p.pop_scope()

        for fc in range(NF):
            for c0, c1 in ntiles(T):
                p.dma(ctx["scrfa"][:, 0:c1 - c0], xs1[fc, :, c0:c1])
                p.vop("tensor_add", xsb[:, fc * T + c0: fc * T + c1],
                      xsb[:, fc * T + c0: fc * T + c1], ctx["scrfa"][:, 0:c1 - c0])

        h_ln_to(p, ctx, xsb, t16, NF)
        for c0, c1 in ntiles(T):
            w = c1 - c0
            for k in range(NF):
                p.dma(ctx["wt"][:, k * 128: k * 128 + 64],
                      wbT[k * 128:(k + 1) * 128, :], engine="gpsimd")
            for k in range(NF):
                p.mm(ctx["mmps"][0:64, 0:w], ctx["wt"][:, k * 128: k * 128 + 64],
                     t16[:, k * T + c0: k * T + c1],
                     start=(k == 0), stop=(k == NF - 1))
            p.vop("tensor_copy", lat16[:, c0:c1], ctx["mmps"][0:64, 0:w])
        for b in range(B):
            p.dma(cc3_in[b * 64:(b + 1) * 64, :],
                  lat16[0:64, b * SLOC:(b + 1) * SLOC])

        p.push_scope()
        p._tps4 = p.ps("s4_tps", [128, 128], BF16)
        v1buf = p.sb("s4_v1buf", [128, NF * T], BF16)
        def epi_v1(p_, mi, c0, c1, ps):
            p_.vop("tensor_scalar", v1buf[:, mi * T + c0: mi * T + c1],
                   ps[:, 0:c1 - c0], bvc[:, mi:mi + 1], None, ALU.add)
        h_proj(p, ctx, wvT, t16, NF, list(range(NF)), epi_v1)
        for b in range(B):
            for tt in range(8):
                for mc in range(NF):
                    p.transpose(p._tps4[:, :], v1buf[:, mc * T + b * SLOC + tt * 128:
                                                     mc * T + b * SLOC + (tt + 1) * 128],
                                identb[:, :])
                    p.vop("tensor_copy", vt_sb[:, mc * 128:(mc + 1) * 128],
                          p._tps4[:, :])
                p.dma(cc3_in[128 + b * 1024 + tt * 128: 128 + b * 1024 + (tt + 1) * 128, :],
                      vt_sb[:, :])
        p.pop_scope()
        p.pop_scope()
        p.collective("AllGather", ALU.bypass, RG8, [cc3_in.opt()], [cc3_out.opt()])

        p.push_scope()
        lf = p.sb("s5_lf", [64, SEQ], BF16)
        vT5 = p.sb("s5_vT", [128, 1024], BF16)
        aoL = p.sb("s5_aoL", [128, NF * QT], BF16)
        P165 = p.sb("s5_P16", [128, QT], BF16)
        mskb5 = p.sb("s5_mskb", [128, QT], BF16)
        negc5 = p.sb("s5_negc", [128, 1], F32)
        crow5 = p.sb("s5_crow", [1, 4], F32)
        ysb = p.sb("s5_ysb", [128, QT], F32)
        bo5 = p.sb("s5_bo", [128, NF], F32)
        y2 = p.ps("s5_y2", [128, 512], F32)
        y3 = p.ps("s5_y3", [128, 512], F32)

        p.dma(bo5[:, :], bo_in[:, :])
        for b in range(B):
            tb0 = b * SLOC
            for r in range(8):
                p.dma(lf[:, r * 1024:(r + 1) * 1024],
                      cc3_out[r * 2176 + b * 64: r * 2176 + (b + 1) * 64, :])
            p.vop("memset", crow5[0:1, 0:1], 0.0)
            for j0 in range(0, SEQ, 512):
                p.sop("activation", ctx["scr16a"][0:64, 0:512],
                      lf[0:64, j0:j0 + 512], AF.Square)
                p.mm(ctx["rps"][0:1, 0:512], ctx["onesP"][0:64, 0:1],
                     ctx["scr16a"][0:64, 0:512])
                p.vop("reduce_max", crow5[0:1, 1:2], ctx["rps"][0:1, 0:512], AX.X)
                p.vop("tensor_tensor", crow5[0:1, 0:1], crow5[0:1, 0:1],
                      crow5[0:1, 1:2], ALU.max)
            p.vop("tensor_scalar", crow5[0:1, 2:3], crow5[0:1, 0:1],
                  -1.0 / 8.0, None, ALU.mult)
            p.mm(ctx["bc1"][:, 0:1], ctx["ones1f"][0:1, :], crow5[0:1, 2:3])
            p.vop("tensor_copy", negc5[:, 0:1], ctx["bc1"][:, 0:1])

            for qt in range(2):
                ql = tb0 + qt * QT
                khi = KHI[qt]
                for dh in range(2):
                    ybanks = [ctx["bc1"], ctx["mmps"], y2, y3]
                    for kc in range(khi):
                        last = (kc == khi - 1)
                        p.mm(ctx["bc2"][:, 0:QT], lf[0:64, kc * KC:(kc + 1) * KC],
                             lat16[0:64, ql:ql + QT])
                        p.sop("activation", P165[:, 0:QT], ctx["bc2"][:, 0:QT],
                              AF.Exp, bias=negc5[:, 0:1], scale=1.0 / 8.0)
                        p.dma(mskb5[:, 0:QT], masks[qt, kc, :, :], engine="gpsimd")
                        p.vop("tensor_mul", P165[:, 0:QT], P165[:, 0:QT],
                              mskb5[:, 0:QT])
                        if dh == 0:
                            p.mm(ctx["rps"][0:1, 0:QT], ctx["onesP"][:, 0:1],
                                 P165[:, 0:QT], start=(kc == 0), stop=last)
                        p.dma(vT5[:, :],
                              cc3_out[(kc // 8) * 2176 + 128 + b * 1024 +
                                      (kc % 8) * 128:
                                      (kc // 8) * 2176 + 128 + b * 1024 +
                                      (kc % 8 + 1) * 128, :], engine="gpsimd")
                        for mc in range(4):
                            p.mm(ybanks[mc][:, 0:QT],
                                 vT5[:, (dh * 4 + mc) * 128:(dh * 4 + mc + 1) * 128],
                                 P165[:, 0:QT], start=(kc == 0), stop=last)
                    if dh == 0:
                        p.vop("reciprocal", ctx["rows"][0][0:1, 0:QT],
                              ctx["rps"][0:1, 0:QT])
                    for mc in range(4):
                        p.vop("tensor_copy", ysb[:, 0:QT], ybanks[mc][:, 0:QT])
                        p.mm(ctx["bc2"][:, 0:QT], ctx["ones1f"][0:1, :],
                             ctx["rows"][0][0:1, 0:QT])
                        p.vop("tensor_mul",
                              aoL[:, (dh * 4 + mc) * QT:(dh * 4 + mc + 1) * QT],
                              ysb[:, 0:QT], ctx["bc2"][:, 0:QT])
                def epi_x3(p_, mi, c0, c1, ps):
                    d = xsb[:, mi * T + ql + c0: mi * T + ql + c1]
                    p_.vop("tensor_scalar", ctx["scrfa"][:, 0:c1 - c0],
                           ps[:, 0:c1 - c0], bo5[:, mi:mi + 1], None, ALU.add)
                    p_.vop("tensor_add", d, d, ctx["scrfa"][:, 0:c1 - c0])
                h_proj(p, ctx, woT, aoL, NF, list(range(NF)), epi_x3, TW=QT)
        p.pop_scope()

        p.push_scope()
        t166 = p.sb("s6_t16", [128, NF * T], BF16)
        gtok6 = p.sb("s6_gtok", [128, 16 * 4], F32)
        growE6 = p.sb("s6_growE", [1, TWQ], F32)
        rwsb6 = p.sb("s6_rwsb", [128, NF * 4], BF16)
        rbsb6 = p.sb("s6_rbsb", [1, 4], F32)
        ninf6 = p.sb("s6_ninf", [128, 4], F32)
        l2nd6 = p.sb("s6_l2nd", [128, 4], F32)
        lg86 = p.sb("s6_lg8", [128, 4], F32)
        m1c6 = p.sb("s6_m1c", [128, 1], F32)
        m2c6 = p.sb("s6_m2c", [128, 1], F32)
        m3c6 = p.sb("s6_m3c", [128, 1], F32)
        msk16 = p.sb("s6_msk1", [128, 4], U8)
        msk26 = p.sb("s6_msk2", [128, 4], U8)
        g86 = p.sb("s6_g8", [128, 4], F32)
        bcol6 = p.sb("s6_bcol", [128, NF], F32)

        p.dma(rbsb6[0:1, :], router_b6[0:1, :])
        for fc in range(NF):
            p.dma(rwsb6[:, fc * 4:(fc + 1) * 4],
                  router_w6[fc * 128:(fc + 1) * 128, :], engine="gpsimd")
        h_ln_to(p, ctx, xsb, t166, NF)

        for tt in range(16):
            for fc in range(NF):
                p.mm(ctx["mmps"][0:128, 0:4],
                     t166[:, fc * T + tt * 128: fc * T + (tt + 1) * 128],
                     rwsb6[:, fc * 4:(fc + 1) * 4],
                     start=(fc == 0), stop=False)
            p.mm(ctx["mmps"][0:128, 0:4], ctx["ones1f"][0:1, :], rbsb6[0:1, 0:4],
                 start=False, stop=True)
            p.vop("tensor_copy", lg86[:, :], ctx["mmps"][0:128, 0:4])
            p.vop("reduce_max", m1c6[:, 0:1], lg86[:, :], AX.X)
            p.vop("tensor_scalar", msk16[:, :], lg86[:, :], m1c6[:, 0:1], None,
                  ALU.is_ge)
            p.vop("tensor_copy", l2nd6[:, :], msk16[:, :])
            p.vop("scalar_tensor_tensor", l2nd6[:, :], l2nd6[:, :], -1e30,
                  lg86[:, :], ALU.mult, ALU.add)
            p.vop("reduce_max", m2c6[:, 0:1], l2nd6[:, :], AX.X)
            p.vop("tensor_scalar", msk26[:, :], lg86[:, :], m2c6[:, 0:1], None,
                  ALU.is_ge)
            p.vop("tensor_scalar", m3c6[:, 0:1], m2c6[:, 0:1], -1.0, None, ALU.mult)
            p.sop("activation", m3c6[:, 0:1], m1c6[:, 0:1], AF.Sigmoid,
                  bias=m3c6[:, 0:1])
            p.vop("tensor_copy", g86[:, :], msk16[:, :])
            p.vop("tensor_scalar", g86[:, :], g86[:, :], m3c6[:, 0:1], None,
                  ALU.mult)
            p.vop("tensor_copy", ninf6[:, :], msk26[:, :])
            p.vop("tensor_copy", lg86[:, :], msk16[:, :])
            p.vop("tensor_mul", lg86[:, :], lg86[:, :], ninf6[:, :])
            p.vop("tensor_sub", ninf6[:, :], ninf6[:, :], lg86[:, :])
            p.vop("tensor_scalar", m1c6[:, 0:1], m3c6[:, 0:1], -1.0, 1.0,
                  ALU.mult, ALU.add)
            p.vop("tensor_scalar", ninf6[:, :], ninf6[:, :], m1c6[:, 0:1], None,
                  ALU.mult)
            p.vop("tensor_add", gtok6[:, tt * 4:(tt + 1) * 4], g86[:, :],
                  ninf6[:, :])

        p.push_scope()
        xg6 = p.sb("s6_xg", [128, NF * TWQ], BF16)
        hid6 = p.sb("s6_hid", [128, 32 * TWQ], BF16)
        for quarter in range(4):
            t0 = quarter * TWQ
            for e in range(4):
                for tt in range(4):
                    gtt = quarter * 4 + tt
                    p.mm(ctx["rps"][0:1, 0:128],
                         gtok6[:, gtt * 4 + e: gtt * 4 + e + 1], idff[:, :])
                    p.vop("tensor_copy", growE6[0:1, tt * 128:(tt + 1) * 128],
                          ctx["rps"][0:1, 0:128])
                for c0, c1 in ntiles(TWQ):
                    w = c1 - c0
                    p.mm(ctx["bc1"][:, 0:w], ctx["ones1f"][0:1, :],
                         growE6[0:1, c0:c1])
                    for fc in range(NF):
                        p.vop("tensor_mul", xg6[:, fc * TWQ + c0: fc * TWQ + c1],
                              t166[:, fc * T + t0 + c0: fc * T + t0 + c1],
                              ctx["bc1"][:, 0:w])
                def epi_h16(p_, mi, c0, c1, ps):
                    p_.sop("activation", hid6[:, mi * TWQ + c0: mi * TWQ + c1],
                           ps[:, 0:c1 - c0], AF.Silu)
                h_proj_view(p, ctx, w1_6, e,
                            lambda k: t166[:, k * T + t0: k * T + t0 + TWQ],
                            NF, list(range(32)), epi_h16, TW=TWQ)
                def epi_h36(p_, mi, c0, c1, ps):
                    d = hid6[:, mi * TWQ + c0: mi * TWQ + c1]
                    p_.vop("tensor_mul", d, d, ps[:, 0:c1 - c0])
                h_proj_view(p, ctx, w3_6, e,
                            lambda k: xg6[:, k * TWQ:(k + 1) * TWQ],
                            NF, list(range(32)), epi_h36, TW=TWQ)
                def epi_y6(p_, mi, c0, c1, ps):
                    d = xsb[:, mi * T + t0 + c0: mi * T + t0 + c1]
                    p_.vop("tensor_add", d, d, ps[:, 0:c1 - c0])
                h_proj_view(p, ctx, w2_6, e,
                            lambda k: hid6[:, k * TWQ:(k + 1) * TWQ],
                            32, list(range(NF)), epi_y6, TW=TWQ)
        p.pop_scope()

        h_ln_to(p, ctx, xsb, t166, NF)
        for wT, b_in2, outT in ((wqT6, bq6, qs2), (wkT6, bk6, ks2),
                                (wvT6, bv6, vs2)):
            p.dma(bcol6[:, :], b_in2[:, :])
            def epi_qkv(p_, mi, c0, c1, ps, o=outT):
                p_.vop("tensor_scalar", ctx["scr16a"][:, 0:c1 - c0],
                       ps[:, 0:c1 - c0], bcol6[:, mi:mi + 1], None, ALU.add)
                p_.dma(o[mi, :, c0:c1], ctx["scr16a"][:, 0:c1 - c0])
            h_proj(p, ctx, wT, t166, NF, list(range(NF)), epi_qkv)
        p.pop_scope()

        p.push_scope()
        p._tps4 = p.ps("s6_tps", [128, 128], BF16)
        edg = p.sb("s6_edg", [128, 1024], BF16)
        for b in range(B):
            for side in range(2):
                tok0 = b * SLOC + (0 if side == 0 else SLOC - 128)
                for fc in range(NF):
                    p.dma(edg[:, fc * 128:(fc + 1) * 128], ks2[fc, :, tok0:tok0 + 128],
                          engine="gpsimd")
                for fc in range(NF):
                    p.dma(cc4k_in[b * 1024 + fc * 128: b * 1024 + (fc + 1) * 128,
                                  side * 128:(side + 1) * 128],
                          edg[:, fc * 128:(fc + 1) * 128])
                for fc in range(NF):
                    p.dma(edg[:, fc * 128:(fc + 1) * 128], vs2[fc, :, tok0:tok0 + 128],
                          engine="gpsimd")
                for fc in range(NF):
                    p.transpose(p._tps4[:, :], edg[:, fc * 128:(fc + 1) * 128],
                                identb[:, :])
                    p.vop("tensor_copy", vt_sb[:, fc * 128:(fc + 1) * 128],
                          p._tps4[:, :])
                p.dma(cc4v_in[b * 256 + side * 128: b * 256 + (side + 1) * 128, :],
                      vt_sb[:, 0:1024])
        p.pop_scope()
        p.collective("AllGather", ALU.bypass, RG8, [cc4k_in.opt()], [cc4k_out.opt()])
        p.collective("AllGather", ALU.bypass, RG8, [cc4v_in.opt()], [cc4v_out.opt()])

        p.push_scope()
        p._tps4 = p.ps("s7_tps", [128, 128], BF16)
        qsb7 = p.sb("s7_qsb", [128, NF * SLOC], BF16)
        ksb7 = p.sb("s7_ksb", [128, NF * HL], BF16)
        vsb7 = p.sb("s7_vsb", [128, 10 * 1024], BF16)
        ao7 = p.sb("s7_ao", [128, NF * SLOC], BF16)
        P167 = p.sb("s7_P16", [128, 384], BF16)
        mskb7 = p.sb("s7_mskb", [128, 384], BF16)
        negc7 = p.sb("s7_negc", [128, 1], F32)
        crow7 = p.sb("s7_crow", [1, 4], F32)
        osb7 = p.sb("s7_osb", [64, 128], F32)
        bo77 = p.sb("s7_bo", [128, NF], F32)
        wrc7 = p.sb("s7_wrc", [128, NF], F32)
        ohc = p.sb("s7_ohc", [128, 16], F32)
        edg7 = p.sb("s7_edg", [128, 1024], BF16)

        p.dma(bo77[:, :], bo7[:, :])
        p.dma(wrc7[:, :], wr_in[:, :])
        for r in range(8):
            p.vop("tensor_scalar", ohc[:, r:r + 1], pidc[:, 0:1],
                  float(r + 1), None, ALU.is_equal)
            p.vop("tensor_scalar", ohc[:, 8 + r:9 + r], pidc[:, 0:1],
                  float(r - 1), None, ALU.is_equal)

        for b in range(B):
            tb0 = b * SLOC
            for fc in range(NF):
                p.dma(qsb7[:, fc * SLOC:(fc + 1) * SLOC], qs2[fc, :, tb0:tb0 + SLOC],
                      engine="gpsimd")
                p.dma(ksb7[:, fc * HL + 128: fc * HL + 128 + SLOC],
                      ks2[fc, :, tb0:tb0 + SLOC], engine="gpsimd")
                p.vop("memset", ksb7[:, fc * HL: fc * HL + 128], 0.0)
                p.vop("memset", ksb7[:, fc * HL + 1152: fc * HL + 1280], 0.0)
            for side in range(2):
                dst_off = 0 if side == 0 else 1152
                src_side = 1 if side == 0 else 0
                for r in range(8):
                    for fc in range(NF):
                        p.dma(edg7[:, fc * 128:(fc + 1) * 128],
                              cc4k_out[r * 2048 + b * 1024 + fc * 128:
                                       r * 2048 + b * 1024 + (fc + 1) * 128,
                                       src_side * 128:(src_side + 1) * 128],
                              engine="gpsimd")
                    oh = ohc[:, r:r + 1] if side == 0 else ohc[:, 8 + r:9 + r]
                    for fc in range(NF):
                        p.vop("scalar_tensor_tensor",
                              ksb7[:, fc * HL + dst_off: fc * HL + dst_off + 128],
                              edg7[:, fc * 128:(fc + 1) * 128], oh,
                              ksb7[:, fc * HL + dst_off: fc * HL + dst_off + 128],
                              ALU.mult, ALU.add)
            for kb in range(10):
                p.vop("memset", vsb7[:, kb * 1024:(kb + 1) * 1024], 0.0)
            for tt in range(8):
                for fc in range(NF):
                    p.dma(edg7[:, fc * 128:(fc + 1) * 128],
                          vs2[fc, :, tb0 + tt * 128: tb0 + (tt + 1) * 128],
                          engine="gpsimd")
                for fc in range(NF):
                    p.transpose(p._tps4[:, :], edg7[:, fc * 128:(fc + 1) * 128],
                                identb[:, :])
                    p.vop("tensor_copy", vsb7[:, (tt + 1) * 1024 + fc * 128:
                                              (tt + 1) * 1024 + (fc + 1) * 128],
                          p._tps4[:, :])
            for side in range(2):
                kb = 0 if side == 0 else 9
                src_side = 1 if side == 0 else 0
                for r in range(8):
                    p.dma(edg7[:, :],
                          cc4v_out[r * 512 + b * 256 + src_side * 128:
                                   r * 512 + b * 256 + (src_side + 1) * 128, :],
                          engine="gpsimd")
                    oh = ohc[:, r:r + 1] if side == 0 else ohc[:, 8 + r:9 + r]
                    p.vop("scalar_tensor_tensor",
                          vsb7[:, kb * 1024:(kb + 1) * 1024], edg7[:, :], oh,
                          vsb7[:, kb * 1024:(kb + 1) * 1024], ALU.mult, ALU.add)
            p.vop("memset", crow7[0:1, 0:1], 0.0)
            p.vop("memset", crow7[0:1, 1:2], 0.0)
            for fc in range(NF):
                for c0, c1 in ntiles(SLOC):
                    p.sop("activation", ctx["scr16a"][:, 0:c1 - c0],
                          qsb7[:, fc * SLOC + c0: fc * SLOC + c1], AF.Square)
                    p.mm(ctx["rps"][0:1, 0:c1 - c0], ctx["onesP"][:, 0:1],
                         ctx["scr16a"][:, 0:c1 - c0])
                    p.vop("reduce_max", crow7[0:1, 2:3],
                          ctx["rps"][0:1, 0:c1 - c0], AX.X)
                    p.vop("tensor_tensor", crow7[0:1, 0:1], crow7[0:1, 0:1],
                          crow7[0:1, 2:3], ALU.max)
                for c0, c1 in ntiles(HL):
                    p.sop("activation", ctx["scr16a"][:, 0:c1 - c0],
                          ksb7[:, fc * HL + c0: fc * HL + c1], AF.Square)
                    p.mm(ctx["rps"][0:1, 0:c1 - c0], ctx["onesP"][:, 0:1],
                         ctx["scr16a"][:, 0:c1 - c0])
                    p.vop("reduce_max", crow7[0:1, 2:3],
                          ctx["rps"][0:1, 0:c1 - c0], AX.X)
                    p.vop("tensor_tensor", crow7[0:1, 1:2], crow7[0:1, 1:2],
                          crow7[0:1, 2:3], ALU.max)
            p.sop("activation", crow7[0:1, 0:1], crow7[0:1, 0:1], AF.Sqrt)
            p.sop("activation", crow7[0:1, 1:2], crow7[0:1, 1:2], AF.Sqrt)
            p.vop("tensor_mul", crow7[0:1, 2:3], crow7[0:1, 0:1], crow7[0:1, 1:2])
            p.vop("tensor_scalar", crow7[0:1, 2:3], crow7[0:1, 2:3],
                  -1.0 / 8.0, None, ALU.mult)
            p.mm(ctx["bc1"][:, 0:1], ctx["ones1f"][0:1, :], crow7[0:1, 2:3])
            p.vop("tensor_copy", negc7[:, 0:1], ctx["bc1"][:, 0:1])

            for qb in range(8):
                p.dma(mskb7[:, :], masks2[b, qb, :, :], engine="gpsimd")
                for h in range(16):
                    hfc = h // 2
                    hr = 64 * (h % 2)
                    for blk in range(3):
                        p.mm(ctx["bc2"][:, blk * 128:(blk + 1) * 128],
                             ksb7[hr:hr + 64,
                                  hfc * HL + (qb + blk) * 128:
                                  hfc * HL + (qb + blk + 1) * 128],
                             qsb7[hr:hr + 64,
                                  hfc * SLOC + qb * 128: hfc * SLOC + (qb + 1) * 128],
                             tile_position=(hr, 0))
                    p.sop("activation", P167[:, :], ctx["bc2"][:, 0:384], AF.Exp,
                          bias=negc7[:, 0:1], scale=1.0 / 8.0)
                    p.vop("tensor_mul", P167[:, :], P167[:, :], mskb7[:, :])
                    p.mm(ctx["rps"][0:1, 0:384], ctx["onesP"][:, 0:1], P167[:, :])
                    p.vop("tensor_copy", ctx["rows"][0][0:1, 0:384],
                          ctx["rps"][0:1, 0:384])
                    p.vop("tensor_add", ctx["rows"][0][0:1, 0:128],
                          ctx["rows"][0][0:1, 0:128], ctx["rows"][0][0:1, 128:256])
                    p.vop("tensor_add", ctx["rows"][0][0:1, 0:128],
                          ctx["rows"][0][0:1, 0:128], ctx["rows"][0][0:1, 256:384])
                    p.vop("reciprocal", ctx["rows"][1][0:1, 0:128],
                          ctx["rows"][0][0:1, 0:128])
                    for blk in range(3):
                        p.mm(ctx["mmps"][0:64, 0:128],
                             vsb7[:, (qb + blk) * 1024 + h * 64:
                                  (qb + blk) * 1024 + (h + 1) * 64],
                             P167[:, blk * 128:(blk + 1) * 128],
                             start=(blk == 0), stop=(blk == 2))
                    p.mm(ctx["bc1"][:, 0:128], ctx["ones1f"][0:1, :],
                         ctx["rows"][1][0:1, 0:128])
                    p.vop("tensor_copy", osb7[:, :], ctx["mmps"][0:64, 0:128])
                    p.vop("tensor_mul",
                          ao7[hr:hr + 64, hfc * SLOC + qb * 128:
                              hfc * SLOC + (qb + 1) * 128],
                          osb7[:, :], ctx["bc1"][0:64, 0:128])
            def epi_x5(p_, mi, c0, c1, ps):
                d = xsb[:, mi * T + tb0 + c0: mi * T + tb0 + c1]
                p_.vop("tensor_scalar", ctx["scrfa"][:, 0:c1 - c0],
                       ps[:, 0:c1 - c0], bo77[:, mi:mi + 1], None, ALU.add)
                p_.vop("tensor_add", d, d, ctx["scrfa"][:, 0:c1 - c0])
            h_proj(p, ctx, woT7, ao7, NF, list(range(NF)), epi_x5, TW=SLOC)

        p.pop_scope()
        p.push_scope()
        t167 = p.sb("s7b_t16", [128, NF * T], BF16)
        scrow_g = p.sb("s7b_scrow", [1, T], F32)
        h_ln_to(p, ctx, xsb, t167, NF)
        wrc7b = p.sb("s7b_wrc", [128, NF], F32)
        p.dma(wrc7b[:, :], wr_in[:, :])
        for c0, c1 in ntiles(T):
            w = c1 - c0
            for fc in range(NF):
                p.vop("tensor_scalar", ctx["scr16a"][:, 0:w],
                      t167[:, fc * T + c0: fc * T + c1], wrc7b[:, fc:fc + 1],
                      None, ALU.mult)
                p.mm(ctx["rps"][0:1, 0:w], ctx["onesP"][:, 0:1],
                     ctx["scr16a"][:, 0:w], start=(fc == 0), stop=(fc == NF - 1))
            p.vop("tensor_copy", scrow_g[0:1, c0:c1], ctx["rps"][0:1, 0:w])
        for b in range(B):
            p.dma(cc5_in[b:b + 1, :], scrow_g[0:1, b * SLOC:(b + 1) * SLOC])
        wrb = p.sb("s7b_wrb", [128, NF], BF16)
        p.vop("tensor_copy", wrb[:, :], wrc7b[:, :])
        for tt in range(16):
            for fc in range(NF):
                p.mm(ctx["mmps"][0:128, 0:1],
                     t167[:, fc * T + tt * 128: fc * T + (tt + 1) * 128],
                     wrb[:, fc:fc + 1], start=(fc == 0), stop=(fc == NF - 1))
            p.vop("tensor_copy", sco_cols[:, tt:tt + 1], ctx["mmps"][0:128, 0:1])
        p.pop_scope()
        p.collective("AllGather", ALU.bypass, RG8, [cc5_in.opt()], [cc5_out.opt()])

        emit_sec8(p, ctx, xsb, sco_cols, idff, identb, dict(
            cc5_out=cc5_out, pos_cols=pos_cols, W1m=W1m, b1m=b1m, W2mT=W2mT,
            b2m=b2m, wqT3=wqT3, wkT3=wkT3, wvT3=wvT3, woT3=woT3, bq3=bq3,
            bk3=bk3, bv3=bv3, bo3=bo3, W1aT=W1aT, b1a=b1a, W2aT=W2aT,
            b2a_col=b2a_col, sigT=sigT, rw3=rw3, rb3=rb3, rw1=rw1, rw3e=rw3e,
            rw2=rw2, sw1=sw1, sw3=sw3, sw2=sw2, x8_out=x8_out))

    def body2(p):
        body(p)
    nc = p.build(body2)
    return nc


def emit_sec8(p, ctx, xsb, scols_sb, idff, identb, io):
    TWQ = 512
    cc5_out = io["cc5_out"]

    p.push_scope()
    t16 = p.sb("s8_t16", [128, NF * T], BF16)
    big = p.sb("s8_big", [128, 16384], BF16)
    rankc = p.sb("s8_rankc", [128, 16], F32)
    scch = p.sb("s8_scch", [1, 512], F32)
    posr = p.sb("s8_posr", [1, 512], F32)
    posi = p.sb("s8_posi", [1, 512], I32)
    pcols = p.sb("s8_pcols", [128, 16], F32)
    cnt = p.sb("s8_cnt", [128, 1], F32)
    colA = p.sb("s8_colA", [128, 1], F32)
    bcolA = p.sb("s8_bcolA", [128, 32], F32)
    bcolB = p.sb("s8_bcolB", [128, NF], F32)
    bcolC = p.sb("s8_bcolC", [128, NF], F32)
    bcolD = p.sb("s8_bcolD", [128, NF], F32)
    b2acol = p.sb("s8_b2acol", [64, 1], F32)
    qc = p.sb("s8_qc", [128, T], BF16)
    kc_ = p.sb("s8_kc", [128, T], BF16)
    vc = p.sb("s8_vc", [128, T], BF16)
    grow = ctx["rows"][2]
    wmrow = ctx["rows"][3]
    r0row = ctx["rows"][0]
    r1row = ctx["rows"][1]
    gt16 = p.sb("s8_gt16", [128, 16 * 16], F32)
    t64 = p.sb("s8_t64", [64, T], BF16)
    gtokE = p.sb("s8_gtokE", [128, 16 * 8], F32)
    ninf = p.sb("s8_ninf", [128, 8], F32)
    l2nd = p.sb("s8_l2nd", [128, 8], F32)
    lg8 = p.sb("s8_lg8", [128, 8], F32)
    m1c = p.sb("s8_m1c", [128, 1], F32)
    m2c = p.sb("s8_m2c", [128, 1], F32)
    m3c = p.sb("s8_m3c", [128, 1], F32)
    msk1 = p.sb("s8_msk1", [128, 8], U8)
    msk2 = p.sb("s8_msk2", [128, 8], U8)
    g8 = p.sb("s8_g8", [128, 8], F32)
    rb8 = p.sb("s8_rb8", [1, 8], F32)
    sg = p.sb("s8_sg", [64, 16], BF16)
    rwsb = p.sb("s8_rwsb", [128, NF * 8], BF16)

    p.dma(pcols[:, :], io["pos_cols"][:, :])

    h_ln_to(p, ctx, xsb, t16, NF)
    for fc in range(NF):
        for c0, c1 in ntiles(T):
            p.vop("tensor_add", xsb[:, fc * T + c0: fc * T + c1],
                  xsb[:, fc * T + c0: fc * T + c1],
                  t16[:, fc * T + c0: fc * T + c1])
    for b in range(B):
        for tt in range(8):
            p.vop("memset", rankc[:, b * 8 + tt: b * 8 + tt + 1], 0.0)
        for j0 in range(0, SEQ, 512):
            p.dma(scch[0:1, :],
                  cc5_out[(j0 // 1024) * 2 + b:(j0 // 1024) * 2 + b + 1,
                          (j0 % 1024):(j0 % 1024) + 512])
            p.gop("iota", posi[0:1, :], [[1, 512]], base=j0, channel_multiplier=0)
            p.vop("tensor_copy", posr[0:1, :], posi[0:1, :])
            p.mm(ctx["bc1"][:, 0:512], ctx["ones1f"][0:1, :], scch[0:1, :])
            p.mm(ctx["bc2"][:, 0:512], ctx["ones1f"][0:1, :], posr[0:1, :])
            for tt in range(8):
                ci = b * 8 + tt
                p.vop("tensor_scalar", ctx["scrfa"][:, 0:512],
                      ctx["bc1"][:, 0:512], scols_sb[:, ci:ci + 1], None,
                      ALU.is_gt)
                p.vop("tensor_scalar", ctx["scrfb"][:, 0:512],
                      ctx["bc1"][:, 0:512], scols_sb[:, ci:ci + 1], None,
                      ALU.is_equal)
                p.vop("tensor_scalar", ctx["scr16a"][:, 0:512],
                      ctx["bc2"][:, 0:512], pcols[:, ci:ci + 1], None, ALU.is_lt)
                p.vop("tensor_mul", ctx["scrfb"][:, 0:512],
                      ctx["scrfb"][:, 0:512], ctx["scr16a"][:, 0:512])
                p.vop("tensor_add", ctx["scrfa"][:, 0:512],
                      ctx["scrfa"][:, 0:512], ctx["scrfb"][:, 0:512])
                p.vop("reduce_sum", cnt[:, 0:1], ctx["scrfa"][:, 0:512], AX.X)
                p.vop("tensor_add", rankc[:, ci:ci + 1], rankc[:, ci:ci + 1],
                      cnt[:, 0:1])
    for b in range(B):
        tb0 = b * SLOC
        for tt in range(8):
            ci = b * 8 + tt
            p.vop("tensor_scalar", colA[:, 0:1], rankc[:, ci:ci + 1],
                  4096.0, None, ALU.is_lt)
            p.mm(ctx["rps"][0:1, 0:128], colA[:, 0:1], idff[:, :])
            p.vop("tensor_copy", grow[0:1, tb0 + tt * 128: tb0 + (tt + 1) * 128],
                  ctx["rps"][0:1, 0:128])
            p.mm(ctx["rps"][0:1, 0:128], scols_sb[:, ci:ci + 1], idff[:, :])
            p.vop("tensor_copy", wmrow[0:1, tb0 + tt * 128: tb0 + (tt + 1) * 128],
                  ctx["rps"][0:1, 0:128])
    p.sop("activation", wmrow[0:1, :], wmrow[0:1, :], AF.Sigmoid)
    p.vop("tensor_mul", wmrow[0:1, :], wmrow[0:1, :], grow[0:1, :])

    p.dma(bcolA[:, :], io["b1m"][:, :])
    p.dma(bcolB[:, :], io["b2m"][:, :])
    for quarter in range(4):
        t0 = quarter * TWQ

        def epi_g(p_, mi, c0, c1, ps):
            w = c1 - c0
            u = ctx["scrfa"]
            p_.vop("tensor_scalar", u[:, 0:w], ps[:, 0:w],
                   bcolA[:, mi:mi + 1], None, ALU.add)
            p_.sop("activation", ctx["scrfb"][:, 0:w], u[:, 0:w], AF.Square)
            p_.vop("tensor_mul", ctx["scrfb"][:, 0:w], ctx["scrfb"][:, 0:w],
                   u[:, 0:w])
            p_.vop("scalar_tensor_tensor", ctx["scrfb"][:, 0:w],
                   ctx["scrfb"][:, 0:w], 0.044715, u[:, 0:w], ALU.mult, ALU.add)
            p_.sop("activation", ctx["scrfb"][:, 0:w], ctx["scrfb"][:, 0:w],
                   AF.Tanh, scale=0.7978845608028654)
            p_.vop("tensor_scalar", ctx["scrfb"][:, 0:w], ctx["scrfb"][:, 0:w],
                   1.0, None, ALU.add)
            p_.vop("tensor_scalar", u[:, 0:w], u[:, 0:w], 0.5, None, ALU.mult)
            p_.vop("tensor_mul", big[:, mi * TWQ + c0: mi * TWQ + c1],
                   u[:, 0:w], ctx["scrfb"][:, 0:w])

        def rd_t(k, t0=t0):
            return t16[:, k * T + t0: k * T + t0 + TWQ]
        h_proj2(p, ctx, io["W1m"], rd_t, NF, list(range(32)), epi_g, TW=TWQ)

        def epi_u(p_, mi, c0, c1, ps, t0=t0):
            w = c1 - c0
            p_.mm(ctx["bc1"][:, 0:w], ctx["ones1f"][0:1, :],
                  wmrow[0:1, t0 + c0: t0 + c1])
            p_.vop("tensor_scalar", ctx["scrfa"][:, 0:w], ps[:, 0:w],
                   bcolB[:, mi:mi + 1], None, ALU.add)
            p_.vop("tensor_mul", ctx["scrfa"][:, 0:w], ctx["scrfa"][:, 0:w],
                   ctx["bc1"][:, 0:w])
            d = xsb[:, mi * T + t0 + c0: mi * T + t0 + c1]
            p_.vop("tensor_add", d, d, ctx["scrfa"][:, 0:w])
        h_proj2(p, ctx, io["W2mT"], lambda k: big[:, k * TWQ:(k + 1) * TWQ],
                32, list(range(NF)), epi_u, TW=TWQ)

    h_ln_to(p, ctx, xsb, t16, NF)
    p.dma(bcolB[:, :], io["b1a"][:, :])

    def epi_r(p_, mi, c0, c1, ps):
        p_.vop("tensor_scalar", ctx["scrfa"][:, 0:c1 - c0], ps[:, 0:c1 - c0],
               bcolB[:, mi:mi + 1], None, ALU.add)
        p_.sop("activation", big[:, mi * T + c0: mi * T + c1],
               ctx["scrfa"][:, 0:c1 - c0], AF.Relu)
    h_proj(p, ctx, io["W1aT"], t16, NF, list(range(NF)), epi_r)
    p.dma(b2acol[:, :], io["b2a_col"][:, :])
    for c0, c1 in ntiles(T):
        w = c1 - c0
        for k in range(NF):
            p.dma(ctx["wt"][:, k * 128: k * 128 + 64],
                  io["W2aT"][k * 128:(k + 1) * 128, :], engine="gpsimd")
        for k in range(NF):
            p.mm(ctx["mmps"][0:64, 0:w], ctx["wt"][:, k * 128: k * 128 + 64],
                 big[:, k * T + c0: k * T + c1],
                 start=(k == 0), stop=(k == NF - 1))
        p.vop("tensor_scalar", ctx["scrfa"][0:64, 0:w], ctx["mmps"][0:64, 0:w],
              b2acol[0:64, 0:1], None, ALU.add)
        p.vop("tensor_copy", t64[:, c0:c1], ctx["scrfa"][0:64, 0:w])
    p.dma(sg[:, :], io["sigT"][:, :], engine="gpsimd")
    for tt in range(16):
        p.mm(ctx["mmps"][0:128, 0:16], t64[:, tt * 128:(tt + 1) * 128], sg[:, :])
        p.vop("tensor_copy", gt16[:, tt * 16:(tt + 1) * 16],
              ctx["mmps"][0:128, 0:16])
        sl = gt16[:, tt * 16:(tt + 1) * 16]
        p.vop("reduce_max", m1c[:, 0:1], sl, AX.X)
        p.vop("tensor_scalar", m2c[:, 0:1], m1c[:, 0:1], -1.0, None, ALU.mult)
        p.sop("activation", sl, sl, AF.Exp, bias=m2c[:, 0:1])
        p.vop("reduce_sum", m1c[:, 0:1], sl, AX.X)
        p.vop("reciprocal", m2c[:, 0:1], m1c[:, 0:1])
        p.vop("tensor_scalar", sl, sl, m2c[:, 0:1], None, ALU.mult)

    p.dma(bcolA[:, 0:NF], io["bq3"][:, :])
    p.dma(bcolB[:, :], io["bk3"][:, :])
    p.dma(bcolC[:, :], io["bv3"][:, :])
    p.dma(bcolD[:, :], io["bo3"][:, :])
    for fc in range(NF):
        for wT, bcol, dst, scal in ((io["wqT3"], bcolA, qc, 0.125),
                                    (io["wkT3"], bcolB, kc_, None),
                                    (io["wvT3"], bcolC, vc, None)):
            for k in range(NF):
                p.dma(ctx["wt"][:, k * 128:(k + 1) * 128],
                      wT[k * 128:(k + 1) * 128, fc * 128:(fc + 1) * 128],
                      engine="gpsimd")
            for c0, c1 in ntiles(T):
                w = c1 - c0
                for k in range(NF):
                    p.mm(ctx["mmps"][:, 0:w], ctx["wt"][:, k * 128:(k + 1) * 128],
                         t16[:, k * T + c0: k * T + c1],
                         start=(k == 0), stop=(k == NF - 1))
                if scal is None:
                    p.vop("tensor_scalar", dst[:, c0:c1], ctx["mmps"][:, 0:w],
                          bcol[:, fc:fc + 1], None, ALU.add)
                else:
                    p.vop("tensor_scalar", dst[:, c0:c1], ctx["mmps"][:, 0:w],
                          bcol[:, fc:fc + 1], scal, ALU.add, ALU.mult)
        for hh in range(2):
            hr = hh * 64
            h_ = fc * 2 + hh
            for tt in range(16):
                p.mm(ctx["rps"][0:1, 0:128],
                     gt16[:, tt * 16 + h_: tt * 16 + h_ + 1], idff[:, :])
                p.vop("tensor_copy", grow[0:1, tt * 128:(tt + 1) * 128],
                      ctx["rps"][0:1, 0:128])
            for t_ in range(2):
                for s_ in range(2):
                    rrow = r0row if s_ == 0 else r1row
                    for c0, c1 in ntiles(SLOC):
                        w = c1 - c0
                        p.vop("tensor_mul", ctx["scrfa"][:, 0:w],
                              qc[:, t_ * SLOC + c0: t_ * SLOC + c1],
                              kc_[:, s_ * SLOC + c0: s_ * SLOC + c1])
                        p.vop("tensor_copy", ctx["scr16a"][:, 0:w],
                              ctx["scrfa"][:, 0:w])
                        p.mm(ctx["rps"][0:1, 0:w], ctx["onesP"][hr:hr + 64, 0:1],
                             ctx["scr16a"][hr:hr + 64, 0:w],
                             tile_position=(hr, 0))
                        p.vop("tensor_copy", rrow[0:1, c0:c1],
                              ctx["rps"][0:1, 0:w])
                p.vop("tensor_sub", r0row[0:1, 0:SLOC], r0row[0:1, 0:SLOC],
                      r1row[0:1, 0:SLOC])
                p.sop("activation", r0row[0:1, 0:SLOC], r0row[0:1, 0:SLOC],
                      AF.Sigmoid)
                for c0, c1 in ntiles(SLOC):
                    w = c1 - c0
                    p.mm(ctx["bc1"][:, 0:w], ctx["ones1f"][0:1, :],
                         r0row[0:1, c0:c1])
                    p.vop("tensor_sub", ctx["scrfa"][:, 0:w],
                          vc[:, 0 * SLOC + c0: 0 * SLOC + c1],
                          vc[:, 1 * SLOC + c0: 1 * SLOC + c1])
                    p.vop("tensor_mul", ctx["scrfa"][:, 0:w],
                          ctx["scrfa"][:, 0:w], ctx["bc1"][:, 0:w])
                    p.vop("tensor_add", ctx["scrfa"][:, 0:w],
                          ctx["scrfa"][:, 0:w],
                          vc[:, 1 * SLOC + c0: 1 * SLOC + c1])
                    p.mm(ctx["bc2"][:, 0:w], ctx["ones1f"][0:1, :],
                         grow[0:1, t_ * SLOC + c0: t_ * SLOC + c1])
                    p.vop("tensor_mul", ctx["scrfa"][hr:hr + 64, 0:w],
                          ctx["scrfa"][hr:hr + 64, 0:w],
                          ctx["bc2"][hr:hr + 64, 0:w])
                    p.vop("tensor_copy",
                          big[hr:hr + 64, fc * T + t_ * SLOC + c0:
                              fc * T + t_ * SLOC + c1],
                          ctx["scrfa"][hr:hr + 64, 0:w])

    def epi_x7(p_, mi, c0, c1, ps):
        d = xsb[:, mi * T + c0: mi * T + c1]
        p_.vop("tensor_scalar", ctx["scrfa"][:, 0:c1 - c0], ps[:, 0:c1 - c0],
               bcolD[:, mi:mi + 1], None, ALU.add)
        p_.vop("tensor_add", d, d, ctx["scrfa"][:, 0:c1 - c0])
    h_proj(p, ctx, io["woT3"], big, NF, list(range(NF)), epi_x7)

    h_ln_to(p, ctx, xsb, t16, NF)
    p.dma(rb8[0:1, :], io["rb3"][0:1, :])
    for fc in range(NF):
        p.dma(rwsb[:, fc * 8:(fc + 1) * 8],
              io["rw3"][fc * 128:(fc + 1) * 128, :], engine="gpsimd")
    for tt in range(16):
        for fc in range(NF):
            p.mm(ctx["mmps"][0:128, 0:8],
                 t16[:, fc * T + tt * 128: fc * T + (tt + 1) * 128],
                 rwsb[:, fc * 8:(fc + 1) * 8], start=(fc == 0), stop=False)
        p.mm(ctx["mmps"][0:128, 0:8], ctx["ones1f"][0:1, :], rb8[0:1, 0:8],
             start=False, stop=True)
        p.vop("tensor_copy", lg8[:, :], ctx["mmps"][0:128, 0:8])
        p.vop("reduce_max", m1c[:, 0:1], lg8[:, :], AX.X)
        p.vop("tensor_scalar", msk1[:, :], lg8[:, :], m1c[:, 0:1], None,
              ALU.is_ge)
        p.vop("tensor_copy", l2nd[:, :], msk1[:, :])
        p.vop("scalar_tensor_tensor", l2nd[:, :], l2nd[:, :], -1e30, lg8[:, :],
              ALU.mult, ALU.add)
        p.vop("reduce_max", m2c[:, 0:1], l2nd[:, :], AX.X)
        p.vop("tensor_scalar", msk2[:, :], lg8[:, :], m2c[:, 0:1], None,
              ALU.is_ge)
        p.vop("tensor_scalar", m3c[:, 0:1], m2c[:, 0:1], -1.0, None, ALU.mult)
        p.sop("activation", m3c[:, 0:1], m1c[:, 0:1], AF.Sigmoid, bias=m3c[:, 0:1])
        p.vop("tensor_copy", g8[:, :], msk1[:, :])
        p.vop("tensor_scalar", g8[:, :], g8[:, :], m3c[:, 0:1], None, ALU.mult)
        p.vop("tensor_copy", ninf[:, :], msk2[:, :])
        p.vop("tensor_copy", lg8[:, :], msk1[:, :])
        p.vop("tensor_mul", lg8[:, :], lg8[:, :], ninf[:, :])
        p.vop("tensor_sub", ninf[:, :], ninf[:, :], lg8[:, :])
        p.vop("tensor_scalar", m1c[:, 0:1], m3c[:, 0:1], -1.0, 1.0,
              ALU.mult, ALU.add)
        p.vop("tensor_scalar", ninf[:, :], ninf[:, :], m1c[:, 0:1], None,
              ALU.mult)
        p.vop("tensor_add", gtokE[:, tt * 8:(tt + 1) * 8], g8[:, :], ninf[:, :])

    for e in range(10):
        routed = e < 8
        if routed:
            for tt in range(16):
                p.mm(ctx["rps"][0:1, 0:128],
                     gtokE[:, tt * 8 + e: tt * 8 + e + 1], idff[:, :])
                p.vop("tensor_copy", grow[0:1, tt * 128:(tt + 1) * 128],
                      ctx["rps"][0:1, 0:128])
        wa = io["rw1"] if routed else io["sw1"]
        wb = io["rw3e"] if routed else io["sw3"]
        wc = io["rw2"] if routed else io["sw2"]
        ei = e if routed else e - 8

        def epi_h1(p_, mi, c0, c1, ps):
            p_.sop("activation", big[:, mi * T + c0: mi * T + c1],
                   ps[:, 0:c1 - c0], AF.Silu)
        h_proj_view(p, ctx, wa, ei, lambda k: t16[:, k * T:(k + 1) * T],
                    NF, [0, 1, 2, 3], epi_h1, TW=T)

        def epi_h3(p_, mi, c0, c1, ps):
            d = big[:, mi * T + c0: mi * T + c1]
            p_.vop("tensor_mul", d, d, ps[:, 0:c1 - c0])
        h_proj_view(p, ctx, wb, ei, lambda k: t16[:, k * T:(k + 1) * T],
                    NF, [0, 1, 2, 3], epi_h3, TW=T)

        def epi_y(p_, mi, c0, c1, ps, routed=routed):
            w = c1 - c0
            if routed:
                p_.mm(ctx["bc1"][:, 0:w], ctx["ones1f"][0:1, :], grow[0:1, c0:c1])
                p_.vop("tensor_copy", ctx["scrfa"][:, 0:w], ps[:, 0:w])
                p_.vop("tensor_mul", ctx["scrfa"][:, 0:w], ctx["scrfa"][:, 0:w],
                       ctx["bc1"][:, 0:w])
                d = xsb[:, mi * T + c0: mi * T + c1]
                p_.vop("tensor_add", d, d, ctx["scrfa"][:, 0:w])
            else:
                d = xsb[:, mi * T + c0: mi * T + c1]
                p_.vop("tensor_add", d, d, ps[:, 0:w])
        h_proj_view(p, ctx, wc, ei, lambda k: big[:, k * T:(k + 1) * T],
                    4, list(range(NF)), epi_y, TW=T)

    for fc in range(NF):
        p.dma(io["x8_out"][fc, :, :], xsb[:, fc * T:(fc + 1) * T])
    p.pop_scope()


def stage_fused(inputs):
    x0 = np.asarray(inputs["L0_attn_x"], np.float32)
    x0s = fm(x0)
    m1 = stage_launch1(inputs, x0)
    m2 = stage_launch2(inputs, x0s, x0s)
    m4 = stage_launch4(inputs, x0s, x0s)
    m6 = stage_launch6(inputs, x0s)
    KHI1 = (SEQ - SLOC + 2 * 512) // 128
    maps = []
    for c in range(NCORES):
        mm2 = m2[c]
        mk = np.empty((2, KHI1, 128, 512), dtype=np.uint8)
        for qt in range(2):
            q0 = SLOC * c + qt * 512
            for kc in range(KHI1):
                kpos = kc * 128 + np.arange(128)[:, None]
                qpos = q0 + np.arange(512)[None, :]
                mk[qt, kc] = (kpos <= qpos).astype(np.uint8)
        lo_g = SLOC * c - 128
        m2w = np.zeros((B, 8, 128, 384), dtype=np.uint8)
        for b in range(B):
            for qb in range(8):
                qg = SLOC * c + qb * 128 + np.arange(128)[None, :]
                for blk in range(3):
                    kg = lo_g + (qb + blk) * 128 + np.arange(128)[:, None]
                    ok = (np.abs(kg - qg) <= 128) & (kg >= 0) & (kg < SEQ)
                    m2w[b, qb, :, blk * 128:(blk + 1) * 128] = ok.astype(np.uint8)
        poso = np.concatenate([np.arange(SLOC * c, SLOC * (c + 1),
                                         dtype=np.float32)] * B)
        pos_cols = np.ascontiguousarray(poso.reshape(16, 128).T)
        m8 = {}
        mm6 = m6[c]
        mp = {
            "xb": m1[c]["xb"], "w_inT": m1[c]["w_inT"],
            "mask_intra": m1[c]["mask_intra"],
            "x0": x0s[c],
            "wg1T": mm2["wg1T"], "wg2T": mm2["wg2T"], "lnw": mm2["lnw"],
            "woutT": mm2["woutT"], "g1": mm2["g1"], "wdown": mm2["wdown"],
            "wq": mm2["wq"], "wq_sw": mm2["wq_sw"], "wk": mm2["wk"],
            "wk_sw": mm2["wk_sw"], "wv": mm2["wv"], "pos": mm2["pos"],
            "invfreq": mm2["invfreq"], "signc": mm2["signc"],
            "wupT": np.ascontiguousarray(np.asarray(inputs["L0_ffn_w_up"])),
            "masks": mk,
            "g2": m4[c]["g2"], "router_w": m4[c]["router_w"],
            "w1": m4[c]["w1"], "w3": m4[c]["w3"], "w2": m4[c]["w2"],
            "wbT": m4[c]["wbT"], "wvT": m4[c]["wvT"], "bv": m4[c]["bv"],
            "woT": np.ascontiguousarray(np.asarray(inputs["L1_attn_W_o"]).T),
            "bo": np.ascontiguousarray(np.asarray(inputs["L1_attn_b_o"]).reshape(NF, 128).T),
            "router_w6": mm6["router_w"], "router_b6": mm6["router_b"],
            "w1_6": mm6["w1"], "w3_6": mm6["w3"], "w2_6": mm6["w2"],
            "wqT6": mm6["wqT"], "wkT6": mm6["wkT"], "wvT6": mm6["wvT"],
            "bq6": mm6["bq"], "bk6": mm6["bk"], "bv6": mm6["bv"],
            "woT7": np.ascontiguousarray(np.asarray(inputs["L2_attn_Wo"]).T),
            "bo7": np.ascontiguousarray(np.asarray(inputs["L2_attn_bo"]).reshape(NF, 128).T),
            "masks2": m2w,
            "wr": np.ascontiguousarray(np.asarray(inputs["L2_ffn_w_router"]).reshape(NF, 128).T),
            "pos_cols": pos_cols,
            "W1m": np.ascontiguousarray(np.asarray(inputs["L2_ffn_W1"])),
            "b1m": np.ascontiguousarray(np.asarray(inputs["L2_ffn_b1"]).reshape(32, 128).T),
            "W2mT": np.ascontiguousarray(np.asarray(inputs["L2_ffn_W2"])),
            "b2m": np.ascontiguousarray(np.asarray(inputs["L2_ffn_b2"]).reshape(NF, 128).T),
            "wqT3": np.ascontiguousarray(np.asarray(inputs["L3_attn_Wq"]).T),
            "wkT3": np.ascontiguousarray(np.asarray(inputs["L3_attn_Wk"]).T),
            "wvT3": np.ascontiguousarray(np.asarray(inputs["L3_attn_Wv"]).T),
            "woT3": np.ascontiguousarray(np.asarray(inputs["L3_attn_Wo"]).T),
            "bq3": np.ascontiguousarray(np.asarray(inputs["L3_attn_bq"]).reshape(NF, 128).T),
            "bk3": np.ascontiguousarray(np.asarray(inputs["L3_attn_bk"]).reshape(NF, 128).T),
            "bv3": np.ascontiguousarray(np.asarray(inputs["L3_attn_bv"]).reshape(NF, 128).T),
            "bo3": np.ascontiguousarray(np.asarray(inputs["L3_attn_bo"]).reshape(NF, 128).T),
            "W1aT": np.ascontiguousarray(np.asarray(inputs["L3_attn_W1"]).T),
            "b1a": np.ascontiguousarray(np.asarray(inputs["L3_attn_b1"]).reshape(NF, 128).T),
            "W2aT": np.ascontiguousarray(np.asarray(inputs["L3_attn_W2"]).T),
            "b2a_col": np.ascontiguousarray(np.asarray(inputs["L3_attn_b2"]).reshape(64, 1)),
            "sigT": np.ascontiguousarray(np.asarray(inputs["L3_attn_head_signatures"]).T),
            "rw3": np.ascontiguousarray(np.asarray(inputs["L3_ffn_router_w"])),
            "rb3": np.ascontiguousarray(np.asarray(inputs["L3_ffn_router_b"]).reshape(1, 8)),
            "rw1": np.ascontiguousarray(np.asarray(inputs["L3_ffn_routed_w1"])),
            "rw3e": np.ascontiguousarray(np.asarray(inputs["L3_ffn_routed_w3"])),
            "rw2": np.ascontiguousarray(np.asarray(inputs["L3_ffn_routed_w2"])),
            "sw1": np.ascontiguousarray(np.asarray(inputs["L3_ffn_shared_w1"])),
            "sw3": np.ascontiguousarray(np.asarray(inputs["L3_ffn_shared_w3"])),
            "sw2": np.ascontiguousarray(np.asarray(inputs["L3_ffn_shared_w2"])),
        }
        maps.append(mp)
    return maps


def kernel(**inputs):
    inputs = {k: np.asarray(v) for k, v in inputs.items()}
    nc = build_fused()
    maps = stage_fused(inputs)
    res = run_bass_kernel_spmd(nc, maps, core_ids=list(range(NCORES)))
    x8_s = [np.asarray(res.results[c]["x8"]) for c in range(NCORES)]
    return fm_inv(x8_s).astype(np.float32)


def kernel_multilaunch(**inputs):
    inputs = {k: np.asarray(v) for k, v in inputs.items()}
    cores = list(range(NCORES))
    x0 = np.asarray(inputs["L0_attn_x"], np.float32)
    x0s = fm(x0)

    nc1 = build_launch1()
    r1 = run_bass_kernel_spmd(nc1, stage_launch1(inputs, x0), core_ids=cores).results
    og_slices = []
    o_by_core = [np.asarray(r1[c]["o"]) for c in range(NCORES)]
    for c2 in range(NCORES):
        og = np.empty((NF, 128, T), np.float32)
        for h in range(8):
            for b in range(B):
                src = b * 4 + h // 2
                og[h, :, b * SLOC:(b + 1) * SLOC] = \
                    o_by_core[src][h % 2][:, SLOC * c2:SLOC * (c2 + 1)]
        og_slices.append(og)

    nc2 = build_launch2()
    r2 = run_bass_kernel_spmd(nc2, stage_launch2(inputs, x0s, og_slices),
                              core_ids=cores).results
    x1_s = [np.asarray(r["x1"]) for r in r2]
    q_s = [np.asarray(r["q"]) for r in r2]
    k_s = [np.asarray(r["k"]) for r in r2]
    v_s = [np.asarray(r["vv"]) for r in r2]

    nc3 = build_launch3()
    r3 = run_bass_kernel_spmd(nc3, stage_launch3(inputs, x1_s, q_s, k_s, v_s),
                              core_ids=cores).results
    res_s = [np.asarray(r["res"]) for r in r3]

    nc4 = build_launch4()
    r4 = run_bass_kernel_spmd(nc4, stage_launch4(inputs, x1_s, res_s),
                              core_ids=cores).results
    x2_s = [np.asarray(r["x2"]) for r in r4]
    lat_s = [np.asarray(r["lat"]) for r in r4]
    v1_s = [np.asarray(r["v1"]) for r in r4]

    KHI1 = (SEQ - SLOC + 2 * 512) // 128
    masks_pc = []
    for c in range(NCORES):
        m = np.empty((2, KHI1, 128, 512), dtype=np.uint8)
        for qt in range(2):
            q0 = SLOC * c + qt * 512
            for kc in range(KHI1):
                kpos = kc * 128 + np.arange(128)[:, None]
                qpos = q0 + np.arange(512)[None, :]
                m[qt, kc] = (kpos <= qpos).astype(np.uint8)
        masks_pc.append(m)

    nc5 = build_launch5()
    r5 = run_bass_kernel_spmd(nc5, stage_launch5(inputs, x2_s, lat_s, v1_s,
                                                 masks_pc),
                              core_ids=cores).results
    x3_s = [np.asarray(r["x3"]) for r in r5]

    nc6 = build_launch6()
    r6 = run_bass_kernel_spmd(nc6, stage_launch6(inputs, x3_s),
                              core_ids=cores).results
    x4_s = [np.asarray(r["x4"]) for r in r6]
    q2_s = [np.asarray(r["q2"]) for r in r6]
    k2_s = [np.asarray(r["k2"]) for r in r6]
    v2_s = [np.asarray(r["v2"]) for r in r6]

    nc7 = build_launch7()
    r7 = run_bass_kernel_spmd(nc7, stage_launch7(inputs, x4_s, q2_s, k2_s, v2_s),
                              core_ids=cores).results
    x5_s = [np.asarray(r["x5"]) for r in r7]
    sc_s = [np.asarray(r["sc"]) for r in r7]

    nc8 = build_launch8()
    r8 = run_bass_kernel_spmd(nc8, stage_launch8(inputs, x5_s, sc_s),
                              core_ids=cores).results
    x8_s = [np.asarray(r["x8"]) for r in r8]

    return fm_inv(x8_s).astype(np.float32)


# version 07
# speedup vs baseline: 2.5016x; 1.0567x over previous
import os
os.environ.setdefault("TRN_ALLOW_ICI_ALL_TO_ALL", "1")

import numpy as np
import ml_dtypes
from contextlib import ExitStack

import concourse.bass as bass
import concourse.mybir as mybir
from concourse.bass_utils import run_bass_kernel_spmd

F32 = mybir.dt.float32
BF16 = mybir.dt.bfloat16
I32 = mybir.dt.int32
U8 = mybir.dt.uint8
AF = mybir.ActivationFunctionType
ALU = mybir.AluOpType
AX = mybir.AxisListType

NCORES = 8
B = 2
SEQ = 8192
D = 1024
SLOC = SEQ // NCORES
T = B * SLOC
NF = D // 128
LN_EPS = 1e-5
TWO_PI = 6.283185307179586
HALF_PI = 1.5707963267948966
BF = ml_dtypes.bfloat16


def _ap_like(x):
    return x is not None and hasattr(x, "offset") and hasattr(x, "tensor") \
        and hasattr(x, "ap")


def _ap_key_env(x):
    try:
        name = x.tensor.name
    except Exception:
        name = str(x.tensor)
    off = int(x.offset)
    span = 1
    try:
        for step, count in x.ap:
            span += abs(int(step)) * (int(count) - 1)
    except Exception:
        span = 1 << 30
    return name, off, off + span


_RW = {
    "dma_start": ((), (), ("out",), ("in_",)),
    "matmul": ((0,), (0, 1, 2), (), ()),
    "transpose": ((0,), (1, 2), (), ()),
    "activation": ((0,), (1,), ("accum_out",), ("bias", "scale")),
    "copy_predicated": ((0,), (0, 1, 2), (), ()),
    "memset": ((0,), (), (), ()),
    "memzero": ((0,), (), (), ()),
    "iota": ((0,), (), (), ()),
    "store": ((0,), (), (), ()),
    "collective_compute": ((), (), (), ()),
}
_DMA_RING = 12


class P:

    def __init__(self, name, strict=True):
        self.nc = bass.Bass("TRN2", target_bir_lowering=False)
        self.name = name
        self.strict = strict
        self.ops = []
        self.cnts = {"T": 0, "V": 0, "S": 0, "G": 0, "CC": 0}
        for k in ("Ds", "Dg"):
            for s in range(_DMA_RING):
                self.cnts[f"{k}{s}"] = 0
        self._ndma = {"Ds": 0, "Dg": 0}
        self._acc = {}
        self._floor = {}
        self._barrier = {}
        self._klg = {}
        self._stack = None
        self._scopes = []

    ENG_CLS = {"tensor": "T", "vector": "V", "scalar": "S", "gpsimd": "G",
               "sync": None}

    def inp(self, name, shape, dtype=F32):
        return self.nc.dram_tensor(name, list(shape), dtype, kind="ExternalInput").ap()

    def out(self, name, shape, dtype=F32):
        return self.nc.dram_tensor(name, list(shape), dtype, kind="ExternalOutput").ap()

    def sb(self, name, shape, dtype=F32):
        return self._stack.enter_context(self.nc.sbuf_tensor(name, list(shape), dtype))

    def ps(self, name, shape, dtype=F32):
        return self._stack.enter_context(self.nc.psum_tensor(name, list(shape), dtype))

    def push_scope(self):
        s = ExitStack()
        self._scopes.append(self._stack)
        self._stack = s

    def pop_scope(self):
        self._stack.close()
        self._stack = self._scopes.pop()
        self._barrier = {k: v for k, v in self.cnts.items() if v > 0}

    def _note(self, ap, is_write, cls, cnt_after, needed):
        name, lo, hi = _ap_key_env(ap)
        lst = self._acc.setdefault(name, [])
        flo = self._floor.get(name)
        if flo:
            for k, v in flo.items():
                if needed.get(k, 0) < v:
                    needed[k] = v
        for (l2, h2, rw2, cls2, cnt2) in lst:
            if cls2 == cls and cnt2 == cnt_after:
                continue
            if l2 < hi and lo < h2:
                if is_write or rw2 == "w":
                    if needed.get(cls2, 0) < cnt2:
                        needed[cls2] = cnt2
        lst.append((lo, hi, "w" if is_write else "r", cls, cnt_after))
        if len(lst) > 48:
            dropped = lst[:24]
            del lst[:24]
            f = self._floor.setdefault(name, {})
            for (_, _, _, c2, n2) in dropped:
                if f.get(c2, 0) < n2:
                    f[c2] = n2

    def op(self, engine, method, *args, _inc=1, _cls=None, **kwargs):
        def emit(eng, m=method, a=args, k=kwargs):
            return getattr(eng, m)(*a, **k)

        if _cls is None:
            _cls = self.ENG_CLS[engine] or "G"
        if self.strict:
            self.ops.append((engine, emit, dict(self.cnts), _cls, _inc))
            self.cnts[_cls] += _inc
            return

        needed = dict(self._barrier)
        rule = _RW.get(method)
        w_idx, r_idx, w_kw, r_kw = rule if rule else ((0,), (), (), ())
        cnt_after = self.cnts[_cls] + _inc
        aps_w, aps_r = [], []
        if method == "collective_compute":
            for a in kwargs.get("ins", []):
                aps_r.append(a)
            for a in kwargs.get("outs", []):
                aps_w.append(a)
        else:
            for i, a in enumerate(args):
                if not _ap_like(a):
                    continue
                if i in w_idx:
                    aps_w.append(a)
                if i in r_idx or (rule is None and i not in w_idx) or \
                   (rule is not None and i not in w_idx and i not in r_idx):
                    aps_r.append(a)
            for kwn, a in kwargs.items():
                if not _ap_like(a):
                    continue
                if kwn in w_kw:
                    aps_w.append(a)
                else:
                    aps_r.append(a)
        for a in aps_r:
            self._note(a, False, _cls, cnt_after, needed)
        for a in aps_w:
            self._note(a, True, _cls, cnt_after, needed)
        needed.pop(None, None)
        if engine == "tensor":
            needed.pop("T", None)
        clos = dict(needed)
        for k, v in list(needed.items()):
            c2 = self._klg.get(k, {}).get(v)
            if c2:
                for k3, v3 in c2.items():
                    if clos.get(k3, 0) < v3:
                        clos[k3] = v3
        pruned = {}
        for k, v in needed.items():
            dominated = False
            for k2, v2 in needed.items():
                if k2 == k:
                    continue
                c2 = self._klg.get(k2, {}).get(v2)
                if c2 and c2.get(k, 0) >= v:
                    dominated = True
                    break
            if not dominated:
                pruned[k] = v
        clos[_cls] = cnt_after
        self._klg.setdefault(_cls, {})[cnt_after] = clos
        self.ops.append((engine, emit, pruned, _cls, _inc))
        self.cnts[_cls] += _inc

    def vop(self, method, *a, **k):
        self.op("vector", method, *a, **k)

    def sop(self, method, *a, **k):
        self.op("scalar", method, *a, **k)

    def gop(self, method, *a, **k):
        self.op("gpsimd", method, *a, **k)

    def dma(self, out, in_, engine="sync"):
        base = "Dg" if engine == "gpsimd" else "Ds"
        if self.strict:
            self.op(engine, "dma_start", _inc=16, _cls=base + "0", out=out,
                    in_=in_)
            return
        j = self._ndma[base]
        self._ndma[base] = j + 1
        slot = j % _DMA_RING
        cls = f"{base}{slot}"
        extra = {}
        if j >= _DMA_RING:
            extra[cls] = 16 * (j // _DMA_RING)
        self.op(engine, "dma_start", _inc=16, _cls=cls, out=out, in_=in_)
        if extra:
            eng, emit, needed, cls2, inc = self.ops[-1]
            for k2, v2 in extra.items():
                if needed.get(k2, 0) < v2:
                    needed[k2] = v2

    def mm(self, out, lhsT, rhs, start=True, stop=True, tile_position=None):
        self.op("tensor", "matmul", out, lhsT, rhs, start=start, stop=stop,
                tile_position=tile_position)

    def transpose(self, out_ps, in_sb, ident):
        self.op("tensor", "transpose", out_ps, in_sb, ident)

    def collective(self, kind, op_, rg, ins, outs):
        self.op("gpsimd", "collective_compute", kind, op_, rg,
                ins=ins, outs=outs, _inc=1, _cls="CC")

    def build(self, body):
        with ExitStack() as stack:
            self._stack = stack
            sems = {k: stack.enter_context(self.nc.semaphore("sem" + k))
                    for k in self.cnts.keys()}
            block = stack.enter_context(self.nc.Block())
            body(self)
            by = {}
            for eng, emit, waits, cls, inc in self.ops:
                by.setdefault(eng, []).append((emit, waits, cls, inc))
            totals = dict(self.cnts)

            def make(eng_name):
                def f(eng):
                    done = {k: 0 for k in self.cnts.keys()}
                    own = self.ENG_CLS.get(eng_name)
                    for emit, waits, cls, inc in by.get(eng_name, []):
                        for k, v in waits.items():
                            if v > done[k]:
                                eng.wait_ge(sems[k], v)
                                done[k] = v
                        r = emit(eng)
                        if r is not None:
                            r.then_inc(sems[cls], inc)
                        else:
                            eng.nop().then_inc(sems[cls], inc)
                    if eng_name == "sync":
                        for k, v in totals.items():
                            if v > done[k]:
                                eng.wait_ge(sems[k], v)
                return f

            block.sync(make("sync"))
            block.tensor(make("tensor"))
            block.vector(make("vector"))
            block.scalar(make("scalar"))
            block.gpsimd(make("gpsimd"))
        return self.nc


def make_ctx(p, TW, n_rows=4):
    ctx = {}
    ctx["TW"] = TW
    ctx["onesP"] = p.sb("cx_onesP", [128, 1], BF16)
    ctx["ones1f"] = p.sb("cx_ones1f", [1, 128], F32)
    ctx["scr16a"] = p.sb("cx_scr16a", [128, 512], BF16)
    ctx["scr16b"] = p.sb("cx_scr16b", [128, 512], BF16)
    ctx["scrfa"] = p.sb("cx_scrfa", [128, 512], F32)
    ctx["scrfb"] = p.sb("cx_scrfb", [128, 512], F32)
    ctx["rows"] = [p.sb(f"cx_row{i}", [1, TW], F32) for i in range(n_rows)]
    ctx["rps"] = p.ps("cx_rps", [1, 512], F32)
    ctx["rps2"] = p.ps("cx_rps2", [1, 512], F32)
    ctx["bc1"] = p.ps("cx_bc1", [128, 512], F32)
    ctx["bc2"] = p.ps("cx_bc2", [128, 512], F32)
    ctx["mmps"] = p.ps("cx_mmps", [128, 512], F32)
    ctx["mmps2"] = p.ps("cx_mmps2", [128, 512], F32)
    ctx["wt"] = p.sb("cx_wt", [128, 32 * 128], BF16)
    ctx["epsr"] = p.sb("cx_epsr", [1, 1], F32)
    p.vop("memset", ctx["onesP"][:], 1.0)
    p.vop("memset", ctx["ones1f"][:], 1.0)
    p.vop("memset", ctx["epsr"][:], LN_EPS)
    return ctx


def ntiles(tw, n=512):
    return [(i * n, min((i + 1) * n, tw)) for i in range((tw + n - 1) // n)]


def h_norm(p, ctx, read_chunk, write_chunk, nchunk, rms=False, nfeat=None, TW=None):
    TW = TW or ctx["TW"]
    onesP = ctx["onesP"]
    ones1f = ctx["ones1f"]
    scr16 = ctx["scr16a"]
    rps, rps2 = ctx["rps"], ctx["rps2"]
    mean, var, rstd, tmp = ctx["rows"][:4]
    bc1, bc2 = ctx["bc1"], ctx["bc2"]
    nfeat = nfeat or (nchunk * 128)

    for c0, c1 in ntiles(TW):
        w = c1 - c0
        for fc in range(nchunk):
            src = read_chunk(fc)
            if not rms:
                p.vop("tensor_copy", scr16[:, 0:w], src[:, c0:c1])
                p.mm(rps[0:1, 0:w], onesP[:, 0:1], scr16[:, 0:w],
                     start=(fc == 0), stop=(fc == nchunk - 1))
            p.sop("activation", scr16[:, 0:w], src[:, c0:c1], AF.Square)
            p.mm(rps2[0:1, 0:w], onesP[:, 0:1], scr16[:, 0:w],
                 start=(fc == 0), stop=(fc == nchunk - 1))
        if not rms:
            p.vop("tensor_scalar", mean[0:1, c0:c1], rps[0:1, 0:w],
                  1.0 / nfeat, None, ALU.mult)
        p.vop("tensor_scalar", var[0:1, c0:c1], rps2[0:1, 0:w],
              1.0 / nfeat, None, ALU.mult)
    if not rms:
        p.sop("activation", tmp[0:1, 0:TW], mean[0:1, 0:TW], AF.Square)
        p.vop("tensor_sub", var[0:1, 0:TW], var[0:1, 0:TW], tmp[0:1, 0:TW])
    p.sop("activation", tmp[0:1, 0:TW], var[0:1, 0:TW], AF.Sqrt,
          bias=ctx["epsr"][0:1, 0:1])
    p.vop("reciprocal", rstd[0:1, 0:TW], tmp[0:1, 0:TW])
    if not rms:
        p.vop("tensor_mul", tmp[0:1, 0:TW], mean[0:1, 0:TW], rstd[0:1, 0:TW])
    for c0, c1 in ntiles(TW):
        w = c1 - c0
        p.mm(bc1[:, 0:w], ones1f[0:1, :], rstd[0:1, c0:c1])
        if not rms:
            p.mm(bc2[:, 0:w], ones1f[0:1, :], tmp[0:1, c0:c1])
        for fc in range(nchunk):
            src = read_chunk(fc)
            p.vop("tensor_mul", ctx["scrfa"][:, 0:w], src[:, c0:c1], bc1[:, 0:w])
            if not rms:
                p.vop("tensor_sub", ctx["scrfa"][:, 0:w],
                      ctx["scrfa"][:, 0:w], bc2[:, 0:w])
            write_chunk(fc, c0, c1)


def h_ln_to(p, ctx, src_sb, dst16, nchunk, gamma=None, rms=False, TW=None,
            src_stride=None, dst_stride=None, src_off=0, dst_off=0):
    TW = TW or ctx["TW"]
    sst = src_stride or TW
    dst_st = dst_stride or TW

    def rd(fc):
        return src_sb[:, src_off + fc * sst: src_off + fc * sst + TW]

    def wr(fc, c0, c1):
        dst = dst16[:, dst_off + fc * dst_st + c0: dst_off + fc * dst_st + c1]
        if gamma is None:
            p.vop("tensor_copy", dst, ctx["scrfa"][:, 0:c1 - c0])
        else:
            p.vop("tensor_scalar", dst, ctx["scrfa"][:, 0:c1 - c0],
                  gamma[:, fc:fc + 1], None, ALU.mult)
    h_norm(p, ctx, rd, wr, nchunk, rms=rms, TW=TW)


def h_proj(p, ctx, wT_dram, xin16, kchunks, mlist, epilogue, TW=None,
           xin_off=0, w_row0=0):
    TW = TW or ctx["TW"]
    wt = ctx["wt"]
    ps = ctx["mmps"]
    dbl = (2 * kchunks * 128 <= 4096)
    for mi, m in enumerate(mlist):
        so = (mi % 2) * kchunks * 128 if dbl else 0
        p.dma(wt[:, so: so + kchunks * 128].rearrange("f (k m) -> f k m", k=kchunks),
              wT_dram[w_row0: w_row0 + kchunks * 128,
                      m * 128:(m + 1) * 128].rearrange("(k f) m -> f k m", f=128),
              engine="gpsimd")
        for ti, (c0, c1) in enumerate(ntiles(TW)):
            psx = ps if (mi + ti) % 2 == 0 else ctx["mmps2"]
            w = c1 - c0
            for k in range(kchunks):
                p.mm(psx[:, 0:w], wt[:, so + k * 128: so + (k + 1) * 128],
                     xin16[:, xin_off + k * TW + c0: xin_off + k * TW + c1],
                     start=(k == 0), stop=(k == kchunks - 1))
            epilogue(p, mi, c0, c1, psx)


def h_epi_copy(dst16, TW):
    def epi(p, mi, c0, c1, ps):
        p.vop("tensor_copy", dst16[:, mi * TW + c0: mi * TW + c1], ps[:, 0:c1 - c0])
    return epi


def h_epi_add_f32(dstf, TW):
    def epi(p, mi, c0, c1, ps):
        d = dstf[:, mi * TW + c0: mi * TW + c1]
        p.vop("tensor_add", d, d, ps[:, 0:c1 - c0])
    return epi


def build_launch1():
    p = P("launch1")
    C = 128
    NCH = SEQ // C

    xb = p.inp("xb", [NF, 128, SEQ])
    w_inT = p.inp("w_inT", [1024, 768])
    mask_in = p.inp("mask_intra", [64, 128], U8)
    o_out = p.out("o", [2, 128, SEQ])

    def body(p):
        ctx = make_ctx(p, C)
        xc = p.sb("xc", [128, NF * C], F32)
        xln = p.sb("xln", [128, NF * C], BF16)
        wsb = p.sb("wsb", [128, NF * 768], BF16)
        S0 = p.sb("S0", [128, 128], F32)
        S1 = p.sb("S1", [128, 128], F32)
        S016 = p.sb("S016", [128, 128], BF16)
        S116 = p.sb("S116", [128, 128], BF16)
        qch = p.sb("qch", [128, C], BF16)
        kch = p.sb("kch", [128, C], F32)
        vch = p.sb("vch", [128, C], BF16)
        gch = p.sb("gch", [128, C], F32)
        Ach = p.sb("Ach", [128, C], F32)
        q1b = p.sb("q1b", [128, C], BF16)
        k1b = p.sb("k1b", [128, C], BF16)
        qib = p.sb("qib", [128, C], BF16)
        k2b = p.sb("k2b", [128, C], BF16)
        vTa = p.sb("vTa", [64, 128], BF16)
        vTb = p.sb("vTb", [64, 128], BF16)
        kTa = p.sb("kTa", [64, 128], BF16)
        kTb = p.sb("kTb", [64, 128], BF16)
        P16 = p.sb("P16", [64, 128], BF16)
        Ecol = p.sb("Ecol", [128, 4], F32)
        zcol = p.sb("zcol", [128, 1], F32)
        osb = p.sb("osb", [128, C], F32)
        msk = p.sb("msk", [64, 128], U8)
        ident = p.sb("ident", [128, 128], BF16)
        tps = p.ps("tps", [64, 128], BF16)

        p.dma(msk[:, :], mask_in[:, :])
        p.vop("memset", S0[:], 0.0)
        p.vop("memset", S1[:], 0.0)
        p.vop("memset", zcol[:], 0.0)
        idi = p.sb("idi", [128, 128], I32)
        idp = p.sb("idp", [128, 1], I32)
        p.gop("iota", idi[:, :], [[1, 128]], base=0, channel_multiplier=0)
        p.gop("iota", idp[:, :], [[0, 1]], base=0, channel_multiplier=1)
        idf = p.sb("idf", [128, 128], F32)
        idpf = p.sb("idpf", [128, 1], F32)
        p.vop("tensor_copy", idf[:, :], idi[:, :])
        p.vop("tensor_copy", idpf[:, :], idp[:, :])
        p.vop("tensor_scalar", idf[:, :], idf[:, :], idpf[:, 0:1], None, ALU.is_equal)
        p.vop("tensor_copy", ident[:, :], idf[:, :])

        for k in range(NF):
            p.dma(wsb[:, k * 768:(k + 1) * 768],
                  w_inT[k * 128:(k + 1) * 128, :], engine="gpsimd")

        for ch in range(NCH):
            s0 = ch * C
            for fc in range(NF):
                p.dma(xc[:, fc * C:(fc + 1) * C], xb[fc, :, s0:s0 + C])
            h_ln_to(p, ctx, xc, xln, NF)

            for unit in range(2):
                S = S0 if unit == 0 else S1
                S16 = S016 if unit == 0 else S116
                for mg in range(3):
                    for k in range(NF):
                        p.mm(ctx["mmps"][:, 0:C],
                             wsb[:, k * 768 + unit * 384 + mg * 128:
                                 k * 768 + unit * 384 + (mg + 1) * 128],
                             xln[:, k * C:(k + 1) * C],
                             start=(k == 0), stop=(k == NF - 1))
                    if mg == 0:
                        p.vop("tensor_copy", vch[:, :], ctx["mmps"][:, 0:C])
                    elif mg == 1:
                        p.sop("activation", qch[:, :], ctx["mmps"][:, 0:C], AF.Silu)
                    else:
                        p.sop("activation", kch[:, :], ctx["mmps"][:, 0:C],
                              AF.Sigmoid, scale=-1.0)
                        p.vop("tensor_scalar", gch[:, :], kch[:, :], -1.0, 1.0,
                              ALU.mult, ALU.add)
                        p.sop("activation", gch[:, :], gch[:, :], AF.Ln)
                p.vop("tensor_tensor_scan", Ach[:, :], gch[:, :], gch[:, :],
                      zcol[:, 0:1], ALU.add, ALU.bypass)

                for j in range(4):
                    lo, hi = 32 * j, 32 * (j + 1)
                    amid = Ach[:, lo + 15:lo + 16]
                    aend = Ach[:, hi - 1:hi]
                    abase = Ach[:, lo - 1:lo] if j > 0 else zcol[:, 0:1]
                    namid = Ecol[:, 0:1]
                    nabase = Ecol[:, 1:2]
                    p.vop("tensor_scalar", namid, amid, -1.0, None, ALU.mult)
                    p.vop("tensor_scalar", nabase, abase, -1.0, None, ALU.mult)
                    p.sop("activation", ctx["scrfa"][:, 0:32], Ach[:, lo:hi],
                          AF.Exp, bias=namid)
                    p.vop("tensor_mul", q1b[:, lo:hi], qch[:, lo:hi],
                          ctx["scrfa"][:, 0:32])
                    p.sop("activation", ctx["scrfa"][:, 0:32], Ach[:, lo:hi],
                          AF.Exp, bias=amid, scale=-1.0)
                    p.vop("tensor_mul", k1b[:, lo:hi], kch[:, lo:hi],
                          ctx["scrfa"][:, 0:32])
                    p.sop("activation", ctx["scrfa"][:, 0:32], Ach[:, lo:hi],
                          AF.Exp, bias=nabase)
                    p.vop("tensor_mul", qib[:, lo:hi], qch[:, lo:hi],
                          ctx["scrfa"][:, 0:32])
                    p.sop("activation", ctx["scrfa"][:, 0:32], Ach[:, lo:hi],
                          AF.Exp, bias=aend, scale=-1.0)
                    p.vop("tensor_mul", k2b[:, lo:hi], kch[:, lo:hi],
                          ctx["scrfa"][:, 0:32])

                for src, dsts in ((k2b, (kTa, kTb)), (vch, (vTa, vTb))):
                    for hfi, dst in enumerate(dsts):
                        p.transpose(tps[:, :], src[:, hfi * 64:(hfi + 1) * 64], ident[:, :])
                        p.vop("tensor_copy", dst[:, :], tps[0:64, :])

                obank = ctx["bc1"]
                pbank = ctx["bc2"]
                sbank = ctx["mmps"]
                for j in range(4):
                    lo, hi = 32 * j, 32 * (j + 1)
                    aend = Ach[:, hi - 1:hi]
                    abase = Ach[:, lo - 1:lo] if j > 0 else zcol[:, 0:1]
                    pb = (j % 2) * 32
                    p.mm(pbank[pb:pb + 32, 0:32], k1b[:, lo:hi], q1b[:, lo:hi],
                         tile_position=(0, pb))
                    p.vop("memset", P16[pb:pb + 32, lo:hi], 0.0)
                    p.vop("copy_predicated", P16[pb:pb + 32, lo:hi],
                          msk[pb:pb + 32, lo:hi], pbank[pb:pb + 32, 0:32])
                    p.vop("tensor_copy", S16[:, :], S[:, :])
                    p.mm(obank[:, lo:hi], S16[:, :], qib[:, lo:hi],
                         start=True, stop=False)
                    vT = vTa if j < 2 else vTb
                    vbase = (j % 2) * 32
                    p.mm(obank[:, lo:hi], vT[vbase:vbase + 32, :],
                         P16[pb:pb + 32, lo:hi], start=False, stop=True)
                    kT = kTa if j < 2 else kTb
                    p.mm(sbank[:, 0:128], kT[vbase:vbase + 32, :],
                         vT[vbase:vbase + 32, :])
                    p.vop("tensor_scalar", Ecol[:, 1:2], abase, -1.0, None, ALU.mult)
                    p.sop("activation", Ecol[:, 2:3], aend, AF.Exp, bias=Ecol[:, 1:2])
                    p.vop("scalar_tensor_tensor", S[:, :], S[:, :], Ecol[:, 2:3],
                          sbank[:, 0:128], ALU.mult, ALU.add)

                p.vop("tensor_copy", osb[:, :], obank[:, 0:C])
                p.dma(o_out[unit, :, s0:s0 + C], osb[:, :])

    nc = p.build(body)
    return nc


def stage_launch1(inputs, x0):
    w_inT = np.ascontiguousarray(np.asarray(inputs["L0_attn_W_in"]).T)
    mask = (np.arange(32)[:, None] <= np.arange(32)[None, :])
    mask4 = np.tile(mask, (2, 4)).astype(np.uint8)
    xb_feat = []
    for b in range(B):
        xb_feat.append(np.ascontiguousarray(
            np.asarray(x0[b]).reshape(SEQ, NF, 128).transpose(1, 2, 0)))
    maps = []
    for c in range(NCORES):
        b = c // 4
        h0 = (2 * c) % 8
        groups = []
        for unit in range(2):
            h = h0 + unit
            for base in (0, 1024, 2048):
                groups.append(w_inT[:, base + h * 128: base + (h + 1) * 128])
        wcat = np.ascontiguousarray(np.concatenate(groups, axis=1))
        maps.append({"xb": xb_feat[b], "w_inT": wcat, "mask_intra": mask4})
    return maps


def build_launch2():
    p = P("launch2")
    TW = 1024

    x0 = p.inp("x0", [NF, 128, T])
    og_in = p.inp("og", [NF, 128, T])
    wg1T = p.inp("wg1T", [1024, 128])
    wg2T = p.inp("wg2T", [128, 1024])
    lnw = p.inp("lnw", [128, NF])
    woutT = p.inp("woutT", [1024, 1024])
    g1in = p.inp("g1", [128, NF])
    wdown = p.inp("wdown", [1024, 512])
    wq = p.inp("wq", [512, 512])
    wq_sw = p.inp("wq_sw", [512, 512])
    wk = p.inp("wk", [512, 512])
    wk_sw = p.inp("wk_sw", [512, 512])
    wv = p.inp("wv", [512, 512])
    pos_i = p.inp("pos", [1, T], I32)
    invfreq = p.inp("invfreq", [128, 1])
    signc = p.inp("signc", [128, 1])

    x1_out = p.out("x1", [NF, 128, T])
    q_out = p.out("q", [4, 128, T], BF16)
    k_out = p.out("k", [4, 128, T], BF16)
    v_out = p.out("vv", [4, 128, T], BF16)

    def body(p):
        ctx = make_ctx(p, TW)
        xsb = p.sb("xsb", [128, NF * TW], F32)
        a16 = p.sb("a16", [128, NF * TW], BF16)
        b16 = p.sb("b16", [128, NF * TW], BF16)
        c16 = p.sb("c16", [128, 4 * TW], BF16)
        d16 = p.sb("d16", [128, 4 * TW], BF16)
        t1 = p.sb("t1", [128, TW], BF16)
        gam = p.sb("gam", [128, NF], F32)
        cosT = p.sb("cosT", [128, TW], F32)
        sinTs = p.sb("sinTs", [128, TW], F32)
        ivf = p.sb("ivf", [128, 1], F32)
        sgn = p.sb("sgn", [128, 1], F32)
        posi_sb = p.sb("posi_sb", [1, TW], I32)
        angi = p.sb("angi", [128, 512], I32)

        p.dma(ivf[:, :], invfreq[:, :])
        p.dma(sgn[:, :], signc[:, :])

        for half in range(2):
            t0 = half * TW
            for fc in range(NF):
                p.dma(xsb[:, fc * TW:(fc + 1) * TW], x0[fc, :, t0:t0 + TW])
                p.dma(b16[:, fc * TW:(fc + 1) * TW], og_in[fc, :, t0:t0 + TW],
                      engine="gpsimd")

            h_ln_to(p, ctx, xsb, a16, NF)

            h_proj(p, ctx, wg1T, a16, NF, [0], h_epi_copy(t1, TW))

            def epi_gate(p_, mi, c0, c1, ps):
                p_.sop("activation", ctx["scr16b"][:, 0:c1 - c0], ps[:, 0:c1 - c0],
                       AF.Sigmoid)
                dst = b16[:, mi * TW + c0: mi * TW + c1]
                p_.vop("tensor_mul", dst, dst, ctx["scr16b"][:, 0:c1 - c0])
            h_proj(p, ctx, wg2T, t1, 1, list(range(NF)), epi_gate)

            p.dma(gam[:, :], lnw[:, :])
            h_ln_to(p, ctx, b16, b16, NF, gamma=gam)

            h_proj(p, ctx, woutT, b16, NF, list(range(NF)), h_epi_add_f32(xsb, TW))
            for fc in range(NF):
                p.dma(x1_out[fc, :, t0:t0 + TW], xsb[:, fc * TW:(fc + 1) * TW])

            h_ln_to(p, ctx, xsb, xsb, NF)
            p.dma(gam[:, :], g1in[:, :])
            h_ln_to(p, ctx, xsb, a16, NF, gamma=gam, rms=True)

            h_proj(p, ctx, wdown, a16, NF, [0, 1, 2, 3], h_epi_copy(c16, TW))

            p.dma(posi_sb[0:1, :], pos_i[0:1, t0:t0 + TW])
            p.vop("tensor_copy", ctx["rows"][0][0:1, :], posi_sb[0:1, :])
            for c0, c1 in ntiles(TW):
                w = c1 - c0
                p.mm(ctx["bc1"][:, 0:w], ctx["ones1f"][0:1, :],
                     ctx["rows"][0][0:1, c0:c1])
                p.vop("tensor_scalar", ctx["scrfa"][:, 0:w], ctx["bc1"][:, 0:w],
                      ivf[:, 0:1], None, ALU.mult)
                for dst, phase in ((sinTs, 0.0), (cosT, HALF_PI)):
                    if phase != 0.0:
                        p.vop("tensor_scalar", ctx["scrfb"][:, 0:w],
                              ctx["scrfa"][:, 0:w], phase, None, ALU.add)
                    else:
                        p.vop("tensor_copy", ctx["scrfb"][:, 0:w],
                              ctx["scrfa"][:, 0:w])
                    p.vop("tensor_scalar", ctx["scrfb"][:, 0:w],
                          ctx["scrfb"][:, 0:w], 1.0 / TWO_PI, None, ALU.mult)
                    p.vop("tensor_copy", angi[:, 0:w], ctx["scrfb"][:, 0:w])
                    p.vop("tensor_copy", dst[:, c0:c1], angi[:, 0:w])
                    p.vop("scalar_tensor_tensor", ctx["scrfb"][:, 0:w],
                          dst[:, c0:c1], -TWO_PI, ctx["scrfa"][:, 0:w],
                          ALU.mult, ALU.add)
                    if phase != 0.0:
                        p.vop("tensor_scalar", ctx["scrfb"][:, 0:w],
                              ctx["scrfb"][:, 0:w], phase, None, ALU.add)
                    p.vop("tensor_scalar", dst[:, c0:c1], ctx["scrfb"][:, 0:w],
                          np.pi, None, ALU.is_gt)
                    p.vop("scalar_tensor_tensor", ctx["scrfb"][:, 0:w],
                          dst[:, c0:c1], -TWO_PI, ctx["scrfb"][:, 0:w],
                          ALU.mult, ALU.add)
                    p.vop("tensor_scalar", dst[:, c0:c1], ctx["scrfb"][:, 0:w],
                          -np.pi, None, ALU.is_lt)
                    p.vop("scalar_tensor_tensor", ctx["scrfb"][:, 0:w],
                          dst[:, c0:c1], TWO_PI, ctx["scrfb"][:, 0:w],
                          ALU.mult, ALU.add)
                    p.sop("activation", dst[:, c0:c1], ctx["scrfb"][:, 0:w], AF.Sin)
            p.vop("tensor_scalar", sinTs[:, :], sinTs[:, :], sgn[:, 0:1], None, ALU.mult)

            h_proj(p, ctx, wq, c16, 4, [0, 1, 2, 3], h_epi_copy(b16, TW))

            def epi_k(p_, mi, c0, c1, ps):
                d = b16[:, (4 + mi) * TW + c0: (4 + mi) * TW + c1]
                p_.vop("tensor_copy", d, ps[:, 0:c1 - c0])
            h_proj(p, ctx, wk, c16, 4, [0, 1, 2, 3], epi_k)
            h_proj(p, ctx, wv, c16, 4, [0, 1, 2, 3], h_epi_copy(d16, TW))

            def mk_rope_epi(base):
                def epi(p_, mi, c0, c1, ps):
                    dst = b16[:, (base + mi) * TW + c0: (base + mi) * TW + c1]
                    p_.vop("tensor_mul", ctx["scrfb"][:, 0:c1 - c0], ps[:, 0:c1 - c0],
                           sinTs[:, c0:c1])
                    p_.vop("tensor_mul", ctx["scrfa"][:, 0:c1 - c0], dst, cosT[:, c0:c1])
                    p_.vop("tensor_add", dst, ctx["scrfa"][:, 0:c1 - c0],
                           ctx["scrfb"][:, 0:c1 - c0])
                return epi
            h_proj(p, ctx, wq_sw, c16, 4, [0, 1, 2, 3], mk_rope_epi(0))
            h_proj(p, ctx, wk_sw, c16, 4, [0, 1, 2, 3], mk_rope_epi(4))

            for m in range(4):
                p.dma(q_out[m, :, t0:t0 + TW], b16[:, m * TW:(m + 1) * TW])
                p.dma(k_out[m, :, t0:t0 + TW], b16[:, (4 + m) * TW:(5 + m) * TW])
                p.dma(v_out[m, :, t0:t0 + TW], d16[:, m * TW:(m + 1) * TW])

    nc = p.build(body)
    return nc


def build_launch3():
    p = P("launch3")
    HD = 32
    QT = 512
    KC = 128
    KHI = [(SEQ - SLOC + (qt + 1) * QT) // KC for qt in range(2)]

    x1 = p.inp("x1", [NF, 128, T])
    q_in = p.inp("q", [4, 128, T], BF16)
    kf = p.inp("kf", [B, 4, 128, SEQ], BF16)
    vfT = p.inp("vfT", [B, SEQ, 512], BF16)
    wupT = p.inp("wupT", [512, 1024])
    masks = p.inp("masks", [2, KHI[1], 128, QT], U8)
    res_out = p.out("res", [NF, 128, T])

    def body(p):
        ctx = make_ctx(p, T)
        xsb = p.sb("xsb", [128, NF * T], F32)
        qown = p.sb("qown", [128, 4 * T], BF16)
        Kg = p.sb("Kg", [128, SEQ], BF16)
        VTg = p.sb("VTg", [128, 64 * 128], BF16)
        aos = p.sb("aos", [32, 16 * SLOC], BF16)
        P16 = p.sb("P16", [128, QT], BF16)
        mskb = p.sb("mskb", [128, QT], BF16)
        negc = p.sb("negc", [128, 1], F32)
        cmaxq = p.sb("cmaxq", [128, 1], F32)
        cmaxk = p.sb("cmaxk", [128, 1], F32)
        crow = p.sb("crow", [1, 4], F32)
        identf = p.sb("identf", [128, 128], F32)
        wup_sb = p.sb("wup_sb", [32, 16 * 128], BF16)
        ops_ = p.ps("ops_", [32, QT], F32)
        osb = p.sb("osb", [32, QT], F32)

        idi = p.sb("idi", [128, 128], I32)
        idp = p.sb("idp", [128, 1], I32)
        idpf = p.sb("idpf", [128, 1], F32)
        p.gop("iota", idi[:, :], [[1, 128]], base=0, channel_multiplier=0)
        p.gop("iota", idp[:, :], [[0, 1]], base=0, channel_multiplier=1)
        p.vop("tensor_copy", identf[:, :], idi[:, :])
        p.vop("tensor_copy", idpf[:, :], idp[:, :])
        p.vop("tensor_scalar", identf[:, :], identf[:, :], idpf[:, 0:1], None,
              ALU.is_equal)

        for fc in range(NF):
            p.dma(xsb[:, fc * T:(fc + 1) * T], x1[fc, :, :])
        h_ln_to(p, ctx, xsb, xsb, NF)
        for m in range(4):
            p.dma(qown[:, m * T:(m + 1) * T], q_in[m, :, :])


        def colmax_abs(dst11, col):
            p.transpose(ctx["rps"][0:1, 0:128], col, identf[:, :])
            p.vop("reduce_max", dst11, ctx["rps"][0:1, 0:128], AX.X)

        for b in range(B):
            tb0 = b * SLOC
            for g in range(4):
                p.dma(Kg[:, :], kf[b, g, :, :])
                vslice = vfT[b, :, 128 * g:128 * (g + 1)]
                for i in range(8):
                    p.dma(VTg[:, i * 8 * 128:(i + 1) * 8 * 128].rearrange(
                              "p (kc h) -> p kc h", kc=8),
                          vslice[i * 1024:(i + 1) * 1024, :].rearrange(
                              "(kc q) h -> q kc h", q=128))
                p.vop("reduce_max", cmaxq[:, 0:1],
                      qown[:, g * T + tb0: g * T + tb0 + SLOC], AX.X,
                      apply_absolute_value=True)
                p.vop("reduce_max", cmaxk[:, 0:1], Kg[:, :], AX.X,
                      apply_absolute_value=True)
                colmax_abs(crow[0:1, 0:1], cmaxq[:, 0:1])
                colmax_abs(crow[0:1, 1:2], cmaxk[:, 0:1])
                p.vop("tensor_mul", crow[0:1, 2:3], crow[0:1, 0:1], crow[0:1, 1:2])
                p.vop("tensor_scalar", crow[0:1, 2:3], crow[0:1, 2:3],
                      -float(np.sqrt(HD)), None, ALU.mult)
                p.mm(ctx["bc1"][:, 0:1], ctx["ones1f"][0:1, :], crow[0:1, 2:3])
                p.vop("tensor_copy", negc[:, 0:1], ctx["bc1"][:, 0:1])

                for h in range(4):
                    hb = 32 * h
                    for qt in range(2):
                        ql = tb0 + qt * QT
                        khi = KHI[qt]
                        for kc in range(khi):
                            last = (kc == khi - 1)
                            p.dma(mskb[:, 0:QT], masks[qt, kc, :, :],
                                  engine="gpsimd")
                            p.mm(ctx["bc2"][:, 0:QT],
                                 Kg[hb:hb + 32, kc * KC:(kc + 1) * KC],
                                 qown[hb:hb + 32, g * T + ql: g * T + ql + QT],
                                 tile_position=(hb, 0))
                            p.sop("activation", P16[:, 0:QT], ctx["bc2"][:, 0:QT],
                                  AF.Exp, bias=negc[:, 0:1],
                                  scale=1.0 / float(np.sqrt(HD)))
                            p.vop("tensor_mul", P16[:, 0:QT], P16[:, 0:QT],
                                  mskb[:, 0:QT])
                            p.mm(ctx["rps"][0:1, 0:QT], ctx["onesP"][:, 0:1],
                                 P16[:, 0:QT], start=(kc == 0), stop=last)
                            p.mm(ops_[0:32, 0:QT],
                                 VTg[:, kc * 128 + hb: kc * 128 + hb + 32],
                                 P16[:, 0:QT], start=(kc == 0), stop=last)
                        p.vop("reciprocal", ctx["rows"][0][0:1, 0:QT],
                              ctx["rps"][0:1, 0:QT])
                        p.mm(ctx["mmps"][0:32, 0:QT], ctx["ones1f"][0:1, 0:32],
                             ctx["rows"][0][0:1, 0:QT])
                        p.vop("tensor_copy", osb[:, 0:QT], ops_[0:32, 0:QT])
                        p.vop("tensor_mul",
                              aos[:, (g * 4 + h) * SLOC + qt * QT:
                                  (g * 4 + h) * SLOC + (qt + 1) * QT],
                              osb[:, 0:QT], ctx["mmps"][0:32, 0:QT])

            for mi in range(NF):
                wof = 0
                for s in range(16):
                    p.dma(wup_sb[0:32, wof + s * 128: wof + (s + 1) * 128],
                          wupT[32 * s:32 * (s + 1), mi * 128:(mi + 1) * 128],
                          engine="gpsimd")
                for ti, (c0, c1) in enumerate(ntiles(SLOC)):
                    psx = ctx["mmps"] if (mi + ti) % 2 == 0 else ctx["mmps2"]
                    w = c1 - c0
                    for s in range(16):
                        p.mm(psx[:, 0:w],
                             wup_sb[0:32, wof + s * 128: wof + (s + 1) * 128],
                             aos[:, s * SLOC + c0: s * SLOC + c1],
                             start=(s == 0), stop=(s == 15))
                    d = xsb[:, mi * T + tb0 + c0: mi * T + tb0 + c1]
                    p.vop("tensor_add", d, d, psx[:, 0:w])

        for fc in range(NF):
            p.dma(res_out[fc, :, :], xsb[:, fc * T:(fc + 1) * T])

    nc = p.build(body)
    return nc


def stage_launch3(inputs, x1_slices, q_slices, k_slices, v_slices):
    KHI1 = (SEQ - SLOC + 2 * 512) // 128
    kfull = np.empty((B, 4, 128, SEQ), dtype=BF)
    vfull_T = np.empty((B, SEQ, 512), dtype=BF)
    for c in range(NCORES):
        ks = np.asarray(k_slices[c]).reshape(4, 128, B, SLOC)
        vs = np.asarray(v_slices[c]).reshape(4, 128, B, SLOC)
        for b in range(B):
            kfull[b, :, :, SLOC * c:SLOC * (c + 1)] = ks[:, :, b, :]
            vfull_T[b, SLOC * c:SLOC * (c + 1), :] = \
                vs[:, :, b, :].reshape(512, SLOC).T
    wupT = np.ascontiguousarray(np.asarray(inputs["L0_ffn_w_up"]))
    maps = []
    for c in range(NCORES):
        m = np.empty((2, KHI1, 128, 512), dtype=np.uint8)
        for qt in range(2):
            q0 = 1024 * c + qt * 512
            for kc in range(KHI1):
                kpos = kc * 128 + np.arange(128)[:, None]
                qpos = q0 + np.arange(512)[None, :]
                m[qt, kc] = (kpos <= qpos).astype(np.uint8)
        maps.append({"x1": x1_slices[c], "q": q_slices[c], "kf": kfull,
                     "vfT": vfull_T, "wupT": wupT, "masks": m})
    return maps


def build_launch4():
    p = P("launch4")
    NEXP = 8
    TWH = 1024

    x1 = p.inp("x1", [NF, 128, T])
    res_in = p.inp("res", [NF, 128, T])
    g2in = p.inp("g2", [128, NF])
    router_w = p.inp("router_w", [1024, 8])
    w1_in = p.inp("w1", [NEXP, 1024, 2048])
    w3_in = p.inp("w3", [NEXP, 1024, 2048])
    w2_in = p.inp("w2", [NEXP, 2048, 1024])
    wbT = p.inp("wbT", [1024, 64])
    wvT = p.inp("wvT", [1024, 1024])
    bv_in = p.inp("bv", [128, NF])

    x2_out = p.out("x2", [NF, 128, T])
    lat_out = p.out("lat", [64, T], BF16)
    v1_out = p.out("v1", [NF, 128, T], BF16)

    def body(p):
        ctx = make_ctx(p, T)
        rsb = p.sb("rsb", [128, NF * T], F32)
        t16 = p.sb("t16", [128, NF * T], BF16)
        xg = p.sb("xg", [128, NF * TWH], BF16)
        hid = p.sb("hid", [128, 16 * TWH], BF16)
        gtok = p.sb("gtok", [128, 16 * NEXP], F32)
        growE = p.sb("growE", [1, TWH], F32)
        rwsb = p.sb("rwsb", [128, NF * 8], BF16)
        gam = p.sb("gam", [128, NF], F32)
        bv = p.sb("bv_sb", [128, NF], F32)
        lat16 = p.sb("lat16", [64, T], BF16)
        identf2 = p.sb("identf2", [128, 128], BF16)
        ninf = p.sb("ninf", [128, 8], F32)
        ninfC = p.sb("ninfC", [128, 8], F32)
        l2nd = p.sb("l2nd", [128, 8], F32)
        lg8 = p.sb("lg8", [128, 8], F32)
        m1c = p.sb("m1c", [128, 1], F32)
        m2c = p.sb("m2c", [128, 1], F32)
        m3c = p.sb("m3c", [128, 1], F32)
        msk1 = p.sb("msk1", [128, 8], U8)
        msk2 = p.sb("msk2", [128, 8], U8)
        g8 = p.sb("g8", [128, 8], F32)

        idi = p.sb("idi", [128, 128], I32)
        idp = p.sb("idp", [128, 1], I32)
        idpf = p.sb("idpf", [128, 1], F32)
        idff = p.sb("idff", [128, 128], F32)
        p.gop("iota", idi[:, :], [[1, 128]], base=0, channel_multiplier=0)
        p.gop("iota", idp[:, :], [[0, 1]], base=0, channel_multiplier=1)
        p.vop("tensor_copy", idff[:, :], idi[:, :])
        p.vop("tensor_copy", idpf[:, :], idp[:, :])
        p.vop("tensor_scalar", idff[:, :], idff[:, :], idpf[:, 0:1], None,
              ALU.is_equal)
        p.vop("tensor_copy", identf2[:, :], idff[:, :])

        p.dma(gam[:, :], g2in[:, :])
        p.dma(bv[:, :], bv_in[:, :])
        for fc in range(NF):
            p.dma(rsb[:, fc * T:(fc + 1) * T], res_in[fc, :, :])
            p.dma(rwsb[:, fc * 8:(fc + 1) * 8], router_w[fc * 128:(fc + 1) * 128, :],
                  engine="gpsimd")
        h_ln_to(p, ctx, rsb, t16, NF, gamma=gam, rms=True)

        for tt in range(16):
            for fc in range(NF):
                p.mm(ctx["mmps"][0:128, 0:8],
                     t16[:, fc * T + tt * 128: fc * T + (tt + 1) * 128],
                     rwsb[:, fc * 8:(fc + 1) * 8],
                     start=(fc == 0), stop=(fc == NF - 1))
            p.vop("tensor_copy", lg8[:, :], ctx["mmps"][0:128, 0:8])
            p.vop("reduce_max", m1c[:, 0:1], lg8[:, :], AX.X)
            p.vop("tensor_scalar", msk1[:, :], lg8[:, :], m1c[:, 0:1], None,
                  ALU.is_ge)
            p.vop("tensor_copy", l2nd[:, :], msk1[:, :])
            p.vop("scalar_tensor_tensor", l2nd[:, :], l2nd[:, :], -1e30,
                  lg8[:, :], ALU.mult, ALU.add)
            p.vop("reduce_max", m2c[:, 0:1], l2nd[:, :], AX.X)
            p.vop("tensor_scalar", msk2[:, :], lg8[:, :], m2c[:, 0:1], None,
                  ALU.is_ge)
            p.vop("tensor_scalar", m3c[:, 0:1], m2c[:, 0:1], -1.0, None, ALU.mult)
            p.sop("activation", m3c[:, 0:1], m1c[:, 0:1], AF.Sigmoid,
                  bias=m3c[:, 0:1])
            p.vop("tensor_copy", g8[:, :], msk1[:, :])
            p.vop("tensor_scalar", g8[:, :], g8[:, :], m3c[:, 0:1], None, ALU.mult)
            p.vop("tensor_copy", ninf[:, :], msk2[:, :])
            p.vop("tensor_copy", lg8[:, :], msk1[:, :])
            p.vop("tensor_mul", lg8[:, :], lg8[:, :], ninf[:, :])
            p.vop("tensor_sub", ninf[:, :], ninf[:, :], lg8[:, :])
            p.vop("tensor_scalar", m1c[:, 0:1], m3c[:, 0:1], -1.0, 1.0,
                  ALU.mult, ALU.add)
            p.vop("tensor_scalar", ninf[:, :], ninf[:, :], m1c[:, 0:1], None,
                  ALU.mult)
            p.vop("tensor_add", gtok[:, tt * 8:(tt + 1) * 8], g8[:, :], ninf[:, :])

        for half in range(2):
            t0 = half * TWH
            for e in range(NEXP):
                for tt in range(8):
                    gtt = half * 8 + tt
                    p.mm(ctx["rps"][0:1, 0:128],
                         gtok[:, gtt * 8 + e: gtt * 8 + e + 1], idff[:, :])
                    p.vop("tensor_copy", growE[0:1, tt * 128:(tt + 1) * 128],
                          ctx["rps"][0:1, 0:128])
                for c0, c1 in ntiles(TWH):
                    w = c1 - c0
                    p.mm(ctx["bc1"][:, 0:w], ctx["ones1f"][0:1, :],
                         growE[0:1, c0:c1])
                    for fc in range(NF):
                        p.vop("tensor_mul", xg[:, fc * TWH + c0: fc * TWH + c1],
                              t16[:, fc * T + t0 + c0: fc * T + t0 + c1],
                              ctx["bc1"][:, 0:w])
                def epi_h1(p_, mi, c0, c1, ps):
                    p_.sop("activation", hid[:, mi * TWH + c0: mi * TWH + c1],
                           ps[:, 0:c1 - c0], AF.Silu)
                def rd_t(k):
                    return t16[:, k * T + t0: k * T + t0 + TWH]
                h_proj_view(p, ctx, w1_in, e, rd_t, NF, list(range(16)), epi_h1,
                            TW=TWH)
                def epi_h3(p_, mi, c0, c1, ps):
                    d = hid[:, mi * TWH + c0: mi * TWH + c1]
                    p_.vop("tensor_mul", d, d, ps[:, 0:c1 - c0])
                h_proj_view(p, ctx, w3_in, e, lambda k: xg[:, k * TWH:(k + 1) * TWH],
                            NF, list(range(16)), epi_h3, TW=TWH)
                def epi_y(p_, mi, c0, c1, ps):
                    d = rsb[:, mi * T + t0 + c0: mi * T + t0 + c1]
                    p_.vop("tensor_add", d, d, ps[:, 0:c1 - c0])
                h_proj_view(p, ctx, w2_in, e, lambda k: hid[:, k * TWH:(k + 1) * TWH],
                            16, list(range(NF)), epi_y, TW=TWH)

        for fc in range(NF):
            for c0, c1 in ntiles(T):
                p.dma(ctx["scrfa"][:, 0:c1 - c0], x1[fc, :, c0:c1])
                p.vop("tensor_add", rsb[:, fc * T + c0: fc * T + c1],
                      rsb[:, fc * T + c0: fc * T + c1], ctx["scrfa"][:, 0:c1 - c0])
            p.dma(x2_out[fc, :, :], rsb[:, fc * T:(fc + 1) * T])

        h_ln_to(p, ctx, rsb, t16, NF)
        for c0, c1 in ntiles(T):
            w = c1 - c0
            for k in range(NF):
                p.dma(ctx["wt"][:, k * 128:(k + 1) * 128][:, 0:64],
                      wbT[k * 128:(k + 1) * 128, :], engine="gpsimd")
            for k in range(NF):
                p.mm(ctx["mmps"][0:64, 0:w], ctx["wt"][:, k * 128: k * 128 + 64],
                     t16[:, k * T + c0: k * T + c1],
                     start=(k == 0), stop=(k == NF - 1))
            p.vop("tensor_copy", lat16[:, c0:c1], ctx["mmps"][0:64, 0:w])
        for i in range(2):
            p.dma(lat_out[:, i * TWH:(i + 1) * TWH], lat16[:, i * TWH:(i + 1) * TWH])

        def epi_v1(p_, mi, c0, c1, ps):
            p_.vop("tensor_scalar", ctx["scr16a"][:, 0:c1 - c0], ps[:, 0:c1 - c0],
                   bv[:, mi:mi + 1], None, ALU.add)
            p_.dma(v1_out[mi, :, c0:c1], ctx["scr16a"][:, 0:c1 - c0])
        h_proj(p, ctx, wvT, t16, NF, list(range(NF)), epi_v1)

    nc = p.build(body)
    return nc


def h_proj_view(p, ctx, w3d, e, read_k, kchunks, mlist, epilogue, TW):
    wt = ctx["wt"]
    ps = ctx["mmps"]
    dbl = (2 * kchunks * 128 <= 4096)
    for mi, m in enumerate(mlist):
        so = (mi % 2) * kchunks * 128 if dbl else 0
        p.dma(wt[:, so: so + kchunks * 128].rearrange("f (k m) -> f k m", k=kchunks),
              w3d[e, 0:kchunks * 128, m * 128:(m + 1) * 128].rearrange(
                  "(k f) m -> f k m", f=128), engine="gpsimd")
        for ti, (c0, c1) in enumerate(ntiles(TW)):
            psx = ps if (mi + ti) % 2 == 0 else ctx["mmps2"]
            w = c1 - c0
            for k in range(kchunks):
                p.mm(psx[:, 0:w], wt[:, so + k * 128: so + (k + 1) * 128],
                     read_k(k)[:, c0:c1],
                     start=(k == 0), stop=(k == kchunks - 1))
            epilogue(p, mi, c0, c1, psx)


def stage_launch4(inputs, x1_slices, res_slices):
    g2 = np.ascontiguousarray(np.asarray(inputs["L0_ffn_g2"]).reshape(NF, 128).T)
    router_w = np.ascontiguousarray(np.asarray(inputs["L0_ffn_router_w"]))
    w1 = np.ascontiguousarray(np.asarray(inputs["L0_ffn_w1"]))
    w3 = np.ascontiguousarray(np.asarray(inputs["L0_ffn_w3"]))
    w2 = np.ascontiguousarray(np.asarray(inputs["L0_ffn_w2"]))
    wbT = np.ascontiguousarray(np.asarray(inputs["L1_attn_W_basis"]).T)
    wvT = np.ascontiguousarray(np.asarray(inputs["L1_attn_W_v"]).T)
    bv = np.ascontiguousarray(np.asarray(inputs["L1_attn_b_v"]).reshape(NF, 128).T)
    maps = []
    for c in range(NCORES):
        maps.append({"x1": x1_slices[c], "res": res_slices[c], "g2": g2,
                     "router_w": router_w, "w1": w1, "w3": w3, "w2": w2,
                     "wbT": wbT, "wvT": wvT, "bv": bv})
    return maps


def build_launch5():
    p = P("launch5")
    QT = 512
    KC = 128
    KHI = [(SEQ - SLOC + (qt + 1) * QT) // KC for qt in range(2)]

    x2 = p.inp("x2", [NF, 128, T])
    lato = p.inp("lato", [64, T], BF16)
    latf = p.inp("latf", [B, 64, SEQ], BF16)
    v1fT = p.inp("v1fT", [B, SEQ, 1024], BF16)
    woT = p.inp("woT", [1024, 1024])
    bo_in = p.inp("bo", [128, NF])
    masks = p.inp("masks", [2, KHI[1], 128, QT], U8)
    x3_out = p.out("x3", [NF, 128, T])

    def body(p):
        ctx = make_ctx(p, T)
        xsb = p.sb("xsb", [128, NF * T], F32)
        lo = p.sb("lo", [64, T], BF16)
        lf = p.sb("lf", [64, SEQ], BF16)
        vT = p.sb("vT", [128, 1024], BF16)
        aoL = p.sb("aoL", [128, NF * QT], BF16)
        P16 = p.sb("P16", [128, QT], BF16)
        mskb = p.sb("mskb", [128, QT], BF16)
        negc = p.sb("negc", [128, 1], F32)
        cm1 = p.sb("cm1", [128, 1], F32)
        cm2 = p.sb("cm2", [128, 1], F32)
        crow = p.sb("crow", [1, 4], F32)
        identf = p.sb("identf", [128, 128], F32)
        ysb = p.sb("ysb", [128, QT], F32)
        bo = p.sb("bo_sb", [128, NF], F32)
        y2 = p.ps("y2", [128, 512], F32)
        y3 = p.ps("y3", [128, 512], F32)

        idi = p.sb("idi", [128, 128], I32)
        idp = p.sb("idp", [128, 1], I32)
        idpf = p.sb("idpf", [128, 1], F32)
        p.gop("iota", idi[:, :], [[1, 128]], base=0, channel_multiplier=0)
        p.gop("iota", idp[:, :], [[0, 1]], base=0, channel_multiplier=1)
        p.vop("tensor_copy", identf[:, :], idi[:, :])
        p.vop("tensor_copy", idpf[:, :], idp[:, :])
        p.vop("tensor_scalar", identf[:, :], identf[:, :], idpf[:, 0:1], None,
              ALU.is_equal)

        p.dma(bo[:, :], bo_in[:, :])
        for fc in range(NF):
            p.dma(xsb[:, fc * T:(fc + 1) * T], x2[fc, :, :])
        p.dma(lo[:, :], lato[:, :])

        for b in range(B):
            tb0 = b * SLOC
            p.dma(lf[:, :], latf[b, :, :])
            p.vop("memset", cm1[:, :], 0.0)
            p.vop("memset", cm2[:, :], 0.0)
            p.vop("reduce_max", cm1[0:64, 0:1], lo[0:64, tb0:tb0 + SLOC], AX.X,
                  apply_absolute_value=True)
            p.vop("reduce_max", cm2[0:64, 0:1], lf[0:64, :], AX.X,
                  apply_absolute_value=True)
            p.transpose(ctx["rps"][0:1, 0:128], cm1[:, 0:1], identf[:, :])
            p.vop("reduce_max", crow[0:1, 0:1], ctx["rps"][0:1, 0:64], AX.X)
            p.transpose(ctx["rps"][0:1, 0:128], cm2[:, 0:1], identf[:, :])
            p.vop("reduce_max", crow[0:1, 1:2], ctx["rps"][0:1, 0:64], AX.X)
            p.vop("tensor_mul", crow[0:1, 2:3], crow[0:1, 0:1], crow[0:1, 1:2])
            p.vop("tensor_scalar", crow[0:1, 2:3], crow[0:1, 2:3],
                  -64.0 / 8.0, None, ALU.mult)
            p.mm(ctx["bc1"][:, 0:1], ctx["ones1f"][0:1, :], crow[0:1, 2:3])
            p.vop("tensor_copy", negc[:, 0:1], ctx["bc1"][:, 0:1])

            for qt in range(2):
                ql = tb0 + qt * QT
                khi = KHI[qt]
                for dh in range(2):
                    ybanks = [ctx["bc1"], ctx["mmps"], y2, y3]
                    for kc in range(khi):
                        last = (kc == khi - 1)
                        p.mm(ctx["bc2"][:, 0:QT], lf[0:64, kc * KC:(kc + 1) * KC],
                             lo[0:64, ql:ql + QT])
                        p.sop("activation", P16[:, 0:QT], ctx["bc2"][:, 0:QT],
                              AF.Exp, bias=negc[:, 0:1], scale=1.0 / 8.0)
                        p.dma(mskb[:, 0:QT], masks[qt, kc, :, :], engine="gpsimd")
                        p.vop("tensor_mul", P16[:, 0:QT], P16[:, 0:QT],
                              mskb[:, 0:QT])
                        if dh == 0:
                            p.mm(ctx["rps"][0:1, 0:QT], ctx["onesP"][:, 0:1],
                                 P16[:, 0:QT], start=(kc == 0), stop=last)
                        p.dma(vT[:, :], v1fT[b, kc * KC:(kc + 1) * KC, :],
                              engine="gpsimd")
                        for mc in range(4):
                            p.mm(ybanks[mc][:, 0:QT],
                                 vT[:, (dh * 4 + mc) * 128:(dh * 4 + mc + 1) * 128],
                                 P16[:, 0:QT], start=(kc == 0), stop=last)
                    if dh == 0:
                        p.vop("reciprocal", ctx["rows"][0][0:1, 0:QT],
                              ctx["rps"][0:1, 0:QT])
                    for mc in range(4):
                        p.vop("tensor_copy", ysb[:, 0:QT], ybanks[mc][:, 0:QT])
                        p.mm(ctx["bc2"][:, 0:QT], ctx["ones1f"][0:1, :],
                             ctx["rows"][0][0:1, 0:QT])
                        p.vop("tensor_mul",
                              aoL[:, (dh * 4 + mc) * QT:(dh * 4 + mc + 1) * QT],
                              ysb[:, 0:QT], ctx["bc2"][:, 0:QT])
                def epi_x3(p_, mi, c0, c1, ps):
                    d = xsb[:, mi * T + ql + c0: mi * T + ql + c1]
                    p_.vop("tensor_scalar", ctx["scrfa"][:, 0:c1 - c0],
                           ps[:, 0:c1 - c0], bo[:, mi:mi + 1], None, ALU.add)
                    p_.vop("tensor_add", d, d, ctx["scrfa"][:, 0:c1 - c0])
                h_proj(p, ctx, woT, aoL, NF, list(range(NF)), epi_x3, TW=QT)

        for fc in range(NF):
            p.dma(x3_out[fc, :, :], xsb[:, fc * T:(fc + 1) * T])

    nc = p.build(body)
    return nc


def stage_launch5(inputs, x2_slices, lat_slices, v1_slices, masks_per_core):
    latf = np.empty((B, 64, SEQ), dtype=BF)
    v1fT = np.empty((B, SEQ, 1024), dtype=BF)
    for c in range(NCORES):
        ls = np.asarray(lat_slices[c]).reshape(64, B, SLOC)
        vs = np.asarray(v1_slices[c]).reshape(NF, 128, B, SLOC)
        for b in range(B):
            latf[b, :, SLOC * c:SLOC * (c + 1)] = ls[:, b, :]
            v1fT[b, SLOC * c:SLOC * (c + 1), :] = \
                vs[:, :, b, :].reshape(1024, SLOC).T
    woT = np.ascontiguousarray(np.asarray(inputs["L1_attn_W_o"]).T)
    bo = np.ascontiguousarray(np.asarray(inputs["L1_attn_b_o"]).reshape(NF, 128).T)
    maps = []
    for c in range(NCORES):
        maps.append({"x2": x2_slices[c], "lato": lat_slices[c], "latf": latf,
                     "v1fT": v1fT, "woT": woT, "bo": bo,
                     "masks": masks_per_core[c]})
    return maps


def build_launch6():
    p = P("launch6")
    NEXP = 4
    TWQ = 512

    x3 = p.inp("x3", [NF, 128, T])
    router_w = p.inp("router_w", [1024, 4])
    router_b = p.inp("router_b", [1, 4])
    w1_in = p.inp("w1", [NEXP, 1024, 4096])
    w3_in = p.inp("w3", [NEXP, 1024, 4096])
    w2_in = p.inp("w2", [NEXP, 4096, 1024])
    wqT = p.inp("wqT", [1024, 1024])
    wkT = p.inp("wkT", [1024, 1024])
    wvT = p.inp("wvT", [1024, 1024])
    bq_in = p.inp("bq", [128, NF])
    bk_in = p.inp("bk", [128, NF])
    bv_in = p.inp("bv", [128, NF])

    x4_out = p.out("x4", [NF, 128, T])
    q2_out = p.out("q2", [NF, 128, T], BF16)
    k2_out = p.out("k2", [NF, 128, T], BF16)
    v2_out = p.out("v2", [NF, 128, T], BF16)

    def body(p):
        ctx = make_ctx(p, T)
        xsb = p.sb("xsb", [128, NF * T], F32)
        t16 = p.sb("t16", [128, NF * T], BF16)
        xg = p.sb("xg", [128, NF * TWQ], BF16)
        hid = p.sb("hid", [128, 32 * TWQ], BF16)
        gtok = p.sb("gtok", [128, 16 * NEXP], F32)
        growE = p.sb("growE", [1, TWQ], F32)
        rwsb = p.sb("rwsb", [128, NF * 4], BF16)
        rbsb = p.sb("rbsb", [1, 4], F32)
        ninf = p.sb("ninf", [128, 4], F32)
        l2nd = p.sb("l2nd", [128, 4], F32)
        lg8 = p.sb("lg8", [128, 4], F32)
        m1c = p.sb("m1c", [128, 1], F32)
        m2c = p.sb("m2c", [128, 1], F32)
        m3c = p.sb("m3c", [128, 1], F32)
        msk1 = p.sb("msk1", [128, 4], U8)
        msk2 = p.sb("msk2", [128, 4], U8)
        g8 = p.sb("g8", [128, 4], F32)
        idff = p.sb("idff", [128, 128], F32)
        bcol = p.sb("bcol", [128, NF], F32)

        idi = p.sb("idi", [128, 128], I32)
        idp = p.sb("idp", [128, 1], I32)
        idpf = p.sb("idpf", [128, 1], F32)
        p.gop("iota", idi[:, :], [[1, 128]], base=0, channel_multiplier=0)
        p.gop("iota", idp[:, :], [[0, 1]], base=0, channel_multiplier=1)
        p.vop("tensor_copy", idff[:, :], idi[:, :])
        p.vop("tensor_copy", idpf[:, :], idp[:, :])
        p.vop("tensor_scalar", idff[:, :], idff[:, :], idpf[:, 0:1], None,
              ALU.is_equal)

        p.dma(rbsb[0:1, :], router_b[0:1, :])
        for fc in range(NF):
            p.dma(xsb[:, fc * T:(fc + 1) * T], x3[fc, :, :])
            p.dma(rwsb[:, fc * 4:(fc + 1) * 4], router_w[fc * 128:(fc + 1) * 128, :],
                  engine="gpsimd")
        h_ln_to(p, ctx, xsb, t16, NF)

        for tt in range(16):
            for fc in range(NF):
                p.mm(ctx["mmps"][0:128, 0:4],
                     t16[:, fc * T + tt * 128: fc * T + (tt + 1) * 128],
                     rwsb[:, fc * 4:(fc + 1) * 4],
                     start=(fc == 0), stop=False)
            p.mm(ctx["mmps"][0:128, 0:4], ctx["ones1f"][0:1, :], rbsb[0:1, 0:4],
                 start=False, stop=True)
            p.vop("tensor_copy", lg8[:, :], ctx["mmps"][0:128, 0:4])
            p.vop("reduce_max", m1c[:, 0:1], lg8[:, :], AX.X)
            p.vop("tensor_scalar", msk1[:, :], lg8[:, :], m1c[:, 0:1], None,
                  ALU.is_ge)
            p.vop("tensor_copy", l2nd[:, :], msk1[:, :])
            p.vop("scalar_tensor_tensor", l2nd[:, :], l2nd[:, :], -1e30,
                  lg8[:, :], ALU.mult, ALU.add)
            p.vop("reduce_max", m2c[:, 0:1], l2nd[:, :], AX.X)
            p.vop("tensor_scalar", msk2[:, :], lg8[:, :], m2c[:, 0:1], None,
                  ALU.is_ge)
            p.vop("tensor_scalar", m3c[:, 0:1], m2c[:, 0:1], -1.0, None, ALU.mult)
            p.sop("activation", m3c[:, 0:1], m1c[:, 0:1], AF.Sigmoid,
                  bias=m3c[:, 0:1])
            p.vop("tensor_copy", g8[:, :], msk1[:, :])
            p.vop("tensor_scalar", g8[:, :], g8[:, :], m3c[:, 0:1], None, ALU.mult)
            p.vop("tensor_copy", ninf[:, :], msk2[:, :])
            p.vop("tensor_copy", lg8[:, :], msk1[:, :])
            p.vop("tensor_mul", lg8[:, :], lg8[:, :], ninf[:, :])
            p.vop("tensor_sub", ninf[:, :], ninf[:, :], lg8[:, :])
            p.vop("tensor_scalar", m1c[:, 0:1], m3c[:, 0:1], -1.0, 1.0,
                  ALU.mult, ALU.add)
            p.vop("tensor_scalar", ninf[:, :], ninf[:, :], m1c[:, 0:1], None,
                  ALU.mult)
            p.vop("tensor_add", gtok[:, tt * NEXP:(tt + 1) * NEXP], g8[:, :],
                  ninf[:, :])

        for quarter in range(4):
            t0 = quarter * TWQ
            for e in range(NEXP):
                for tt in range(4):
                    gtt = quarter * 4 + tt
                    p.mm(ctx["rps"][0:1, 0:128],
                         gtok[:, gtt * NEXP + e: gtt * NEXP + e + 1], idff[:, :])
                    p.vop("tensor_copy", growE[0:1, tt * 128:(tt + 1) * 128],
                          ctx["rps"][0:1, 0:128])
                for c0, c1 in ntiles(TWQ):
                    w = c1 - c0
                    p.mm(ctx["bc1"][:, 0:w], ctx["ones1f"][0:1, :],
                         growE[0:1, c0:c1])
                    for fc in range(NF):
                        p.vop("tensor_mul", xg[:, fc * TWQ + c0: fc * TWQ + c1],
                              t16[:, fc * T + t0 + c0: fc * T + t0 + c1],
                              ctx["bc1"][:, 0:w])
                def epi_h1(p_, mi, c0, c1, ps):
                    p_.sop("activation", hid[:, mi * TWQ + c0: mi * TWQ + c1],
                           ps[:, 0:c1 - c0], AF.Silu)
                h_proj_view(p, ctx, w1_in, e,
                            lambda k: t16[:, k * T + t0: k * T + t0 + TWQ],
                            NF, list(range(32)), epi_h1, TW=TWQ)
                def epi_h3(p_, mi, c0, c1, ps):
                    d = hid[:, mi * TWQ + c0: mi * TWQ + c1]
                    p_.vop("tensor_mul", d, d, ps[:, 0:c1 - c0])
                h_proj_view(p, ctx, w3_in, e,
                            lambda k: xg[:, k * TWQ:(k + 1) * TWQ],
                            NF, list(range(32)), epi_h3, TW=TWQ)
                def epi_y(p_, mi, c0, c1, ps):
                    d = xsb[:, mi * T + t0 + c0: mi * T + t0 + c1]
                    p_.vop("tensor_add", d, d, ps[:, 0:c1 - c0])
                h_proj_view(p, ctx, w2_in, e,
                            lambda k: hid[:, k * TWQ:(k + 1) * TWQ],
                            32, list(range(NF)), epi_y, TW=TWQ)

        for fc in range(NF):
            p.dma(x4_out[fc, :, :], xsb[:, fc * T:(fc + 1) * T])

        h_ln_to(p, ctx, xsb, t16, NF)
        for wT, b_in, outT in ((wqT, bq_in, q2_out), (wkT, bk_in, k2_out),
                               (wvT, bv_in, v2_out)):
            p.dma(bcol[:, :], b_in[:, :])
            def epi_qkv(p_, mi, c0, c1, ps, o=outT):
                p_.vop("tensor_scalar", ctx["scr16a"][:, 0:c1 - c0],
                       ps[:, 0:c1 - c0], bcol[:, mi:mi + 1], None, ALU.add)
                p_.dma(o[mi, :, c0:c1], ctx["scr16a"][:, 0:c1 - c0])
            h_proj(p, ctx, wT, t16, NF, list(range(NF)), epi_qkv)

    nc = p.build(body)
    return nc


def stage_launch6(inputs, x3_slices):
    rw = np.ascontiguousarray(np.asarray(inputs["L1_ffn_router_w"]))
    rb = np.ascontiguousarray(np.asarray(inputs["L1_ffn_router_b"]).reshape(1, 4))
    w1 = np.ascontiguousarray(np.asarray(inputs["L1_ffn_w1"]))
    w3 = np.ascontiguousarray(np.asarray(inputs["L1_ffn_w3"]))
    w2 = np.ascontiguousarray(np.asarray(inputs["L1_ffn_w2"]))
    wqT = np.ascontiguousarray(np.asarray(inputs["L2_attn_Wq"]).T)
    wkT = np.ascontiguousarray(np.asarray(inputs["L2_attn_Wk"]).T)
    wvT = np.ascontiguousarray(np.asarray(inputs["L2_attn_Wv"]).T)
    bq = np.ascontiguousarray(np.asarray(inputs["L2_attn_bq"]).reshape(NF, 128).T)
    bk = np.ascontiguousarray(np.asarray(inputs["L2_attn_bk"]).reshape(NF, 128).T)
    bv = np.ascontiguousarray(np.asarray(inputs["L2_attn_bv"]).reshape(NF, 128).T)
    maps = []
    for c in range(NCORES):
        maps.append({"x3": x3_slices[c], "router_w": rw, "router_b": rb,
                     "w1": w1, "w3": w3, "w2": w2, "wqT": wqT, "wkT": wkT,
                     "wvT": wvT, "bq": bq, "bk": bk, "bv": bv})
    return maps


def build_launch7():
    p = P("launch7")
    HL = 1280

    x4 = p.inp("x4", [NF, 128, T])
    q2o = p.inp("q2o", [NF, 128, T], BF16)
    khalo = p.inp("khalo", [B, NF, 128, HL], BF16)
    vhaloT = p.inp("vhaloT", [B, HL, 1024], BF16)
    woT = p.inp("woT", [1024, 1024])
    bo_in = p.inp("bo", [128, NF])
    masks2 = p.inp("masks2", [B, 8, 128, 384], U8)
    wr_in = p.inp("wr", [128, NF])
    x5_out = p.out("x5", [NF, 128, T])
    sc_out = p.out("sc", [1, T])

    def body(p):
        ctx = make_ctx(p, T)
        xsb = p.sb("xsb", [128, NF * T], F32)
        qsb = p.sb("qsb", [128, NF * T], BF16)
        ksb = p.sb("ksb", [128, NF * HL], BF16)
        vsb = p.sb("vsb", [128, 10 * 1024], BF16)
        ao = p.sb("ao", [128, NF * SLOC], BF16)
        P16 = p.sb("P16", [128, 384], BF16)
        mskb = p.sb("mskb", [128, 384], BF16)
        negc = p.sb("negc", [128, 1], F32)
        cm1 = p.sb("cm1", [128, 1], F32)
        cm2 = p.sb("cm2", [128, 1], F32)
        crow = p.sb("crow", [1, 4], F32)
        identf = p.sb("identf", [128, 128], F32)
        osb = p.sb("osb", [64, 128], F32)
        bo = p.sb("bo_sb", [128, NF], F32)
        wrc = p.sb("wrc", [128, NF], F32)
        t16 = qsb

        idi = p.sb("idi", [128, 128], I32)
        idp = p.sb("idp", [128, 1], I32)
        idpf = p.sb("idpf", [128, 1], F32)
        p.gop("iota", idi[:, :], [[1, 128]], base=0, channel_multiplier=0)
        p.gop("iota", idp[:, :], [[0, 1]], base=0, channel_multiplier=1)
        p.vop("tensor_copy", identf[:, :], idi[:, :])
        p.vop("tensor_copy", idpf[:, :], idp[:, :])
        p.vop("tensor_scalar", identf[:, :], identf[:, :], idpf[:, 0:1], None,
              ALU.is_equal)

        p.dma(bo[:, :], bo_in[:, :])
        p.dma(wrc[:, :], wr_in[:, :])
        for fc in range(NF):
            p.dma(xsb[:, fc * T:(fc + 1) * T], x4[fc, :, :])
            p.dma(qsb[:, fc * T:(fc + 1) * T], q2o[fc, :, :])

        for b in range(B):
            tb0 = b * SLOC
            for fc in range(NF):
                p.dma(ksb[:, fc * HL:(fc + 1) * HL], khalo[b, fc, :, :])
            for kb in range(10):
                p.dma(vsb[:, kb * 1024:(kb + 1) * 1024],
                      vhaloT[b, kb * 128:(kb + 1) * 128, :])
            p.vop("memset", crow[0:1, 0:1], 0.0)
            p.vop("memset", crow[0:1, 1:2], 0.0)
            for fc in range(NF):
                for c0, c1 in ntiles(T):
                    p.sop("activation", ctx["scr16a"][:, 0:c1 - c0],
                          qsb[:, fc * T + c0: fc * T + c1], AF.Square)
                    p.mm(ctx["rps"][0:1, 0:c1 - c0], ctx["onesP"][:, 0:1],
                         ctx["scr16a"][:, 0:c1 - c0])
                    p.vop("reduce_max", crow[0:1, 2:3],
                          ctx["rps"][0:1, 0:c1 - c0], AX.X)
                    p.vop("tensor_tensor", crow[0:1, 0:1], crow[0:1, 0:1],
                          crow[0:1, 2:3], ALU.max)
                for c0, c1 in ntiles(HL):
                    p.sop("activation", ctx["scr16a"][:, 0:c1 - c0],
                          ksb[:, fc * HL + c0: fc * HL + c1], AF.Square)
                    p.mm(ctx["rps"][0:1, 0:c1 - c0], ctx["onesP"][:, 0:1],
                         ctx["scr16a"][:, 0:c1 - c0])
                    p.vop("reduce_max", crow[0:1, 2:3],
                          ctx["rps"][0:1, 0:c1 - c0], AX.X)
                    p.vop("tensor_tensor", crow[0:1, 1:2], crow[0:1, 1:2],
                          crow[0:1, 2:3], ALU.max)
            p.sop("activation", crow[0:1, 0:1], crow[0:1, 0:1], AF.Sqrt)
            p.sop("activation", crow[0:1, 1:2], crow[0:1, 1:2], AF.Sqrt)
            p.vop("tensor_mul", crow[0:1, 2:3], crow[0:1, 0:1], crow[0:1, 1:2])
            p.vop("tensor_scalar", crow[0:1, 2:3], crow[0:1, 2:3],
                  -1.0 / 8.0, None, ALU.mult)
            p.mm(ctx["bc1"][:, 0:1], ctx["ones1f"][0:1, :], crow[0:1, 2:3])
            p.vop("tensor_copy", negc[:, 0:1], ctx["bc1"][:, 0:1])

            for qb in range(8):
                p.dma(mskb[:, :], masks2[b, qb, :, :], engine="gpsimd")
                for h in range(16):
                    hfc = h // 2
                    hr = 64 * (h % 2)
                    for blk in range(3):
                        p.mm(ctx["bc2"][:, blk * 128:(blk + 1) * 128],
                             ksb[hr:hr + 64,
                                 hfc * HL + (qb + blk) * 128: hfc * HL + (qb + blk + 1) * 128],
                             qsb[hr:hr + 64,
                                 hfc * T + tb0 + qb * 128: hfc * T + tb0 + (qb + 1) * 128],
                             tile_position=(hr, 0))
                    p.sop("activation", P16[:, :], ctx["bc2"][:, 0:384], AF.Exp,
                          bias=negc[:, 0:1], scale=1.0 / 8.0)
                    p.vop("tensor_mul", P16[:, :], P16[:, :], mskb[:, :])
                    p.mm(ctx["rps"][0:1, 0:384], ctx["onesP"][:, 0:1], P16[:, :])
                    p.vop("tensor_copy", ctx["rows"][0][0:1, 0:384],
                          ctx["rps"][0:1, 0:384])
                    p.vop("tensor_add", ctx["rows"][0][0:1, 0:128],
                          ctx["rows"][0][0:1, 0:128], ctx["rows"][0][0:1, 128:256])
                    p.vop("tensor_add", ctx["rows"][0][0:1, 0:128],
                          ctx["rows"][0][0:1, 0:128], ctx["rows"][0][0:1, 256:384])
                    p.vop("reciprocal", ctx["rows"][1][0:1, 0:128],
                          ctx["rows"][0][0:1, 0:128])
                    for blk in range(3):
                        p.mm(ctx["mmps"][0:64, 0:128],
                             vsb[:, (qb + blk) * 1024 + h * 64:
                                 (qb + blk) * 1024 + (h + 1) * 64],
                             P16[:, blk * 128:(blk + 1) * 128],
                             start=(blk == 0), stop=(blk == 2))
                    p.mm(ctx["bc1"][:, 0:128], ctx["ones1f"][0:1, :],
                         ctx["rows"][1][0:1, 0:128])
                    p.vop("tensor_copy", osb[:, :], ctx["mmps"][0:64, 0:128])
                    p.vop("tensor_mul",
                          ao[hr:hr + 64, hfc * SLOC + qb * 128:
                             hfc * SLOC + (qb + 1) * 128],
                          osb[:, :], ctx["bc1"][0:64, 0:128])
            def epi_x5(p_, mi, c0, c1, ps):
                d = xsb[:, mi * T + tb0 + c0: mi * T + tb0 + c1]
                p_.vop("tensor_scalar", ctx["scrfa"][:, 0:c1 - c0],
                       ps[:, 0:c1 - c0], bo[:, mi:mi + 1], None, ALU.add)
                p_.vop("tensor_add", d, d, ctx["scrfa"][:, 0:c1 - c0])
            h_proj(p, ctx, woT, ao, NF, list(range(NF)), epi_x5, TW=SLOC)

        for fc in range(NF):
            p.dma(x5_out[fc, :, :], xsb[:, fc * T:(fc + 1) * T])

        h_ln_to(p, ctx, xsb, t16, NF)
        for c0, c1 in ntiles(T):
            w = c1 - c0
            for fc in range(NF):
                p.vop("tensor_scalar", ctx["scr16a"][:, 0:w],
                      t16[:, fc * T + c0: fc * T + c1], wrc[:, fc:fc + 1],
                      None, ALU.mult)
                p.mm(ctx["rps"][0:1, 0:w], ctx["onesP"][:, 0:1],
                     ctx["scr16a"][:, 0:w], start=(fc == 0), stop=(fc == NF - 1))
            p.vop("tensor_copy", ctx["rows"][3][0:1, c0:c1], ctx["rps"][0:1, 0:w])
        p.dma(sc_out[0:1, :], ctx["rows"][3][0:1, :])

    nc = p.build(body)
    return nc


def stage_launch7(inputs, x4_slices, q2_slices, k2_slices, v2_slices):
    HL = 1280
    k2f = np.zeros((B, 1024, SEQ), dtype=BF)
    v2f = np.zeros((B, 1024, SEQ), dtype=BF)
    for c in range(NCORES):
        ks = np.asarray(k2_slices[c]).reshape(NF, 128, B, SLOC)
        vs = np.asarray(v2_slices[c]).reshape(NF, 128, B, SLOC)
        for b in range(B):
            k2f[b, :, SLOC * c:SLOC * (c + 1)] = ks[:, :, b, :].reshape(1024, SLOC)
            v2f[b, :, SLOC * c:SLOC * (c + 1)] = vs[:, :, b, :].reshape(1024, SLOC)
    woT = np.ascontiguousarray(np.asarray(inputs["L2_attn_Wo"]).T)
    bo = np.ascontiguousarray(np.asarray(inputs["L2_attn_bo"]).reshape(NF, 128).T)
    wr = np.ascontiguousarray(np.asarray(inputs["L2_ffn_w_router"]).reshape(NF, 128).T)
    maps = []
    for c in range(NCORES):
        lo_g = SLOC * c - 128
        khalo = np.zeros((B, NF, 128, HL), dtype=BF)
        vhaloT = np.zeros((B, HL, 1024), dtype=BF)
        for b in range(B):
            for j in range(HL):
                g = lo_g + j
                if 0 <= g < SEQ:
                    khalo[b, :, :, j] = k2f[b, :, g].reshape(NF, 128)
                    vhaloT[b, j, :] = v2f[b, :, g]
        m2 = np.zeros((B, 8, 128, 384), dtype=np.uint8)
        for b in range(B):
            for qb in range(8):
                qg = SLOC * c + qb * 128 + np.arange(128)[None, :]
                for blk in range(3):
                    kg = lo_g + (qb + blk) * 128 + np.arange(128)[:, None]
                    ok = (np.abs(kg - qg) <= 128) & (kg >= 0) & (kg < SEQ)
                    m2[b, qb, :, blk * 128:(blk + 1) * 128] = ok.astype(np.uint8)
        maps.append({"x4": x4_slices[c], "q2o": q2_slices[c], "khalo": khalo,
                     "vhaloT": vhaloT, "woT": woT, "bo": bo, "masks2": m2,
                     "wr": wr})
    return maps


def build_launch8():
    p = P("launch8")
    TWQ = 512

    x5 = p.inp("x5", [NF, 128, T])
    scf = p.inp("scf", [B, SEQ])
    sco_cols = p.inp("sco_cols", [128, 16])
    pos_cols = p.inp("pos_cols", [128, 16])
    W1m = p.inp("W1m", [1024, 4096])
    b1m = p.inp("b1m", [128, 32])
    W2mT = p.inp("W2mT", [4096, 1024])
    b2m = p.inp("b2m", [128, NF])
    wqT3 = p.inp("wqT3", [1024, 1024])
    wkT3 = p.inp("wkT3", [1024, 1024])
    wvT3 = p.inp("wvT3", [1024, 1024])
    woT3 = p.inp("woT3", [1024, 1024])
    bq3 = p.inp("bq3", [128, NF])
    bk3 = p.inp("bk3", [128, NF])
    bv3 = p.inp("bv3", [128, NF])
    bo3 = p.inp("bo3", [128, NF])
    W1aT = p.inp("W1aT", [1024, 1024])
    b1a = p.inp("b1a", [128, NF])
    W2aT = p.inp("W2aT", [1024, 64])
    b2a_col = p.inp("b2a_col", [64, 1])
    sigT = p.inp("sigT", [64, 16])
    rw3 = p.inp("rw3", [1024, 8])
    rb3 = p.inp("rb3", [1, 8])
    rw1 = p.inp("rw1", [8, 1024, 512])
    rw3e = p.inp("rw3e", [8, 1024, 512])
    rw2 = p.inp("rw2", [8, 512, 1024])
    sw1 = p.inp("sw1", [2, 1024, 512])
    sw3 = p.inp("sw3", [2, 1024, 512])
    sw2 = p.inp("sw2", [2, 512, 1024])

    x8_out = p.out("x8", [NF, 128, T])

    def body(p):
        ctx = make_ctx(p, T)
        xsb = p.sb("xsb", [128, NF * T], F32)
        t16 = p.sb("t16", [128, NF * T], BF16)
        big = p.sb("big", [128, 16384], BF16)
        rankc = p.sb("rankc", [128, 16], F32)
        scch = p.sb("scch", [1, 512], F32)
        posr = p.sb("posr", [1, 512], F32)
        posi = p.sb("posi", [1, 512], I32)
        scols = p.sb("scols", [128, 16], F32)
        pcols = p.sb("pcols", [128, 16], F32)
        cnt = p.sb("cnt", [128, 1], F32)
        colA = p.sb("colA", [128, 1], F32)
        bcolA = p.sb("bcolA", [128, 32], F32)
        bcolB = p.sb("bcolB", [128, NF], F32)
        bcolC = p.sb("bcolC", [128, NF], F32)
        bcolD = p.sb("bcolD", [128, NF], F32)
        b2acol = p.sb("b2acol", [64, 1], F32)
        idff = p.sb("idff", [128, 128], F32)
        qc = p.sb("qc", [128, T], BF16)
        kc_ = p.sb("kc_", [128, T], BF16)
        vc = p.sb("vc", [128, T], BF16)
        grow = ctx["rows"][2]
        wmrow = ctx["rows"][3]
        r0row = ctx["rows"][0]
        r1row = ctx["rows"][1]
        gt16 = p.sb("gt16", [128, 16 * 16], F32)
        t64 = p.sb("t64", [64, T], BF16)
        gtokE = p.sb("gtokE", [128, 16 * 8], F32)
        ninf = p.sb("ninf", [128, 8], F32)
        l2nd = p.sb("l2nd", [128, 8], F32)
        lg8 = p.sb("lg8", [128, 8], F32)
        m1c = p.sb("m1c", [128, 1], F32)
        m2c = p.sb("m2c", [128, 1], F32)
        m3c = p.sb("m3c", [128, 1], F32)
        msk1 = p.sb("msk1", [128, 8], U8)
        msk2 = p.sb("msk2", [128, 8], U8)
        g8 = p.sb("g8", [128, 8], F32)
        rb8 = p.sb("rb8", [1, 8], F32)
        sg = p.sb("sg", [64, 16], BF16)
        rwsb = p.sb("rwsb8", [128, NF * 8], BF16)

        idi = p.sb("idi", [128, 128], I32)
        idp = p.sb("idp", [128, 1], I32)
        idpf = p.sb("idpf", [128, 1], F32)
        p.gop("iota", idi[:, :], [[1, 128]], base=0, channel_multiplier=0)
        p.gop("iota", idp[:, :], [[0, 1]], base=0, channel_multiplier=1)
        p.vop("tensor_copy", idff[:, :], idi[:, :])
        p.vop("tensor_copy", idpf[:, :], idp[:, :])
        p.vop("tensor_scalar", idff[:, :], idff[:, :], idpf[:, 0:1], None,
              ALU.is_equal)

        for fc in range(NF):
            p.dma(xsb[:, fc * T:(fc + 1) * T], x5[fc, :, :])
        p.dma(scols[:, :], sco_cols[:, :])
        p.dma(pcols[:, :], pos_cols[:, :])

        h_ln_to(p, ctx, xsb, t16, NF)
        for fc in range(NF):
            for c0, c1 in ntiles(T):
                p.vop("tensor_add", xsb[:, fc * T + c0: fc * T + c1],
                      xsb[:, fc * T + c0: fc * T + c1],
                      t16[:, fc * T + c0: fc * T + c1])
        for b in range(B):
            for tt in range(8):
                p.vop("memset", rankc[:, b * 8 + tt: b * 8 + tt + 1], 0.0)
            for j0 in range(0, SEQ, 512):
                p.dma(scch[0:1, :], scf[b:b + 1, j0:j0 + 512])
                p.gop("iota", posi[0:1, :], [[1, 512]], base=j0,
                      channel_multiplier=0)
                p.vop("tensor_copy", posr[0:1, :], posi[0:1, :])
                p.mm(ctx["bc1"][:, 0:512], ctx["ones1f"][0:1, :], scch[0:1, :])
                p.mm(ctx["bc2"][:, 0:512], ctx["ones1f"][0:1, :], posr[0:1, :])
                for tt in range(8):
                    ci = b * 8 + tt
                    p.vop("tensor_scalar", ctx["scrfa"][:, 0:512],
                          ctx["bc1"][:, 0:512], scols[:, ci:ci + 1], None,
                          ALU.is_gt)
                    p.vop("tensor_scalar", ctx["scrfb"][:, 0:512],
                          ctx["bc1"][:, 0:512], scols[:, ci:ci + 1], None,
                          ALU.is_equal)
                    p.vop("tensor_scalar", ctx["scr16a"][:, 0:512],
                          ctx["bc2"][:, 0:512], pcols[:, ci:ci + 1], None,
                          ALU.is_lt)
                    p.vop("tensor_mul", ctx["scrfb"][:, 0:512],
                          ctx["scrfb"][:, 0:512], ctx["scr16a"][:, 0:512])
                    p.vop("tensor_add", ctx["scrfa"][:, 0:512],
                          ctx["scrfa"][:, 0:512], ctx["scrfb"][:, 0:512])
                    p.vop("reduce_sum", cnt[:, 0:1], ctx["scrfa"][:, 0:512], AX.X)
                    p.vop("tensor_add", rankc[:, ci:ci + 1], rankc[:, ci:ci + 1],
                          cnt[:, 0:1])
        for b in range(B):
            tb0 = b * SLOC
            for tt in range(8):
                ci = b * 8 + tt
                p.vop("tensor_scalar", colA[:, 0:1], rankc[:, ci:ci + 1],
                      4096.0, None, ALU.is_lt)
                p.vop("tensor_mul", colA[:, 0:1], colA[:, 0:1], scols[:, ci:ci + 1]) \
                    if False else None
                p.mm(ctx["rps"][0:1, 0:128], colA[:, 0:1], idff[:, :])
                p.vop("tensor_copy", grow[0:1, tb0 + tt * 128: tb0 + (tt + 1) * 128],
                      ctx["rps"][0:1, 0:128])
                p.mm(ctx["rps"][0:1, 0:128], scols[:, ci:ci + 1], idff[:, :])
                p.vop("tensor_copy", wmrow[0:1, tb0 + tt * 128: tb0 + (tt + 1) * 128],
                      ctx["rps"][0:1, 0:128])
        p.sop("activation", wmrow[0:1, :], wmrow[0:1, :], AF.Sigmoid)
        p.vop("tensor_mul", wmrow[0:1, :], wmrow[0:1, :], grow[0:1, :])

        p.dma(bcolA[:, :], b1m[:, :])
        p.dma(bcolB[:, :], b2m[:, :])
        for quarter in range(4):
            t0 = quarter * TWQ

            def epi_g(p_, mi, c0, c1, ps):
                w = c1 - c0
                u = ctx["scrfa"]
                p_.vop("tensor_scalar", u[:, 0:w], ps[:, 0:w],
                       bcolA[:, mi:mi + 1], None, ALU.add)
                p_.sop("activation", ctx["scrfb"][:, 0:w], u[:, 0:w], AF.Square)
                p_.vop("tensor_mul", ctx["scrfb"][:, 0:w], ctx["scrfb"][:, 0:w],
                       u[:, 0:w])
                p_.vop("scalar_tensor_tensor", ctx["scrfb"][:, 0:w],
                       ctx["scrfb"][:, 0:w], 0.044715, u[:, 0:w],
                       ALU.mult, ALU.add)
                p_.sop("activation", ctx["scrfb"][:, 0:w], ctx["scrfb"][:, 0:w],
                       AF.Tanh, scale=0.7978845608028654)
                p_.vop("tensor_scalar", ctx["scrfb"][:, 0:w], ctx["scrfb"][:, 0:w],
                       1.0, None, ALU.add)
                p_.vop("tensor_scalar", u[:, 0:w], u[:, 0:w], 0.5, None, ALU.mult)
                p_.vop("tensor_mul", big[:, mi * TWQ + c0: mi * TWQ + c1],
                       u[:, 0:w], ctx["scrfb"][:, 0:w])

            def rd_t(k, t0=t0):
                return t16[:, k * T + t0: k * T + t0 + TWQ]
            h_proj2(p, ctx, W1m, rd_t, NF, list(range(32)), epi_g, TW=TWQ)

            def epi_u(p_, mi, c0, c1, ps, t0=t0):
                w = c1 - c0
                p_.mm(ctx["bc1"][:, 0:w], ctx["ones1f"][0:1, :],
                      wmrow[0:1, t0 + c0: t0 + c1])
                p_.vop("tensor_scalar", ctx["scrfa"][:, 0:w], ps[:, 0:w],
                       bcolB[:, mi:mi + 1], None, ALU.add)
                p_.vop("tensor_mul", ctx["scrfa"][:, 0:w], ctx["scrfa"][:, 0:w],
                       ctx["bc1"][:, 0:w])
                d = xsb[:, mi * T + t0 + c0: mi * T + t0 + c1]
                p_.vop("tensor_add", d, d, ctx["scrfa"][:, 0:w])
            h_proj2(p, ctx, W2mT, lambda k: big[:, k * TWQ:(k + 1) * TWQ],
                    32, list(range(NF)), epi_u, TW=TWQ)

        h_ln_to(p, ctx, xsb, t16, NF)
        p.dma(bcolB[:, :], b1a[:, :])

        def epi_r(p_, mi, c0, c1, ps):
            p_.vop("tensor_scalar", ctx["scrfa"][:, 0:c1 - c0], ps[:, 0:c1 - c0],
                   bcolB[:, mi:mi + 1], None, ALU.add)
            p_.sop("activation", big[:, mi * T + c0: mi * T + c1],
                   ctx["scrfa"][:, 0:c1 - c0], AF.Relu)
        h_proj(p, ctx, W1aT, t16, NF, list(range(NF)), epi_r)
        p.dma(b2acol[:, :], b2a_col[:, :])
        for c0, c1 in ntiles(T):
            w = c1 - c0
            for k in range(NF):
                p.dma(ctx["wt"][:, k * 128: k * 128 + 64],
                      W2aT[k * 128:(k + 1) * 128, :], engine="gpsimd")
            for k in range(NF):
                p.mm(ctx["mmps"][0:64, 0:w], ctx["wt"][:, k * 128: k * 128 + 64],
                     big[:, k * T + c0: k * T + c1],
                     start=(k == 0), stop=(k == NF - 1))
            p.vop("tensor_scalar", ctx["scrfa"][0:64, 0:w], ctx["mmps"][0:64, 0:w],
                  b2acol[0:64, 0:1], None, ALU.add)
            p.vop("tensor_copy", t64[:, c0:c1], ctx["scrfa"][0:64, 0:w])
        p.dma(sg[:, :], sigT[:, :], engine="gpsimd")
        for tt in range(16):
            p.mm(ctx["mmps"][0:128, 0:16], t64[:, tt * 128:(tt + 1) * 128],
                 sg[:, :])
            p.vop("tensor_copy", gt16[:, tt * 16:(tt + 1) * 16],
                  ctx["mmps"][0:128, 0:16])
            sl = gt16[:, tt * 16:(tt + 1) * 16]
            p.vop("reduce_max", m1c[:, 0:1], sl, AX.X)
            p.vop("tensor_scalar", m2c[:, 0:1], m1c[:, 0:1], -1.0, None, ALU.mult)
            p.sop("activation", sl, sl, AF.Exp, bias=m2c[:, 0:1])
            p.vop("reduce_sum", m1c[:, 0:1], sl, AX.X)
            p.vop("reciprocal", m2c[:, 0:1], m1c[:, 0:1])
            p.vop("tensor_scalar", sl, sl, m2c[:, 0:1], None, ALU.mult)

        p.dma(bcolA[:, 0:NF], bq3[:, :])
        p.dma(bcolB[:, :], bk3[:, :])
        p.dma(bcolC[:, :], bv3[:, :])
        p.dma(bcolD[:, :], bo3[:, :])
        for fc in range(NF):
            for wT, bcol, dst, scal in ((wqT3, bcolA, qc, 0.125),
                                        (wkT3, bcolB, kc_, None),
                                        (wvT3, bcolC, vc, None)):
                for k in range(NF):
                    p.dma(ctx["wt"][:, k * 128:(k + 1) * 128],
                          wT[k * 128:(k + 1) * 128, fc * 128:(fc + 1) * 128],
                          engine="gpsimd")
                for c0, c1 in ntiles(T):
                    w = c1 - c0
                    for k in range(NF):
                        p.mm(ctx["mmps"][:, 0:w], ctx["wt"][:, k * 128:(k + 1) * 128],
                             t16[:, k * T + c0: k * T + c1],
                             start=(k == 0), stop=(k == NF - 1))
                    if scal is None:
                        p.vop("tensor_scalar", dst[:, c0:c1], ctx["mmps"][:, 0:w],
                              bcol[:, fc:fc + 1], None, ALU.add)
                    else:
                        p.vop("tensor_scalar", dst[:, c0:c1], ctx["mmps"][:, 0:w],
                              bcol[:, fc:fc + 1], scal, ALU.add, ALU.mult)
            for hh in range(2):
                hr = hh * 64
                h_ = fc * 2 + hh
                for tt in range(16):
                    p.mm(ctx["rps"][0:1, 0:128],
                         gt16[:, tt * 16 + h_: tt * 16 + h_ + 1], idff[:, :])
                    p.vop("tensor_copy", grow[0:1, tt * 128:(tt + 1) * 128],
                          ctx["rps"][0:1, 0:128])
                for t_ in range(2):
                    for s_ in range(2):
                        rrow = r0row if s_ == 0 else r1row
                        for c0, c1 in ntiles(SLOC):
                            w = c1 - c0
                            p.vop("tensor_mul", ctx["scrfa"][:, 0:w],
                                  qc[:, t_ * SLOC + c0: t_ * SLOC + c1],
                                  kc_[:, s_ * SLOC + c0: s_ * SLOC + c1])
                            p.vop("tensor_copy", ctx["scr16a"][:, 0:w],
                                  ctx["scrfa"][:, 0:w])
                            p.mm(ctx["rps"][0:1, 0:w],
                                 ctx["onesP"][hr:hr + 64, 0:1],
                                 ctx["scr16a"][hr:hr + 64, 0:w],
                                 tile_position=(hr, 0))
                            p.vop("tensor_copy", rrow[0:1, c0:c1],
                                  ctx["rps"][0:1, 0:w])
                    p.vop("tensor_sub", r0row[0:1, :], r0row[0:1, :], r1row[0:1, :])
                    p.sop("activation", r0row[0:1, :], r0row[0:1, :], AF.Sigmoid)
                    for c0, c1 in ntiles(SLOC):
                        w = c1 - c0
                        p.mm(ctx["bc1"][:, 0:w], ctx["ones1f"][0:1, :],
                             r0row[0:1, c0:c1])
                        p.vop("tensor_sub", ctx["scrfa"][:, 0:w],
                              vc[:, 0 * SLOC + c0: 0 * SLOC + c1],
                              vc[:, 1 * SLOC + c0: 1 * SLOC + c1])
                        p.vop("tensor_mul", ctx["scrfa"][:, 0:w],
                              ctx["scrfa"][:, 0:w], ctx["bc1"][:, 0:w])
                        p.vop("tensor_add", ctx["scrfa"][:, 0:w],
                              ctx["scrfa"][:, 0:w],
                              vc[:, 1 * SLOC + c0: 1 * SLOC + c1])
                        p.mm(ctx["bc2"][:, 0:w], ctx["ones1f"][0:1, :],
                             grow[0:1, t_ * SLOC + c0: t_ * SLOC + c1])
                        p.vop("tensor_mul", ctx["scrfa"][hr:hr + 64, 0:w],
                              ctx["scrfa"][hr:hr + 64, 0:w],
                              ctx["bc2"][hr:hr + 64, 0:w])
                        p.vop("tensor_copy",
                              big[hr:hr + 64, fc * T + t_ * SLOC + c0:
                                  fc * T + t_ * SLOC + c1],
                              ctx["scrfa"][hr:hr + 64, 0:w])

        def epi_x7(p_, mi, c0, c1, ps):
            d = xsb[:, mi * T + c0: mi * T + c1]
            p_.vop("tensor_scalar", ctx["scrfa"][:, 0:c1 - c0], ps[:, 0:c1 - c0],
                   bcolD[:, mi:mi + 1], None, ALU.add)
            p_.vop("tensor_add", d, d, ctx["scrfa"][:, 0:c1 - c0])
        h_proj(p, ctx, woT3, big, NF, list(range(NF)), epi_x7)

        h_ln_to(p, ctx, xsb, t16, NF)
        p.dma(rb8[0:1, :], rb3[0:1, :])
        for fc in range(NF):
            p.dma(rwsb[:, fc * 8:(fc + 1) * 8], rw3[fc * 128:(fc + 1) * 128, :],
                  engine="gpsimd")
        for tt in range(16):
            for fc in range(NF):
                p.mm(ctx["mmps"][0:128, 0:8],
                     t16[:, fc * T + tt * 128: fc * T + (tt + 1) * 128],
                     rwsb[:, fc * 8:(fc + 1) * 8],
                     start=(fc == 0), stop=False)
            p.mm(ctx["mmps"][0:128, 0:8], ctx["ones1f"][0:1, :], rb8[0:1, 0:8],
                 start=False, stop=True)
            p.vop("tensor_copy", lg8[:, :], ctx["mmps"][0:128, 0:8])
            p.vop("reduce_max", m1c[:, 0:1], lg8[:, :], AX.X)
            p.vop("tensor_scalar", msk1[:, :], lg8[:, :], m1c[:, 0:1], None,
                  ALU.is_ge)
            p.vop("tensor_copy", l2nd[:, :], msk1[:, :])
            p.vop("scalar_tensor_tensor", l2nd[:, :], l2nd[:, :], -1e30,
                  lg8[:, :], ALU.mult, ALU.add)
            p.vop("reduce_max", m2c[:, 0:1], l2nd[:, :], AX.X)
            p.vop("tensor_scalar", msk2[:, :], lg8[:, :], m2c[:, 0:1], None,
                  ALU.is_ge)
            p.vop("tensor_scalar", m3c[:, 0:1], m2c[:, 0:1], -1.0, None, ALU.mult)
            p.sop("activation", m3c[:, 0:1], m1c[:, 0:1], AF.Sigmoid,
                  bias=m3c[:, 0:1])
            p.vop("tensor_copy", g8[:, :], msk1[:, :])
            p.vop("tensor_scalar", g8[:, :], g8[:, :], m3c[:, 0:1], None, ALU.mult)
            p.vop("tensor_copy", ninf[:, :], msk2[:, :])
            p.vop("tensor_copy", lg8[:, :], msk1[:, :])
            p.vop("tensor_mul", lg8[:, :], lg8[:, :], ninf[:, :])
            p.vop("tensor_sub", ninf[:, :], ninf[:, :], lg8[:, :])
            p.vop("tensor_scalar", m1c[:, 0:1], m3c[:, 0:1], -1.0, 1.0,
                  ALU.mult, ALU.add)
            p.vop("tensor_scalar", ninf[:, :], ninf[:, :], m1c[:, 0:1], None,
                  ALU.mult)
            p.vop("tensor_add", gtokE[:, tt * 8:(tt + 1) * 8], g8[:, :], ninf[:, :])

        for e in range(10):
            routed = e < 8
            if routed:
                for tt in range(16):
                    p.mm(ctx["rps"][0:1, 0:128],
                         gtokE[:, tt * 8 + e: tt * 8 + e + 1], idff[:, :])
                    p.vop("tensor_copy", grow[0:1, tt * 128:(tt + 1) * 128],
                          ctx["rps"][0:1, 0:128])
            wa = rw1 if routed else sw1
            wb = rw3e if routed else sw3
            wc = rw2 if routed else sw2
            ei = e if routed else e - 8

            def epi_h1(p_, mi, c0, c1, ps):
                p_.sop("activation", big[:, mi * T + c0: mi * T + c1],
                       ps[:, 0:c1 - c0], AF.Silu)
            h_proj_view(p, ctx, wa, ei, lambda k: t16[:, k * T:(k + 1) * T],
                        NF, [0, 1, 2, 3], epi_h1, TW=T)

            def epi_h3(p_, mi, c0, c1, ps):
                d = big[:, mi * T + c0: mi * T + c1]
                p_.vop("tensor_mul", d, d, ps[:, 0:c1 - c0])
            h_proj_view(p, ctx, wb, ei, lambda k: t16[:, k * T:(k + 1) * T],
                        NF, [0, 1, 2, 3], epi_h3, TW=T)

            def epi_y(p_, mi, c0, c1, ps, routed=routed):
                w = c1 - c0
                if routed:
                    p_.mm(ctx["bc1"][:, 0:w], ctx["ones1f"][0:1, :],
                          grow[0:1, c0:c1])
                    p_.vop("tensor_copy", ctx["scrfa"][:, 0:w], ps[:, 0:w])
                    p_.vop("tensor_mul", ctx["scrfa"][:, 0:w],
                           ctx["scrfa"][:, 0:w], ctx["bc1"][:, 0:w])
                    d = xsb[:, mi * T + c0: mi * T + c1]
                    p_.vop("tensor_add", d, d, ctx["scrfa"][:, 0:w])
                else:
                    d = xsb[:, mi * T + c0: mi * T + c1]
                    p_.vop("tensor_add", d, d, ps[:, 0:w])
            h_proj_view(p, ctx, wc, ei, lambda k: big[:, k * T:(k + 1) * T],
                        4, list(range(NF)), epi_y, TW=T)

        for fc in range(NF):
            p.dma(x8_out[fc, :, :], xsb[:, fc * T:(fc + 1) * T])

    nc = p.build(body)
    return nc


def h_proj2(p, ctx, wT_dram, read_k, kchunks, mlist, epilogue, TW):
    wt = ctx["wt"]
    ps = ctx["mmps"]
    rk = read_k if callable(read_k) else (lambda k: read_k)
    dbl = (2 * kchunks * 128 <= 4096)
    for mi, m in enumerate(mlist):
        so = (mi % 2) * kchunks * 128 if dbl else 0
        p.dma(wt[:, so: so + kchunks * 128].rearrange("f (k m) -> f k m", k=kchunks),
              wT_dram[0:kchunks * 128, m * 128:(m + 1) * 128].rearrange(
                  "(k f) m -> f k m", f=128), engine="gpsimd")
        for ti, (c0, c1) in enumerate(ntiles(TW)):
            psx = ps if (mi + ti) % 2 == 0 else ctx["mmps2"]
            w = c1 - c0
            for k in range(kchunks):
                p.mm(psx[:, 0:w], wt[:, so + k * 128: so + (k + 1) * 128],
                     rk(k)[:, c0:c1], start=(k == 0), stop=(k == kchunks - 1))
            epilogue(p, mi, c0, c1, psx)


def stage_launch8(inputs, x5_slices, sc_slices):
    scf = np.empty((B, SEQ), dtype=np.float32)
    for c in range(NCORES):
        s = np.asarray(sc_slices[c]).reshape(1, B, SLOC)
        for b in range(B):
            scf[b, SLOC * c:SLOC * (c + 1)] = s[0, b]

    def rs(v):
        return np.ascontiguousarray(np.asarray(v).reshape(NF, 128).T)
    maps = []
    for c in range(NCORES):
        sco = np.asarray(sc_slices[c]).reshape(T)
        sco_cols = np.ascontiguousarray(sco.reshape(16, 128).T).astype(np.float32)
        poso = np.concatenate([np.arange(SLOC * c, SLOC * (c + 1),
                                         dtype=np.float32)] * B)
        pos_cols = np.ascontiguousarray(poso.reshape(16, 128).T)
        maps.append({
            "x5": x5_slices[c], "scf": scf, "sco_cols": sco_cols,
            "pos_cols": pos_cols,
            "W1m": np.ascontiguousarray(np.asarray(inputs["L2_ffn_W1"])),
            "b1m": np.ascontiguousarray(np.asarray(inputs["L2_ffn_b1"]).reshape(32, 128).T),
            "W2mT": np.ascontiguousarray(np.asarray(inputs["L2_ffn_W2"])),
            "b2m": rs(inputs["L2_ffn_b2"]),
            "wqT3": np.ascontiguousarray(np.asarray(inputs["L3_attn_Wq"]).T),
            "wkT3": np.ascontiguousarray(np.asarray(inputs["L3_attn_Wk"]).T),
            "wvT3": np.ascontiguousarray(np.asarray(inputs["L3_attn_Wv"]).T),
            "woT3": np.ascontiguousarray(np.asarray(inputs["L3_attn_Wo"]).T),
            "bq3": rs(inputs["L3_attn_bq"]), "bk3": rs(inputs["L3_attn_bk"]),
            "bv3": rs(inputs["L3_attn_bv"]), "bo3": rs(inputs["L3_attn_bo"]),
            "W1aT": np.ascontiguousarray(np.asarray(inputs["L3_attn_W1"]).T),
            "b1a": rs(inputs["L3_attn_b1"]),
            "W2aT": np.ascontiguousarray(np.asarray(inputs["L3_attn_W2"]).T),
            "b2a_col": np.ascontiguousarray(np.asarray(inputs["L3_attn_b2"]).reshape(64, 1)),
            "sigT": np.ascontiguousarray(np.asarray(inputs["L3_attn_head_signatures"]).T),
            "rw3": np.ascontiguousarray(np.asarray(inputs["L3_ffn_router_w"])),
            "rb3": np.ascontiguousarray(np.asarray(inputs["L3_ffn_router_b"]).reshape(1, 8)),
            "rw1": np.ascontiguousarray(np.asarray(inputs["L3_ffn_routed_w1"])),
            "rw3e": np.ascontiguousarray(np.asarray(inputs["L3_ffn_routed_w3"])),
            "rw2": np.ascontiguousarray(np.asarray(inputs["L3_ffn_routed_w2"])),
            "sw1": np.ascontiguousarray(np.asarray(inputs["L3_ffn_shared_w1"])),
            "sw3": np.ascontiguousarray(np.asarray(inputs["L3_ffn_shared_w3"])),
            "sw2": np.ascontiguousarray(np.asarray(inputs["L3_ffn_shared_w2"])),
        })
    return maps


def fm(x):
    xr = np.asarray(x).reshape(B, NCORES, SLOC, NF, 128).transpose(3, 4, 0, 1, 2)
    return [np.ascontiguousarray(xr[:, :, :, c, :].reshape(NF, 128, T))
            for c in range(NCORES)]


def fm_inv(slices):
    out = np.empty((B, NCORES, SLOC, NF, 128), dtype=np.asarray(slices[0]).dtype)
    for c in range(NCORES):
        xr = np.asarray(slices[c]).reshape(NF, 128, B, SLOC)
        out[:, c] = xr.transpose(2, 3, 0, 1)
    return out.reshape(B, SEQ, NF * 128)


def swap_perm(n):
    q = np.arange(n)
    q[0::2] += 1
    q[1::2] -= 1
    return q


def rope_invfreq_col():
    hd = 32
    inv = 1.0 / (10000.0 ** (np.arange(0, hd, 2, dtype=np.float64) / hd))
    col = np.zeros(128, dtype=np.float32)
    for r in range(128):
        col[r] = inv[(r % 32) // 2]
    return col.reshape(128, 1)


def rope_sign_col():
    col = np.where(np.arange(128) % 2 == 0, -1.0, 1.0).astype(np.float32)
    return col.reshape(128, 1)


def stage_launch2(inputs, x0_slices, og_slices):
    wg1T = np.ascontiguousarray(inputs["L0_attn_W_g1"].T)
    wg2T = np.ascontiguousarray(inputs["L0_attn_W_g2"].T)
    lnw = np.ascontiguousarray(np.asarray(inputs["L0_attn_ln_w"]).reshape(NF, 128).T)
    woutT = np.ascontiguousarray(inputs["L0_attn_W_out"].T)
    g1 = np.ascontiguousarray(np.asarray(inputs["L0_ffn_g1"]).reshape(NF, 128).T)
    wdown = np.ascontiguousarray(inputs["L0_ffn_w_down"])
    sp = swap_perm(512)
    wq = np.ascontiguousarray(inputs["L0_ffn_wq"])
    wk = np.ascontiguousarray(inputs["L0_ffn_wk"])
    wv = np.ascontiguousarray(inputs["L0_ffn_wv"])
    wq_sw = np.ascontiguousarray(wq[:, sp])
    wk_sw = np.ascontiguousarray(wk[:, sp])
    ivf = rope_invfreq_col()
    sgn = rope_sign_col()
    maps = []
    for c in range(NCORES):
        pos = np.concatenate([np.arange(SLOC * c, SLOC * (c + 1), dtype=np.int32)] * B)
        maps.append({
            "x0": x0_slices[c], "og": og_slices[c],
            "wg1T": wg1T, "wg2T": wg2T, "lnw": lnw, "woutT": woutT,
            "g1": g1, "wdown": wdown, "wq": wq, "wq_sw": wq_sw,
            "wk": wk, "wk_sw": wk_sw, "wv": wv,
            "pos": pos.reshape(1, T), "invfreq": ivf, "signc": sgn,
        })
    return maps



RG8 = [[0, 1, 2, 3, 4, 5, 6, 7]]


def build_fused(strict=False):
    p = P("fused", strict=strict)
    C = 128
    NCH = SEQ // C
    QT = 512
    KC = 128
    KHI = [(SEQ - SLOC + (qt + 1) * QT) // KC for qt in range(2)]
    HL = 1280
    TWQ = 512
    TWH = 1024

    xb = p.inp("xb", [NF, 128, SEQ])
    x0 = p.inp("x0", [NF, 128, T])
    w_inT = p.inp("w_inT", [1024, 768])
    mask_in = p.inp("mask_intra", [64, 128], U8)
    wg1T = p.inp("wg1T", [1024, 128])
    wg2T = p.inp("wg2T", [128, 1024])
    lnw = p.inp("lnw", [128, NF])
    woutT = p.inp("woutT", [1024, 1024])
    g1in = p.inp("g1", [128, NF])
    wdown = p.inp("wdown", [1024, 512])
    wq = p.inp("wq", [512, 512])
    wq_sw = p.inp("wq_sw", [512, 512])
    wk = p.inp("wk", [512, 512])
    wk_sw = p.inp("wk_sw", [512, 512])
    wv = p.inp("wv", [512, 512])
    pos_i = p.inp("pos", [1, T], I32)
    invfreq = p.inp("invfreq", [128, 1])
    signc = p.inp("signc", [128, 1])
    wupT = p.inp("wupT", [512, 1024])
    masks = p.inp("masks", [2, KHI[1], 128, QT], U8)
    g2in = p.inp("g2", [128, NF])
    router_w = p.inp("router_w", [1024, 8])
    w1_in = p.inp("w1", [8, 1024, 2048])
    w3_in = p.inp("w3", [8, 1024, 2048])
    w2_in = p.inp("w2", [8, 2048, 1024])
    wbT = p.inp("wbT", [1024, 64])
    wvT = p.inp("wvT", [1024, 1024])
    bv_in = p.inp("bv", [128, NF])
    woT = p.inp("woT", [1024, 1024])
    bo_in = p.inp("bo", [128, NF])
    router_w6 = p.inp("router_w6", [1024, 4])
    router_b6 = p.inp("router_b6", [1, 4])
    w1_6 = p.inp("w1_6", [4, 1024, 4096])
    w3_6 = p.inp("w3_6", [4, 1024, 4096])
    w2_6 = p.inp("w2_6", [4, 4096, 1024])
    wqT6 = p.inp("wqT6", [1024, 1024])
    wkT6 = p.inp("wkT6", [1024, 1024])
    wvT6 = p.inp("wvT6", [1024, 1024])
    bq6 = p.inp("bq6", [128, NF])
    bk6 = p.inp("bk6", [128, NF])
    bv6 = p.inp("bv6", [128, NF])
    woT7 = p.inp("woT7", [1024, 1024])
    bo7 = p.inp("bo7", [128, NF])
    masks2 = p.inp("masks2", [B, 8, 128, 384], U8)
    wr_in = p.inp("wr", [128, NF])
    pos_cols = p.inp("pos_cols", [128, 16])
    W1m = p.inp("W1m", [1024, 4096])
    b1m = p.inp("b1m", [128, 32])
    W2mT = p.inp("W2mT", [4096, 1024])
    b2m = p.inp("b2m", [128, NF])
    wqT3 = p.inp("wqT3", [1024, 1024])
    wkT3 = p.inp("wkT3", [1024, 1024])
    wvT3 = p.inp("wvT3", [1024, 1024])
    woT3 = p.inp("woT3", [1024, 1024])
    bq3 = p.inp("bq3", [128, NF])
    bk3 = p.inp("bk3", [128, NF])
    bv3 = p.inp("bv3", [128, NF])
    bo3 = p.inp("bo3", [128, NF])
    W1aT = p.inp("W1aT", [1024, 1024])
    b1a = p.inp("b1a", [128, NF])
    W2aT = p.inp("W2aT", [1024, 64])
    b2a_col = p.inp("b2a_col", [64, 1])
    sigT = p.inp("sigT", [64, 16])
    rw3 = p.inp("rw3", [1024, 8])
    rb3 = p.inp("rb3", [1, 8])
    rw1 = p.inp("rw1", [8, 1024, 512])
    rw3e = p.inp("rw3e", [8, 1024, 512])
    rw2 = p.inp("rw2", [8, 512, 1024])
    sw1 = p.inp("sw1", [2, 1024, 512])
    sw3 = p.inp("sw3", [2, 1024, 512])
    sw2 = p.inp("sw2", [2, 512, 1024])

    x8_out = p.out("x8", [NF, 128, T])

    cc1_in = p.nc.dram_tensor("cc1_in", [8 * 256, 1024], F32).ap()
    cc1_out = p.nc.dram_tensor("cc1_out", [8 * 256, 1024], F32).ap()
    cc2k_in = p.nc.dram_tensor("cc2k_in", [1024, 1024], BF16).ap()
    cc2k_out = p.nc.dram_tensor("cc2k_out", [8 * 1024, 1024], BF16).ap()
    cc2v_in = p.nc.dram_tensor("cc2v_in", [2048, 512], BF16).ap()
    cc2v_out = p.nc.dram_tensor("cc2v_out", [8 * 2048, 512], BF16).ap()
    cc3_in = p.nc.dram_tensor("cc3_in", [2176, 1024], BF16).ap()
    cc3_out = p.nc.dram_tensor("cc3_out", [8 * 2176, 1024], BF16).ap()
    cc4k_in = p.nc.dram_tensor("cc4k_in", [2048, 256], BF16).ap()
    cc4k_out = p.nc.dram_tensor("cc4k_out", [8 * 2048, 256], BF16).ap()
    cc4v_in = p.nc.dram_tensor("cc4v_in", [512, 1024], BF16).ap()
    cc4v_out = p.nc.dram_tensor("cc4v_out", [8 * 512, 1024], BF16).ap()
    cc5_in = p.nc.dram_tensor("cc5_in", [2, 1024], F32).ap()
    cc5_out = p.nc.dram_tensor("cc5_out", [16, 1024], F32).ap()
    qs0 = p.nc.dram_tensor("qs0", [4, 128, T], BF16).ap()
    xs1 = p.nc.dram_tensor("xs1", [NF, 128, T], F32).ap()
    qs2 = p.nc.dram_tensor("qs2", [NF, 128, T], BF16).ap()
    ks2 = p.nc.dram_tensor("ks2", [NF, 128, T], BF16).ap()
    vs2 = p.nc.dram_tensor("vs2", [NF, 128, T], BF16).ap()

    def body(p):
        ctx = make_ctx(p, T)
        xsb = p.sb("g_xsb", [128, NF * T], F32)
        lat16 = p.sb("g_lat16", [64, T], BF16)
        sco_cols = p.sb("g_sco_cols", [128, 16], F32)
        identf = p.sb("g_identf", [128, 128], F32)
        identb = p.sb("g_identb", [128, 128], BF16)
        idff = identf
        pidc = p.sb("g_pidc", [128, 1], F32)
        vt_sb = p.sb("g_vt", [128, 1024], BF16)

        idi = p.sb("g_idi", [128, 128], I32)
        idp = p.sb("g_idp", [128, 1], I32)
        idpf = p.sb("g_idpf", [128, 1], F32)
        pid_i = p.sb("g_pidi", [1, 1], I32)
        pid_f = p.sb("g_pidf", [1, 1], F32)
        p.gop("iota", idi[:, :], [[1, 128]], base=0, channel_multiplier=0)
        p.gop("iota", idp[:, :], [[0, 1]], base=0, channel_multiplier=1)
        p.vop("tensor_copy", identf[:, :], idi[:, :])
        p.vop("tensor_copy", idpf[:, :], idp[:, :])
        p.vop("tensor_scalar", identf[:, :], identf[:, :], idpf[:, 0:1], None,
              ALU.is_equal)
        p.vop("tensor_copy", identb[:, :], identf[:, :])

        def emit_pid(eng):
            pid = p.nc.partition_id(engines=[mybir.EngineType.Pool])
            return eng.store(pid_i[0:1, 0:1], pid)
        p.ops.append(("gpsimd", emit_pid, dict(p.cnts), "G", 1))
        p.cnts["G"] += 1
        if not p.strict:
            p._note(pid_i[0:1, 0:1], True, "G", p.cnts["G"], {})
        p.vop("tensor_copy", pid_f[:, :], pid_i[:, :])
        p.mm(ctx["bc1"][:, 0:1], ctx["ones1f"][0:1, :], pid_f[0:1, 0:1])
        p.vop("tensor_copy", pidc[:, 0:1], ctx["bc1"][:, 0:1])

        p.push_scope()
        xc = p.sb("s1_xc", [128, NF * C], F32)
        xln = p.sb("s1_xln", [128, NF * C], BF16)
        wsb = p.sb("s1_wsb", [128, NF * 768], BF16)
        S0 = p.sb("s1_S0", [128, 128], F32)
        S1 = p.sb("s1_S1", [128, 128], F32)
        S016 = p.sb("s1_S016", [128, 128], BF16)
        S116 = p.sb("s1_S116", [128, 128], BF16)
        qch = p.sb("s1_qch", [128, C], BF16)
        kch = p.sb("s1_kch", [128, C], F32)
        vch = p.sb("s1_vch", [128, C], BF16)
        gch = p.sb("s1_gch", [128, C], F32)
        Ach = p.sb("s1_Ach", [128, C], F32)
        q1b = p.sb("s1_q1b", [128, C], BF16)
        k1b = p.sb("s1_k1b", [128, C], BF16)
        qib = p.sb("s1_qib", [128, C], BF16)
        k2b = p.sb("s1_k2b", [128, C], BF16)
        vTa = p.sb("s1_vTa", [64, 128], BF16)
        vTb = p.sb("s1_vTb", [64, 128], BF16)
        kTa = p.sb("s1_kTa", [64, 128], BF16)
        kTb = p.sb("s1_kTb", [64, 128], BF16)
        P16 = p.sb("s1_P16", [64, 128], BF16)
        Ecol = p.sb("s1_Ecol", [128, 4], F32)
        zcol = p.sb("s1_zcol", [128, 1], F32)
        osb = p.sb("s1_osb", [128, C], F32)
        msk = p.sb("s1_msk", [64, 128], U8)
        tps = p.ps("s1_tps", [64, 128], BF16)

        p.dma(msk[:, :], mask_in[:, :])
        p.vop("memset", S0[:], 0.0)
        p.vop("memset", S1[:], 0.0)
        p.vop("memset", zcol[:], 0.0)
        for k in range(NF):
            p.dma(wsb[:, k * 768:(k + 1) * 768],
                  w_inT[k * 128:(k + 1) * 128, :], engine="gpsimd")

        for ch in range(NCH):
            s0_ = ch * C
            for fc in range(NF):
                p.dma(xc[:, fc * C:(fc + 1) * C], xb[fc, :, s0_:s0_ + C])
            h_ln_to(p, ctx, xc, xln, NF, TW=C)

            for unit in range(2):
                S = S0 if unit == 0 else S1
                S16 = S016 if unit == 0 else S116
                for mg in range(3):
                    for k in range(NF):
                        p.mm(ctx["mmps"][:, 0:C],
                             wsb[:, k * 768 + unit * 384 + mg * 128:
                                 k * 768 + unit * 384 + (mg + 1) * 128],
                             xln[:, k * C:(k + 1) * C],
                             start=(k == 0), stop=(k == NF - 1))
                    if mg == 0:
                        p.vop("tensor_copy", vch[:, :], ctx["mmps"][:, 0:C])
                    elif mg == 1:
                        p.sop("activation", qch[:, :], ctx["mmps"][:, 0:C], AF.Silu)
                    else:
                        p.sop("activation", kch[:, :], ctx["mmps"][:, 0:C],
                              AF.Sigmoid, scale=-1.0)
                        p.vop("tensor_scalar", gch[:, :], kch[:, :], -1.0, 1.0,
                              ALU.mult, ALU.add)
                        p.sop("activation", gch[:, :], gch[:, :], AF.Ln)
                p.vop("tensor_tensor_scan", Ach[:, :], gch[:, :], gch[:, :],
                      zcol[:, 0:1], ALU.add, ALU.bypass)
                for j in range(4):
                    lo, hi = 32 * j, 32 * (j + 1)
                    amid = Ach[:, lo + 15:lo + 16]
                    aend = Ach[:, hi - 1:hi]
                    abase = Ach[:, lo - 1:lo] if j > 0 else zcol[:, 0:1]
                    namid = Ecol[:, 0:1]
                    nabase = Ecol[:, 1:2]
                    p.vop("tensor_scalar", namid, amid, -1.0, None, ALU.mult)
                    p.vop("tensor_scalar", nabase, abase, -1.0, None, ALU.mult)
                    p.sop("activation", ctx["scrfa"][:, 0:32], Ach[:, lo:hi],
                          AF.Exp, bias=namid)
                    p.vop("tensor_mul", q1b[:, lo:hi], qch[:, lo:hi],
                          ctx["scrfa"][:, 0:32])
                    p.sop("activation", ctx["scrfa"][:, 0:32], Ach[:, lo:hi],
                          AF.Exp, bias=amid, scale=-1.0)
                    p.vop("tensor_mul", k1b[:, lo:hi], kch[:, lo:hi],
                          ctx["scrfa"][:, 0:32])
                    p.sop("activation", ctx["scrfa"][:, 0:32], Ach[:, lo:hi],
                          AF.Exp, bias=nabase)
                    p.vop("tensor_mul", qib[:, lo:hi], qch[:, lo:hi],
                          ctx["scrfa"][:, 0:32])
                    p.sop("activation", ctx["scrfa"][:, 0:32], Ach[:, lo:hi],
                          AF.Exp, bias=aend, scale=-1.0)
                    p.vop("tensor_mul", k2b[:, lo:hi], kch[:, lo:hi],
                          ctx["scrfa"][:, 0:32])
                for src_, dsts in ((k2b, (kTa, kTb)), (vch, (vTa, vTb))):
                    for hfi, dst in enumerate(dsts):
                        p.transpose(tps[:, :], src_[:, hfi * 64:(hfi + 1) * 64],
                                    identb[:, :])
                        p.vop("tensor_copy", dst[:, :], tps[0:64, :])
                obank = ctx["bc1"]
                pbank = ctx["bc2"]
                sbank = ctx["mmps"]
                for j in range(4):
                    lo, hi = 32 * j, 32 * (j + 1)
                    aend = Ach[:, hi - 1:hi]
                    abase = Ach[:, lo - 1:lo] if j > 0 else zcol[:, 0:1]
                    pb = (j % 2) * 32
                    p.mm(pbank[pb:pb + 32, 0:32], k1b[:, lo:hi], q1b[:, lo:hi],
                         tile_position=(0, pb))
                    p.vop("memset", P16[pb:pb + 32, lo:hi], 0.0)
                    p.vop("copy_predicated", P16[pb:pb + 32, lo:hi],
                          msk[pb:pb + 32, lo:hi], pbank[pb:pb + 32, 0:32])
                    p.vop("tensor_copy", S16[:, :], S[:, :])
                    p.mm(obank[:, lo:hi], S16[:, :], qib[:, lo:hi],
                         start=True, stop=False)
                    vT = vTa if j < 2 else vTb
                    vbase = (j % 2) * 32
                    p.mm(obank[:, lo:hi], vT[vbase:vbase + 32, :],
                         P16[pb:pb + 32, lo:hi], start=False, stop=True)
                    kT = kTa if j < 2 else kTb
                    p.mm(sbank[:, 0:128], kT[vbase:vbase + 32, :],
                         vT[vbase:vbase + 32, :])
                    p.vop("tensor_scalar", Ecol[:, 1:2], abase, -1.0, None, ALU.mult)
                    p.sop("activation", Ecol[:, 2:3], aend, AF.Exp, bias=Ecol[:, 1:2])
                    p.vop("scalar_tensor_tensor", S[:, :], S[:, :], Ecol[:, 2:3],
                          sbank[:, 0:128], ALU.mult, ALU.add)
                p.vop("tensor_copy", osb[:, :], obank[:, 0:C])
                p.dma(cc1_in[(ch // 8) * 256 + unit * 128:
                             (ch // 8) * 256 + (unit + 1) * 128,
                             (ch % 8) * 128:(ch % 8 + 1) * 128], osb[:, :])
        p.pop_scope()
        p.collective("AllToAll", ALU.bypass, RG8, [cc1_in.opt()], [cc1_out.opt()])

        p.push_scope()
        TW = 1024
        a16 = p.sb("s2_a16", [128, NF * TW], BF16)
        b16 = p.sb("s2_b16", [128, NF * TW], BF16)
        c16 = p.sb("s2_c16", [128, 4 * TW], BF16)
        d16 = p.sb("s2_d16", [128, 4 * TW], BF16)
        t1 = p.sb("s2_t1", [128, TW], BF16)
        gam = p.sb("s2_gam", [128, NF], F32)
        cosT = p.sb("s2_cosT", [128, TW], F32)
        sinTs = p.sb("s2_sinTs", [128, TW], F32)
        ivf = p.sb("s2_ivf", [128, 1], F32)
        sgn = p.sb("s2_sgn", [128, 1], F32)
        posi_sb = p.sb("s2_posi", [1, TW], I32)
        angi = p.sb("s2_angi", [128, 512], I32)
        tpsb = p.ps("s2_tps", [128, 128], BF16)

        p.dma(ivf[:, :], invfreq[:, :])
        p.dma(sgn[:, :], signc[:, :])

        for half in range(2):
            t0 = half * TW
            bsel = half
            for fc in range(NF):
                p.dma(xsb[:, fc * T + t0: fc * T + t0 + TW], x0[fc, :, t0:t0 + TW])
                srcb = (bsel * 4 + fc // 2) * 256 + (fc % 2) * 128
                p.dma(b16[:, fc * TW:(fc + 1) * TW],
                      cc1_out[srcb:srcb + 128, :], engine="gpsimd")

            h_ln_to(p, ctx, xsb, a16, NF, TW=TW, src_stride=T, src_off=t0)

            h_proj(p, ctx, wg1T, a16, NF, [0], h_epi_copy(t1, TW), TW=TW)

            def epi_gate(p_, mi, c0, c1, ps):
                p_.sop("activation", ctx["scr16b"][:, 0:c1 - c0], ps[:, 0:c1 - c0],
                       AF.Sigmoid)
                dst = b16[:, mi * TW + c0: mi * TW + c1]
                p_.vop("tensor_mul", dst, dst, ctx["scr16b"][:, 0:c1 - c0])
            h_proj(p, ctx, wg2T, t1, 1, list(range(NF)), epi_gate, TW=TW)

            p.dma(gam[:, :], lnw[:, :])
            h_ln_to(p, ctx, b16, b16, NF, gamma=gam, TW=TW)

            def epi_wout(p_, mi, c0, c1, ps):
                d = xsb[:, mi * T + t0 + c0: mi * T + t0 + c1]
                p_.vop("tensor_add", d, d, ps[:, 0:c1 - c0])
            h_proj(p, ctx, woutT, b16, NF, list(range(NF)), epi_wout, TW=TW)
            for fc in range(NF):
                p.dma(xs1[fc, :, t0:t0 + TW], xsb[:, fc * T + t0: fc * T + t0 + TW])

            h_ln_to(p, ctx, xsb, xsb, NF, TW=TW, src_stride=T, src_off=t0,
                    dst_stride=T, dst_off=t0)
            p.dma(gam[:, :], g1in[:, :])
            h_ln_to(p, ctx, xsb, a16, NF, gamma=gam, rms=True, TW=TW,
                    src_stride=T, src_off=t0)

            h_proj(p, ctx, wdown, a16, NF, [0, 1, 2, 3], h_epi_copy(c16, TW), TW=TW)

            p.dma(posi_sb[0:1, :], pos_i[0:1, t0:t0 + TW])
            p.vop("tensor_copy", ctx["rows"][0][0:1, 0:TW], posi_sb[0:1, :])
            for c0, c1 in ntiles(TW):
                w = c1 - c0
                p.mm(ctx["bc1"][:, 0:w], ctx["ones1f"][0:1, :],
                     ctx["rows"][0][0:1, c0:c1])
                p.vop("tensor_scalar", ctx["scrfa"][:, 0:w], ctx["bc1"][:, 0:w],
                      ivf[:, 0:1], None, ALU.mult)
                for dst, phase in ((sinTs, 0.0), (cosT, HALF_PI)):
                    if phase != 0.0:
                        p.vop("tensor_scalar", ctx["scrfb"][:, 0:w],
                              ctx["scrfa"][:, 0:w], phase, None, ALU.add)
                    else:
                        p.vop("tensor_copy", ctx["scrfb"][:, 0:w],
                              ctx["scrfa"][:, 0:w])
                    p.vop("tensor_scalar", ctx["scrfb"][:, 0:w],
                          ctx["scrfb"][:, 0:w], 1.0 / TWO_PI, None, ALU.mult)
                    p.vop("tensor_copy", angi[:, 0:w], ctx["scrfb"][:, 0:w])
                    p.vop("tensor_copy", dst[:, c0:c1], angi[:, 0:w])
                    p.vop("scalar_tensor_tensor", ctx["scrfb"][:, 0:w],
                          dst[:, c0:c1], -TWO_PI, ctx["scrfa"][:, 0:w],
                          ALU.mult, ALU.add)
                    if phase != 0.0:
                        p.vop("tensor_scalar", ctx["scrfb"][:, 0:w],
                              ctx["scrfb"][:, 0:w], phase, None, ALU.add)
                    p.vop("tensor_scalar", dst[:, c0:c1], ctx["scrfb"][:, 0:w],
                          np.pi, None, ALU.is_gt)
                    p.vop("scalar_tensor_tensor", ctx["scrfb"][:, 0:w],
                          dst[:, c0:c1], -TWO_PI, ctx["scrfb"][:, 0:w],
                          ALU.mult, ALU.add)
                    p.vop("tensor_scalar", dst[:, c0:c1], ctx["scrfb"][:, 0:w],
                          -np.pi, None, ALU.is_lt)
                    p.vop("scalar_tensor_tensor", ctx["scrfb"][:, 0:w],
                          dst[:, c0:c1], TWO_PI, ctx["scrfb"][:, 0:w],
                          ALU.mult, ALU.add)
                    p.sop("activation", dst[:, c0:c1], ctx["scrfb"][:, 0:w], AF.Sin)
            p.vop("tensor_scalar", sinTs[:, :], sinTs[:, :], sgn[:, 0:1], None,
                  ALU.mult)

            h_proj(p, ctx, wq, c16, 4, [0, 1, 2, 3], h_epi_copy(b16, TW), TW=TW)

            def epi_k(p_, mi, c0, c1, ps):
                d = b16[:, (4 + mi) * TW + c0: (4 + mi) * TW + c1]
                p_.vop("tensor_copy", d, ps[:, 0:c1 - c0])
            h_proj(p, ctx, wk, c16, 4, [0, 1, 2, 3], epi_k, TW=TW)
            h_proj(p, ctx, wv, c16, 4, [0, 1, 2, 3], h_epi_copy(d16, TW), TW=TW)

            def mk_rope_epi(base):
                def epi(p_, mi, c0, c1, ps):
                    dst = b16[:, (base + mi) * TW + c0: (base + mi) * TW + c1]
                    p_.vop("tensor_mul", ctx["scrfb"][:, 0:c1 - c0],
                           ps[:, 0:c1 - c0], sinTs[:, c0:c1])
                    p_.vop("tensor_mul", ctx["scrfa"][:, 0:c1 - c0], dst,
                           cosT[:, c0:c1])
                    p_.vop("tensor_add", dst, ctx["scrfa"][:, 0:c1 - c0],
                           ctx["scrfb"][:, 0:c1 - c0])
                return epi
            h_proj(p, ctx, wq_sw, c16, 4, [0, 1, 2, 3], mk_rope_epi(0), TW=TW)
            h_proj(p, ctx, wk_sw, c16, 4, [0, 1, 2, 3], mk_rope_epi(4), TW=TW)

            for m in range(4):
                p.dma(qs0[m, :, t0:t0 + TW], b16[:, m * TW:(m + 1) * TW])
                p.dma(cc2k_in[bsel * 512 + m * 128: bsel * 512 + (m + 1) * 128, :],
                      b16[:, (4 + m) * TW:(5 + m) * TW])
            for tt in range(8):
                for mc in range(4):
                    p.transpose(tpsb[:, :], d16[:, mc * TW + tt * 128:
                                                mc * TW + (tt + 1) * 128],
                                identb[:, :])
                    p.vop("tensor_copy", vt_sb[:, mc * 128:(mc + 1) * 128],
                          tpsb[:, :])
                p.dma(cc2v_in[bsel * 1024 + tt * 128: bsel * 1024 + (tt + 1) * 128, :],
                      vt_sb[:, 0:512])
        p.pop_scope()
        p.collective("AllGather", ALU.bypass, RG8, [cc2k_in.opt()], [cc2k_out.opt()])
        p.collective("AllGather", ALU.bypass, RG8, [cc2v_in.opt()], [cc2v_out.opt()])

        p.push_scope()
        qown = p.sb("s3_qown", [128, 4 * T], BF16)
        Kg = p.sb("s3_Kg", [128, SEQ], BF16)
        VTg = p.sb("s3_VTg", [128, 64 * 128], BF16)
        aos = p.sb("s3_aos", [32, 16 * SLOC], BF16)
        P16a = p.sb("s3_P16", [128, QT], BF16)
        P16b = p.sb("s3_P16b", [128, QT], BF16)
        mskb = p.sb("s3_mskb", [128, QT], BF16)
        mskb2 = p.sb("s3_mskb2", [128, QT], BF16)
        negc = p.sb("s3_negc", [128, 1], F32)
        cmaxq = p.sb("s3_cmaxq", [128, 1], F32)
        cmaxk = p.sb("s3_cmaxk", [128, 1], F32)
        crow = p.sb("s3_crow", [1, 4], F32)
        wup_sb = p.sb("s3_wup", [32, 16 * 128], BF16)
        ops_ = p.ps("s3_ops", [32, QT], F32)
        ops2_ = p.ps("s3_ops2", [32, QT], F32)

        h_ln_to(p, ctx, xsb, xsb, NF)
        for m in range(4):
            p.dma(qown[:, m * T:(m + 1) * T], qs0[m, :, :])

        def colmax_abs(dst11, col):
            p.transpose(ctx["rps"][0:1, 0:128], col, identf[:, :])
            p.vop("reduce_max", dst11, ctx["rps"][0:1, 0:128], AX.X)

        for b in range(B):
            tb0 = b * SLOC
            for g in range(4):
                for r in range(8):
                    p.dma(Kg[:, r * 1024:(r + 1) * 1024],
                          cc2k_out[r * 1024 + b * 512 + g * 128:
                                   r * 1024 + b * 512 + (g + 1) * 128, :])
                    p.dma(VTg[:, r * 1024:(r + 1) * 1024].rearrange(
                              "p (kc h) -> p kc h", kc=8),
                          cc2v_out[r * 2048 + b * 1024: r * 2048 + (b + 1) * 1024,
                                   128 * g:128 * (g + 1)].rearrange(
                              "(kc q) h -> q kc h", q=128))
                p.vop("reduce_max", cmaxq[:, 0:1],
                      qown[:, g * T + tb0: g * T + tb0 + SLOC], AX.X,
                      apply_absolute_value=True)
                p.vop("reduce_max", cmaxk[:, 0:1], Kg[:, :], AX.X,
                      apply_absolute_value=True)
                colmax_abs(crow[0:1, 0:1], cmaxq[:, 0:1])
                colmax_abs(crow[0:1, 1:2], cmaxk[:, 0:1])
                p.vop("tensor_mul", crow[0:1, 2:3], crow[0:1, 0:1], crow[0:1, 1:2])
                p.vop("tensor_scalar", crow[0:1, 2:3], crow[0:1, 2:3],
                      -float(np.sqrt(32.0)), None, ALU.mult)
                p.mm(ctx["bc1"][:, 0:1], ctx["ones1f"][0:1, :], crow[0:1, 2:3])
                p.vop("tensor_copy", negc[:, 0:1], ctx["bc1"][:, 0:1])

                for qt in range(2):
                    ql = tb0 + qt * QT
                    khi = KHI[qt]
                    for hp in (0, 2):
                        dens = [ctx["rps"], ctx["rps2"]]
                        obk = [ops_, ops2_]
                        for kc in range(khi):
                            last = (kc == khi - 1)
                            mk = mskb if kc % 2 == 0 else mskb2
                            p.dma(mk[:, 0:QT], masks[qt, kc, :, :],
                                  engine="gpsimd")
                            for hi_, h in enumerate((hp, hp + 1)):
                                hb = 32 * h
                                sbank = ctx["bc2"] if hi_ == 0 else ctx["mmps2"]
                                Pu = P16a if hi_ == 0 else P16b
                                p.mm(sbank[:, 0:QT],
                                     Kg[hb:hb + 32, kc * KC:(kc + 1) * KC],
                                     qown[hb:hb + 32, g * T + ql: g * T + ql + QT],
                                     tile_position=(hb, 0))
                                p.sop("activation", Pu[:, 0:QT],
                                      sbank[:, 0:QT], AF.Exp,
                                      bias=negc[:, 0:1],
                                      scale=1.0 / float(np.sqrt(32.0)))
                                p.vop("tensor_mul", Pu[:, 0:QT], Pu[:, 0:QT],
                                      mk[:, 0:QT])
                                p.mm(dens[hi_][0:1, 0:QT], ctx["onesP"][:, 0:1],
                                     Pu[:, 0:QT], start=(kc == 0), stop=last)
                                p.mm(obk[hi_][0:32, 0:QT],
                                     VTg[:, kc * 128 + hb: kc * 128 + hb + 32],
                                     Pu[:, 0:QT], start=(kc == 0), stop=last)
                        for hi_, h in enumerate((hp, hp + 1)):
                            bcb = ctx["mmps"] if hi_ == 0 else ctx["mmps2"]
                            scf = ctx["scrfa"] if hi_ == 0 else ctx["scrfb"]
                            p.vop("reciprocal", ctx["rows"][hi_][0:1, 0:QT],
                                  dens[hi_][0:1, 0:QT])
                            p.mm(bcb[0:32, 0:QT], ctx["ones1f"][0:1, 0:32],
                                 ctx["rows"][hi_][0:1, 0:QT])
                            p.vop("tensor_copy", scf[0:32, 0:QT],
                                  bcb[0:32, 0:QT])
                            p.vop("tensor_mul",
                                  aos[:, (g * 4 + h) * SLOC + qt * QT:
                                      (g * 4 + h) * SLOC + (qt + 1) * QT],
                                  scf[0:32, 0:QT], obk[hi_][0:32, 0:QT])

            for mi in range(NF):
                wof = 0
                for s in range(16):
                    p.dma(wup_sb[0:32, wof + s * 128: wof + (s + 1) * 128],
                          wupT[32 * s:32 * (s + 1), mi * 128:(mi + 1) * 128],
                          engine="gpsimd")
                for ti, (c0, c1) in enumerate(ntiles(SLOC)):
                    psx = ctx["mmps"] if (mi + ti) % 2 == 0 else ctx["mmps2"]
                    w = c1 - c0
                    for s in range(16):
                        p.mm(psx[:, 0:w],
                             wup_sb[0:32, wof + s * 128: wof + (s + 1) * 128],
                             aos[:, s * SLOC + c0: s * SLOC + c1],
                             start=(s == 0), stop=(s == 15))
                    d = xsb[:, mi * T + tb0 + c0: mi * T + tb0 + c1]
                    p.vop("tensor_add", d, d, psx[:, 0:w])
        p.pop_scope()

        p.push_scope()
        t16 = p.sb("s4_t16", [128, NF * T], BF16)
        gam4 = p.sb("s4_gam", [128, NF], F32)
        gtok = p.sb("s4_gtok", [128, 16 * 8], F32)
        growE = p.sb("s4_growE", [1, TWH], F32)
        rwsb = p.sb("s4_rwsb", [128, NF * 8], BF16)
        ninf = p.sb("s4_ninf", [128, 8], F32)
        l2nd = p.sb("s4_l2nd", [128, 8], F32)
        lg8 = p.sb("s4_lg8", [128, 8], F32)
        m1c = p.sb("s4_m1c", [128, 1], F32)
        m2c = p.sb("s4_m2c", [128, 1], F32)
        m3c = p.sb("s4_m3c", [128, 1], F32)
        msk1 = p.sb("s4_msk1", [128, 8], U8)
        msk2 = p.sb("s4_msk2", [128, 8], U8)
        g8 = p.sb("s4_g8", [128, 8], F32)
        bvc = p.sb("s4_bvc", [128, NF], F32)

        p.dma(gam4[:, :], g2in[:, :])
        p.dma(bvc[:, :], bv_in[:, :])
        for fc in range(NF):
            p.dma(rwsb[:, fc * 8:(fc + 1) * 8],
                  router_w[fc * 128:(fc + 1) * 128, :], engine="gpsimd")
        h_ln_to(p, ctx, xsb, t16, NF, gamma=gam4, rms=True)

        for tt in range(16):
            for fc in range(NF):
                p.mm(ctx["mmps"][0:128, 0:8],
                     t16[:, fc * T + tt * 128: fc * T + (tt + 1) * 128],
                     rwsb[:, fc * 8:(fc + 1) * 8],
                     start=(fc == 0), stop=(fc == NF - 1))
            p.vop("tensor_copy", lg8[:, :], ctx["mmps"][0:128, 0:8])
            p.vop("reduce_max", m1c[:, 0:1], lg8[:, :], AX.X)
            p.vop("tensor_scalar", msk1[:, :], lg8[:, :], m1c[:, 0:1], None,
                  ALU.is_ge)
            p.vop("tensor_copy", l2nd[:, :], msk1[:, :])
            p.vop("scalar_tensor_tensor", l2nd[:, :], l2nd[:, :], -1e30,
                  lg8[:, :], ALU.mult, ALU.add)
            p.vop("reduce_max", m2c[:, 0:1], l2nd[:, :], AX.X)
            p.vop("tensor_scalar", msk2[:, :], lg8[:, :], m2c[:, 0:1], None,
                  ALU.is_ge)
            p.vop("tensor_scalar", m3c[:, 0:1], m2c[:, 0:1], -1.0, None, ALU.mult)
            p.sop("activation", m3c[:, 0:1], m1c[:, 0:1], AF.Sigmoid,
                  bias=m3c[:, 0:1])
            p.vop("tensor_copy", g8[:, :], msk1[:, :])
            p.vop("tensor_scalar", g8[:, :], g8[:, :], m3c[:, 0:1], None, ALU.mult)
            p.vop("tensor_copy", ninf[:, :], msk2[:, :])
            p.vop("tensor_copy", lg8[:, :], msk1[:, :])
            p.vop("tensor_mul", lg8[:, :], lg8[:, :], ninf[:, :])
            p.vop("tensor_sub", ninf[:, :], ninf[:, :], lg8[:, :])
            p.vop("tensor_scalar", m1c[:, 0:1], m3c[:, 0:1], -1.0, 1.0,
                  ALU.mult, ALU.add)
            p.vop("tensor_scalar", ninf[:, :], ninf[:, :], m1c[:, 0:1], None,
                  ALU.mult)
            p.vop("tensor_add", gtok[:, tt * 8:(tt + 1) * 8], g8[:, :], ninf[:, :])

        p.push_scope()
        xg4 = p.sb("s4_xg", [128, NF * TWH], BF16)
        hid4 = p.sb("s4_hid", [128, 16 * TWH], BF16)
        for half in range(2):
            t0 = half * TWH
            for e in range(8):
                for tt in range(8):
                    gtt = half * 8 + tt
                    p.mm(ctx["rps"][0:1, 0:128],
                         gtok[:, gtt * 8 + e: gtt * 8 + e + 1], idff[:, :])
                    p.vop("tensor_copy", growE[0:1, tt * 128:(tt + 1) * 128],
                          ctx["rps"][0:1, 0:128])
                for c0, c1 in ntiles(TWH):
                    w = c1 - c0
                    p.mm(ctx["bc1"][:, 0:w], ctx["ones1f"][0:1, :],
                         growE[0:1, c0:c1])
                    for fc in range(NF):
                        p.vop("tensor_mul", xg4[:, fc * TWH + c0: fc * TWH + c1],
                              t16[:, fc * T + t0 + c0: fc * T + t0 + c1],
                              ctx["bc1"][:, 0:w])
                def epi_h1(p_, mi, c0, c1, ps):
                    p_.sop("activation", hid4[:, mi * TWH + c0: mi * TWH + c1],
                           ps[:, 0:c1 - c0], AF.Silu)
                h_proj_view(p, ctx, w1_in, e,
                            lambda k: t16[:, k * T + t0: k * T + t0 + TWH],
                            NF, list(range(16)), epi_h1, TW=TWH)
                def epi_h3(p_, mi, c0, c1, ps):
                    d = hid4[:, mi * TWH + c0: mi * TWH + c1]
                    p_.vop("tensor_mul", d, d, ps[:, 0:c1 - c0])
                h_proj_view(p, ctx, w3_in, e,
                            lambda k: xg4[:, k * TWH:(k + 1) * TWH],
                            NF, list(range(16)), epi_h3, TW=TWH)
                def epi_y(p_, mi, c0, c1, ps):
                    d = xsb[:, mi * T + t0 + c0: mi * T + t0 + c1]
                    p_.vop("tensor_add", d, d, ps[:, 0:c1 - c0])
                h_proj_view(p, ctx, w2_in, e,
                            lambda k: hid4[:, k * TWH:(k + 1) * TWH],
                            16, list(range(NF)), epi_y, TW=TWH)
        p.pop_scope()

        for fc in range(NF):
            for c0, c1 in ntiles(T):
                p.dma(ctx["scrfa"][:, 0:c1 - c0], xs1[fc, :, c0:c1])
                p.vop("tensor_add", xsb[:, fc * T + c0: fc * T + c1],
                      xsb[:, fc * T + c0: fc * T + c1], ctx["scrfa"][:, 0:c1 - c0])

        h_ln_to(p, ctx, xsb, t16, NF)
        for c0, c1 in ntiles(T):
            w = c1 - c0
            for k in range(NF):
                p.dma(ctx["wt"][:, k * 128: k * 128 + 64],
                      wbT[k * 128:(k + 1) * 128, :], engine="gpsimd")
            for k in range(NF):
                p.mm(ctx["mmps"][0:64, 0:w], ctx["wt"][:, k * 128: k * 128 + 64],
                     t16[:, k * T + c0: k * T + c1],
                     start=(k == 0), stop=(k == NF - 1))
            p.vop("tensor_copy", lat16[:, c0:c1], ctx["mmps"][0:64, 0:w])
        for b in range(B):
            p.dma(cc3_in[b * 64:(b + 1) * 64, :],
                  lat16[0:64, b * SLOC:(b + 1) * SLOC])

        p.push_scope()
        p._tps4 = p.ps("s4_tps", [128, 128], BF16)
        v1buf = p.sb("s4_v1buf", [128, NF * T], BF16)
        def epi_v1(p_, mi, c0, c1, ps):
            p_.vop("tensor_scalar", v1buf[:, mi * T + c0: mi * T + c1],
                   ps[:, 0:c1 - c0], bvc[:, mi:mi + 1], None, ALU.add)
        h_proj(p, ctx, wvT, t16, NF, list(range(NF)), epi_v1)
        for b in range(B):
            for tt in range(8):
                for mc in range(NF):
                    p.transpose(p._tps4[:, :], v1buf[:, mc * T + b * SLOC + tt * 128:
                                                     mc * T + b * SLOC + (tt + 1) * 128],
                                identb[:, :])
                    p.vop("tensor_copy", vt_sb[:, mc * 128:(mc + 1) * 128],
                          p._tps4[:, :])
                p.dma(cc3_in[128 + b * 1024 + tt * 128: 128 + b * 1024 + (tt + 1) * 128, :],
                      vt_sb[:, :])
        p.pop_scope()
        p.pop_scope()
        p.collective("AllGather", ALU.bypass, RG8, [cc3_in.opt()], [cc3_out.opt()])

        p.push_scope()
        lf = p.sb("s5_lf", [64, SEQ], BF16)
        vT5 = p.sb("s5_vT", [128, 1024], BF16)
        vT5b = p.sb("s5_vTb", [128, 1024], BF16)
        aoL = p.sb("s5_aoL", [128, NF * QT], BF16)
        P165 = p.sb("s5_P16", [128, QT], BF16)
        P165b = p.sb("s5_P16b", [128, QT], BF16)
        mskb5 = p.sb("s5_mskb", [128, QT], BF16)
        mskb5b = p.sb("s5_mskbb", [128, QT], BF16)
        negc5 = p.sb("s5_negc", [128, 1], F32)
        crow5 = p.sb("s5_crow", [1, 4], F32)
        ysb = p.sb("s5_ysb", [128, QT], F32)
        bo5 = p.sb("s5_bo", [128, NF], F32)
        y2 = p.ps("s5_y2", [128, 512], F32)
        y3 = p.ps("s5_y3", [128, 512], F32)

        p.dma(bo5[:, :], bo_in[:, :])
        for b in range(B):
            tb0 = b * SLOC
            for r in range(8):
                p.dma(lf[:, r * 1024:(r + 1) * 1024],
                      cc3_out[r * 2176 + b * 64: r * 2176 + (b + 1) * 64, :])
            p.vop("memset", crow5[0:1, 0:1], 0.0)
            for j0 in range(0, SEQ, 512):
                p.sop("activation", ctx["scr16a"][0:64, 0:512],
                      lf[0:64, j0:j0 + 512], AF.Square)
                p.mm(ctx["rps"][0:1, 0:512], ctx["onesP"][0:64, 0:1],
                     ctx["scr16a"][0:64, 0:512])
                p.vop("reduce_max", crow5[0:1, 1:2], ctx["rps"][0:1, 0:512], AX.X)
                p.vop("tensor_tensor", crow5[0:1, 0:1], crow5[0:1, 0:1],
                      crow5[0:1, 1:2], ALU.max)
            p.vop("tensor_scalar", crow5[0:1, 2:3], crow5[0:1, 0:1],
                  -1.0 / 8.0, None, ALU.mult)
            p.mm(ctx["bc1"][:, 0:1], ctx["ones1f"][0:1, :], crow5[0:1, 2:3])
            p.vop("tensor_copy", negc5[:, 0:1], ctx["bc1"][:, 0:1])

            for qt in range(2):
                ql = tb0 + qt * QT
                khi = KHI[qt]
                for dh in range(2):
                    ybanks = [ctx["bc1"], ctx["mmps"], y2, y3]
                    for kc in range(khi):
                        last = (kc == khi - 1)
                        sbank = ctx["bc2"] if kc % 2 == 0 else ctx["mmps2"]
                        Pu = P165 if kc % 2 == 0 else P165b
                        mk = mskb5 if kc % 2 == 0 else mskb5b
                        vb = vT5 if kc % 2 == 0 else vT5b
                        p.mm(sbank[:, 0:QT], lf[0:64, kc * KC:(kc + 1) * KC],
                             lat16[0:64, ql:ql + QT])
                        p.sop("activation", Pu[:, 0:QT], sbank[:, 0:QT],
                              AF.Exp, bias=negc5[:, 0:1], scale=1.0 / 8.0)
                        p.dma(mk[:, 0:QT], masks[qt, kc, :, :], engine="gpsimd")
                        p.vop("tensor_mul", Pu[:, 0:QT], Pu[:, 0:QT],
                              mk[:, 0:QT])
                        if dh == 0:
                            p.mm(ctx["rps"][0:1, 0:QT], ctx["onesP"][:, 0:1],
                                 Pu[:, 0:QT], start=(kc == 0), stop=last)
                        p.dma(vb[:, :],
                              cc3_out[(kc // 8) * 2176 + 128 + b * 1024 +
                                      (kc % 8) * 128:
                                      (kc // 8) * 2176 + 128 + b * 1024 +
                                      (kc % 8 + 1) * 128, :], engine="gpsimd")
                        for mc in range(4):
                            p.mm(ybanks[mc][:, 0:QT],
                                 vb[:, (dh * 4 + mc) * 128:(dh * 4 + mc + 1) * 128],
                                 Pu[:, 0:QT], start=(kc == 0), stop=last)
                    if dh == 0:
                        p.vop("reciprocal", ctx["rows"][0][0:1, 0:QT],
                              ctx["rps"][0:1, 0:QT])
                    for mc in range(4):
                        p.vop("tensor_copy", ysb[:, 0:QT], ybanks[mc][:, 0:QT])
                        p.mm(ctx["bc2"][:, 0:QT], ctx["ones1f"][0:1, :],
                             ctx["rows"][0][0:1, 0:QT])
                        p.vop("tensor_mul",
                              aoL[:, (dh * 4 + mc) * QT:(dh * 4 + mc + 1) * QT],
                              ysb[:, 0:QT], ctx["bc2"][:, 0:QT])
                def epi_x3(p_, mi, c0, c1, ps):
                    d = xsb[:, mi * T + ql + c0: mi * T + ql + c1]
                    p_.vop("tensor_scalar", ctx["scrfa"][:, 0:c1 - c0],
                           ps[:, 0:c1 - c0], bo5[:, mi:mi + 1], None, ALU.add)
                    p_.vop("tensor_add", d, d, ctx["scrfa"][:, 0:c1 - c0])
                h_proj(p, ctx, woT, aoL, NF, list(range(NF)), epi_x3, TW=QT)
        p.pop_scope()

        p.push_scope()
        t166 = p.sb("s6_t16", [128, NF * T], BF16)
        gtok6 = p.sb("s6_gtok", [128, 16 * 4], F32)
        growE6 = p.sb("s6_growE", [1, TWQ], F32)
        rwsb6 = p.sb("s6_rwsb", [128, NF * 4], BF16)
        rbsb6 = p.sb("s6_rbsb", [1, 4], F32)
        ninf6 = p.sb("s6_ninf", [128, 4], F32)
        l2nd6 = p.sb("s6_l2nd", [128, 4], F32)
        lg86 = p.sb("s6_lg8", [128, 4], F32)
        m1c6 = p.sb("s6_m1c", [128, 1], F32)
        m2c6 = p.sb("s6_m2c", [128, 1], F32)
        m3c6 = p.sb("s6_m3c", [128, 1], F32)
        msk16 = p.sb("s6_msk1", [128, 4], U8)
        msk26 = p.sb("s6_msk2", [128, 4], U8)
        g86 = p.sb("s6_g8", [128, 4], F32)
        bcol6 = p.sb("s6_bcol", [128, NF], F32)

        p.dma(rbsb6[0:1, :], router_b6[0:1, :])
        for fc in range(NF):
            p.dma(rwsb6[:, fc * 4:(fc + 1) * 4],
                  router_w6[fc * 128:(fc + 1) * 128, :], engine="gpsimd")
        h_ln_to(p, ctx, xsb, t166, NF)

        for tt in range(16):
            for fc in range(NF):
                p.mm(ctx["mmps"][0:128, 0:4],
                     t166[:, fc * T + tt * 128: fc * T + (tt + 1) * 128],
                     rwsb6[:, fc * 4:(fc + 1) * 4],
                     start=(fc == 0), stop=False)
            p.mm(ctx["mmps"][0:128, 0:4], ctx["ones1f"][0:1, :], rbsb6[0:1, 0:4],
                 start=False, stop=True)
            p.vop("tensor_copy", lg86[:, :], ctx["mmps"][0:128, 0:4])
            p.vop("reduce_max", m1c6[:, 0:1], lg86[:, :], AX.X)
            p.vop("tensor_scalar", msk16[:, :], lg86[:, :], m1c6[:, 0:1], None,
                  ALU.is_ge)
            p.vop("tensor_copy", l2nd6[:, :], msk16[:, :])
            p.vop("scalar_tensor_tensor", l2nd6[:, :], l2nd6[:, :], -1e30,
                  lg86[:, :], ALU.mult, ALU.add)
            p.vop("reduce_max", m2c6[:, 0:1], l2nd6[:, :], AX.X)
            p.vop("tensor_scalar", msk26[:, :], lg86[:, :], m2c6[:, 0:1], None,
                  ALU.is_ge)
            p.vop("tensor_scalar", m3c6[:, 0:1], m2c6[:, 0:1], -1.0, None, ALU.mult)
            p.sop("activation", m3c6[:, 0:1], m1c6[:, 0:1], AF.Sigmoid,
                  bias=m3c6[:, 0:1])
            p.vop("tensor_copy", g86[:, :], msk16[:, :])
            p.vop("tensor_scalar", g86[:, :], g86[:, :], m3c6[:, 0:1], None,
                  ALU.mult)
            p.vop("tensor_copy", ninf6[:, :], msk26[:, :])
            p.vop("tensor_copy", lg86[:, :], msk16[:, :])
            p.vop("tensor_mul", lg86[:, :], lg86[:, :], ninf6[:, :])
            p.vop("tensor_sub", ninf6[:, :], ninf6[:, :], lg86[:, :])
            p.vop("tensor_scalar", m1c6[:, 0:1], m3c6[:, 0:1], -1.0, 1.0,
                  ALU.mult, ALU.add)
            p.vop("tensor_scalar", ninf6[:, :], ninf6[:, :], m1c6[:, 0:1], None,
                  ALU.mult)
            p.vop("tensor_add", gtok6[:, tt * 4:(tt + 1) * 4], g86[:, :],
                  ninf6[:, :])

        p.push_scope()
        xg6 = p.sb("s6_xg", [128, NF * TWQ], BF16)
        hid6 = p.sb("s6_hid", [128, 32 * TWQ], BF16)
        for quarter in range(4):
            t0 = quarter * TWQ
            for e in range(4):
                for tt in range(4):
                    gtt = quarter * 4 + tt
                    p.mm(ctx["rps"][0:1, 0:128],
                         gtok6[:, gtt * 4 + e: gtt * 4 + e + 1], idff[:, :])
                    p.vop("tensor_copy", growE6[0:1, tt * 128:(tt + 1) * 128],
                          ctx["rps"][0:1, 0:128])
                for c0, c1 in ntiles(TWQ):
                    w = c1 - c0
                    p.mm(ctx["bc1"][:, 0:w], ctx["ones1f"][0:1, :],
                         growE6[0:1, c0:c1])
                    for fc in range(NF):
                        p.vop("tensor_mul", xg6[:, fc * TWQ + c0: fc * TWQ + c1],
                              t166[:, fc * T + t0 + c0: fc * T + t0 + c1],
                              ctx["bc1"][:, 0:w])
                def epi_h16(p_, mi, c0, c1, ps):
                    p_.sop("activation", hid6[:, mi * TWQ + c0: mi * TWQ + c1],
                           ps[:, 0:c1 - c0], AF.Silu)
                h_proj_view(p, ctx, w1_6, e,
                            lambda k: t166[:, k * T + t0: k * T + t0 + TWQ],
                            NF, list(range(32)), epi_h16, TW=TWQ)
                def epi_h36(p_, mi, c0, c1, ps):
                    d = hid6[:, mi * TWQ + c0: mi * TWQ + c1]
                    p_.vop("tensor_mul", d, d, ps[:, 0:c1 - c0])
                h_proj_view(p, ctx, w3_6, e,
                            lambda k: xg6[:, k * TWQ:(k + 1) * TWQ],
                            NF, list(range(32)), epi_h36, TW=TWQ)
                def epi_y6(p_, mi, c0, c1, ps):
                    d = xsb[:, mi * T + t0 + c0: mi * T + t0 + c1]
                    p_.vop("tensor_add", d, d, ps[:, 0:c1 - c0])
                h_proj_view(p, ctx, w2_6, e,
                            lambda k: hid6[:, k * TWQ:(k + 1) * TWQ],
                            32, list(range(NF)), epi_y6, TW=TWQ)
        p.pop_scope()

        h_ln_to(p, ctx, xsb, t166, NF)
        for wT, b_in2, outT in ((wqT6, bq6, qs2), (wkT6, bk6, ks2),
                                (wvT6, bv6, vs2)):
            p.dma(bcol6[:, :], b_in2[:, :])
            def epi_qkv(p_, mi, c0, c1, ps, o=outT):
                p_.vop("tensor_scalar", ctx["scr16a"][:, 0:c1 - c0],
                       ps[:, 0:c1 - c0], bcol6[:, mi:mi + 1], None, ALU.add)
                p_.dma(o[mi, :, c0:c1], ctx["scr16a"][:, 0:c1 - c0])
            h_proj(p, ctx, wT, t166, NF, list(range(NF)), epi_qkv)
        p.pop_scope()

        p.push_scope()
        p._tps4 = p.ps("s6_tps", [128, 128], BF16)
        edg = p.sb("s6_edg", [128, 1024], BF16)
        for b in range(B):
            for side in range(2):
                tok0 = b * SLOC + (0 if side == 0 else SLOC - 128)
                for fc in range(NF):
                    p.dma(edg[:, fc * 128:(fc + 1) * 128], ks2[fc, :, tok0:tok0 + 128],
                          engine="gpsimd")
                for fc in range(NF):
                    p.dma(cc4k_in[b * 1024 + fc * 128: b * 1024 + (fc + 1) * 128,
                                  side * 128:(side + 1) * 128],
                          edg[:, fc * 128:(fc + 1) * 128])
                for fc in range(NF):
                    p.dma(edg[:, fc * 128:(fc + 1) * 128], vs2[fc, :, tok0:tok0 + 128],
                          engine="gpsimd")
                for fc in range(NF):
                    p.transpose(p._tps4[:, :], edg[:, fc * 128:(fc + 1) * 128],
                                identb[:, :])
                    p.vop("tensor_copy", vt_sb[:, fc * 128:(fc + 1) * 128],
                          p._tps4[:, :])
                p.dma(cc4v_in[b * 256 + side * 128: b * 256 + (side + 1) * 128, :],
                      vt_sb[:, 0:1024])
        p.pop_scope()
        p.collective("AllGather", ALU.bypass, RG8, [cc4k_in.opt()], [cc4k_out.opt()])
        p.collective("AllGather", ALU.bypass, RG8, [cc4v_in.opt()], [cc4v_out.opt()])

        p.push_scope()
        p._tps4 = p.ps("s7_tps", [128, 128], BF16)
        qsb7 = p.sb("s7_qsb", [128, NF * SLOC], BF16)
        ksb7 = p.sb("s7_ksb", [128, NF * HL], BF16)
        vsb7 = p.sb("s7_vsb", [128, 10 * 1024], BF16)
        ao7 = p.sb("s7_ao", [128, NF * SLOC], BF16)
        P167 = p.sb("s7_P16", [128, 384], BF16)
        P167b = p.sb("s7_P16b", [128, 384], BF16)
        mskb7 = p.sb("s7_mskb", [128, 384], BF16)
        mskb7b = p.sb("s7_mskbb", [128, 384], BF16)
        negc7 = p.sb("s7_negc", [128, 1], F32)
        crow7 = p.sb("s7_crow", [1, 4], F32)
        osb7 = p.sb("s7_osb", [64, 128], F32)
        osb7b = p.sb("s7_osbb", [64, 128], F32)
        ops7 = p.ps("s7_ops2", [64, 512], F32)
        bo77 = p.sb("s7_bo", [128, NF], F32)
        wrc7 = p.sb("s7_wrc", [128, NF], F32)
        ohc = p.sb("s7_ohc", [128, 16], F32)
        edg7 = p.sb("s7_edg", [128, 1024], BF16)

        p.dma(bo77[:, :], bo7[:, :])
        p.dma(wrc7[:, :], wr_in[:, :])
        for r in range(8):
            p.vop("tensor_scalar", ohc[:, r:r + 1], pidc[:, 0:1],
                  float(r + 1), None, ALU.is_equal)
            p.vop("tensor_scalar", ohc[:, 8 + r:9 + r], pidc[:, 0:1],
                  float(r - 1), None, ALU.is_equal)

        for b in range(B):
            tb0 = b * SLOC
            for fc in range(NF):
                p.dma(qsb7[:, fc * SLOC:(fc + 1) * SLOC], qs2[fc, :, tb0:tb0 + SLOC],
                      engine="gpsimd")
                p.dma(ksb7[:, fc * HL + 128: fc * HL + 128 + SLOC],
                      ks2[fc, :, tb0:tb0 + SLOC], engine="gpsimd")
                p.vop("memset", ksb7[:, fc * HL: fc * HL + 128], 0.0)
                p.vop("memset", ksb7[:, fc * HL + 1152: fc * HL + 1280], 0.0)
            for side in range(2):
                dst_off = 0 if side == 0 else 1152
                src_side = 1 if side == 0 else 0
                for r in range(8):
                    for fc in range(NF):
                        p.dma(edg7[:, fc * 128:(fc + 1) * 128],
                              cc4k_out[r * 2048 + b * 1024 + fc * 128:
                                       r * 2048 + b * 1024 + (fc + 1) * 128,
                                       src_side * 128:(src_side + 1) * 128],
                              engine="gpsimd")
                    oh = ohc[:, r:r + 1] if side == 0 else ohc[:, 8 + r:9 + r]
                    for fc in range(NF):
                        p.vop("scalar_tensor_tensor",
                              ksb7[:, fc * HL + dst_off: fc * HL + dst_off + 128],
                              edg7[:, fc * 128:(fc + 1) * 128], oh,
                              ksb7[:, fc * HL + dst_off: fc * HL + dst_off + 128],
                              ALU.mult, ALU.add)
            for kb in range(10):
                p.vop("memset", vsb7[:, kb * 1024:(kb + 1) * 1024], 0.0)
            for tt in range(8):
                for fc in range(NF):
                    p.dma(edg7[:, fc * 128:(fc + 1) * 128],
                          vs2[fc, :, tb0 + tt * 128: tb0 + (tt + 1) * 128],
                          engine="gpsimd")
                for fc in range(NF):
                    p.transpose(p._tps4[:, :], edg7[:, fc * 128:(fc + 1) * 128],
                                identb[:, :])
                    p.vop("tensor_copy", vsb7[:, (tt + 1) * 1024 + fc * 128:
                                              (tt + 1) * 1024 + (fc + 1) * 128],
                          p._tps4[:, :])
            for side in range(2):
                kb = 0 if side == 0 else 9
                src_side = 1 if side == 0 else 0
                for r in range(8):
                    p.dma(edg7[:, :],
                          cc4v_out[r * 512 + b * 256 + src_side * 128:
                                   r * 512 + b * 256 + (src_side + 1) * 128, :],
                          engine="gpsimd")
                    oh = ohc[:, r:r + 1] if side == 0 else ohc[:, 8 + r:9 + r]
                    p.vop("scalar_tensor_tensor",
                          vsb7[:, kb * 1024:(kb + 1) * 1024], edg7[:, :], oh,
                          vsb7[:, kb * 1024:(kb + 1) * 1024], ALU.mult, ALU.add)
            p.vop("memset", crow7[0:1, 0:1], 0.0)
            p.vop("memset", crow7[0:1, 1:2], 0.0)
            for fc in range(NF):
                for c0, c1 in ntiles(SLOC):
                    p.sop("activation", ctx["scr16a"][:, 0:c1 - c0],
                          qsb7[:, fc * SLOC + c0: fc * SLOC + c1], AF.Square)
                    p.mm(ctx["rps"][0:1, 0:c1 - c0], ctx["onesP"][:, 0:1],
                         ctx["scr16a"][:, 0:c1 - c0])
                    p.vop("reduce_max", crow7[0:1, 2:3],
                          ctx["rps"][0:1, 0:c1 - c0], AX.X)
                    p.vop("tensor_tensor", crow7[0:1, 0:1], crow7[0:1, 0:1],
                          crow7[0:1, 2:3], ALU.max)
                for c0, c1 in ntiles(HL):
                    p.sop("activation", ctx["scr16a"][:, 0:c1 - c0],
                          ksb7[:, fc * HL + c0: fc * HL + c1], AF.Square)
                    p.mm(ctx["rps"][0:1, 0:c1 - c0], ctx["onesP"][:, 0:1],
                         ctx["scr16a"][:, 0:c1 - c0])
                    p.vop("reduce_max", crow7[0:1, 2:3],
                          ctx["rps"][0:1, 0:c1 - c0], AX.X)
                    p.vop("tensor_tensor", crow7[0:1, 1:2], crow7[0:1, 1:2],
                          crow7[0:1, 2:3], ALU.max)
            p.sop("activation", crow7[0:1, 0:1], crow7[0:1, 0:1], AF.Sqrt)
            p.sop("activation", crow7[0:1, 1:2], crow7[0:1, 1:2], AF.Sqrt)
            p.vop("tensor_mul", crow7[0:1, 2:3], crow7[0:1, 0:1], crow7[0:1, 1:2])
            p.vop("tensor_scalar", crow7[0:1, 2:3], crow7[0:1, 2:3],
                  -1.0 / 8.0, None, ALU.mult)
            p.mm(ctx["bc1"][:, 0:1], ctx["ones1f"][0:1, :], crow7[0:1, 2:3])
            p.vop("tensor_copy", negc7[:, 0:1], ctx["bc1"][:, 0:1])

            for qb in range(8):
                mk = mskb7 if qb % 2 == 0 else mskb7b
                p.dma(mk[:, :], masks2[b, qb, :, :], engine="gpsimd")
                for h in range(16):
                    hfc = h // 2
                    hr = 64 * (h % 2)
                    sbank = ctx["bc2"] if h % 2 == 0 else ctx["mmps2"]
                    Pu = P167 if h % 2 == 0 else P167b
                    dbank = ctx["rps"] if h % 2 == 0 else ctx["rps2"]
                    obank = ctx["mmps"] if h % 2 == 0 else ops7
                    ro = 0 if h % 2 == 0 else 2
                    ou = osb7 if h % 2 == 0 else osb7b
                    for blk in range(3):
                        p.mm(sbank[:, blk * 128:(blk + 1) * 128],
                             ksb7[hr:hr + 64,
                                  hfc * HL + (qb + blk) * 128:
                                  hfc * HL + (qb + blk + 1) * 128],
                             qsb7[hr:hr + 64,
                                  hfc * SLOC + qb * 128: hfc * SLOC + (qb + 1) * 128],
                             tile_position=(hr, 0))
                    p.sop("activation", Pu[:, :], sbank[:, 0:384], AF.Exp,
                          bias=negc7[:, 0:1], scale=1.0 / 8.0)
                    p.vop("tensor_mul", Pu[:, :], Pu[:, :], mk[:, :])
                    p.mm(dbank[0:1, 0:384], ctx["onesP"][:, 0:1], Pu[:, :])
                    p.vop("tensor_copy", ctx["rows"][ro][0:1, 0:384],
                          dbank[0:1, 0:384])
                    p.vop("tensor_add", ctx["rows"][ro][0:1, 0:128],
                          ctx["rows"][ro][0:1, 0:128],
                          ctx["rows"][ro][0:1, 128:256])
                    p.vop("tensor_add", ctx["rows"][ro][0:1, 0:128],
                          ctx["rows"][ro][0:1, 0:128],
                          ctx["rows"][ro][0:1, 256:384])
                    p.vop("reciprocal", ctx["rows"][ro + 1][0:1, 0:128],
                          ctx["rows"][ro][0:1, 0:128])
                    for blk in range(3):
                        p.mm(obank[0:64, 0:128],
                             vsb7[:, (qb + blk) * 1024 + h * 64:
                                  (qb + blk) * 1024 + (h + 1) * 64],
                             Pu[:, blk * 128:(blk + 1) * 128],
                             start=(blk == 0), stop=(blk == 2))
                    p.mm(ctx["bc1"][:, 0:128], ctx["ones1f"][0:1, :],
                         ctx["rows"][ro + 1][0:1, 0:128])
                    p.vop("tensor_copy", ou[:, :], obank[0:64, 0:128])
                    p.vop("tensor_mul",
                          ao7[hr:hr + 64, hfc * SLOC + qb * 128:
                              hfc * SLOC + (qb + 1) * 128],
                          ou[:, :], ctx["bc1"][0:64, 0:128])
            def epi_x5(p_, mi, c0, c1, ps):
                d = xsb[:, mi * T + tb0 + c0: mi * T + tb0 + c1]
                p_.vop("tensor_scalar", ctx["scrfa"][:, 0:c1 - c0],
                       ps[:, 0:c1 - c0], bo77[:, mi:mi + 1], None, ALU.add)
                p_.vop("tensor_add", d, d, ctx["scrfa"][:, 0:c1 - c0])
            h_proj(p, ctx, woT7, ao7, NF, list(range(NF)), epi_x5, TW=SLOC)

        p.pop_scope()
        p.push_scope()
        t167 = p.sb("s7b_t16", [128, NF * T], BF16)
        scrow_g = p.sb("s7b_scrow", [1, T], F32)
        h_ln_to(p, ctx, xsb, t167, NF)
        wrc7b = p.sb("s7b_wrc", [128, NF], F32)
        p.dma(wrc7b[:, :], wr_in[:, :])
        for c0, c1 in ntiles(T):
            w = c1 - c0
            for fc in range(NF):
                p.vop("tensor_scalar", ctx["scr16a"][:, 0:w],
                      t167[:, fc * T + c0: fc * T + c1], wrc7b[:, fc:fc + 1],
                      None, ALU.mult)
                p.mm(ctx["rps"][0:1, 0:w], ctx["onesP"][:, 0:1],
                     ctx["scr16a"][:, 0:w], start=(fc == 0), stop=(fc == NF - 1))
            p.vop("tensor_copy", scrow_g[0:1, c0:c1], ctx["rps"][0:1, 0:w])
        for b in range(B):
            p.dma(cc5_in[b:b + 1, :], scrow_g[0:1, b * SLOC:(b + 1) * SLOC])
        wrb = p.sb("s7b_wrb", [128, NF], BF16)
        p.vop("tensor_copy", wrb[:, :], wrc7b[:, :])
        for tt in range(16):
            for fc in range(NF):
                p.mm(ctx["mmps"][0:128, 0:1],
                     t167[:, fc * T + tt * 128: fc * T + (tt + 1) * 128],
                     wrb[:, fc:fc + 1], start=(fc == 0), stop=(fc == NF - 1))
            p.vop("tensor_copy", sco_cols[:, tt:tt + 1], ctx["mmps"][0:128, 0:1])
        p.pop_scope()
        p.collective("AllGather", ALU.bypass, RG8, [cc5_in.opt()], [cc5_out.opt()])

        emit_sec8(p, ctx, xsb, sco_cols, idff, identb, dict(
            cc5_out=cc5_out, pos_cols=pos_cols, W1m=W1m, b1m=b1m, W2mT=W2mT,
            b2m=b2m, wqT3=wqT3, wkT3=wkT3, wvT3=wvT3, woT3=woT3, bq3=bq3,
            bk3=bk3, bv3=bv3, bo3=bo3, W1aT=W1aT, b1a=b1a, W2aT=W2aT,
            b2a_col=b2a_col, sigT=sigT, rw3=rw3, rb3=rb3, rw1=rw1, rw3e=rw3e,
            rw2=rw2, sw1=sw1, sw3=sw3, sw2=sw2, x8_out=x8_out))

    def body2(p):
        body(p)
    nc = p.build(body2)
    return nc


def emit_sec8(p, ctx, xsb, scols_sb, idff, identb, io):
    TWQ = 512
    cc5_out = io["cc5_out"]

    p.push_scope()
    t16 = p.sb("s8_t16", [128, NF * T], BF16)
    big = p.sb("s8_big", [128, 16384], BF16)
    rankc = p.sb("s8_rankc", [128, 16], F32)
    scch = p.sb("s8_scch", [1, 512], F32)
    posr = p.sb("s8_posr", [1, 512], F32)
    posi = p.sb("s8_posi", [1, 512], I32)
    pcols = p.sb("s8_pcols", [128, 16], F32)
    cnt = p.sb("s8_cnt", [128, 1], F32)
    colA = p.sb("s8_colA", [128, 1], F32)
    bcolA = p.sb("s8_bcolA", [128, 32], F32)
    bcolB = p.sb("s8_bcolB", [128, NF], F32)
    bcolC = p.sb("s8_bcolC", [128, NF], F32)
    bcolD = p.sb("s8_bcolD", [128, NF], F32)
    b2acol = p.sb("s8_b2acol", [64, 1], F32)
    qc = p.sb("s8_qc", [128, T], BF16)
    kc_ = p.sb("s8_kc", [128, T], BF16)
    vc = p.sb("s8_vc", [128, T], BF16)
    grow = ctx["rows"][2]
    wmrow = ctx["rows"][3]
    r0row = ctx["rows"][0]
    r1row = ctx["rows"][1]
    gt16 = p.sb("s8_gt16", [128, 16 * 16], F32)
    t64 = p.sb("s8_t64", [64, T], BF16)
    gtokE = p.sb("s8_gtokE", [128, 16 * 8], F32)
    ninf = p.sb("s8_ninf", [128, 8], F32)
    l2nd = p.sb("s8_l2nd", [128, 8], F32)
    lg8 = p.sb("s8_lg8", [128, 8], F32)
    m1c = p.sb("s8_m1c", [128, 1], F32)
    m2c = p.sb("s8_m2c", [128, 1], F32)
    m3c = p.sb("s8_m3c", [128, 1], F32)
    msk1 = p.sb("s8_msk1", [128, 8], U8)
    msk2 = p.sb("s8_msk2", [128, 8], U8)
    g8 = p.sb("s8_g8", [128, 8], F32)
    rb8 = p.sb("s8_rb8", [1, 8], F32)
    sg = p.sb("s8_sg", [64, 16], BF16)
    rwsb = p.sb("s8_rwsb", [128, NF * 8], BF16)

    p.dma(pcols[:, :], io["pos_cols"][:, :])

    h_ln_to(p, ctx, xsb, t16, NF)
    for fc in range(NF):
        for c0, c1 in ntiles(T):
            p.vop("tensor_add", xsb[:, fc * T + c0: fc * T + c1],
                  xsb[:, fc * T + c0: fc * T + c1],
                  t16[:, fc * T + c0: fc * T + c1])
    for b in range(B):
        for tt in range(8):
            p.vop("memset", rankc[:, b * 8 + tt: b * 8 + tt + 1], 0.0)
        for j0 in range(0, SEQ, 512):
            p.dma(scch[0:1, :],
                  cc5_out[(j0 // 1024) * 2 + b:(j0 // 1024) * 2 + b + 1,
                          (j0 % 1024):(j0 % 1024) + 512])
            p.gop("iota", posi[0:1, :], [[1, 512]], base=j0, channel_multiplier=0)
            p.vop("tensor_copy", posr[0:1, :], posi[0:1, :])
            p.mm(ctx["bc1"][:, 0:512], ctx["ones1f"][0:1, :], scch[0:1, :])
            p.mm(ctx["bc2"][:, 0:512], ctx["ones1f"][0:1, :], posr[0:1, :])
            for tt in range(8):
                ci = b * 8 + tt
                p.vop("tensor_scalar", ctx["scrfa"][:, 0:512],
                      ctx["bc1"][:, 0:512], scols_sb[:, ci:ci + 1], None,
                      ALU.is_gt)
                p.vop("tensor_scalar", ctx["scrfb"][:, 0:512],
                      ctx["bc1"][:, 0:512], scols_sb[:, ci:ci + 1], None,
                      ALU.is_equal)
                p.vop("tensor_scalar", ctx["scr16a"][:, 0:512],
                      ctx["bc2"][:, 0:512], pcols[:, ci:ci + 1], None, ALU.is_lt)
                p.vop("tensor_mul", ctx["scrfb"][:, 0:512],
                      ctx["scrfb"][:, 0:512], ctx["scr16a"][:, 0:512])
                p.vop("tensor_add", ctx["scrfa"][:, 0:512],
                      ctx["scrfa"][:, 0:512], ctx["scrfb"][:, 0:512])
                p.vop("reduce_sum", cnt[:, 0:1], ctx["scrfa"][:, 0:512], AX.X)
                p.vop("tensor_add", rankc[:, ci:ci + 1], rankc[:, ci:ci + 1],
                      cnt[:, 0:1])
    for b in range(B):
        tb0 = b * SLOC
        for tt in range(8):
            ci = b * 8 + tt
            p.vop("tensor_scalar", colA[:, 0:1], rankc[:, ci:ci + 1],
                  4096.0, None, ALU.is_lt)
            p.mm(ctx["rps"][0:1, 0:128], colA[:, 0:1], idff[:, :])
            p.vop("tensor_copy", grow[0:1, tb0 + tt * 128: tb0 + (tt + 1) * 128],
                  ctx["rps"][0:1, 0:128])
            p.mm(ctx["rps"][0:1, 0:128], scols_sb[:, ci:ci + 1], idff[:, :])
            p.vop("tensor_copy", wmrow[0:1, tb0 + tt * 128: tb0 + (tt + 1) * 128],
                  ctx["rps"][0:1, 0:128])
    p.sop("activation", wmrow[0:1, :], wmrow[0:1, :], AF.Sigmoid)
    p.vop("tensor_mul", wmrow[0:1, :], wmrow[0:1, :], grow[0:1, :])

    p.dma(bcolA[:, :], io["b1m"][:, :])
    p.dma(bcolB[:, :], io["b2m"][:, :])
    for quarter in range(4):
        t0 = quarter * TWQ

        def epi_g(p_, mi, c0, c1, ps):
            w = c1 - c0
            u = ctx["scrfa"]
            p_.vop("tensor_scalar", u[:, 0:w], ps[:, 0:w],
                   bcolA[:, mi:mi + 1], None, ALU.add)
            p_.sop("activation", ctx["scrfb"][:, 0:w], u[:, 0:w], AF.Square)
            p_.vop("tensor_mul", ctx["scrfb"][:, 0:w], ctx["scrfb"][:, 0:w],
                   u[:, 0:w])
            p_.vop("scalar_tensor_tensor", ctx["scrfb"][:, 0:w],
                   ctx["scrfb"][:, 0:w], 0.044715, u[:, 0:w], ALU.mult, ALU.add)
            p_.sop("activation", ctx["scrfb"][:, 0:w], ctx["scrfb"][:, 0:w],
                   AF.Tanh, scale=0.7978845608028654)
            p_.vop("tensor_scalar", ctx["scrfb"][:, 0:w], ctx["scrfb"][:, 0:w],
                   1.0, None, ALU.add)
            p_.vop("tensor_scalar", u[:, 0:w], u[:, 0:w], 0.5, None, ALU.mult)
            p_.vop("tensor_mul", big[:, mi * TWQ + c0: mi * TWQ + c1],
                   u[:, 0:w], ctx["scrfb"][:, 0:w])

        def rd_t(k, t0=t0):
            return t16[:, k * T + t0: k * T + t0 + TWQ]
        h_proj2(p, ctx, io["W1m"], rd_t, NF, list(range(32)), epi_g, TW=TWQ)

        def epi_u(p_, mi, c0, c1, ps, t0=t0):
            w = c1 - c0
            p_.mm(ctx["bc1"][:, 0:w], ctx["ones1f"][0:1, :],
                  wmrow[0:1, t0 + c0: t0 + c1])
            p_.vop("tensor_scalar", ctx["scrfa"][:, 0:w], ps[:, 0:w],
                   bcolB[:, mi:mi + 1], None, ALU.add)
            p_.vop("tensor_mul", ctx["scrfa"][:, 0:w], ctx["scrfa"][:, 0:w],
                   ctx["bc1"][:, 0:w])
            d = xsb[:, mi * T + t0 + c0: mi * T + t0 + c1]
            p_.vop("tensor_add", d, d, ctx["scrfa"][:, 0:w])
        h_proj2(p, ctx, io["W2mT"], lambda k: big[:, k * TWQ:(k + 1) * TWQ],
                32, list(range(NF)), epi_u, TW=TWQ)

    h_ln_to(p, ctx, xsb, t16, NF)
    p.dma(bcolB[:, :], io["b1a"][:, :])

    def epi_r(p_, mi, c0, c1, ps):
        p_.vop("tensor_scalar", ctx["scrfa"][:, 0:c1 - c0], ps[:, 0:c1 - c0],
               bcolB[:, mi:mi + 1], None, ALU.add)
        p_.sop("activation", big[:, mi * T + c0: mi * T + c1],
               ctx["scrfa"][:, 0:c1 - c0], AF.Relu)
    h_proj(p, ctx, io["W1aT"], t16, NF, list(range(NF)), epi_r)
    p.dma(b2acol[:, :], io["b2a_col"][:, :])
    for c0, c1 in ntiles(T):
        w = c1 - c0
        for k in range(NF):
            p.dma(ctx["wt"][:, k * 128: k * 128 + 64],
                  io["W2aT"][k * 128:(k + 1) * 128, :], engine="gpsimd")
        for k in range(NF):
            p.mm(ctx["mmps"][0:64, 0:w], ctx["wt"][:, k * 128: k * 128 + 64],
                 big[:, k * T + c0: k * T + c1],
                 start=(k == 0), stop=(k == NF - 1))
        p.vop("tensor_scalar", ctx["scrfa"][0:64, 0:w], ctx["mmps"][0:64, 0:w],
              b2acol[0:64, 0:1], None, ALU.add)
        p.vop("tensor_copy", t64[:, c0:c1], ctx["scrfa"][0:64, 0:w])
    p.dma(sg[:, :], io["sigT"][:, :], engine="gpsimd")
    for tt in range(16):
        p.mm(ctx["mmps"][0:128, 0:16], t64[:, tt * 128:(tt + 1) * 128], sg[:, :])
        p.vop("tensor_copy", gt16[:, tt * 16:(tt + 1) * 16],
              ctx["mmps"][0:128, 0:16])
        sl = gt16[:, tt * 16:(tt + 1) * 16]
        p.vop("reduce_max", m1c[:, 0:1], sl, AX.X)
        p.vop("tensor_scalar", m2c[:, 0:1], m1c[:, 0:1], -1.0, None, ALU.mult)
        p.sop("activation", sl, sl, AF.Exp, bias=m2c[:, 0:1])
        p.vop("reduce_sum", m1c[:, 0:1], sl, AX.X)
        p.vop("reciprocal", m2c[:, 0:1], m1c[:, 0:1])
        p.vop("tensor_scalar", sl, sl, m2c[:, 0:1], None, ALU.mult)

    p.dma(bcolA[:, 0:NF], io["bq3"][:, :])
    p.dma(bcolB[:, :], io["bk3"][:, :])
    p.dma(bcolC[:, :], io["bv3"][:, :])
    p.dma(bcolD[:, :], io["bo3"][:, :])
    for fc in range(NF):
        for wT, bcol, dst, scal in ((io["wqT3"], bcolA, qc, 0.125),
                                    (io["wkT3"], bcolB, kc_, None),
                                    (io["wvT3"], bcolC, vc, None)):
            for k in range(NF):
                p.dma(ctx["wt"][:, k * 128:(k + 1) * 128],
                      wT[k * 128:(k + 1) * 128, fc * 128:(fc + 1) * 128],
                      engine="gpsimd")
            for c0, c1 in ntiles(T):
                w = c1 - c0
                for k in range(NF):
                    p.mm(ctx["mmps"][:, 0:w], ctx["wt"][:, k * 128:(k + 1) * 128],
                         t16[:, k * T + c0: k * T + c1],
                         start=(k == 0), stop=(k == NF - 1))
                if scal is None:
                    p.vop("tensor_scalar", dst[:, c0:c1], ctx["mmps"][:, 0:w],
                          bcol[:, fc:fc + 1], None, ALU.add)
                else:
                    p.vop("tensor_scalar", dst[:, c0:c1], ctx["mmps"][:, 0:w],
                          bcol[:, fc:fc + 1], scal, ALU.add, ALU.mult)
        for hh in range(2):
            hr = hh * 64
            h_ = fc * 2 + hh
            for tt in range(16):
                p.mm(ctx["rps"][0:1, 0:128],
                     gt16[:, tt * 16 + h_: tt * 16 + h_ + 1], idff[:, :])
                p.vop("tensor_copy", grow[0:1, tt * 128:(tt + 1) * 128],
                      ctx["rps"][0:1, 0:128])
            for t_ in range(2):
                for s_ in range(2):
                    rrow = r0row if s_ == 0 else r1row
                    for c0, c1 in ntiles(SLOC):
                        w = c1 - c0
                        p.vop("tensor_mul", ctx["scrfa"][:, 0:w],
                              qc[:, t_ * SLOC + c0: t_ * SLOC + c1],
                              kc_[:, s_ * SLOC + c0: s_ * SLOC + c1])
                        p.vop("tensor_copy", ctx["scr16a"][:, 0:w],
                              ctx["scrfa"][:, 0:w])
                        p.mm(ctx["rps"][0:1, 0:w], ctx["onesP"][hr:hr + 64, 0:1],
                             ctx["scr16a"][hr:hr + 64, 0:w],
                             tile_position=(hr, 0))
                        p.vop("tensor_copy", rrow[0:1, c0:c1],
                              ctx["rps"][0:1, 0:w])
                p.vop("tensor_sub", r0row[0:1, 0:SLOC], r0row[0:1, 0:SLOC],
                      r1row[0:1, 0:SLOC])
                p.sop("activation", r0row[0:1, 0:SLOC], r0row[0:1, 0:SLOC],
                      AF.Sigmoid)
                for c0, c1 in ntiles(SLOC):
                    w = c1 - c0
                    p.mm(ctx["bc1"][:, 0:w], ctx["ones1f"][0:1, :],
                         r0row[0:1, c0:c1])
                    p.vop("tensor_sub", ctx["scrfa"][:, 0:w],
                          vc[:, 0 * SLOC + c0: 0 * SLOC + c1],
                          vc[:, 1 * SLOC + c0: 1 * SLOC + c1])
                    p.vop("tensor_mul", ctx["scrfa"][:, 0:w],
                          ctx["scrfa"][:, 0:w], ctx["bc1"][:, 0:w])
                    p.vop("tensor_add", ctx["scrfa"][:, 0:w],
                          ctx["scrfa"][:, 0:w],
                          vc[:, 1 * SLOC + c0: 1 * SLOC + c1])
                    p.mm(ctx["bc2"][:, 0:w], ctx["ones1f"][0:1, :],
                         grow[0:1, t_ * SLOC + c0: t_ * SLOC + c1])
                    p.vop("tensor_mul", ctx["scrfa"][hr:hr + 64, 0:w],
                          ctx["scrfa"][hr:hr + 64, 0:w],
                          ctx["bc2"][hr:hr + 64, 0:w])
                    p.vop("tensor_copy",
                          big[hr:hr + 64, fc * T + t_ * SLOC + c0:
                              fc * T + t_ * SLOC + c1],
                          ctx["scrfa"][hr:hr + 64, 0:w])

    def epi_x7(p_, mi, c0, c1, ps):
        d = xsb[:, mi * T + c0: mi * T + c1]
        p_.vop("tensor_scalar", ctx["scrfa"][:, 0:c1 - c0], ps[:, 0:c1 - c0],
               bcolD[:, mi:mi + 1], None, ALU.add)
        p_.vop("tensor_add", d, d, ctx["scrfa"][:, 0:c1 - c0])
    h_proj(p, ctx, io["woT3"], big, NF, list(range(NF)), epi_x7)

    h_ln_to(p, ctx, xsb, t16, NF)
    p.dma(rb8[0:1, :], io["rb3"][0:1, :])
    for fc in range(NF):
        p.dma(rwsb[:, fc * 8:(fc + 1) * 8],
              io["rw3"][fc * 128:(fc + 1) * 128, :], engine="gpsimd")
    for tt in range(16):
        for fc in range(NF):
            p.mm(ctx["mmps"][0:128, 0:8],
                 t16[:, fc * T + tt * 128: fc * T + (tt + 1) * 128],
                 rwsb[:, fc * 8:(fc + 1) * 8], start=(fc == 0), stop=False)
        p.mm(ctx["mmps"][0:128, 0:8], ctx["ones1f"][0:1, :], rb8[0:1, 0:8],
             start=False, stop=True)
        p.vop("tensor_copy", lg8[:, :], ctx["mmps"][0:128, 0:8])
        p.vop("reduce_max", m1c[:, 0:1], lg8[:, :], AX.X)
        p.vop("tensor_scalar", msk1[:, :], lg8[:, :], m1c[:, 0:1], None,
              ALU.is_ge)
        p.vop("tensor_copy", l2nd[:, :], msk1[:, :])
        p.vop("scalar_tensor_tensor", l2nd[:, :], l2nd[:, :], -1e30, lg8[:, :],
              ALU.mult, ALU.add)
        p.vop("reduce_max", m2c[:, 0:1], l2nd[:, :], AX.X)
        p.vop("tensor_scalar", msk2[:, :], lg8[:, :], m2c[:, 0:1], None,
              ALU.is_ge)
        p.vop("tensor_scalar", m3c[:, 0:1], m2c[:, 0:1], -1.0, None, ALU.mult)
        p.sop("activation", m3c[:, 0:1], m1c[:, 0:1], AF.Sigmoid, bias=m3c[:, 0:1])
        p.vop("tensor_copy", g8[:, :], msk1[:, :])
        p.vop("tensor_scalar", g8[:, :], g8[:, :], m3c[:, 0:1], None, ALU.mult)
        p.vop("tensor_copy", ninf[:, :], msk2[:, :])
        p.vop("tensor_copy", lg8[:, :], msk1[:, :])
        p.vop("tensor_mul", lg8[:, :], lg8[:, :], ninf[:, :])
        p.vop("tensor_sub", ninf[:, :], ninf[:, :], lg8[:, :])
        p.vop("tensor_scalar", m1c[:, 0:1], m3c[:, 0:1], -1.0, 1.0,
              ALU.mult, ALU.add)
        p.vop("tensor_scalar", ninf[:, :], ninf[:, :], m1c[:, 0:1], None,
              ALU.mult)
        p.vop("tensor_add", gtokE[:, tt * 8:(tt + 1) * 8], g8[:, :], ninf[:, :])

    for e in range(10):
        routed = e < 8
        if routed:
            for tt in range(16):
                p.mm(ctx["rps"][0:1, 0:128],
                     gtokE[:, tt * 8 + e: tt * 8 + e + 1], idff[:, :])
                p.vop("tensor_copy", grow[0:1, tt * 128:(tt + 1) * 128],
                      ctx["rps"][0:1, 0:128])
        wa = io["rw1"] if routed else io["sw1"]
        wb = io["rw3e"] if routed else io["sw3"]
        wc = io["rw2"] if routed else io["sw2"]
        ei = e if routed else e - 8

        def epi_h1(p_, mi, c0, c1, ps):
            p_.sop("activation", big[:, mi * T + c0: mi * T + c1],
                   ps[:, 0:c1 - c0], AF.Silu)
        h_proj_view(p, ctx, wa, ei, lambda k: t16[:, k * T:(k + 1) * T],
                    NF, [0, 1, 2, 3], epi_h1, TW=T)

        def epi_h3(p_, mi, c0, c1, ps):
            d = big[:, mi * T + c0: mi * T + c1]
            p_.vop("tensor_mul", d, d, ps[:, 0:c1 - c0])
        h_proj_view(p, ctx, wb, ei, lambda k: t16[:, k * T:(k + 1) * T],
                    NF, [0, 1, 2, 3], epi_h3, TW=T)

        def epi_y(p_, mi, c0, c1, ps, routed=routed):
            w = c1 - c0
            if routed:
                p_.mm(ctx["bc1"][:, 0:w], ctx["ones1f"][0:1, :], grow[0:1, c0:c1])
                p_.vop("tensor_copy", ctx["scrfa"][:, 0:w], ps[:, 0:w])
                p_.vop("tensor_mul", ctx["scrfa"][:, 0:w], ctx["scrfa"][:, 0:w],
                       ctx["bc1"][:, 0:w])
                d = xsb[:, mi * T + c0: mi * T + c1]
                p_.vop("tensor_add", d, d, ctx["scrfa"][:, 0:w])
            else:
                d = xsb[:, mi * T + c0: mi * T + c1]
                p_.vop("tensor_add", d, d, ps[:, 0:w])
        h_proj_view(p, ctx, wc, ei, lambda k: big[:, k * T:(k + 1) * T],
                    4, list(range(NF)), epi_y, TW=T)

    for fc in range(NF):
        p.dma(io["x8_out"][fc, :, :], xsb[:, fc * T:(fc + 1) * T])
    p.pop_scope()


def stage_fused(inputs):
    x0 = np.asarray(inputs["L0_attn_x"], np.float32)
    x0s = fm(x0)
    m1 = stage_launch1(inputs, x0)
    m2 = stage_launch2(inputs, x0s, x0s)
    m4 = stage_launch4(inputs, x0s, x0s)
    m6 = stage_launch6(inputs, x0s)
    KHI1 = (SEQ - SLOC + 2 * 512) // 128
    maps = []
    for c in range(NCORES):
        mm2 = m2[c]
        mk = np.empty((2, KHI1, 128, 512), dtype=np.uint8)
        for qt in range(2):
            q0 = SLOC * c + qt * 512
            for kc in range(KHI1):
                kpos = kc * 128 + np.arange(128)[:, None]
                qpos = q0 + np.arange(512)[None, :]
                mk[qt, kc] = (kpos <= qpos).astype(np.uint8)
        lo_g = SLOC * c - 128
        m2w = np.zeros((B, 8, 128, 384), dtype=np.uint8)
        for b in range(B):
            for qb in range(8):
                qg = SLOC * c + qb * 128 + np.arange(128)[None, :]
                for blk in range(3):
                    kg = lo_g + (qb + blk) * 128 + np.arange(128)[:, None]
                    ok = (np.abs(kg - qg) <= 128) & (kg >= 0) & (kg < SEQ)
                    m2w[b, qb, :, blk * 128:(blk + 1) * 128] = ok.astype(np.uint8)
        poso = np.concatenate([np.arange(SLOC * c, SLOC * (c + 1),
                                         dtype=np.float32)] * B)
        pos_cols = np.ascontiguousarray(poso.reshape(16, 128).T)
        m8 = {}
        mm6 = m6[c]
        mp = {
            "xb": m1[c]["xb"], "w_inT": m1[c]["w_inT"],
            "mask_intra": m1[c]["mask_intra"],
            "x0": x0s[c],
            "wg1T": mm2["wg1T"], "wg2T": mm2["wg2T"], "lnw": mm2["lnw"],
            "woutT": mm2["woutT"], "g1": mm2["g1"], "wdown": mm2["wdown"],
            "wq": mm2["wq"], "wq_sw": mm2["wq_sw"], "wk": mm2["wk"],
            "wk_sw": mm2["wk_sw"], "wv": mm2["wv"], "pos": mm2["pos"],
            "invfreq": mm2["invfreq"], "signc": mm2["signc"],
            "wupT": np.ascontiguousarray(np.asarray(inputs["L0_ffn_w_up"])),
            "masks": mk,
            "g2": m4[c]["g2"], "router_w": m4[c]["router_w"],
            "w1": m4[c]["w1"], "w3": m4[c]["w3"], "w2": m4[c]["w2"],
            "wbT": m4[c]["wbT"], "wvT": m4[c]["wvT"], "bv": m4[c]["bv"],
            "woT": np.ascontiguousarray(np.asarray(inputs["L1_attn_W_o"]).T),
            "bo": np.ascontiguousarray(np.asarray(inputs["L1_attn_b_o"]).reshape(NF, 128).T),
            "router_w6": mm6["router_w"], "router_b6": mm6["router_b"],
            "w1_6": mm6["w1"], "w3_6": mm6["w3"], "w2_6": mm6["w2"],
            "wqT6": mm6["wqT"], "wkT6": mm6["wkT"], "wvT6": mm6["wvT"],
            "bq6": mm6["bq"], "bk6": mm6["bk"], "bv6": mm6["bv"],
            "woT7": np.ascontiguousarray(np.asarray(inputs["L2_attn_Wo"]).T),
            "bo7": np.ascontiguousarray(np.asarray(inputs["L2_attn_bo"]).reshape(NF, 128).T),
            "masks2": m2w,
            "wr": np.ascontiguousarray(np.asarray(inputs["L2_ffn_w_router"]).reshape(NF, 128).T),
            "pos_cols": pos_cols,
            "W1m": np.ascontiguousarray(np.asarray(inputs["L2_ffn_W1"])),
            "b1m": np.ascontiguousarray(np.asarray(inputs["L2_ffn_b1"]).reshape(32, 128).T),
            "W2mT": np.ascontiguousarray(np.asarray(inputs["L2_ffn_W2"])),
            "b2m": np.ascontiguousarray(np.asarray(inputs["L2_ffn_b2"]).reshape(NF, 128).T),
            "wqT3": np.ascontiguousarray(np.asarray(inputs["L3_attn_Wq"]).T),
            "wkT3": np.ascontiguousarray(np.asarray(inputs["L3_attn_Wk"]).T),
            "wvT3": np.ascontiguousarray(np.asarray(inputs["L3_attn_Wv"]).T),
            "woT3": np.ascontiguousarray(np.asarray(inputs["L3_attn_Wo"]).T),
            "bq3": np.ascontiguousarray(np.asarray(inputs["L3_attn_bq"]).reshape(NF, 128).T),
            "bk3": np.ascontiguousarray(np.asarray(inputs["L3_attn_bk"]).reshape(NF, 128).T),
            "bv3": np.ascontiguousarray(np.asarray(inputs["L3_attn_bv"]).reshape(NF, 128).T),
            "bo3": np.ascontiguousarray(np.asarray(inputs["L3_attn_bo"]).reshape(NF, 128).T),
            "W1aT": np.ascontiguousarray(np.asarray(inputs["L3_attn_W1"]).T),
            "b1a": np.ascontiguousarray(np.asarray(inputs["L3_attn_b1"]).reshape(NF, 128).T),
            "W2aT": np.ascontiguousarray(np.asarray(inputs["L3_attn_W2"]).T),
            "b2a_col": np.ascontiguousarray(np.asarray(inputs["L3_attn_b2"]).reshape(64, 1)),
            "sigT": np.ascontiguousarray(np.asarray(inputs["L3_attn_head_signatures"]).T),
            "rw3": np.ascontiguousarray(np.asarray(inputs["L3_ffn_router_w"])),
            "rb3": np.ascontiguousarray(np.asarray(inputs["L3_ffn_router_b"]).reshape(1, 8)),
            "rw1": np.ascontiguousarray(np.asarray(inputs["L3_ffn_routed_w1"])),
            "rw3e": np.ascontiguousarray(np.asarray(inputs["L3_ffn_routed_w3"])),
            "rw2": np.ascontiguousarray(np.asarray(inputs["L3_ffn_routed_w2"])),
            "sw1": np.ascontiguousarray(np.asarray(inputs["L3_ffn_shared_w1"])),
            "sw3": np.ascontiguousarray(np.asarray(inputs["L3_ffn_shared_w3"])),
            "sw2": np.ascontiguousarray(np.asarray(inputs["L3_ffn_shared_w2"])),
        }
        maps.append(mp)
    return maps


def kernel(**inputs):
    inputs = {k: np.asarray(v) for k, v in inputs.items()}
    nc = build_fused()
    maps = stage_fused(inputs)
    res = run_bass_kernel_spmd(nc, maps, core_ids=list(range(NCORES)))
    x8_s = [np.asarray(res.results[c]["x8"]) for c in range(NCORES)]
    return fm_inv(x8_s).astype(np.float32)


def kernel_multilaunch(**inputs):
    inputs = {k: np.asarray(v) for k, v in inputs.items()}
    cores = list(range(NCORES))
    x0 = np.asarray(inputs["L0_attn_x"], np.float32)
    x0s = fm(x0)

    nc1 = build_launch1()
    r1 = run_bass_kernel_spmd(nc1, stage_launch1(inputs, x0), core_ids=cores).results
    og_slices = []
    o_by_core = [np.asarray(r1[c]["o"]) for c in range(NCORES)]
    for c2 in range(NCORES):
        og = np.empty((NF, 128, T), np.float32)
        for h in range(8):
            for b in range(B):
                src = b * 4 + h // 2
                og[h, :, b * SLOC:(b + 1) * SLOC] = \
                    o_by_core[src][h % 2][:, SLOC * c2:SLOC * (c2 + 1)]
        og_slices.append(og)

    nc2 = build_launch2()
    r2 = run_bass_kernel_spmd(nc2, stage_launch2(inputs, x0s, og_slices),
                              core_ids=cores).results
    x1_s = [np.asarray(r["x1"]) for r in r2]
    q_s = [np.asarray(r["q"]) for r in r2]
    k_s = [np.asarray(r["k"]) for r in r2]
    v_s = [np.asarray(r["vv"]) for r in r2]

    nc3 = build_launch3()
    r3 = run_bass_kernel_spmd(nc3, stage_launch3(inputs, x1_s, q_s, k_s, v_s),
                              core_ids=cores).results
    res_s = [np.asarray(r["res"]) for r in r3]

    nc4 = build_launch4()
    r4 = run_bass_kernel_spmd(nc4, stage_launch4(inputs, x1_s, res_s),
                              core_ids=cores).results
    x2_s = [np.asarray(r["x2"]) for r in r4]
    lat_s = [np.asarray(r["lat"]) for r in r4]
    v1_s = [np.asarray(r["v1"]) for r in r4]

    KHI1 = (SEQ - SLOC + 2 * 512) // 128
    masks_pc = []
    for c in range(NCORES):
        m = np.empty((2, KHI1, 128, 512), dtype=np.uint8)
        for qt in range(2):
            q0 = SLOC * c + qt * 512
            for kc in range(KHI1):
                kpos = kc * 128 + np.arange(128)[:, None]
                qpos = q0 + np.arange(512)[None, :]
                m[qt, kc] = (kpos <= qpos).astype(np.uint8)
        masks_pc.append(m)

    nc5 = build_launch5()
    r5 = run_bass_kernel_spmd(nc5, stage_launch5(inputs, x2_s, lat_s, v1_s,
                                                 masks_pc),
                              core_ids=cores).results
    x3_s = [np.asarray(r["x3"]) for r in r5]

    nc6 = build_launch6()
    r6 = run_bass_kernel_spmd(nc6, stage_launch6(inputs, x3_s),
                              core_ids=cores).results
    x4_s = [np.asarray(r["x4"]) for r in r6]
    q2_s = [np.asarray(r["q2"]) for r in r6]
    k2_s = [np.asarray(r["k2"]) for r in r6]
    v2_s = [np.asarray(r["v2"]) for r in r6]

    nc7 = build_launch7()
    r7 = run_bass_kernel_spmd(nc7, stage_launch7(inputs, x4_s, q2_s, k2_s, v2_s),
                              core_ids=cores).results
    x5_s = [np.asarray(r["x5"]) for r in r7]
    sc_s = [np.asarray(r["sc"]) for r in r7]

    nc8 = build_launch8()
    r8 = run_bass_kernel_spmd(nc8, stage_launch8(inputs, x5_s, sc_s),
                              core_ids=cores).results
    x8_s = [np.asarray(r["x8"]) for r in r8]

    return fm_inv(x8_s).astype(np.float32)
